# version 08
# baseline (speedup 1.0000x reference)
import base64
import contextlib
import sys
import zlib

import numpy as np

try:
    import concourse.bass as bass
except ImportError:
    sys.path.insert(0, "/opt/trn_rl_repo")
    import concourse.bass as bass
import concourse.mybir as mybir
from concourse.bass_isa import ReduceOp
from concourse.bass_utils import run_bass_kernel_spmd

F32 = mybir.dt.float32
BF16 = mybir.dt.bfloat16
F16 = mybir.dt.float16
AFT = mybir.ActivationFunctionType
ALU = mybir.AluOpType
AX = mybir.AxisListType

B, S, H = 4, 2048, 2048
NH, NKV, HD = 16, 4, 128
DH = 5632
TQ = 1024
NTC = 16
NQT = 8
NHC = 16
NHB = 22
NDB = 8
EPS_LN = 1.0e-5
EPS_Q = 1.0e-5
SCAL = float(HD) ** -0.5
EXP_BIAS = -4.0


def _split3(x):
    c1 = np.float32(np.ldexp(np.floor(np.ldexp(x, 14)), -14))
    r = x - float(c1)
    c2 = np.float32(np.ldexp(np.floor(np.ldexp(r, 28)), -28))
    c3 = np.float32(r - float(c2))
    return float(c1), float(c2), float(c3)


SC_HI = float(np.float32(SCAL))
SC_LO = float(np.float32(SCAL - float(np.float32(SCAL))))
NELI = 1.0 / float(H * DH)
NELI_HI = float(np.float32(NELI))
NELI_LO = float(np.float32(NELI - NELI_HI))
I127 = 1.0 / 127.0
I127_HI = float(np.float32(I127))
I127_LO = float(np.float32(I127 - I127_HI))

ARENA = 212288
O_TB, O_FR, O_KK, O_U0, O_B0 = 0, 8192, 16384, 24576, 32768
O_XTH = 0
O_WQT0, O_WQT1 = 32768, 49152
O_WKT, O_WVT = 65536, 81920
O_KT, O_VV = 98304, 114688
O_QT = 131072
O_AOUT = 0
O_WOT = 65536
O_XRES0 = 32768
O_Y1T0 = 49152
O_XINTT = 163840
O_XQT = 131072
O_HH = 0
O_WUP = 98304
O_T1 = 98304
O_T2 = 0
O_W2P0, O_W2P1 = 45056, 67584
O_W2S0, O_W2S1 = 143360, 154624
O_TBL = 163840
O_Y1N = 167936
O_MST0, O_MST1 = 186368, 190464
O_USCR, O_BSCR, O_TMP1 = 200704, 203520, 206336
O_KRAW, O_KROT = 200704, 201728
O_TMPK, O_TMPK2 = 202752, 204800
O_RHI = 206848
O_ST4 = 203520
O_WST = 90112
O_TSQ = 90112
O_OUTT0, O_OUTT1 = 0, 8192
O_COLS = 209152
O_ONESB = 209536
O_ONESR = 209600
O_P64 = 209888
O_PT0, O_PT1 = 210176, 211200

C_M1, C_M2, C_M3 = 0, 1, 2
C_TH1P, C_TH1N, C_TH2P, C_TH2N, C_TH3P, C_TH3N = 3, 4, 5, 6, 7, 8
C_IVH, C_IVL = 9, 10
C_SMAX, C_C1, C_C3, C_CH, C_C2 = 16, 24, 32, 40, 48
C_LS, C_LQ = 56, 64
C_T0, C_T1c, C_T2c, C_T3c, C_T4c = 72, 73, 74, 75, 76
C_LM, C_LV, C_LR, C_LR2 = 86, 87, 88, 89
C_MACC = 80
C_EB = 84


class Sched:
    ENGS = ("sync", "gpsimd", "vector", "scalar", "tensor")

    def __init__(self):
        self.ops = {e: [] for e in self.ENGS}
        self.val = {}
        self.cuts = []

    def sem(self, name):
        self.val.setdefault(name, 0)

    def wait(self, eng, name, val=None):
        v = self.val[name] if val is None else val
        if v > 0:
            self.ops[eng].append(("w", name, v))

    def do(self, eng, fn, inc=None, amt=1):
        if inc is not None:
            self.val[inc] += amt
            self.ops[eng].append(("i", fn, inc, amt))
        else:
            self.ops[eng].append(("i", fn, None, 0))

    def cutpoint(self):
        self.cuts.append({e: len(self.ops[e]) for e in self.ENGS})

    def barrier(self):
        self.cutpoint()
        marks = {}
        for e in self.ENGS:
            s = "bar_" + e
            self.val[s] += 1
            self.ops[e].append(("i", (lambda ee: ee.nop()), s, 1))
            marks[e] = self.val[s]
        for e in self.ENGS:
            for o in self.ENGS:
                if o != e:
                    self.ops[e].append(("w", "bar_" + o, marks[o]))


def build_program(cut=None):
    nc = bass.Bass("TRN2", target_bir_lowering=False, debug=False,
                   detect_race_conditions=False)

    def din(name, shape, dt=F32):
        return nc.dram_tensor(name, shape, dt, kind="ExternalInput").ap()

    xT = din("xT", [H, S])
    xres = din("xres", [TQ, H])
    rope = din("rope", [128, 4096])
    wqT = din("wqT", [H, H])
    wkT = din("wkT", [H, 512])
    wvT = din("wvT", [H, 512])
    woT = din("woT", [H, H])
    w1T = din("w1T", [H, DH])
    w3T = din("w3T", [H, DH])
    w2T = din("w2T", [DH, H])
    p64 = din("p64", [128, 128])
    ones32 = din("ones32", [128, 32])
    onesr = din("onesr", [1, 128])
    out = nc.dram_tensor("out", [TQ, H], F32, kind="ExternalOutput").ap()

    es = contextlib.ExitStack()
    arena = es.enter_context(nc.sbuf_tensor("arena", [128, ARENA], mybir.dt.uint8))
    base = nc.lookup_mloc(arena).addr

    def vt(name, shape, dtype, off):
        h = nc.virtual_sbuf_tensor(name, shape, dtype)
        nc._tensor(h.name, h.shape, h.dtype, type="SB")
        mloc = nc.lookup_mloc(h)
        mloc.allocated = True
        mloc.addr = base + off
        return h

    TB = vt("TB", [128, 2048], F32, O_TB)
    FR = vt("FR", [128, 2048], F32, O_FR)
    KKt = vt("KKt", [128, 2048], F32, O_KK)
    U0 = vt("U0", [128, 2048], F32, O_U0)
    B0 = vt("B0", [128, 2048], F32, O_B0)
    XTH = vt("XTH", [128, 16384], BF16, O_XTH)
    WQT = [vt("WQT0", [128, 8192], BF16, O_WQT0), vt("WQT1", [128, 8192], BF16, O_WQT1)]
    WKT = vt("WKT", [128, 8192], BF16, O_WKT)
    WVT = vt("WVT", [128, 8192], BF16, O_WVT)
    KT = vt("KT", [128, 8192], BF16, O_KT)
    VV = vt("VV", [128, 8192], BF16, O_VV)
    QT = vt("QT", [128, 16384], BF16, O_QT)
    AOUT = vt("AOUT", [128, 16384], BF16, O_AOUT)
    WOT = vt("WOT", [128, 32768], BF16, O_WOT)
    XRS = [vt("XRS0", [128, 2048], F32, O_XRES0), vt("XRS1", [128, 2048], F32, O_XRES0 + 8192)]
    Y1T = [vt("Y1Ta", [128, 2048], F32, O_Y1T0), vt("Y1Tb", [128, 2048], F32, O_Y1T0 + 8192)]
    XINTT = vt("XINTT", [128, 2048], BF16, O_XINTT)
    XQT = vt("XQT", [128, 16384], BF16, O_XQT)
    HHt = vt("HHt", [128, 45056], F16, O_HH)
    WUP = vt("WUP", [128, 16384], BF16, O_WUP)
    T1 = vt("T1", [128, 22528], F16, O_T1)
    T2 = vt("T2", [128, 22528], F16, O_T2)
    W2P = [vt("W2P0", [128, 11264], F16, O_W2P0), vt("W2P1", [128, 11264], F16, O_W2P1)]
    W2S = [vt("W2S0", [128, 2816], F32, O_W2S0), vt("W2S1", [128, 2816], F32, O_W2S1)]
    TBL = vt("TBLS", [128, 8192], F16, O_TBL)
    Y1N = [vt(f"Y1N{t}", [128, 2048], BF16, O_Y1N + 4096 * t) for t in range(8)]
    MST = [vt("MST0", [128, 1024], F32, O_MST0), vt("MST1", [128, 1024], F32, O_MST1)]
    USCR = vt("USCR", [128, 704], F32, O_USCR)
    BSCR = vt("BSCR", [128, 704], F32, O_BSCR)
    TMP1 = vt("TMP1", [128, 704], F32, O_TMP1)
    PTM4 = vt("PTM4", [128, 512], F32, O_XINTT)
    PUS4 = vt("PUS4", [128, 512], F32, O_XINTT + 2048)
    PTM5 = vt("PTM5", [128, 704], F32, O_W2S0)
    PUS5 = vt("PUS5", [128, 704], F32, O_W2S0 + 2816)
    PBS5 = vt("PBS5", [128, 704], F32, O_W2S0 + 5632)
    PTM6 = vt("PTM6", [128, 704], F32, O_TSQ + 1024)
    PUS6 = vt("PUS6", [128, 704], F32, O_TSQ + 3840)
    KRAW = vt("KRAW", [128, 512], BF16, O_KRAW)
    KROT = vt("KROT", [128, 512], BF16, O_KROT)
    TMPK = vt("TMPK", [128, 512], F32, O_TMPK)
    TMPK2 = vt("TMPK2", [128, 512], F32, O_TMPK2)
    RF = vt("RF", [1, 512], F32, O_TMPK)
    RHI = vt("RHI", [1, 512], BF16, O_RHI)
    RLO = vt("RLO", [1, 512], BF16, O_RHI + 1024)
    ST4 = [vt(f"ST4_{i}", [128, 256], F32, O_ST4 + 1024 * i) for i in range(4)]
    WST = [vt(f"WST{i}", [128, 512], F32, O_WST + 2048 * i) for i in range(4)]
    TSQ = vt("TSQ", [128, 2048], F32, O_TSQ)
    OUTT = [vt("OUTT0", [128, 2048], F32, O_OUTT0), vt("OUTT1", [128, 2048], F32, O_OUTT1)]
    COLS = vt("COLS", [128, 96], F32, O_COLS)
    ONESB = vt("ONESB", [128, 32], BF16, O_ONESB)
    ONESR = vt("ONESR", [1, 128], BF16, O_ONESR)
    P64S = vt("P64S", [128, 128], BF16, O_P64)
    PT = [vt("PTa", [128, 512], BF16, O_PT0), vt("PTb", [128, 512], BF16, O_PT1)]

    pb = [es.enter_context(nc.psum_tensor(f"pb{i}", [128, 512], F32)) for i in range(8)]

    sems = {}
    sem_names = (["c0", "m0", "m1", "x0", "x1", "tX", "tH", "u0", "u1", "u2", "u3",
                  "s0", "s1", "g0", "g1", "gq0", "gq1", "gw1", "gw2", "gc",
                  "a0", "a1", "eT", "eV", "eC", "eG"] + ["bar_" + e for e in Sched.ENGS])
    for s in sem_names:
        sems[s] = es.enter_context(nc.semaphore(s))

    sch = Sched()
    for s in sem_names:
        sch.sem(s)

    def C(i):
        return COLS[:, i:i + 1]

    def vsync():
        sch.wait("vector", "eV", sch.val["eV"])

    def vsmall(fn):
        sch.do("vector", fn, "eV")
        vsync()

    sch.do("sync", lambda e: e.dma_start(TB[:, :], rope[:, 0:2048]), "c0", 16)
    sch.do("sync", lambda e: e.dma_start(FR[:, :], rope[:, 2048:4096]), "c0", 16)
    sch.do("gpsimd", lambda e: e.dma_start(ONESB[:, :], ones32), "gc", 16)
    sch.do("gpsimd", lambda e: e.dma_start(ONESR[:, :], onesr), "gc", 16)
    sch.do("gpsimd", lambda e: e.dma_start(P64S[:, :], p64), "gc", 16)
    mk_cl = sch.val["c0"]

    sch.cutpoint()
    sch.wait("vector", "c0", mk_cl)
    sch.do("vector", lambda e: e.tensor_copy(TBL[:, 0:2048], TB[:, :]))
    sch.do("vector", lambda e: e.tensor_copy(TBL[:, 2048:4096], FR[:, :]))
    sch.do("vector", lambda e: e.tensor_scalar(U0[:, :], TB[:, :], SC_HI, None, ALU.mult))
    sch.do("vector", lambda e: e.scalar_tensor_tensor(U0[:, :], TB[:, :], SC_LO, U0[:, :], ALU.mult, ALU.add))
    sch.do("vector", lambda e: e.tensor_copy(TBL[:, 4096:6144], U0[:, :]))
    sch.do("vector", lambda e: e.tensor_scalar(U0[:, :], FR[:, :], SC_HI, None, ALU.mult))
    sch.do("vector", lambda e: e.scalar_tensor_tensor(U0[:, :], FR[:, :], SC_LO, U0[:, :], ALU.mult, ALU.add))
    sch.do("vector", lambda e: e.tensor_copy(TBL[:, 6144:8192], U0[:, :]), "eV")
    mk_tbl = sch.val["eV"]
    sch.cutpoint()

    sch.do("vector", lambda e: e.memset(COLS[:, C_MACC:C_MACC + 3], 0.0))
    sch.do("vector", lambda e: e.memset(COLS[:, C_EB:C_EB + 1], EXP_BIAS), "eV")
    mean_marks = {}
    mean_red = {}
    mean_jobs = []
    for mat_ap, rows, idx, w in [(w1T, H, 0, 704), (w3T, H, 1, 704), (w2T, DH, 2, 1024)]:
        cols = mat_ap.shape[1]
        assert cols % w == 0
        for rc in range(rows // 128):
            for cc in range(cols // w):
                mean_jobs.append((mat_ap[rc * 128:(rc + 1) * 128, cc * w:(cc + 1) * w], idx, w))
    mean_state = {"next": 0, "glob": 0, "idx": {}}

    def pump_mean(k):
        for _ in range(k):
            if mean_state["next"] >= len(mean_jobs):
                return
            src, idx, w = mean_jobs[mean_state["next"]]
            mean_state["next"] += 1
            g = mean_state["glob"]
            par = g % 2
            if g >= 2:
                sch.wait("sync", "eV", mean_marks[g - 2])
            ms = "m%d" % par
            sch.do("sync", (lambda d, s_: lambda e: e.dma_start(d, s_))(MST[par][:, 0:w], src), ms, 16)
            mk = sch.val[ms]
            sch.wait("vector", ms, mk)
            sch.do("vector", (lambda p, w_: lambda e: e.reduce_sum(C(C_T0 + p), MST[p][:, 0:w_], AX.X, apply_absolute_value=True))(par, w), "eV")
            mean_red[g] = sch.val["eV"]
            if g >= 1:
                pidx = mean_state["idx"][g - 1]
                sch.wait("vector", "eV", mean_red[g - 1])
                sch.do("vector", (lambda p, a: lambda e: e.tensor_tensor(C(a), C(a), C(C_T0 + p), ALU.add))((g - 1) % 2, C_MACC + pidx), "eV")
            else:
                sch.do("vector", lambda e: e.engine_nop(), "eV")
            mean_marks[g] = sch.val["eV"]
            mean_state["idx"][g] = idx
            mean_state["glob"] += 1

    sch.do("gpsimd", lambda e: e.dma_start(WKT[:, :].rearrange("p (hc o) -> p hc o", hc=16), wkT.rearrange("(hc p) o -> p hc o", p=128)), "g0", 16)
    sch.do("gpsimd", lambda e: e.dma_start(WVT[:, :].rearrange("p (hc o) -> p hc o", hc=16), wvT.rearrange("(hc p) o -> p hc o", p=128)), "g0", 16)
    mk_wkv = sch.val["g0"]
    sch.wait("gpsimd", "eV", mk_tbl)
    sch.do("gpsimd", lambda e: e.dma_start(XTH[:, :].rearrange("p (hc t) -> p hc t", hc=16), xT[:, 0:1024].rearrange("(hc p) t -> p hc t", p=128)), "g1", 16)
    mk_xth0 = sch.val["g1"]
    sch.cutpoint()

    ev = {}

    def rope_evac(psum_bank, dst_ap, tok0, scaled):
        sch.do("vector", (lambda b: lambda e: e.tensor_copy(KRAW[:, :], pb[b][:, :]))(psum_bank), "eV")
        ev[("pb", psum_bank)] = sch.val["eV"]
        mk_raw = sch.val["eV"]
        sch.wait("tensor", "eV", mk_raw)
        sch.wait("tensor", "eV", ev.get("rotfree", 0))
        sch.do("tensor", lambda e: e.matmul(pb[2][:, :], P64S[:, :], KRAW[:, :], start=True, stop=True), "eT")
        mk_rot = sch.val["eT"]
        co = (6144 if scaled else 2048) + tok0
        so = (4096 if scaled else 0) + tok0
        sch.wait("vector", "eT", mk_rot)
        sch.do("vector", (lambda o: lambda e: e.tensor_tensor(TMPK[:, :], KRAW[:, :], TBL[:, o:o + 512], ALU.mult))(co))
        sch.do("vector", (lambda o: lambda e: e.tensor_tensor(TMPK2[:, :], pb[2][:, :], TBL[:, o:o + 512], ALU.mult))(so))
        sch.do("vector", (lambda d: lambda e: e.tensor_tensor(d, TMPK[:, :], TMPK2[:, :], ALU.add))(dst_ap), "eV")
        ev["rotfree"] = sch.val["eV"]
        return sch.val["eV"]

    def kv_proj_half(half, xmark):
        for g in range(NKV):
            for ts in range(2):
                pump_mean(2)
                bank = (g * 2 + ts) % 2
                tok0 = half * 1024 + ts * 512
                sch.wait("tensor", "eV", ev.get(("pb", bank), 0))
                sch.wait("tensor", "g0", mk_wkv)
                sch.wait("tensor", "g1", xmark)
                for hc in range(NHC):
                    sch.do("tensor", (lambda g_, hc_, ts_, b_: lambda e: e.matmul(
                        pb[b_][:, :], WKT[:, hc_ * 512 + g_ * 128:hc_ * 512 + g_ * 128 + 128],
                        XTH[:, hc_ * 1024 + ts_ * 512:hc_ * 1024 + ts_ * 512 + 512],
                        start=(hc_ == 0), stop=(hc_ == NHC - 1)))(g, hc, ts, bank),
                        "eT" if hc == NHC - 1 else None)
                sch.wait("vector", "eT", sch.val["eT"])
                rope_evac(bank, KT[:, g * 2048 + tok0:g * 2048 + tok0 + 512], tok0, True)
        for tcr in range(8):
            pump_mean(2)
            tc = half * 8 + tcr
            bank = tcr % 2
            sch.wait("tensor", "eV", ev.get(("pb", bank), 0))
            for hc in range(NHC):
                sch.do("tensor", (lambda tcr_, hc_, b_: lambda e: e.matmul(
                    pb[b_][:, :], XTH[:, hc_ * 1024 + tcr_ * 128:hc_ * 1024 + tcr_ * 128 + 128],
                    WVT[:, hc_ * 512:hc_ * 512 + 512],
                    start=(hc_ == 0), stop=(hc_ == NHC - 1)))(tcr, hc, bank),
                    "eT" if hc == NHC - 1 else None)
            sch.wait("vector", "eT", sch.val["eT"])
            sch.do("vector", (lambda tc_, b_: lambda e: e.tensor_copy(VV[:, tc_ * 512:tc_ * 512 + 512], pb[b_][:, :]))(tc, bank), "eV")
            ev[("pb", bank)] = sch.val["eV"]

    kv_proj_half(0, mk_xth0)
    sch.cutpoint()

    wq_done = {}
    for ob in range(4):
        sch.cutpoint()
        par = ob % 2
        if ob == 0:
            sch.wait("gpsimd", "eV", mk_tbl)
        if ob >= 2:
            sch.wait("gpsimd", "eT", wq_done.get(ob - 2, 0))
        sch.do("gpsimd", (lambda o: lambda e: e.dma_start(
            WQT[o % 2][:, :].rearrange("p (hc o) -> p hc o", hc=16), wqT[:, o * 512:(o + 1) * 512].rearrange("(hc p) o -> p hc o", p=128)))(ob), "gq%d" % (ob % 2), 16)
        mk_wq = sch.val["gq%d" % (ob % 2)]
        for oc2 in range(4):
            n = ob * 4 + oc2
            for ts in range(2):
                pump_mean(2)
                bank = (oc2 * 2 + ts) % 2
                sch.wait("tensor", "eV", ev.get(("pb", bank), 0))
                sch.wait("tensor", "gq%d" % (ob % 2), mk_wq)
                for hc in range(NHC):
                    sch.do("tensor", (lambda p_, oc_, hc_, ts_, b_: lambda e: e.matmul(
                        pb[b_][:, :], WQT[p_][:, hc_ * 512 + oc_ * 128:hc_ * 512 + oc_ * 128 + 128],
                        XTH[:, hc_ * 1024 + ts_ * 512:hc_ * 1024 + ts_ * 512 + 512],
                        start=(hc_ == 0), stop=(hc_ == NHC - 1)))(par, oc2, hc, ts, bank),
                        "eT" if hc == NHC - 1 else None)
                if oc2 == 3 and ts == 1:
                    wq_done[ob] = sch.val["eT"]
                sch.wait("vector", "eT", sch.val["eT"])
                rope_evac(bank, QT[:, n * 1024 + ts * 512:n * 1024 + ts * 512 + 512], ts * 512, False)

    sch.cutpoint()
    sch.wait("gpsimd", "eT", sch.val["eT"])
    sch.do("gpsimd", lambda e: e.dma_start(XTH[:, :].rearrange("p (hc t) -> p hc t", hc=16), xT[:, 1024:2048].rearrange("(hc p) t -> p hc t", p=128)), "g1", 16)
    kv_proj_half(1, sch.val["g1"])

    sch.barrier()

    sch.do("gpsimd", lambda e: e.dma_start(
        WOT[:, 0:16384].rearrange("p (hc o) -> p hc o", hc=16), woT[:, 0:1024].rearrange("(hc p) o -> p hc o", p=128)), "gw1", 16)

    pt_exp = {}
    pt_pv = {}
    pv_free = 0
    rb_free = 0
    for n in range(NH):
        g = n // 4
        for qh in range(2):
            pump_mean(6)
            sch.wait("tensor", "eV", pv_free)
            for kc in range(NTC):
                sbank = kc % 2
                ptb = kc % 2
                sch.wait("tensor", "eC", pt_exp.get(sbank, 0))
                sch.do("tensor", (lambda g_, kc_, n_, qh_, b_: lambda e: e.matmul(
                    pb[b_][:, :], KT[:, g_ * 2048 + kc_ * 128:g_ * 2048 + kc_ * 128 + 128],
                    QT[:, n_ * 1024 + qh_ * 512:n_ * 1024 + qh_ * 512 + 512],
                    start=True, stop=True))(g, kc, n, qh, sbank), "eT")
                mk_s = sch.val["eT"]
                sch.wait("scalar", "eT", mk_s)
                sch.wait("scalar", "eT", pt_pv.get(ptb, 0))
                sch.do("scalar", (lambda b_, p_: lambda e: e.activation(
                    PT[p_][:, :], pb[b_][:, :], AFT.Exp, bias=C(C_EB), scale=1.0))(sbank, ptb), "eC")
                pt_exp[sbank] = sch.val["eC"]
                mk_exp = sch.val["eC"]
                sch.wait("tensor", "eC", mk_exp)
                sch.do("tensor", (lambda kc_, g_, p_: lambda e: e.matmul(
                    pb[2][:, :], VV[:, kc_ * 512 + g_ * 128:kc_ * 512 + g_ * 128 + 128],
                    PT[p_][:, :], start=(kc_ == 0), stop=(kc_ == NTC - 1)))(kc, g, ptb),
                    None)
                sch.do("tensor", (lambda kc_, p_: lambda e: e.matmul(
                    pb[3][0:1, :], ONESB[:, 0:1], PT[p_][:, :],
                    start=(kc_ == 0), stop=(kc_ == NTC - 1)))(kc, ptb),
                    "eT")
                pt_pv[ptb] = sch.val["eT"]
            mk_pv = sch.val["eT"]
            sch.wait("vector", "eT", mk_pv)
            sch.do("vector", lambda e: e.reciprocal(RF[:, :], pb[3][0:1, :]))
            sch.do("vector", lambda e: e.tensor_copy(RHI[:, :], RF[:, :]))
            sch.do("vector", lambda e: e.tensor_tensor(RLO[:, :], RF[:, :], RHI[:, :], ALU.subtract))
            sch.do("vector", lambda e: e.tensor_copy(TMPK2[:, :], pb[2][:, :]), "eV")
            mk_rl = sch.val["eV"]
            pv_free = mk_rl
            sch.wait("tensor", "eV", mk_rl)
            sch.wait("tensor", "eV", rb_free)
            sch.do("tensor", lambda e: e.matmul(pb[4][:, :], ONESR[:, :], RHI[:, :], start=True, stop=False))
            sch.do("tensor", lambda e: e.matmul(pb[4][:, :], ONESR[:, :], RLO[:, :], start=False, stop=True), "eT")
            mk_rb = sch.val["eT"]
            sch.wait("vector", "eT", mk_rb)
            sch.do("vector", (lambda n_, qh_: lambda e: e.tensor_tensor(
                AOUT[:, n_ * 1024 + qh_ * 512:n_ * 1024 + qh_ * 512 + 512],
                TMPK2[:, :], pb[4][:, :], ALU.mult))(n, qh), "eV")
            rb_free = sch.val["eV"]
    mk_attn = sch.val["eV"]

    sch.wait("gpsimd", "eV", mk_attn)
    sch.do("gpsimd", lambda e: e.dma_start(
        WOT[:, 16384:32768].rearrange("p (hc o) -> p hc o", hc=16), woT[:, 1024:2048].rearrange("(hc p) o -> p hc o", p=128)), "gw2", 16)
    mk_wot = sch.val["gw2"]

    pump_mean(len(mean_jobs))
    g_last = mean_state["glob"] - 1
    sch.wait("vector", "eV", mean_red[g_last])
    vsmall((lambda p, a: lambda e: e.tensor_tensor(C(a), C(a), C(C_T0 + p), ALU.add))(g_last % 2, C_MACC + mean_state["idx"][g_last]))
    NEL = float(H * DH)
    vsmall(lambda e: e.memset(KROT[:, 2:3], 0.0))
    vsmall(lambda e: e.memset(KROT[0:1, 2:3], 1.0))
    for i in range(3):
        vsmall((lambda s: lambda e: e.tensor_copy(KRAW[:, 0:1], C(C_MACC + s)))(i))
        sch.wait("tensor", "eV", sch.val["eV"])
        sch.do("tensor", lambda e: e.matmul(pb[5][0:1, 1:2], KROT[:, 2:3], KRAW[:, 0:1], start=True, stop=True), "eT")
        sch.wait("vector", "eT", sch.val["eT"])
        vsmall(lambda e: e.tensor_copy(KROT[0:1, 3:4], pb[5][0:1, 1:2]))
        sch.wait("tensor", "eV", sch.val["eV"])
        sch.do("tensor", lambda e: e.matmul(pb[6][:, 1:2], ONESR[:, :], KROT[0:1, 3:4], start=True, stop=True), "eT")
        sch.wait("vector", "eT", sch.val["eT"])
        vsmall(lambda e: e.tensor_copy(KRAW[:, 2:3], pb[6][:, 1:2]))
        vsmall((lambda s: lambda e: e.tensor_tensor(TMPK2[:, 0:1], C(C_MACC + s), KRAW[:, 2:3], ALU.subtract))(i))
        vsmall(lambda e: e.tensor_copy(KROT[:, 0:1], TMPK2[:, 0:1]))
        vsmall(lambda e: e.tensor_tensor(TMPK[:, 0:1], TMPK2[:, 0:1], KROT[:, 0:1], ALU.subtract))
        vsmall(lambda e: e.tensor_copy(KRAW[:, 1:2], TMPK[:, 0:1]))
        sch.wait("tensor", "eV", sch.val["eV"])
        sch.do("tensor", lambda e: e.matmul(pb[5][0:1, 0:1], ONESB[:, 0:1], KRAW[:, 2:3], start=True, stop=False))
        sch.do("tensor", lambda e: e.matmul(pb[5][0:1, 0:1], ONESB[:, 0:1], KROT[:, 0:1], start=False, stop=False))
        sch.do("tensor", lambda e: e.matmul(pb[5][0:1, 0:1], ONESB[:, 0:1], KRAW[:, 1:2], start=False, stop=True), "eT")
        sch.wait("vector", "eT", sch.val["eT"])
        vsmall(lambda e: e.tensor_scalar(TMPK2[0:1, 3:4], pb[5][0:1, 0:1], NELI_HI, None, ALU.mult))
        vsmall(lambda e: e.scalar_tensor_tensor(RF[0:1, 0:1], pb[5][0:1, 0:1], NELI_LO, TMPK2[0:1, 3:4], ALU.mult, ALU.add))
        vsmall(lambda e: e.tensor_scalar(RF[0:1, 0:1], RF[0:1, 0:1], 1.0, EPS_Q, ALU.mult, ALU.max))
        vsmall(lambda e: e.tensor_copy(RHI[0:1, 0:1], RF[0:1, 0:1]))
        vsmall(lambda e: e.tensor_tensor(TMPK2[0:1, 1:2], RF[0:1, 0:1], RHI[0:1, 0:1], ALU.subtract))
        vsmall(lambda e: e.tensor_copy(RLO[0:1, 0:1], TMPK2[0:1, 1:2]))
        vsmall(lambda e: e.tensor_tensor(TMPK[0:1, 1:2], TMPK2[0:1, 1:2], RLO[0:1, 0:1], ALU.subtract))
        vsmall(lambda e: e.tensor_copy(RLO[0:1, 1:2], TMPK[0:1, 1:2]))
        sch.wait("tensor", "eV", sch.val["eV"])
        sch.do("tensor", lambda e: e.matmul(pb[6][:, 0:1], ONESR[:, :], RHI[0:1, 0:1], start=True, stop=False))
        sch.do("tensor", lambda e: e.matmul(pb[6][:, 0:1], ONESR[:, :], RLO[0:1, 0:1], start=False, stop=False))
        sch.do("tensor", lambda e: e.matmul(pb[6][:, 0:1], ONESR[:, :], RLO[0:1, 1:2], start=False, stop=True), "eT")
        sch.wait("vector", "eT", sch.val["eT"])
        vsmall((lambda s: lambda e: e.tensor_copy(C(C_M1 + s), pb[6][:, 0:1]))(i))
    for i in range(3):
        vsmall((lambda s: lambda e: e.tensor_scalar(C(C_TH1P + 2 * s), C(C_M1 + s), 0.5, None, ALU.mult))(i))
        vsmall((lambda s: lambda e: e.tensor_scalar(C(C_TH1N + 2 * s), C(C_M1 + s), -0.5, None, ALU.mult))(i))
    mk_th = sch.val["eV"]

    sch.barrier()

    xqt_mk = {}
    xint_mk = {}
    sq_mk = {}
    for t in range(NQT):
        sch.cutpoint()
        par = t % 2
        if t >= 2:
            sch.wait("sync", "eV", xint_mk.get(t - 2, 0))
        xs_ = "x%d" % (t % 2)
        sch.do("sync", (lambda t_: lambda e: e.dma_start(XRS[t_ % 2][:, :], xres[t_ * 128:(t_ + 1) * 128, :]))(t), xs_, 16)
        mk_xr = sch.val[xs_]
        for ob in range(4):
            bank = ob % 2
            sch.wait("tensor", "eV", ev.get(("yb", bank), 0))
            sch.wait("tensor", "gw1", 16)
            if ob >= 2:
                sch.wait("tensor", "gw2", mk_wot)
            for hc in range(NHC):
                wo_off = (ob // 2) * 16384 + (ob % 2) * 512
                sch.do("tensor", (lambda t_, wo_, hc_, b_: lambda e: e.matmul(
                    pb[b_][:, :], AOUT[:, hc_ * 1024 + t_ * 128:hc_ * 1024 + t_ * 128 + 128],
                    WOT[:, wo_ + hc_ * 1024:wo_ + hc_ * 1024 + 512],
                    start=(hc_ == 0), stop=(hc_ == NHC - 1)))(t, wo_off, hc, bank),
                    "eT" if hc == NHC - 1 else None)
            mk_y = sch.val["eT"]
            sch.wait("vector", "eT", mk_y)
            sch.wait("vector", xs_, mk_xr)
            sch.do("vector", (lambda t_, ob_, b_: lambda e: e.tensor_tensor(
                Y1T[t_ % 2][:, ob_ * 512:ob_ * 512 + 512], pb[b_][:, :],
                XRS[t_ % 2][:, ob_ * 512:ob_ * 512 + 512], ALU.add))(t, ob, bank), "eV")
            ev[("yb", bank)] = sch.val["eV"]
        mk_y1 = sch.val["eV"]
        sch.cutpoint()
        yt = Y1T[par]
        xrt = XRS[par]
        sch.do("vector", (lambda y: lambda e: e.reduce_sum(C(C_LM), y[:, :], AX.X))(yt), "eV")
        vsync()
        sch.do("vector", (lambda y, d: lambda e: e.scalar_tensor_tensor(
            d[:, :], y[:, :], 1.0, y[:, :], ALU.mult, ALU.mult))(yt, xrt))
        sch.do("vector", (lambda d: lambda e: e.reduce_sum(C(C_LV), d[:, :], AX.X))(xrt), "eV")
        vsync()
        vsmall(lambda e: e.tensor_scalar(C(C_LM), C(C_LM), 1.0 / 2048.0, None, ALU.mult))
        sq_mk[t] = sch.val["eV"]
        vsmall(lambda e: e.tensor_scalar(C(C_LV), C(C_LV), 1.0 / 2048.0, EPS_LN, ALU.mult, ALU.add))
        vsmall(lambda e: e.tensor_tensor(C(C_LR), C(C_LM), C(C_LM), ALU.mult))
        vsmall(lambda e: e.tensor_tensor(C(C_LV), C(C_LV), C(C_LR), ALU.subtract))
        mk_var = sch.val["eV"]
        sch.cutpoint()
        sch.wait("scalar", "eV", mk_var)
        sch.do("scalar", lambda e: e.activation(C(C_LR), C(C_LV), AFT.Sqrt), "eC")
        sch.wait("vector", "eC", sch.val["eC"])
        vsmall(lambda e: e.reciprocal(C(C_LR2), C(C_LR)))
        for _ in range(2):
            vsmall(lambda e: e.tensor_tensor(C(C_LR), C(C_LR2), C(C_LR2), ALU.mult))
            vsmall(lambda e: e.tensor_tensor(C(C_LR), C(C_LR), C(C_LV), ALU.mult))
            vsmall(lambda e: e.tensor_scalar(C(C_LR), C(C_LR), -0.5, 1.5, ALU.mult, ALU.add))
            vsmall(lambda e: e.tensor_tensor(C(C_LR2), C(C_LR2), C(C_LR), ALU.mult))
        sch.do("vector", (lambda y, d: lambda e: e.tensor_scalar(
            d[:, :], y[:, :], C(C_LM), C(C_LR2), ALU.subtract, ALU.mult))(yt, xrt))
        sch.do("vector", (lambda t_, d: lambda e: e.tensor_copy(Y1N[t_][:, :], d[:, :]))(t, xrt), "eV")
        sch.cutpoint()
        sch.do("vector", (lambda d: lambda e: e.reduce_max(C(C_LV), d[:, :], AX.X, apply_absolute_value=True))(xrt), "eV")
        vsync()
        vsmall(lambda e: e.tensor_scalar(C(C_LV), C(C_LV), 1.0, EPS_Q, ALU.mult, ALU.max))
        vsmall((lambda t_: lambda e: e.tensor_copy(C(C_SMAX + t_), C(C_LV)))(t))
        vsmall(lambda e: e.reciprocal(C(C_LR), C(C_LV)))
        vsmall(lambda e: e.tensor_scalar(C(C_LR), C(C_LR), 127.0, None, ALU.mult))
        for sl in range(4):
            sch.do("vector", (lambda d, s_: lambda e: e.tensor_scalar(
                TMP1[:, 0:512], d[:, s_ * 512:s_ * 512 + 512], C(C_LR), None, ALU.mult))(xrt, sl))
            first = True
            for j in range(7, -1, -1):
                k = float(2 ** j)
                thr = k - 128.5
                src = TMP1 if first else USCR
                sch.do("vector", (lambda s, kk, th: lambda e: e.tensor_scalar(
                    BSCR[:, 0:512], s[:, 0:512], th, -kk, ALU.is_ge, ALU.mult))(src, k, thr))
                sch.do("vector", (lambda s: lambda e: e.tensor_tensor(
                    USCR[:, 0:512], s[:, 0:512], BSCR[:, 0:512], ALU.add))(src))
                first = False
            sch.do("vector", lambda e: e.tensor_tensor(BSCR[:, 0:512], TMP1[:, 0:512], USCR[:, 0:512], ALU.subtract))
            sch.do("vector", (lambda s_: lambda e: e.tensor_scalar(
                XINTT[:, s_ * 512:s_ * 512 + 512], BSCR[:, 0:512], -128.0, None, ALU.add))(sl), "eV")
        mk_xint = sch.val["eV"]
        xint_mk[t] = mk_xint
        sch.wait("vector", "eV", mk_th)
        vsmall((lambda t_: lambda e: e.tensor_scalar(C(C_LV), C(C_SMAX + t_), I127_HI, None, ALU.mult))(t))
        vsmall((lambda t_: lambda e: e.scalar_tensor_tensor(C(C_LV), C(C_SMAX + t_), I127_LO, C(C_LV), ALU.mult, ALU.add))(t))
        vsmall((lambda t_: lambda e: e.tensor_tensor(C(C_C1 + t_), C(C_LV), C(C_M1), ALU.mult))(t))
        vsmall((lambda t_: lambda e: e.tensor_tensor(C(C_C3 + t_), C(C_LV), C(C_M2), ALU.mult))(t))
        sch.wait("sync", "eV", mk_xint)
        sch.do("sync", (lambda t_: lambda e: e.dma_start_transpose(
            XQT[:, :].rearrange("p (dc tt) -> p dc tt", dc=16)[:, :, t_ * 128:(t_ + 1) * 128],
            XINTT[:, :]))(t), "tX", 16)
        xqt_mk[t] = sch.val["tX"]

    sch.barrier()

    sch.do("vector", lambda e: e.memset(COLS[:, C_CH:C_CH + 8], 0.0), "eV")
    up_mm = {}
    tern_mk = {}
    up_tern = {}

    def up_tern_stage(hb):
        par = hb % 2
        if hb >= 2:
            sch.wait("gpsimd", "eT", up_mm.get(hb - 2, 0))
        for mi, (mat, thp, thn, wofs) in enumerate(
                [(w1T, C_TH1P, C_TH1N, 0), (w3T, C_TH2P, C_TH2N, 8192)]):
            for pc in range(8):
                slot = (mi * 8 + pc) % 4
                st = WST[slot]
                key = ("st", slot)
                sch.wait("sync", "eG", tern_mk.get(key, 0))
                us_ = "u%d" % slot
                src = mat[:, hb * 256:(hb + 1) * 256].rearrange("(dc p) c -> p dc c", p=128)[:, 2 * pc:2 * pc + 2, :]
                sch.do("sync", (lambda d, s_: lambda e: e.dma_start(d[:, :].rearrange("p (dc c) -> p dc c", dc=2), s_))(st, src), us_, 16)
                mk_pc = sch.val[us_]
                sch.wait("gpsimd", us_, mk_pc)
                dst0 = wofs + par * 4096 + pc * 512
                sch.do("gpsimd", (lambda st_, tp: lambda e: e.tensor_scalar(
                    PTM4[:, :], st_[:, :], C(tp), None, ALU.is_gt))(st, thp))
                sch.do("gpsimd", (lambda st_, tn: lambda e: e.tensor_scalar(
                    PUS4[:, :], st_[:, :], C(tn), None, ALU.is_lt))(st, thn))
                sch.do("gpsimd", (lambda d0: lambda e: e.tensor_tensor(
                    WUP[:, d0:d0 + 512], PTM4[:, :], PUS4[:, :], ALU.subtract))(dst0), "eG")
                tern_mk[key] = sch.val["eG"]
        up_tern[hb] = sch.val["eG"]

    up_tern_stage(0)
    for hb in range(NHB):
        par = hb % 2
        if hb + 1 < NHB:
            up_tern_stage(hb + 1)
        mk_tern = up_tern[hb]
        for t in range(NQT):
            b1 = t % 2
            b3 = 2 + t % 2
            sch.wait("tensor", "eV", ev.get(("h1", b1), 0))
            sch.wait("tensor", "eV", ev.get(("h3", b3), 0))
            sch.wait("tensor", "eG", mk_tern)
            sch.wait("tensor", "tX", 128)
            for dc in range(NHC):
                sch.do("tensor", (lambda t_, dc_, p_, b_: lambda e: e.matmul(
                    pb[b_][:, 0:256], XQT[:, dc_ * 1024 + t_ * 128:dc_ * 1024 + t_ * 128 + 128],
                    WUP[:, p_ * 4096 + dc_ * 256:p_ * 4096 + dc_ * 256 + 256],
                    start=(dc_ == 0), stop=(dc_ == NHC - 1)))(t, dc, par, b1), None)
            for dc in range(NHC):
                sch.do("tensor", (lambda t_, dc_, p_, b_: lambda e: e.matmul(
                    pb[b_][:, 0:256], XQT[:, dc_ * 1024 + t_ * 128:dc_ * 1024 + t_ * 128 + 128],
                    WUP[:, 8192 + p_ * 4096 + dc_ * 256:8192 + p_ * 4096 + dc_ * 256 + 256],
                    start=(dc_ == 0), stop=(dc_ == NHC - 1)))(t, dc, par, b3),
                    "eT" if dc == NHC - 1 else None)
            mk_mm = sch.val["eT"]
            up_mm[hb] = mk_mm
            sch.wait("vector", "eT", mk_mm)
            sch.do("vector", (lambda b_, t_: lambda e: e.tensor_scalar(
                ST4[0][:, :], pb[b_][:, 0:256], C(C_C1 + t_), None, ALU.mult))(b1, t), "eV")
            mk_t1 = sch.val["eV"]
            ev[("h1", b1)] = mk_t1
            sch.wait("scalar", "eV", mk_t1)
            sch.do("scalar", lambda e: e.activation(ST4[1][:, :], ST4[0][:, :], AFT.Silu), "eC")
            mk_silu = sch.val["eC"]
            sch.wait("vector", "eC", mk_silu)
            ho = t * 5632 + hb * 256
            sch.do("vector", (lambda b_, t_, o: lambda e: e.scalar_tensor_tensor(
                HHt[:, o:o + 256], pb[b_][:, 0:256], C(C_C3 + t_), ST4[1][:, :], ALU.mult, ALU.mult))(b3, t, ho), "eV")
            ev[("h3", b3)] = sch.val["eV"]
            sch.do("vector", (lambda o: lambda e: e.reduce_max(
                C(C_T0), HHt[:, o:o + 256], AX.X, apply_absolute_value=True))(ho), "eV")
            vsync()
            sch.do("vector", (lambda t_: lambda e: e.tensor_tensor(C(C_CH + t_), C(C_CH + t_), C(C_T0), ALU.max))(t), "eV")

    sch.barrier()

    tr_mk = {}
    for t in range(NQT):
        vsmall((lambda t_: lambda e: e.tensor_scalar(C(C_T0), C(C_CH + t_), 1.0, EPS_Q, ALU.mult, ALU.max))(t))
        vsmall(lambda e: e.reciprocal(C(C_T1c), C(C_T0)))
        vsmall(lambda e: e.tensor_scalar(C(C_T1c), C(C_T1c), 127.0, None, ALU.mult))
        vsmall(lambda e: e.tensor_scalar(C(C_T2c), C(C_T0), I127_HI, None, ALU.mult))
        vsmall(lambda e: e.scalar_tensor_tensor(C(C_T2c), C(C_T0), I127_LO, C(C_T2c), ALU.mult, ALU.add))
        vsmall((lambda t_: lambda e: e.tensor_tensor(C(C_C2 + t_), C(C_T2c), C(C_M3), ALU.mult))(t))
        for sl in range(8):
            o0 = t * 5632 + sl * 704
            sch.do("vector", (lambda o: lambda e: e.tensor_scalar(
                TMP1[:, :], HHt[:, o:o + 704], C(C_T1c), None, ALU.mult))(o0))
            first = True
            for j in range(7, -1, -1):
                k = float(2 ** j)
                thr = k - 128.5
                src = TMP1 if first else USCR
                sch.do("vector", (lambda s, kk, th: lambda e: e.tensor_scalar(
                    BSCR[:, :], s[:, :], th, -kk, ALU.is_ge, ALU.mult))(src, k, thr))
                sch.do("vector", (lambda s: lambda e: e.tensor_tensor(USCR[:, :], s[:, :], BSCR[:, :], ALU.add))(src))
                first = False
            sch.do("vector", lambda e: e.tensor_tensor(BSCR[:, :], TMP1[:, :], USCR[:, :], ALU.subtract))
            sch.do("vector", (lambda o: lambda e: e.tensor_scalar(
                HHt[:, o:o + 704], BSCR[:, :], -128.0, None, ALU.add))(o0), "eV")
        mk_lad = sch.val["eV"]
        sch.wait("sync", "eV", mk_lad)
        if t < 4:
            dst = T1[:, :].rearrange("p (hc tt) -> p hc tt", hc=44)[:, :, t * 128:(t + 1) * 128]
        else:
            dst = T2[:, :].rearrange("p (hc tt) -> p hc tt", hc=44)[:, :, (t - 4) * 128:(t - 3) * 128]
            sch.wait("sync", "tH", tr_mk[3])
        sch.do("sync", (lambda t_, d: lambda e: e.dma_start_transpose(d, HHt[:, t_ * 5632:(t_ + 1) * 5632]))(t, dst), "tH", 16)
        tr_mk[t] = sch.val["tH"]
    mk_trdone = sch.val["tH"]

    sch.barrier()

    sch.do("vector", lambda e: e.memset(COLS[:, C_LS:C_LS + 16], 0.0), "eV")
    sch.wait("gpsimd", "tH", mk_trdone)
    w2t_mk = {}
    zev_mk = {}
    zmm_mk = {}
    w2t_done = {}

    def w2_tern_stage(db):
        par = db % 2
        if db >= 2:
            sch.wait("gpsimd", "eT", zmm_mk.get(db - 2, 0))
        for sla in range(4):
            stg = W2S[sla % 2]
            sch.wait("sync", "eG", w2t_mk.get((db - 1, sla + 2), 0) if db >= 1 else 0)
            if sla >= 2:
                sch.wait("sync", "eG", w2t_mk.get((db, sla - 2), 0))
            ss_ = "s%d" % (sla % 2)
            src = w2T[:, db * 256:(db + 1) * 256].rearrange("(hc p) c -> p hc c", p=128)[:, sla * 11:(sla + 1) * 11, :]
            sch.do("sync", (lambda d, s_: lambda e: e.dma_start(d[:, :].rearrange("p (hc c) -> p hc c", hc=11), s_))(stg, src), ss_, 16)
            mk_sl = sch.val[ss_]
            sch.wait("gpsimd", ss_, mk_sl)
            for sl in range(4):
                o0 = sl * 704
                sch.do("gpsimd", (lambda st_, o: lambda e: e.tensor_scalar(
                    PTM6[:, :], st_[:, o:o + 704], C(C_TH3P), None, ALU.is_gt))(stg, o0))
                sch.do("gpsimd", (lambda st_, o: lambda e: e.tensor_scalar(
                    PUS6[:, :], st_[:, o:o + 704], C(C_TH3N), None, ALU.is_lt))(stg, o0))
                sch.do("gpsimd", (lambda p_, sa, o: lambda e: e.tensor_tensor(
                    W2P[p_][:, sa * 2816 + o:sa * 2816 + o + 704], PTM6[:, :], PUS6[:, :], ALU.subtract))(par, sla, o0), "eG")
            w2t_mk[(db, sla)] = sch.val["eG"]
        w2t_done[db] = sch.val["eG"]

    w2_tern_stage(0)
    for db in range(NDB):
        par = db % 2
        if db + 1 < NDB:
            w2_tern_stage(db + 1)
        mk_w2t = w2t_done[db]
        for t in range(NQT):
            bank = par * 4 + (t % 4)
            half = (t // 4) * 256
            sch.wait("tensor", "eV", zev_mk.get((db - 2, t), 0))
            sch.wait("tensor", "eG", mk_w2t)
            if db == 0:
                sch.wait("tensor", "tH", mk_trdone)
            src_t = T1 if t < 4 else T2
            for hc in range(44):
                sch.do("tensor", (lambda s_, hc_, t_, b_, h_, p_: lambda e: e.matmul(
                    pb[b_][:, h_:h_ + 256],
                    s_[:, hc_ * 512 + (t_ % 4) * 128:hc_ * 512 + (t_ % 4) * 128 + 128],
                    W2P[p_][:, hc_ * 256:hc_ * 256 + 256],
                    start=(hc_ == 0), stop=(hc_ == 43)))(src_t, hc, t, bank, half, par),
                    "eT" if hc == 43 else None)
        mk_zmm = sch.val["eT"]
        zmm_mk[db] = mk_zmm
        for t in range(NQT):
            bank = par * 4 + (t % 4)
            half = (t // 4) * 256
            sch.wait("vector", "eT", mk_zmm)
            zsl = (lambda t_, d_: Y1N[t_][:, d_ * 256:d_ * 256 + 256])(t, db)
            sch.do("vector", (lambda b_, h_, t_, z: lambda e: e.scalar_tensor_tensor(
                z, pb[b_][:, h_:h_ + 256], C(C_C2 + t_), z, ALU.mult, ALU.add))(bank, half, t, zsl), "eV")
            zev_mk[(db, t)] = sch.val["eV"]
            mk_z = sch.val["eV"]
            sch.do("vector", (lambda z: lambda e: e.reduce_sum(C(C_T1c), z, AX.X))(zsl), "eV")
            vsync()
            sch.do("vector", (lambda t_: lambda e: e.tensor_tensor(C(C_LS + t_), C(C_LS + t_), C(C_T1c), ALU.add))(t))
            sch.do("vector", (lambda z: lambda e: e.scalar_tensor_tensor(
                TSQ[:, 0:256], z, 1.0, z, ALU.mult, ALU.mult))(zsl))
            sch.do("vector", lambda e: e.reduce_sum(C(C_T2c), TSQ[:, 0:256], AX.X), "eV")
            vsync()
            sch.do("vector", (lambda t_: lambda e: e.tensor_tensor(C(C_LQ + t_), C(C_LQ + t_), C(C_T2c), ALU.add))(t), "eV")

    sch.barrier()

    st_mk = {}
    for t in range(NQT):
        par = t % 2
        if t >= 2:
            sch.wait("vector", "a%d" % (t % 2), st_mk[t - 2])
        vsmall((lambda t_: lambda e: e.tensor_scalar(C(C_T0), C(C_LS + t_), 1.0 / 2048.0, None, ALU.mult))(t))
        vsmall((lambda t_: lambda e: e.tensor_scalar(C(C_T1c), C(C_LQ + t_), 1.0 / 2048.0, EPS_LN, ALU.mult, ALU.add))(t))
        vsmall(lambda e: e.tensor_tensor(C(C_T2c), C(C_T0), C(C_T0), ALU.mult))
        vsmall(lambda e: e.tensor_tensor(C(C_T1c), C(C_T1c), C(C_T2c), ALU.subtract))
        sch.wait("scalar", "eV", sch.val["eV"])
        sch.do("scalar", lambda e: e.activation(C(C_T2c), C(C_T1c), AFT.Sqrt), "eC")
        sch.wait("vector", "eC", sch.val["eC"])
        vsmall(lambda e: e.reciprocal(C(C_T3c), C(C_T2c)))
        for _ in range(2):
            vsmall(lambda e: e.tensor_tensor(C(C_T2c), C(C_T3c), C(C_T3c), ALU.mult))
            vsmall(lambda e: e.tensor_tensor(C(C_T2c), C(C_T2c), C(C_T1c), ALU.mult))
            vsmall(lambda e: e.tensor_scalar(C(C_T2c), C(C_T2c), -0.5, 1.5, ALU.mult, ALU.add))
            vsmall(lambda e: e.tensor_tensor(C(C_T3c), C(C_T3c), C(C_T2c), ALU.mult))
        sch.do("vector", (lambda t_, p_: lambda e: e.tensor_scalar(
            OUTT[p_][:, :], Y1N[t_][:, :], C(C_T0), C(C_T3c), ALU.subtract, ALU.mult))(t, par), "eV")
        mk_o = sch.val["eV"]
        sch.wait("scalar", "eV", mk_o)
        sch.do("scalar", (lambda t_, p_: lambda e: e.dma_start(out[t_ * 128:(t_ + 1) * 128, :], OUTT[p_][:, :]))(t, par), "a%d" % par, 16)
        st_mk[t] = sch.val["a%d" % par]
    sch.wait("scalar", "a0", sch.val["a0"])
    sch.wait("scalar", "a1", sch.val["a1"])
    sch.wait("sync", "a0", sch.val["a0"])
    sch.wait("sync", "a1", sch.val["a1"])

    if cut is not None:
        sch.ops = {e: sch.ops[e][:sch.cuts[cut][e]] for e in sch.ops}
        nv = {s: 0 for s in sch.val}
        for e in sch.ops:
            for op in sch.ops[e]:
                if op[0] == "i" and op[2] is not None:
                    nv[op[2]] += op[3]
        sch.val = nv
    import os
    dumps = os.environ.get("KDUMP", "")
    if dumps:
        sch.barrier()
        loc = locals()
        for i, nm in enumerate(dumps.split(",")):
            if ":" in nm:
                b, ix = nm.split(":")
                th = loc[b][int(ix)]
                nm = b + ix
            else:
                th = loc[nm]
            dsh = list(th.shape)
            dout = nc.dram_tensor(f"dump_{nm}", dsh, th.dtype, kind="ExternalOutput").ap()
            sch.do("sync", (lambda d, t: lambda e: e.dma_start(d, t[:, :]))(dout, th), "c0", 16)
        sch.wait("sync", "c0", sch.val["c0"])
    _QUEUE = {"c0": "sync", "m0": "sync", "m1": "sync", "x0": "sync", "x1": "sync",
              "tX": "sync", "tH": "sync", "u0": "sync", "u1": "sync", "u2": "sync",
              "u3": "sync", "s0": "sync", "s1": "sync",
              "g0": "gpsimd", "g1": "gpsimd", "gq0": "gpsimd", "gq1": "gpsimd",
              "gw1": "gpsimd", "gw2": "gpsimd", "gc": "gpsimd",
              "a0": "scalar", "a1": "scalar"}
    totals = {s: 0 for s in _QUEUE}
    for e in sch.ops:
        for op in sch.ops[e]:
            if op[0] == "i" and op[2] in totals:
                totals[op[2]] += op[3]
    for s, eng in _QUEUE.items():
        if totals[s]:
            sch.ops[eng].append(("w", s, totals[s]))
    with es:
        with nc.Block() as block:
            def make(ename):
                def body(eng):
                    for op in sch.ops[ename]:
                        if op[0] == "w":
                            eng.wait_ge(sems[op[1]], op[2])
                        else:
                            ins = op[1](eng)
                            if op[2] is not None:
                                ins.then_inc(sems[op[2]], op[3])
                return body

            block.sync(make("sync"))
            block.gpsimd(make("gpsimd"))
            block.vector(make("vector"))
            block.scalar(make("scalar"))
            block.tensor(make("tensor"))
    global _LAST_SCHED, _LAST_TENSORS
    _LAST_SCHED = sch
    _LAST_TENSORS = {k: v for k, v in locals().items()
                     if k in ("QT", "KT", "VV", "AOUT", "XQT", "XINTT", "HHt",
                              "T1", "T2", "Y1N", "TBL", "COLS", "WUP", "W2P", "XTH", "WOT")}
    return nc


_LAST_SCHED = None


_NC_CACHE = None


def _host_inputs(L0_attn_x, L0_attn_Wq, L0_attn_Wk, L0_attn_Wv, L0_attn_Wo,
                 L0_ffn_w1, L0_ffn_w2, L0_ffn_w3):
    x = np.ascontiguousarray(np.asarray(L0_attn_x, dtype=np.float32))
    wq = np.asarray(L0_attn_Wq, dtype=np.float32)
    wk = np.asarray(L0_attn_Wk, dtype=np.float32)
    wv = np.asarray(L0_attn_Wv, dtype=np.float32)
    wo = np.asarray(L0_attn_Wo, dtype=np.float32)
    w1 = np.asarray(L0_ffn_w1, dtype=np.float32)
    w2 = np.asarray(L0_ffn_w2, dtype=np.float32)
    w3 = np.asarray(L0_ffn_w3, dtype=np.float32)

    shared = {
        "wqT": np.ascontiguousarray(wq.T),
        "wkT": np.ascontiguousarray(wk.T),
        "wvT": np.ascontiguousarray(wv.T),
        "woT": np.ascontiguousarray(wo.T),
        "w1T": np.ascontiguousarray(w1.T),
        "w3T": np.ascontiguousarray(w3.T),
        "w2T": np.ascontiguousarray(w2.T),
    }
    inv32 = np.frombuffer(base64.b64decode(_INV_B64), dtype=np.float32)
    t32 = np.arange(S, dtype=np.float32)
    freqs = np.einsum("i,j->ij", t32, inv32)
    dsin = np.frombuffer(zlib.decompress(base64.b64decode(_DSIN_B64)), dtype=np.int8)
    dcos = np.frombuffer(zlib.decompress(base64.b64decode(_DCOS_B64)), dtype=np.int8)
    sin64 = (np.sin(freqs).view(np.int32) + dsin.reshape(S, 64).astype(np.int32)).view(np.float32)
    cos64 = (np.cos(freqs).view(np.int32) + dcos.reshape(S, 64).astype(np.int32)).view(np.float32)
    sin_t = np.concatenate([sin64, sin64], axis=1).T
    cos_t = np.concatenate([cos64, cos64], axis=1).T
    p64 = np.zeros((128, 128), dtype=np.float32)
    for d in range(64):
        p64[d + 64, d] = -1.0
        p64[d, d + 64] = 1.0
    shared["p64"] = p64
    shared["ones32"] = np.ones((128, 32), dtype=np.float32)
    shared["onesr"] = np.ones((1, 128), dtype=np.float32)

    in_maps = []
    for c in range(8):
        b, half = c // 2, c % 2
        perm = np.concatenate([np.arange(half * TQ, (half + 1) * TQ),
                               np.arange((1 - half) * TQ, (2 - half) * TQ)])
        m = dict(shared)
        m["xT"] = np.ascontiguousarray(x[b].T[:, perm])
        m["xres"] = np.ascontiguousarray(x[b, half * TQ:(half + 1) * TQ, :])
        m["rope"] = np.ascontiguousarray(
            np.concatenate([sin_t[:, perm], cos_t[:, perm]], axis=1))
        in_maps.append(m)
    return in_maps


def kernel(L0_attn_x, L0_attn_Wq, L0_attn_Wk, L0_attn_Wv, L0_attn_Wo,
           L0_ffn_w1, L0_ffn_w2, L0_ffn_w3):
    global _NC_CACHE
    in_maps = _host_inputs(L0_attn_x, L0_attn_Wq, L0_attn_Wk, L0_attn_Wv,
                           L0_attn_Wo, L0_ffn_w1, L0_ffn_w2, L0_ffn_w3)
    if _NC_CACHE is None:
        _NC_CACHE = build_program()
    res = run_bass_kernel_spmd(_NC_CACHE, in_maps, core_ids=list(range(8)))
    y = np.empty((B, S, H), dtype=np.float32)
    for c in range(8):
        b, half = c // 2, c % 2
        y[b, half * TQ:(half + 1) * TQ, :] = np.asarray(res.results[c]["out"])
    return y


_INV_B64 = (
    "AACAP9evXT8R+T8/4D0mP5r1Dz/PU/k+m+jXPhv4uj6b6KE+BDWMPiPUcj7tR1I+hxg2PkCwHT53jQg+1n/sPc3MzD14WbE9DZSZ"
    "PU3+hD3CVWY9P3ZHPRW6LD1IkxU944YBPdJU4DxPQ8I8izmoPDmtkTwzTXw88ntaPBEzPTwK1yM8LeENPLC59TsVytQ7nES4O8yR"
    "nzt3Loo7DlJvOzg+TzsPdzM7DWkbO2+UBjsoFek6XNfJOo7JrjoOXJc6bxKDOuIBYzqMlEQ6RDsqOhZqEzqsT/85JRfdOdd0vzlg"
    "y6U5cpKPORWoeDnkU1c5U3c6ORZ5ITly1As54izyOA=="
)
_DSIN_B64 = (
    "eNrFfYuyJEmKq+v/P1p3qs7JCHeQAM9qu9tmO9uPyswIf4AQINb6t7+4/o//4n/0DXi/CP+fn4BY/+d/EftrzB8In//l+483C4Lw"
    "g7xdTfDzEbxfRPkjzWPw52PPZ4GrzWf6m8tVBC8f+r84h/va/UeXGVevgT+/ivdvv1vAY/34xYvwP3xx8JtP0Rynm/uDdjsmW/Yv"
    "1ouXXwf5wuC/HrurM0S74MPFw68Bwa8lvHh+/PsO/Pzkz3Os0x5PPvoYH/y1ebxedvzDgWL6o/zu7f8DFIDj1/EcZnQf5GFJebkY"
    "1H+A0y/g/qPgZgYv146H+eX4ZFJe2iGq+UfD+zzk573F0ceN6VMGqVt/4mcXeXyQ9yfws4X3X4Dv19dhNn7z8N+YkM/n+a33pH5k"
    "Xjw6vvA/WK/lL4Bcf4b4YwP4ven6vbzfYfj/IHjY7R/WpQE6/yBOTD76BuL8BMc2lKcd5u3SIX7HG0uNMTWWvzLsdhNPyGRQDK8u"
    "71cHiJ9T/1k+xMeG+07upwV0z48OukF9Kaevj+cuH1fxG2CBdXz+zgF+hWR+bz7EMl15wp+jhGP/vvLfWCUmRHmKcLkEmAHSxgDx"
    "94MvhsDwAhEf4kHcop7NQLU3N5E0pKW4uPw4kCgDqO0A8BvGI5jwOQGjNo/940un/40vZfYdvDl+X1JgF6FCE9/z/ocRPvYFBgPS"
    "hb8F0a+rVHzibEmg3oDjC0Bvs7lTi8M4GLziIvAbRayBES9jyB844GNWyAvEAFfwBiGr2QrsyJXpT7fvzu3CkNtKzo0o8G/xJT7W"
    "+9lDvFTeIH7i58/thx2XHDRf5HJQORx98h/BPl8zfs3k8XQi4QLPlp8rf/7eYKLn80oHwiKWbG9v+vPzUAIqnMB1PuM8qviaWDmO"
    "Mm6O0PvDuPNJE9r7YiVR7mO2PxQkIKJnamikTwwELi7Ha7uvoYJUFzFE/j1sSAgz9vFIoyEjK6CkEMPVvUjC/BwRAOiA/WQR8E/H"
    "n/lCTrkghmB0lMvZnWe4QpwuAxyiuYl+dlO2ryAm7/49iRvNAzOOyMSS9H4MvpBfOsUYQHyRzONV6JOdEdcw+/yAl0whIUG6QQBh"
    "X5jd+cXXvCPCxePd12AzQjzh5sSDs7xGX9AgWMVxNf8K+AWQ2FmIYTKYJ9i+izvCUn1H3OME/OANBGf63zONjulZ2B0278JoxVi/"
    "fAo53Pp4msdLiPp7+Q1lgPoB3Sf+wW2yv2LdCu47Drcz3benGKJ0ZHR2bBy8nDCQk6DUcCiXnFn/QJjv3W8Ok7fMQ33wOYkm2f0i"
    "BwaIYtE5M3775/A9nuHJhWy/Tx8FIP03JMPA0c27sPg9fPvNY2J0xpidLafcBKQVZrLozZGGDMcmeaBP0uPAohznwfj4ebxfguiY"
    "W1uNkUcYmlHLuWL2+ZuHIC7u8eQF8HwpRvkTLkO9YFiIgHhdiVCPOFk/bbVvysCQnN6EAqCCP89K8i4UhnGA9f35jbTo3n9YTIF9"
    "wSOFMjHiln+f7SDvvXbmTZQvREvcaUuMSRzCfB22Pz64wiijjODV4SLWM1b/NvaNLOKEQzz4K8wwcH30sZ3c1gEjAa7HD+CGA2YI"
    "x36/BGx//zm2WJG0w4Aswj1ct1vgqiZqM44iSBGoclDny/qtMLFFY7dBG7FhQsUheSAaz1TtJKH+/JxC2teNAdE0C3dy7rKE9Gsk"
    "05+f12Hibx7u2wp4A9jae/GHukFxkye3al9BftP2kTwMquh4DAHG6/ZPZYfQSOgwX6xaYxgvN+JdmgAoisO8rgnANxM45i8hVvKK"
    "QTu2PTlQXl01Up3hMgKjMGF3JaDRv6PZOYj7T8ov5NQa6NwxiNuTzNAAQmUToOEvv+vnCcn33wiUMiIcM43DPPwv747kurDvA2/Y"
    "z6t6hoib6gDYGxD+YwRLlXHA2CzmuIc39rK1FpMaGij4dFE/tea8OyL7hHqrqgDgg7c3BhdPMDcvg9pNMVIv6OTmf6hDWFIAGn8j"
    "dQ1ctAHCcT/3xxgZkK5URlfXnuAwYhM38NR/AcrkzincVK7wE4GBtwHYN0wcRSzDtnVs8AjzZq49BcPOxRoAyfvQPdaM0eIO9vEP"
    "Ko9UPkTwuTgYsFkEiu3MBu4ME+zC88Rq7qsNt5/zyuv7G4gT7pymPd97A/bpeRFgbRsHbo34hOjK78/P+pL3hiuL6VvpuD3EQZ5+"
    "38R8U0Nt3BttVIn+UimTV6Q/cuuYSYdqww/JfEKggzXP5GPMwzDfYWos2wVgVOEcekyTwIxCfsMy5N8qEG6ncN5/K03qXfT+ccGY"
    "98MD0Uq4HDKbV/hkQPBl+Z6B290GQpfuX2TCC3bipowJSzbA1IE8fRTCm+ospDN8d37+GAJ64PtHpUX6ED637LcO4dVTeK8zZmXB"
    "b951XMuCFZYswXQMTx0119DRMBkhHIGMwBkmANyt+NF/yg4AQccTuGpFqWsQYC0yzYLdwgE+5BthAHwJgFHFtTcuaIkTVXEPz1Pu"
    "KHR9rst+D1hiSM04IdbjZ6DN3L9zHCVWp0azrKIXfVCG/y0ARPqRbLubCHYqHgPre4kT6YLT9uM9AqOOetExD7QB1Ix7zZm7K/Eq"
    "Jg/Ku707MfDmARIGhEFvmzn9e5nKrAwcgYNAvWB492GB87AEjyaMwOwMcCuaqeoF+vCe674WjjCsMeY10D8JvLS7rZQGKqpAVYFj"
    "VR7nF4wgneTxgU5EKtr7+wpP7Cd/FIbK4r0vSvChvw5varmSA+IR6IP6EzMOhhe0eYm2uS6FsPatRukXShSOcGnh2aYDGm2uRO0/"
    "tzNSOxIGP1Tfv9Bny5HoVbMAQoiMk/07DjUnn4f+O14zuIE6HIePz+5Jzh6ZQawr5Lkz7j9tYF37Hqt0eL/gtPEyBt8XziRv7eVV"
    "s1F3MCn15HB5ga9rQBKG+F22bwXADvIPo9CLAT1y3Lz0nE9+R58croM7eXKtpRTZsiMuY49/GSNN9mgfIkBweGj0+NDxI0b8TWCQ"
    "NYEI7bUUeznZAbPAHJcGYvZPrKAjEl0Q1TA4XD9oEDSWiMBYd9G8Cm18U+OncdihAqKT+PujifyUUA0gFMJJL/XAcuwEVeaBddG7"
    "l1bycWu40dMsgy76kJ7vizz5snEJxHSvpgAI/5wQ20vpwfr+cJ2eG/kCc7EFkG/I8uljpS5f75tsuVdRlyWn+xufHpPYKcEV6dGa"
    "B+aY8t09yCOCBHa5AxSOmMJTYGb/ULZwtrQqDkMCV1bhv1rYj+0Mznspgburg6PpP0KZq/jxLvR030WU96ejGhkIvWEOW7OvIYBi"
    "i29+wadohpXZHWYuIqjhoDfa3G1HupS83ENq3ocTyiu9KFbWU56VE0FBM2cLQEis54NMGzvcIcVzBSj0oHMJjGQQJOs5LYFAuv+G"
    "pWD14q9+2AqiPrPQ6w25j3idmHxwjwC3GzzpNeEqoXdTiPHpvv1lfgFBgdS2n4DIgdHjvcho7C7wDh+tINmyI4qoaGnc5lmrEKcp"
    "/K2oqIuXeLi+ZHSac6QzL/PYH745fBK4vf5lRFuw4z6X4J7vNC0QUcEIPPgwH30T4pGqPGkYcKCf8S9qt9rEHyUs7GuJIQ/GsBL+"
    "vSplgzIv3y3SzTZ97tr+rnqh6XF3W9XPgtiatQ4/cezrzKC/zhshPPD/JNbaxPZ+gbOCHa+Kj8Qy4FYNB6g7KFD64L9fwFzCN6hg"
    "PvNG9IYCjf2Cik4ny3BWbg9MGDVRMhcegD27GOl/KKTH0zfPMriG8IC9f3BXlKGHlIZJKm3VV5XAR9b/Xb8BRUPJZ/T7jyBT8PJu"
    "87wnA3Mh86Yw9T+79h22Npj1DwU174tN9wEvag9zNDoxeexFtjzv4HCeCnK5zJh5pGAa4E8grfUWRMyZCcWVEZ4OZDlxE9cSuu2c"
    "/m7XflGJZMFGRWNUhRre2eDRKm+kDKaMydgUKaMHwNhS1tR6KbX9YzyPUDp432GAkZAqdJDX4w/GsH8TLmav/wKV/IY5hN2NZEqk"
    "bPbfAGAMWkQTBYTpTeUa1rKfMqZrtQNhInvDN208ipvlhWHBGX+t8loiSKxvJs6J28eURfvET60Gwst+YRxdZFtHRBA0+yLUHBeu"
    "KpB5hFPAqH4xTCtyktczo36IwUynikGtxbSQBEkD9bAFbT5XWhDc1o6taG3ytaJhT2lhzUUAEt7wUOSS7g3KgkNimJnpYvpps5PI"
    "5+9PtcxJtO6tqG3+BOkSMAnT9vcuFgzyCAd0oohVb/cE+OUFz9GoxrensTMT0CqUF91N7ruaU35UgeTsa+huqSdAZsl5TtAOW9L1"
    "W7eHaeL3oItwnBmsZooZBPPxr/NQGa7hrAjeEDeoG/CpuRvcax4jMZLp+cyz4yfJc6Q9L3UXjl8AmQryRuKRf68PN6OSxe1L80KI"
    "qHA4Nkx64bQPnMHJmDNDTdmAiezY9wFl7RSkG0MeRjaFlkw011wNhOfEgrkFebCnEB1FGq8ExN1C8JaF8B8MygjeVhVzuRc+TRHv"
    "+vZU31aspbXgRRaXXM2DguEEGpoi/jfmpyz9DxTUeReSA6JSopgByq4YpGBA3ZdC2qAAKc0kmNykoeftBGeAhguAn5fEUcAiF3mi"
    "/5H/kadBpueMjdTaKeQyGAi9TYC/FwGHv9Y8YmxRls5lmusw66FmsrUZQvC2iQeCoMUtKzmtXuE5903MYuI0Xv7qQVDBIa5cQm1E"
    "HLbxRcGnfn2QRiJAbC4Gu/65bWQOv9nUD1UZ6e2bJgb4zKQgJZHPUAKdXBe9SzAEEtgSDdCR6t0wUxqI0JOrcJhawJhBBQNUbL4w"
    "ZtLE5WJTAaOqgSt04W8reTiCkfZXNSsKMVaG+yO82zTDAAmwhnH/ySVsMeH7ZxkFAWXPMkT0EfXAsVQFrAzB4zcBd69kH7RgMj8l"
    "4hCb0o5HxYOiUUd6RQltihk5CB4O+8GAPndFwYkUR4iD5UgC+G0SC1i1nkFfMVThli+/Jc5cRUMY+4vAPYTceoBRwRDrNyMBR0+h"
    "iTHMU/44cQfrqAHGNx2ZYfJdkd8dSLXP1NT58nixda0k4Bnv/TeDKCCBVNAyrDsI8Hv6iStjlVnbbzo+EebHE9PtxlIaIdejbMhg"
    "9M8ylobOQsWx1FMYoCuyEyiPJykVCh5uDroFSRNpL/j4Qv0lmlIoXuh5eqyS2KDyqGcLTEWC8rl9wBIE4LiT8ACSo2TDZ3qaAPJZ"
    "AwACOpSaQxMZfxbQr0aSu34MVWSKSQsYikWeVbAm5wnbghRWkbryGNG8zlqouEwCc/DpqJeKSfcS5U0dF38q5sk4OQzwW3mkHICA"
    "0mni6IsOfT/wsMb7v+zWo/BbEz0k5LA9N9EVGV7MeY/pWDaKU0H/cjesp10rqgR+8bVS6wmrp5xEfEBncllObI3tOq5022dANj9L"
    "JeJQ2Kr8B/grZYVyDIUOQTfdm3T+G/YJhavqS9+ZJykII0ZrwKAsNznCH9sVU9PE9lkmtqKjKMw9QrorI3HRs7w96XGczjcYcbwc"
    "2Lb6P9zEnpaf4twFCeE25d8U+CMcD8iRZvQHUX7SgFAztBr4X7XwQXvXqJYrRDb52jMavPqeUzpDju5o7HhbPCyKC+eCF/N5lKiD"
    "F2yPxY6ZUMm1/AMdGKaoIs/mKB+iPfCKGnvtF3ebfBc/jW49mtAFlEtp3H3RpT6jIiqhX1lcGLeT0hJzV2EvV0Ym9kAMlnGPddAS"
    "pgrHCjARKiJ6AnuL4DE4jWjO0AaiDWWXBTuVZC5yBiE/AJnxMWM836QwWBqipNJdkS9wyzFaz237yKIFd7Q7Iga047/e0hmGsiOu"
    "kpchZPz9Q4J8WbqYRci6SJCyEtFNTwEKu/vpIaFYuCWqQXMAouSoa9zHw1pPpi/BeswI54O57rsJGKdAPi28XANZE6R1bibeJfVo"
    "aqtekH7rNsUlEMREIZOjZ3g1kEewOY79Zcog9l/wu2dHIdM2C2WooLBlIaKGEGq2nQEYMIwT6zqJcR6HfQL6WZspintC9X3YskYE"
    "mCF4x2+UtneSKQ2kjkLBnBxrz3xff4S6CCff3VPrPzCcSG4hr14mBxD4od15z1M4eOhezM9tNrSuoiAvc6lyXVWFZfxwFDpBIwoP"
    "6fBOwhorbS2HTwN6mcrIa5A/FsEumLpiJymiK9y8v6L+pzUkP1VOhYDZ+R9kbTPjxKsmk1XkFsP03un0hgmp0G6erXU6KsD1/IPu"
    "gNPTLZph7QzmPzWYGBSlqD8BZzqQEDBoyxIWZ2FIqaOT8qgTEKLhEzrIZ/aq0KBNq47XQpxQUwDXrBcTPhBvE2hGJQHzWZR0oGBU"
    "OvSMQGEYIHUbitOiCtrvgTWh5psHzulQ5D65EHV4i9mAjB0Y7EAOxt36R/C1ipQFWgmo7bpIObWmEt2a3jR6zBUoApNODKjYFc4V"
    "r2EbCmPk3DyrJdKXSH1RwgrL+KX+0VPWi2Yw7W/JhoAxtKXFin492AEOBoFvGCtlcTlqvFheo3U2QfNwBJDet40bi3YdDH/7cl4x"
    "MnulVcO2eiovK0ZqDUJW1ZtqiPjlsJ7k73jE0Mk21NaZbDhq0J2KJPcyxlp4V9edPMx6T9UbTtriClw3TVvG8suLyOnwEAzWOw5h"
    "Y8FfNuCRwukgGxFNIJQVWKQ9k+iXAcFYU8S9aP3XMLppNJQhXiz0J4yxXOQ2JiJELoP/FDiTlps5/WZma+oSIp7AMfiiP4KIqOir"
    "s/9jFWAH8rmYQav4oJOlza2rScOyHqPJ+9OE/hwdM5yGpRjBJ8i6YGoAmez9Xe84YtsYrgIockOwriXbmj4mDa47gw0F6ljV/chZ"
    "p3udIc4hokUhQtBNgt6dG+qWj+zkmMBAVD3CjGGQyByhdx0tBRCV0+4CaMHDgRj+fi90igEA71U7yriV3Df+curfW6bAJKI70//N"
    "eTqaYQ4ll3nAjTmUNqOKhhqOOhkrTbL+qq3mgfH9Lyrgz5hJwZCbU8zvdUtnYV803AF+7hbxr0JJp0alii0weFhbnMI7QSIIDz+s"
    "/5fZB6hmElbOQLShrLXWaDdfpSuwBAms41AniIFuD+Rszes6NsurvCV+6AidWPuG/r2X7C2xAzH9gzPiadPVpfkfMbC4jVvSdCge"
    "j18Q2Ia8ex8Xy2MUuANoXQzMH2DBEXMY7hCK2Rfn05wDVpSYD2C9K2ECVnsFtqM5uAxtUr4+zYOzaKvrWxTpY+ubAJj5FWneAmVQ"
    "/+HhKdu7FC/Kbp2OPwov1dEOIUpVZsOkRLmAZw4ePaPNzLWxG5oHJSAnX4zoA7eVvRxmh2dTEHl+kE3m0bKZA9iKkoZQ7MZMRRW/"
    "wxuYxya0eXQuI2SS+ByZ5cAuYr9vzaEGwDq5mJQH+GwLavqGlLxJzmu4PHnsGw50ku3AxkmBu/q+6v5DTQAK/USNdAls+NEgyF2r"
    "hEW71H0lqlTSEriZO136KwSNVJE6qOY7dWzJNY8kKFymIKTLATenhvd2E9hjTBmBrCz6gCormx03Tc+ORgpK7P3WcyDClEFHOsTk"
    "SSRyxRKjKgS/Ii8ys4tqxSzFs/IovG0eKbT9w9NyxtMMWxdGG9FzMEpM+cxcr7jFc0Sn4MwdCEP7LNYRILqlLezuWzUAQa23AWHj"
    "rYcZrOGsp7vDqewxJfWbKh5g7h+Hz3CvQfjl9ZtFA7agL97dbC3R8ubWRW6YHNX5oaD6uGbTn/Dgi4No2OqpmFiGC7XUn299KHrY"
    "PCk6nD5nALNpwZeff6agFwGcLqQO24Gx0HiY42oTseLqRx/OdDNXmXwLJB2VXADr23Umqia96we6zcYGsbOr/TadAEDniuQARTag"
    "VlgrWsIE65voegJAy2nFf3ngG/h8euPB7weqiZK3mNy/qMDO4QepebcPfIXQk5CrzRW4NE3gVAFkPPVgV8j9SY5RDx+iCHJZUxXm"
    "/MAaAY4sbBEA8+Gv9isD3qKYCe/be3cBxlHodkOCunOqwuPIpkvjXSQNyOKLpX6beIDUATxpqzgyiEJOq3p2oLzcc4+bN1JOkd8y"
    "5XC/xTVPAB5+ArmmATmYlqMub0o3XJv4DqRwCDvR8FrrkP/SLXss3lmcK2wEJgfC9hBU4Ko62J4BjfsMCgqPI5tirZ5sTsFz1lKE"
    "c3zoKIdPNa3pMgm/aEMis3G5wmO3lRhOhIXL1ccZGCP84qiUSQEJo/YBoVjV7A6Tyo38k6YDYuvd7aGuCJB5xkDhKojhTHQmHbU5"
    "sCQ7fULlmEeSzHtx90UJRBe0MZ1J7k1Eze3bnSn9C5/3bXt72uBar0Ag59CoHKApIYC9Qds4Xj1nipZUhooc+pTuyY6+pwpD/Ft6"
    "h/n9B7OVkjHyHmdkk4UjAV0KowQBCqybJsjVqHOCM3IJ0v6g6eESOjcuGUq5zPmkRMkSM4QUDvJjnR0Q7A+HdMSxOqcYjQUXtqxB"
    "DxyGoMvuWfD55MjsrbUJeOy2mNR1JfTAmJa8OuwPZ0eXVFDIBxs7kAX0DJz7tXXEgrO1MdLmoLib60vxPPULTJ7GX/o3LLBhDuRQ"
    "YS0DqOTaIGa4UTAARWFgHgXnDZaNfPtCKvBgKxHucDWDSLm56P1ng5Cq9Dy7iPQYLwSMxl8ch0+VmvCcI2Njt5S430S0ivc/eNKj"
    "8hXMY1HHkQgSkKAw2IlD9jcJDOkAvfnvyC6cbiP7X3Ssktx9nlNO1kY8OZk4HP5TIZhY94nK8knfH4OA4/cHBTTIF76ulatJYiXH"
    "26UY3XTd/qnPYaOoJsoMfgPCrg5LcX/2GbElnGVGIMit4uTcBvLb1wMi/b/nO4warUfTZ8DEvtQd74f2BHnOMF38/o2mq+eGHQ7b"
    "jj+Fc1yE5bZHh+e9JWhJKveV+NSSfLA6FUQnPYSBsASuVOINDVl2BqKbR3vMbzBXk40aqDaWHHdsuBpKqrHSCjVAU0j88gQPZihZ"
    "s4kelo0EdSuPLWZmQhDZPZ7HylwRkudnGQ4c7p3XhpDrO6na9SYONso5DuRJyJnWRMGdmkJ5YOURasbflPD+3II9i2ZeA4PDuvqs"
    "zRPP4PS2m+VqSuEhf57KyThSVg1MyyBb4vlgHAhtW6egBiermLdzKO0GTWkbAAm9h6CJ+mkWh4f867R7IRDnp5pg24LHHBRrkR8K"
    "o4z0FlyWH2vc19o7pMBDuHvZIq84Ld0FvUABso+6q99sQnKALOw9jvDvY8uIPm5WE9u5NgVS3thlJOB7PUYMNtuN4WE8ajjRSDCf"
    "0vf41LukLNhMie/lccGntK5rfGIIVomsqVQO0YzXgSz1SUVwefQXfNYNpwxtqf2v97tq0rfRFI+Ral0BD33MApV3nIvcm2KZxoAj"
    "KOHKg0cpOA+xZBicn6VHfkR8Mr6/HOiVCursFRA/dGNcVz1KhBrzX352aq6SY2fg6GfTKZbeNEL3SmlDGf/l6vqwp9/vtA2esz9S"
    "cOXLfEgmszTFtBNzis47NWzhpEoPz7odkkJY4JwjFGg5sNK8OoZ0RI8B/Zy7OJJU1MfqRzCcwmXYPbfDsdn4gpHtQ+wJooK/atis"
    "ZVSKKn8E+aWq2rU5+zgRKYYdzGr996ltbQee96azEQx0pOYKsig2TrhtuCxbCsTfTOz/OYOTNbsRjwc+Du+j3cClUrsqYkgalGq8"
    "cTODkwhG4tVvZUvBUVwEiBgRJQEc8DYOk1SNlMU5JygPckB5/+TryMpJT74HdEnpjnrK3ShY+PsDOZxqm16+5uqBeyWoKuuzFxfS"
    "xgKT9fe8CCwQZQXhSyFXjrnT20AYihZ6Y4G1ahonQ0SceWCmdZzOWEZjhSAYzT35iRjh5BNPTwAFXp2Nht6ZRaaHlt1pxmhX961C"
    "hqusr3u2oOYpm48VDp+9BHq6G6rW2xLeWLrmlcGRzYgAFVXBmFGsahwqR8IHp8FH32stCinYgK6BpvOnUIWebr7SXl17AVTfOaAo"
    "se3z5Tg5m3o4aZCaAQPWd3x4bpy9YBiaq09dLVzdZKaXwOplvblNOM0S/pw8c6+024mpAKfmRzO4RzLuUxtT0XcRRdoCKCiAz80a"
    "6zJXc3rBHSiSyTARFsujPrd0xA10suAYDoRaCPBDl+9nTM0htc+8647mWErP4HYveKJzUBGUI+4lF7P78xsSX3YYBCsbfkZqeVx6"
    "P+BJ5I95DEFmi8XDBpCVj45WApaVDI6XLTJFadqmQ+FXEL7rtFOxK5Vya39mr0HNiJqx5Gclv1Dwq4corM43x/lVDNjk3o0D+okc"
    "jRBatdPukZcNnEtkFkosEaS+yvwRVzJoZ9MaaEY4tG4jxwvk0bs3REGn7cQ5Q4YVD6dixJEijhN/8tLcksMXZ/tD8LDprEerTWBD"
    "9trdQ+ThKxUDxMiN6Bpwg2wkVHDG5Qwlg8ddMv6twuZN/AqGzyOHBaRtqRFd/7fIHU1+863CIxMbGpuX0XMGaGoPaVKCxj+NJh/B"
    "+3kO5JjU4qJjKkX/OS7axAowwrDSRn0XTexxYezxFGL2i50F7lAyU3Mtj1Jdg8XDP6Zaricc1SwQesYM7RAyRmbt7CTGh1sGTRxA"
    "jye3xvqmDsU2YRZU+4lUuIcUrIfZROPDluEc1ZNb+Rdtd0zYxdN0sRJAvLAx7nzzsB3Mdsg2PZc3BBmsUuBXOcZgT/7iHIECf+T4"
    "wQlgMJSUXLhQpGNJE7R2hdYfepX08txMTCK5Fxmm8Wfm/qNEIxXfzOoAReYo82usA2h+ikGDQUBM4tmBzKpUbqUxHdHnycKVsyuj"
    "kOAVA2hj9yQaYT+8CsLb/OqDWSstEF/n+2QEMaCyxOzPZnAQ07GMwyZBP0yMmiYVj0SpS1vZMDwsPj7y62g1oMUYrCf3/z9EiakQ"
    "R+Acf0H3q+AGmfDZe0ckyQqWBA4BOAyK/WTzYio0pNUiHJwQctuX9f/KvcCVmCvVwg47nCXOPkK3I7i4Wny/Tl2JfENlhBN0drHn"
    "jcbDyzSlcR6NV4GLlu5FrJyPpMxQgSqrk7AFFAKqYHaActEicU4xuo4sYCFNspBHnoPJj5oRFI48NfVufWNa1Hk8EcF0tiRcxMCg"
    "pyNuJn47/+1escRvETZtpcv0I0AYtOd05ArkhKr39YHKOfePYtCYhjL1JJhMk+I1ehCo3lnOo18SAoGtZe/7yexsQ06RpHWyCpE+"
    "KJBRAlafB4QiH2ZTQM9gjp94lbskEOsIDJHz+v0WDOXYYoICUf+IVeJA7SvUABRfu3YeFpjQQwMtfxmReQmqyCsIRTJUNyCSy8Vs"
    "J7S5aRNanSN/wlTZpxPDBjDRPWOgYJxPFmgkxqoCHJmshlBAkOEXIC0tRtOfP8efqYQXA/qIFNNuTd8Y7FdwTzduIxmQG46pk58I"
    "g6Nwr+WD5JcupAhWzjNFQdWWpaFMMTQNDMclZlmmBQ4oJBiw920jtQ4R/M2OeijwzwHDwZk9JtaMA9k9crd2UtaWOxG5Wvx1JOj2"
    "4g3EIaTF9kEOV1gzBVWeZRas+HlaSuENzzJTXEt9ATCHZGfqyhG+1JkvKhdhzgdPnSwIGiPZxwL/OM/L4ZVEjNUMR1iAaUua1uP0"
    "3pzrNyqkzBeSPj8Iob2XCASWRpM19FyDmllFsVdS15kvoWnMX+abLd9mo1r6Z3oKxxgjIxzB1LJGjC9FelXxgZKYRkUJlA5ZKo8K"
    "StOKizwFoOzcla+dp3pPmtidycmQ7bnJZ2G/NQz7OGBOoCYmY7DqB88KiHhlRkQnl3I5y5uGCViDBqy32Zo51WBkDHXwpXZOx49x"
    "9HIqPEA3iAI1Tz7JPCJ/i1V1sO/QdVnTTZ85q6xSEE+W73+UjJuxHcYNnVVucCka7tyYrnhENRqYmSUY1TQP4x9LEAXcz9p97TP6"
    "uHZzzsn5OxUTWeGa4XH6JQHo7IdQn2b03kcvivMgsdaZby0FYPtRB56dtrdCY7CIlziJoCh5jAwkUNapINvJSpI6j3tj6FbZ5B1q"
    "w1/MfnX+T+uHI904uvUo4uDdHHKowFXnFmEL2BHijJVP9PTH35Ipypxlx2cwbQcSQP52lhnJZhOIEdCoAvSggLjWXgzScFAJd1Qd"
    "Zef3yfz1irlM2h1wAxnnRp+G0URaIGgshRBcCr1eyuhVrRFloourk62ioWSp5PKFw8CAP7qaorbbwNnBR8vsjggv8Ob3BsV9IRNu"
    "okdYdri8u00+jrmmYJkwmeGYUpxfHyi4ynwy2wF1V0qeugBtxBqBU7fErJlSZl25WpKKdvvqErI9axSrHVyF5oMnAEGmDP8qlMuX"
    "1PIa1ZFrjcQmg8aq9ryRP9BnTP0+jf2HaxGOFVbCJQFZvR1zC7cKflNyKjXHQ2PcJ1HL7i/Pc4C7FxmNKaj+vZnAhFAjz6O9HyXf"
    "6DKQSsc/ijhbA7UVSuPI4nCtchi9gw/Ruh3bAcz2cOVo2VtKyCQIG/LmgeiUnO0x8BXb6vSToJaYlzoVAUFERhiphZ+CY4JtTUKq"
    "eRNZuSTA948yYhwKHhoT+oE8DRHb5xfRUuR9j0qIVnB/hXlrmF3C7nK4/BjPHnzmASO/q1c4ly3RTlZTMs8rn5AjYmHeXpvj1x8M"
    "p90VK5Ytcm5G1Vh9AI3DQO6gLFsHTvuxZs8W+KRJyZCRhFAacmNIJIt9L+Tsdz3EZ5NrIaa0DKckHnRIQKs8OKvghaaYDkh5nO16"
    "Ek5NkNR7uBc87s1AHICHc7rIM+vvLOwnSjSENdNdJlcNM1P1T5PHYnHZ0vkDNUEOZrFaDHmMmJnj+k/+evjsm4HvLwhHDExNMhd4"
    "iwaDTT0H4bGkfNY58JTDlTC3931+XBGY6OxQttSmX+bxbL0QUdZoQorlmjSW6HlOGWPCBtsw77/bQ1RjOc3y5OjND7h+6yV/tI+x"
    "d3O7mQggrVXEcLaLHB90U7kRYEfCnzf3+RjGA5b+D2qIpScAy/5pNpREDR+4QzDduFaoCdBdCAw+X0Rvhg5SPSufboezSAv9+uFA"
    "x0PG/bT9av7ECnOfAF0AmZLmdmqm2fa/6RJATd2KnDSHZyCzx6gjD1TsnVQw5Wn6KE0f9xGmbWPq8YuDElhY0i54gIGCpRrOBzeC"
    "m3oGOiNxct2qw1V3zNYjSJOEa2rtoNZJF4yftkNlASdHFkvMCS8nMqJ+Lt/N+0uKTSU21QbvS1shCKgldZUAenWeruHHE1v6QOcS"
    "wqSZv58/TRqGdfS0BwztOcTBQrypjboCD1pSUh7tZSEIlFHmnlgBB0uKHMl9ijcuFUXCEHT0uXx20Eia+TSkAsdCbpElLXatuKHq"
    "uCBHWYhevfRfkHwtHncOVU5x/l1iP6NoQl0GhBPKEYwdeS3TYEfyhAec3D/Ka95Pom+VH7Bk+Y0kqEgHzdVr8D0GOIburEcBufMf"
    "qJ88VmKtpYdS6RnTSQFUTCfDcjaXGHgNyJAZgsrO0AEIGpFQaustmMRjfE8TppVZOYlPxjMsM5lySTdKCs74VMUeMxLWDETojKmp"
    "FgeHRKL1njyiii8koTCs+x/8Ccw4yEFiEy3/i08SB0YiGpGkPHCDH9h6oEHq7V+RKRstn+LuUQn+SJLpc8TIYtoS5meAvhaOXVIA"
    "u0GHkOX0+7tnzA6vyYUus70TlXpuY3V+mPwMlyZOWcPPVOsQOouaYJzoduovNnD8WexCP6EZV1ED1ONrvOcfNnjMHQc8ZgaVnY1g"
    "3rVXAaDYv9Arx80QVgrAAUBAmYo4ivSCwQ1zTHTmIJO81KIVV0VFhb6dq0BA1kJgKGhfe4cLSsZDBHIsJ3Bo8X4eNb3zCSCyKtNR"
    "xGJOSCi5uEpfIh21vgAFn+jOkSBoqsp+g9292x7nsayIU9TNIYD0dBMYs48eYcn/KSCYa95AO0wVGjfDHS4095grCcJh2MyY+38Z"
    "M2hCQfPR3Djfia3j2V4IitLA4X9oocEBH+qIkWKALxd2mWGUsSXlKUNbnq7i766nbacdCbt29gKASVi+2P9nMWXuhy2HTDy6VVEI"
    "CQ6s4/CCvAueAioRH+Ka4di9dxBbRZXoSygiP+drp8SVKRXNjT0pONjf9pSDpshqzGPKmBjvtVle/cYTSTcM4Fa4TGVDDw2sICIK"
    "S0mRwwun7H73SQSOZNmbzmbuyhkonnaN0H5fKDc+HNykVVjZESf9OKKe3FKFQcAOZ2rlCZZhUw6D3UvDuc1UHEsxSrcb/mD/Dbnq"
    "GrLcW7Uwv6fVqWamVljyA4HPZLj+3NMFktIcsSRoaQQk0HsxUEmQD33q4Wlkzv2ia5XdZEIBQO1fP4zN0NCkD9EaZt7x2Szuv8ma"
    "NhNGtyHgYIErivjpdZwsJn3Ks8oucqpSMHShEpf2BzYE5WPqWFhGJzuIUoWonNBWjyflNAQEAvnMA9RNbZ+U/ecIiFmM21790GTF"
    "OdWrx+wGX47G7vIgM5/vaHvKYA/dXsUNFBSn/QUmoDAgXmAPhiUd6RlGk5c2zmPJ7MCQ8l2qGIcmSpn+wnhS1NZRGyv4MTlxxS+k"
    "Bp41nhXPcxYybEiE6sIwngLQZbqgyUNHehh0xUMS6qBvagJGe66HFxio3wl+BixRIIpiDgSyAeYTUXfxVI1ilr1mjWJWGL6CrZbA"
    "bjKSHsc7gOoqYKRlZ3EZR+b7XUs/b8gZXEbu+gJYW1NkDTje7sck3TSSkD0c7tbt9rfKEZbfOPafiaYiVmWJdojMeO/2oCxlOhBv"
    "DwvieTKfnjFOYJugZYpbjnIdcJDPRBW7BlF/+xWMijWqiqKyXtyefau5qBUk44QEriDixpq+cP7x08q2l7IYoMncmojQg2NASFFW"
    "+ztvCPnAQcYtWueO+1jimsNAJKDjDIU6dkm2fAuWaeJfnP/HJFYacvIYmyo3BAy9/3iPKrhWpz8K2+8qVDub4JnLDvY57A4bGsjB"
    "c1bsj6htBBTNrhdASahlGR+l6pzXKEWeNHZ35ldyTk/2YdAsZi+sQR/hj0qwqcnbs33+7AnWO5y8D8zI4ZXFmfMDYPnJ7cJ1hOOJ"
    "WDY4ubzUV9fSavCI/PMSqAFMlUvfB1fCBcjIkAj628pFSHPrPiYZZrB5eodkw8vYK9wxyCDYpF8LkDoZ8j1kTzar8+KuXGLKRH/j"
    "uAFHL8UgC4XjBrJuIEJz4KIBMZIeoC2NLWWfTJb2gKOt9MyvWKDZnaNnEPMWXOaQThdwR2QE7WB6AQWWfIV+fy8g8juxoUA66zxy"
    "+3SUXVOQGzaGMgmWhELgPX0DgCKufrloUFoGF8RDvVl9x3/jM9fB1knWUHFFPCnEFHcREpcIJTIV11CNweGAIXsqdGi5Rugz6e5P"
    "Wkt6XswG+xlmMhafltl4m9TJDWAmD8XoeFK3Lw4BnUiMYp1RW5hpPVbgyMvUMuGEgn4MsuzHuWtLG7AtFljZj+UPzsBWGCNmEKIX"
    "PoJRWkekGXIg+emtgEYX54RqVUCpZtJtVr8YxxWijN/RKSVFSD+GivHciQZiW1hPoxgGoNd9h0FJMDzFpJzUTJmYgC19VlFbCkkM"
    "0kVgtbJKbLXBcxeLdIFDNgEaQZeumYeZ5j9y0lBVOroJlmD6U4SY+oXEkaiTjaDbUku2JOowUkiQIbAUO3Uq2+KElJp9xTQR06FD"
    "5R7BozGA25SurItPQf0tpSIwvGsM9OBavXovV34TmBlUJVLxcLpiKqjKO2GZH4qLhvMs0Ez1qkyJTAWhaSCiO10IYZ35UcbOD5xd"
    "b4CP316lAYSm2ZU9Gx1n9+BKsoyfazEbfIqECUNOVBTpiOt5Q8bAXCD+mxKBK/Cy6fg5MXFaBM0YJVRWkoWhoQ6MWhZvFw+ZjaM3"
    "VGFZ6tF1NHRFrl5jmD6kJqvzl5Q2hPBLwT5liAMf1rLh75gfH0KH1/D9fqddAkPppXJd9rzSKZjjHWlx4gdqb0ld2ZgK9MpqLlhq"
    "AZsgmY5AmLuqWQ3vgAJsx8QLhspLTzbtSQdrKdmSsNSq59ja9M244hryFDrO3PIcDDQzzwkOMaQ8vO6GH7rKRzNHkWfsI/yX1zKm"
    "NKEsZWL00SM98++Hiy+oqSNpmKLHqFQ+m5GCtJPkMA3N1gKnnqEhpGBAHU79ihv+hEquAlURoM7JD98qFlAUNK3xf8gtUp9L4hvo"
    "4eLEfesZ64DcweVOu8c6oj8UANwEKK1J+pQ9vTk8+rAyGCM9cZzVeCescvBrfX83zd23vgtnQSpyNJPAIvVP85RFaFGUqYFmvjbL"
    "FozDxvcUM6hphn4ZD71Kqk0EGkJbnGn6zNk8fjYf1X2IrtpJHQ9UE8TBxrBgIjaBEpAwKGrOcquf36eO//ryASaWhVuAgLlCiIkm"
    "yKHlp9uAv+GXmP8GHfQzLsApNgKpG45ODk0Z2FPLy+eAUbqkmVdMhaT684jjEul5+NFf6MBf2XxHO6E5NeSN5j8hLmdqwMI0KIU7"
    "uVCUbaQKwRFNDIpfHeCWqigEEemyPn0SWTE3gNtrGJrrsKvVs4kWcFwUE412YuTw55CT+Htjydg8qTCOm7gp1bSmkvJjagFsGm4w"
    "mUxiwnhO7DSNdaKE/2/8/v4/WCzijEBvgkD6sfZdoUK5SNiSEJhJPWsHGoo5NuQMG5YeaWqU/BCrK7tDScyLr4WmORCNNyoBvfRU"
    "rCRpawcApSVebsVZ1ojCKMbKPB9RJnhfDbl+i66ecauRpuKLQuW4sI9UjJQAJ3VMolKurigPZrpVqgDfBIM4XJ+ChWSyIEa4xh+I"
    "oYSb7lKhDPVkpl1mbh0l7YpUJO3yP3TLJ48PtkY3zYofkUZ79/AUTUvbggEA1f7bdOF1+A+RFWyVn7WoPCQxYUFg/d3wgg7WcMC3"
    "1aNFXyN/pTQpdTxHcdvKGZKiX5UKCFMsLBSTTxlxTOKHWloTmhFdKF5XCCGnVD64V4gwkFAFfY8oLphqIdhBbqQOcJawjAN4TB9A"
    "Q5gXlAX2KMhOl8yVPkSGccBJWQOK0IImMNKXkk6psK6hdeVNsGxlH1MwdqbTGfFq8K1oLTGOR+3TW3kGeXLAerbZcYup2YXUzcui"
    "W6HAteEQwd8fchLOnxWzqHT0TlOHJugLhh1HnJviB9B41lX1Gy13FcFStH8DAYwinibztRsqurwwKly+i/0dd3gdwJuNmztCiFZY"
    "QqJ7S/RA6+GfZcLP0TuMLWv2joGYZiYPbeAevVoqyhMFyJ5fRWp3wKp/Hwb/jLUPsPVLQQQC7v0pGKCP/ZZa6pCQGAhFYjnyTLwu"
    "s7ZLdzDnNMs5PAgdp1pKUp2N1eOp4me5ALeGaM7S9QjHmXWqIE0oXHvASxYT2NpRbkrKjB1o5pnrhzLH59mzUPswLaIjsrtREP6z"
    "/MSnWpIA0gTdToZNlkmxc4ynN8vjGmdV/1EFESylc0SrV0rrWdiJlsHeWcPkJZkDCpF55XZgTulwRn3EVOLJ1xpzsGp05GtFQKzY"
    "r8QwjhPJHXZp+Ar7z/7acRn0O4FJ7BwVAV2vH9ZRtRR68YEZgbVsrY0ihhIHTObE81/aDcZesIp8f6o4wDiJjC7+iz9Nn05uAUA0"
    "WeuxwbCR0fl+kIH6djsoJxiL6nqdgCq0hXhB2KmNfYoZuPfV9IGGTHricwJSgpMjludaAZJHMep8AGRho6nCPFVj88Q4VEL6zQQl"
    "/TxcY/NDbRVhkJEtdmXEa4Sc4JVB5nlYYbazQpRCo+otjknrZxIg2aM3ST/FlAlh1U+CDM7fgqly/e+ncLAXFIxoUPCFNkXLtMpF"
    "RYAyVXV9CUzGT9iW1AIiVZeyAMiLF7DUpEXMK+hzkRu18pVFJGSw/9TUdRKFScWA25UQSXVWaYDP6o263s1/5QaJno3ReP1B2Ucj"
    "SVJ1AdUggmaQhB6Igx6upnJWLjeBxbFGfwsv4wRSW4lASIqRff+wKnsvF2XFkBjejR6VfV0Bd2ZzwYnk6Fp5RvjDKfCNwQqZJ292"
    "eJ4/JWrGyZMV6QCzkMg5TRQ7y1becK7fB7f55QQPTY/S14vkakRLXPuHplB3fgnoEAa3XRyMMImygbMv8T+fwh8VnSvhM5PPAkrZ"
    "YwaK9g6Ff9RpZxJhgJEvE/32/zPDFEAEY64oUQYz+IhToQlBhy+sKj1kP1tDsI0FQ25gT4uQ0qU8lStZ7N9qZ0MnkbGor0C48QcK"
    "97JE7CY3MEViOLEhrXqLFfE4xjfzmE/KksLOMSnD/V1+nvLbuqwmOVOfSeR9UvuKzoURTrbKCTBkcfuuxKE6ANxFbA70NJhAr5Qu"
    "3xloJv86USRnrpgXL5CLGSifTrx4a13ISa7A/pvNA2Ki7BJb7ZBtI1dZPiKm8R379+IXevo5cSsYGWJBubypaXSQaGBMug6NbGjR"
    "CVCvU+A3KzBBrA4MeYGGjWyiUVdlNJY7prb4DLmBKAzO/Gg4u3lZk6p4tpqULULNDINcGLP1p2DmhGhm2MqZqa7VOPG8GGwEK+Lv"
    "Kec6O1/9Se5GgiSlJcSOfa66HT4fQMbGG3L/8hvf78Q6Y5gqh36xjZI4zmaEQ13ZN1tszQ1MIuG7z/6GqbTX86IjxIOOTOk4KTcC"
    "w1sR0INDpEiYliDP0IGGfh54W+5jdJlcKgJ/EjoO2LxyOH9U1yJKCa/HnsVoptZxoefu6Qj4tdX2HVJwhDr/REZUKwUxyTjlDGKt"
    "LO/z/foIwvxHjupv0yXH21vDPdTTBByUg927oOvcIZ8HZVpxjqaM2Hkvbf7Oxi+uJ2M5sSMPXt0/IPq8MAR38iJbVh2rrXKK8B0j"
    "mcD8+16qNv8LguX9FS6J31jw3ZrkpgYUnF/lP266ocqBW6Z8kAfBOFqlRDGgDpDCBN06Uo3d6KmtaLBXyFjJz0BvgrEpTxN+hoPr"
    "F7RHA1Z+61M5wz9shIxbq/OJVbFHdTInZpvxY5KxDWkQxj+eJ2fL+yDn32Ei/QCKSG9/kmdzaphq/beAWyk/nGUXKJp+fUoAj3va"
    "k/QpQlqlgbf3D3Fg4wnepAAOmx+zDcAzGYyU6GMRN1WyVu6WKv3HU+0jCVjs+QNoK8cdGFuJK1VH/QQZFOcnydf6dUMzG6SMPyn1"
    "x84Fk5/3EyY20IkGtfw0rrI+SKDlPWRtMif4W9MTYYjpoA+/cLXkKj+PldLGMaj3LeiprMKZRHMsAvOOo2ClE4AAqCcvjzCwaEd3"
    "/gsNDmIIzrmpTmB1FdCRp3Tnr/L8PINWNu7vJNp4/DShdDPhq79BUSnEdBIHYtxHJzHWTQTv/uSoAVYY8y0MwVxu7Cr4EnIjlyhc"
    "fGKS50y8xFPFwiRk27lOxKaF1Y6Iyophx6SMQm2nF7ZI9Clna2Bv2zAQ2447QkHfOHry46vIr2LUuNhqON3GiRLu87zQ5RQFGTyr"
    "qQTw5SpMJJwzQSF8kXWBaITZuXe53Cztsgk/OLGy0ZE6UKDQceXpAVHFxqGBAbLp/UQHXdrBtkErAZAMs6oMqXwqPh8byLFHZobc"
    "sEQ5MRfycfZq7oZjj7xB6CHS1AibUJl92H12+HoBCJpyoTC6g51NzMweOzyPia0oSnn7Ddhlg2oZPF06Ewf1Mo+PK2vrY0uPx9ww"
    "WQHJHAxqFclqJE0+P2otEIvmMf59gz629SNq+mUrCeCOCgrd+IH35m3118Dsl+jubHREMS3LFaDQTqHrRq/BmubO47BhQbGa/Eog"
    "nynsMBDiByl14/McCCrfZ1c5nEab0TNCCZZFjTauDw/J4dmDoKuZ2R06dt0epQ3SXPThYRB0yv73Yqr0gyp1+ZMei+Ldc3GrXIWl"
    "mpz189f/AyyVpFI="
)
_DCOS_B64 = (
    "eNqtXYuWJMlqC/3/R8v2dFdlABKQvd5z7Ds729WVDwKEAHHO//c/fPXTKH94/Q/+28d/P8r/8uHv7+DffhMO+J/u4fyXe/i9Bv7p"
    "8/iv75Cf68bfrh+fJ/hfHgHPf7iE6xG8/gW/D/3f5/BjPfgPdvj+CfDnEvD7DoCXr5Hl1fOPj/97FXj9Wcb7wd8vBe8/heu9o5yJ"
    "F8+Q+HUEf7tgPgcZfz+FvO7pP7rUly4LuG7oxQUguPB/D5Hv7vz7Zc9B+NtNfMyXeP/0PrfBP37yf+/6c3yw9yJQf8AfLuJ6Gnjj"
    "xaAfNv8cUH686H+JA698AO7j/u9rfy0Jb685/iv+du38WNBfHNAvlOL7aI5T0MPaAHH+E/bhKZ/nn1xRfAovjP0kH/YfPl/gxN4K"
    "Gc/w9pn++gyoiIk/WCBfwyD/lZyDONOfcb8LLG6G+dofCILN/X+OPP6ck7D9Ky7cZ8XPeBFIxI/hjQvk924RffD2LOA/IAgWIEre"
    "14UXX8/vI3txfi43Fo/1CyDHmjbicyPYvkL8PYaF08LXh57hATAcqfHXsYZK3nHsle1c3/i32wnYgXiVU/JGkc/zwBsT4mtOAsZN"
    "vfEA2o7xRcPcvYTwP/gXlxeHgBAY+nMAXqLI9H0vEyF80ARfkCqIP4MrD1k5M4ualiYgnOzXnXBrtwgp4NcOVj4Q36+qPucNij4p"
    "bcHq9CA9M+COvQTwxu55/nBu+b18/CEXRgJ7vHMfvrTeV9BBxXEWq5su4+uz+DwKnhOYDL5LAN/xiRTQRWZRbyFpiWcNhsbzLKjv"
    "ZXEPjCa1jfyoycvr7JcavO7T6CdWIByKvxFH1Ii+565YzwDX3ygPIAdsrlxV4ONmPpPKz8fsoX0JAGEvjOePPMB7EM7gDH+h67qi"
    "gMygfQ7AEsHgghKBACRenSHkJGo04y7MYmOA+LlZFNICu1d42y8L8cF3EQSZRXjNw1Rjwnj/8wH7A4353vhPDX38c0nt45DGJLgU"
    "TnDelSR1kvvNybCgIqATqVUZwLzz1b0rFowaE0wPAAqULGJoyl5xGc7q+P7vCcPX9P8DiQmFGvDqANwI4A+FNGbaDvskvnyCfzy+"
    "JZ+bflOmD1AcOCYMF6kG3OiBowP9vxwVIvt7rKq/Bcc5vwFe7HJd7lAQowMKb5EL0ytmwveWwAIIMCaB0G7+4wD4JXGWX8yb//2t"
    "ZOGF9SJbVUYQaNEjw//jn3zw3xAATi5hy5zSvgUW773yHdfDLVf6Y1jvM4l47aMRoieT3iCw50F8STDypetrbEp94Mk1vm/wdmbg"
    "u6P34X2+5POrSjKPog/aI2xyNX5/FfGCQeVDPZ6XZYxv0GWCj1hlfwk2vSs7IpHmgVeYfhV5auZ5H6/FY8C5W6h4IRC8uYnCus4p"
    "IOEZtY8D3aWQrOGO/wJAfw8M70txMMCL8wuVwns8x8+vZ4EQL6EAwxdfhSwOkeKn5kELjfo6Gp7yX+kgWpG4TLGXyew4O2CeI/Ei"
    "Dl4BEMQGiCUPQo/juHdASLUDLCkEPtxtjb2L68f3IaNyYBsAp2MEl9iPwgGEiE+svPbTBnpjFsyNFJfFACHr53ndjCez2TckJJVR"
    "v0v/HuQ3Ik/FN2ZrxmSE6ID5mAW42ISbzMJYQONVu+eJHOSqhtWknHMzHyGI+MsURgTDDIXvgDQBINwQ9fH8SA5wBvEQXXcY2kkh"
    "4F+IfQsKrSu3zgCExXHklI1j9M8MHLbe5zDdLvL52+TfzQOYAyhS+C8AovVjGJ4Klt6Dpu60qgkz9S6m6DFHoPYpcczf8AliJRX/"
    "yz/IuVt3gEFcwVaZLScKoMY4usyKHj88OBGvuaj7k/LLeg7sn6tnk/piEQEE3bXv5KacX+ANrPpfQ1N92wYff1uYs0lW/vUJAj9F"
    "NP4xA8O7AlykgPjKiL5nH5+MjImPWz5C4UDxrpMgzGNsatkxhYsRmUv0RNl6MmbAaBziHQnmA0SNSs7AASF2HYORAsGrImTJIJ+G"
    "PPzlDKHiu876dcnyxfXjD6UbmoCp5nksV1mv/M59sEBwzL5QAOIND12w18yGgZW0uilIril8vrANiQJkWs7T9FhAwhzGTnjMNmrp"
    "0n8ZAF844fti8Umx2Vls7dmh4BJf8lB8kwFH4vX6BdfnwenmK474DYCT98MYUf4wSvhhstoMivYyfgMgZsfHGnvPVJReAHsuKahf"
    "8+fFAJc+zpXXoOp8RAGW7gKg8XUsiJw1uMWGO6AMhSxOGbYf5js8AdyHDsEt+SaOmB5UZ0SE52rov0op4Gt/HweITfi6fhBn1QaI"
    "7/+R+W8WRni1LN0UavgdgwHf+IIyj+vmUSFslrqdZuE8KX1510pV5k4qA9XVE7rcgLvLz807uaGiMmo1cWJOs/JQ/YyhxRn5VGGJ"
    "5QPEzfdh4fsFNEpM4O/5e4Mhb/f9faNYPfybfPo2oXAaxKDmKbYMDA1Nx7MbZRSnG2GWkDv/a108d4dOXtCuCzAXjnBTb6vsKRKX"
    "QAOqN4zkt5ONsxOGJB3PuosbdepAkcm9BbtpTToiUZsAE680zORy4GAn7ocXSK+p86aPn4VGRhygw46Bu7rAvk1xXBxjINs6gpLN"
    "3INwPS0yxeTnMLF3X9fkBKuewoJ7DMeOeRh1dYAjgsRdUm1jEJsKP749EJh80HdYCTmA+gDS9v4tyPFT0ENmjx6CumU+48PCBcAw"
    "XiENJD4nFDTQM4f1FUEmSR4/QALZXRu/BQTjG1Kk8xU/Kaj1Fgf8woX7rqbE2dG8r3oHXaR/y4BfjWwLBu6DPR6Ip9mqRQcJFKhf"
    "vX/odHyhQwRLwH6H8d4J2jAcmdubvDDiUFREH34DA+CiI5eP7nOtiIOQUw3685+BgDrJbQigJ0VuUR7X/QfxBl9WP0LXfiimYwzB"
    "NL7zvvddD9szTJZSGA9AdQMv9xomaFmEZfiOff7w5HJvwZFExKb4RBGe9OfRp07mnOFs+89haOpPTEaHHOVxWEySEyVf6oJDz/1e"
    "oCJ0JDfnn0c0b8b50WUTYGqURLnLLvNrual1+Mi1rqWChmdVFxME7JIort7go3wBM8bNsQNDVeARewiw5A6+OTtubQN48z2ZYomc"
    "C44v5rOlgy4qfof/BBSd/efU2o3XxlPAAFOd1yQBPIX7muTEmKMXJTe8OQPomJWxgSaAPab6Azkm35seDfhXRFeCwpRAYPFWOYlo"
    "sakubMSsavPxcpYc0RIe6oopuV+W0Ow3zncBZDZ5Pia8Zp+YD3ssILC5LsqU9cJALgfHnRlCv2cc34XCXQLQ2c9vm0z8cmTzXgCw"
    "KrwTB9rp3MdviRChh49lKBEevrFUG/hi8FU+9+rBewbs/j6ihNbefaIa+Rf3vpiCkgUAPP8L4/tmZoI7RTyeOx67XjLx61kKP9ip"
    "sMDyyF+zv3pYVgo/jxt7oSGEBo+vmqAhhilwOU/02A1RdoilE+tPjYxjIoWW3fu9MPxTOaGDf/ykKbxpk1QC3wT/Atwvz7Oawiv9"
    "D0xadlxBF4WsdzE0BWCs6Id7ZAbuJbRxn78ZWiU/+cyHYwAOiDiYCOnQvosrcAizEExkPIDgVrEZY+eGmd2Jiep/Wd4/7DAiX6ig"
    "lXEGweDDHaE8iff1/SS6Jjbc+WqGDGxfIDINCFWz+ifTsRBiYFfJsfgh5csI5FUbv4JYJ0KxV8hK/C296CtA2GFmTsFHGvL1HKYg"
    "QjBiVdxxo31+UbTxZo7YZCSSQnLIjHyRQde5XyvmSpfRZ0FZdhE2lo815zlPMXMGq7tD8RQg7haYMYHDaFbtNaDtBeGrzmwUFn75"
    "FFhIpf5mts+cCx2c8Aessm83aFsCSSvDsegMjJ+3mRCjLz5cYfgf88Y1Oh49Am6XrCOP+hvuBVSvHjP7DFp9UBSfyTBW5fqAM0sS"
    "MyHeYsa7G6EJi6dRg5Flr1/8xkizTg4QhsX+SpmhjW3MnEfw7aZDWOG4oJsxwDumpFsc2yWCY7b6pMv0QozjhM/QJeBs/h0hgFlH"
    "ieKHqIKKsd+A+6CQ7zOeOb0DEVJ4NpKoEI6IHTtwPE/yLT2T5TlOHJwRDmDs0erPDGtk4VCB2XV2L3MXaFCxq8Ax9VBu9Mf55W29"
    "ljY7AITuth8dsSV6FEqu3CkowDgn7NyH6D+MOdviASATtwjp+MIyoFDcRKA8WZAGrav2Q7t55JJR2iiZ2Ydq0uhPyxyZktGTmiOw"
    "a2PWYuJ8QUCq1IqbMjJb/hoTcQn+ZYFejhZGy7nLtcr6NdeI1Sd555HeQQmD3PLIJrHaDBHcMCTIWL1yyVQ96+RCAIbpnUYCdSwB"
    "0sHBrkficTI4TjaqVLDSzpZLagPmQHErgJPpe/QJGBTDp2CDLpPQsdr5gmlmkGKdBCl/gMQMtk/vbliP9rdZKsgAYNPcwgsRFKMi"
    "gjd5cVBP55KC/Rlz49XQnGTU0EEeeioWG/4u3QqQtnHYJiPApwesjtoFPSLU+hYs38IF3qk8bIJx1fmMDnUDXUKTKe3sfh98fwuG"
    "YE7AGVAIegPETV1fLTG5hbWnS1FQTXP9yK3hPKWXaqFgC/VpnquC0z/EfmQIAzutl5asZsDoQv0/ZR18zq0dIy0bJ6hw8Gj1n6WP"
    "LCXrS83knQoTERUYYNUNzMFFM3WsQ0iMsEgvtZ0fRlISOHfn1KIFV0i44PSjj+X6ERrR4jauroMgKhWwhvZuvzcrVcgnebmGMMfs"
    "D6LeMqrwjIuXn3e3yCLRUn8cEjCq3JnHUXcihahdHBvw4IhJbAIwh9ujSgtlhFWd4Pf7g3/I55nSDioAgZ83E5LIAJmCVuoniIWz"
    "A1NmMM4Q5cNLsdOte4Rx3FrtlHvBQsn3J+IgKHzZjQanJBLH7a7iucnrzsoYgESY3NwReJILP9f3z4f3y3XDEJMcZ9BwDZMypOQ7"
    "8vFuQ6rt916IH7d/Tb7yIlM4nG4oe5xnwJJ8bFiGhJEARUdk81khxX7BwwtkQZe3pcCtdnGNXWSsQSCSA05AyXqq5S41dP+6oR6v"
    "+jCCbDDbBIyyr1eMtHfcK2uHOeVPjQ+AsVfn2ik/UbD37PGhCg1clwo0TuXIvHz1g6hULdjLmCJT1L+ZEW9eD/PFXz2o6ifeaFld"
    "GgzT85cPFzerTHZNSVQRnRqjmLwNxRpuVgI/+Qg9xLxBNOouIcPUkB7DMR7QdfnzkxMxivjA69+r1rns/+kKeRSXzIpfNjOgvGTM"
    "9vu82/7WENTRwe/HT0PsjukLqOAQWcb8pwTsPFDAJgO9lDKoIwA8jgsYXC/gRZcd8GTNDkPyoFUwoIRK4zLY0mGCOEl6jQAMNQrY"
    "jBJz/kmkGBL2qSMzmvKXPDTrvUP6uRkojmtQhbiykbRk8eaIxa4znLNR/r1uryQ7uK6VrchzBsApWgHHVaBhk+Za0YNn3LGwUf+Q"
    "a9qZWgC9BjmEw4fCHg0JYpZ+c9e9QUz8ycR9siDdIMZz2prm0BmA+JJbKCuXyG72dzwH79byuTfBofU/H8PNaj7XMugGvLUzRPdo"
    "ABrzQTwLLNzascMN+tdnFb03W+BYifXJV7mfwK2vYyqr2yg9ww8zLNQTqOjYSN3NgP6gQaU9o/qrZB9x+V809mvzMw72f7UbiCh1"
    "pj0yQrw9jJOiCjNL+EWmd4TfsMbIyfXEEUz8t4u8RKHkkm2/Oth2C0lRrBEC/nMIAOEPFNe/F6PPXQELCMujyRSOHfg0JPiHAByg"
    "juTa7kVq8A7jWZx3B3TKHG9HwPLsBnAvaoUKHYWlADMK8714iyICK0iCYaqu2E9HZaX3N9s9wt6jkIBSJmv/AhPulcUhqFFA4TN7"
    "YqfnYvlzPaGFXQFG8tZQBLCMtAjD8YwpIG4mY4p9FUUlAbktNP53UrFM0afhD7QcmpYVyNrf+DI5HC0w5X3oF/nxdLWlxQjb7We/"
    "WqV3FehXn24hoZhzmrTGknMCfSNpntwYqy7h4hWZadhIIB8uQHu9oplAut4rLd8/bUg6rsnmETHlGyEiVHptWAkYig65cY0/RLvh"
    "P43wxGssfOv/fJpPgnwaYJ8uKay9TOeIOYWfLeN8lpdVZjb9xolGV+u8hwjO4HbJVyHfhl+5D0UOGDETWQU/dh4ymUakY57cewhA"
    "6HqsW/sXuIoC0nPOXazYP6wGUcjYXm+b/jc2JzUn6kKNnnqgYRU55Fj8NrsC3g2dvoAqwwPqCOLxTZK4ExXzX0/nXZH3P2ThA+5O"
    "0F165Zm1bwtUeQyOgUja2UbqQYRZidCG+ig+PvljPAaEy5KK4/KHCJKRqhnOUsb6AT/p/U8JXHJEjFkNx2p8jxlpjofRPUvCQnT4"
    "LZ8d5Pl1trGgXmVUbCeP2s7j6dWkAXTUNeRQJxUU0WYYTRIYx0ieRbbY+FTIEbg++UMh7lM5Bqvie9bSJcahJ+ZeS57j5jj0fQCJ"
    "aY8joBw3YBFN5SmMnvcDbHcFjgVQoClAoBovtQhmUyOArU9yNXyWFCP5HSEly17rXRbaEHUlgdEc1g3O4BNgFufLMs6BnvW9BYOi"
    "53piK0cNUvrec73ZXd7vnfFj/f60Q1uMvaARbHgDwZOBoIQWroVAKdN7vYbumVUiRWojL8ZcQI6SqbefbQTKIeGJBEXiTx2LFH4C"
    "bBsap8eh+rsOMduOXkRuPg/zTadpSxuhjxpUQcgJXSoTqvclk8ebfPLaTBw3SvzJ/X2bO+KCC2FbqFJtH9+PFr99vE9uJnj+8k9b"
    "aJObrI2CefsCUl6FZjeDFFi5RPtYCMQz1CBWycIKSOJs11ExKh066+XMvvaypc0OTTlDcztjlNZm3Xke28WRNLjq+yOSFz73Qjys"
    "fH8NSdpexAFAr06J9r0lzHX9hd5jy1fGJDgW9Kvo8l/tdp9CWmcE9V33htyyB7nXHQNx9yCl6+Fh3KPN5XPU3ZPZ5j5XKuYZ2viF"
    "hMRCk1fkZTZ6i1J7FIPah4rFTwFXfEB3FZJNJH0bvyjOpXyGkAY5biHbSJz1DTzy82ShlYhhjFVxqgpT2O+9JYvi1mBz/bMu6FWA"
    "fvXK/nFfN+QB9w9P/Th+czXjReFmHng//yaFvmfI/rwOCiiJ27gFqLT7RbVEtWC1pB+S5IF6Ev1B+AiW3YedyiOIIIYJWfkOZjsA"
    "H+8TTZOZoDnLSMSkAcoBnWGkwtFszmbHeKJPp/i7xlQEvofskeI9pEwS63ewDsBFB8m+g3ocL5ESgTQ4QjSy8zT5633C2NMzLfCD"
    "yXlVTqeYvCRYHwoYlM1iSHdD1PUpTzf1WpFfGGDrg1GdB9LHJb3E8kdEFZpxmSjKgKKJGRwlwPP+gFiVMfZH855Rwcw+g0XcipbN"
    "yZpoKqCkCjrWEl6Qj1zUP5t9hR8a3YxVR6bYt22R7RacT+lLMLzX/XPxzCeHxBHmyNBKimCbjxo0oswd8DYBpAjQYFTe4mqKVlYi"
    "feyXc3XITtMRwGI8CLwKOJgZ4X635Uk5qx3iUH/P8v2Ygj9ceu4JOCB2efO37Mfw/Ri3qNE86EbX4Uu2gYNkzh3anYbsXWN4Jhgw"
    "YSRWuXM1E6g1iIn4CNLYBrMq/B/YD36nZ4CGeYdL9+/Jx3kFd3l3nw56XJE9/gDCg4Qy7Jk+5XApP99f9Wo1LFhvr7yp3ai1hLOJ"
    "3YjVcpQ5qmulMvfvH8LDYRyWk3Y9StBMHQn3EpCO+M3BLq5Bvy5HphBg2Z38s/VhKfhO3XrHXfsONZV5x8/x6EAhoHMvkWn1P8XT"
    "oelNt2fEnikX/9Q5yZXH3+unq46jLxkgqXq29yCaTJ+gNONnEaKJIfq5w/fZIHMzFJhqeDDn71A/IGLxGju7ZdHgYwK2D33GhuFn"
    "hU5CVbKNI+zA7C/H5S3gOyiJ5FLPRID6lIADdoxbqo+TrgxEhTVg5vxWbgFF533887Of37r1F2N4Um+K5+6RbtYOQk5gjxIK0Lw7"
    "BoY7Ko7cXRdIS1VQyYr+KVZvY0dLsHArmsAQwoDUP9+1wLHNDr4SMhaEUkE6bu1/2EEwTCDJ04vSDMcx/4kXjPXm2aISmiwVyO+f"
    "OvCwWOyugMQNg0uDH92YONVlUg3yFslWSLr9I2Gldc1LHM8lxaGB5ruF/MsDQzgZnpkSQeZqn2oE9XkMrkeUYRS55Wy6rI5CL2EG"
    "XT4WQNScv69w6TKnfBE/jn9slre6vD8VAV4pfoTkBiZifXbjlESXbmWS4xHWDwBFvGXgX5AEw5jxB2kDghy8L7SYnGCNGDJXlSCX"
    "aA1JA+6bvwoB9BMgSTdQ6hE7AAod8ssuZXa+c7meZzoiZEtaDRN4tR8WSct8c0nMmn48YqZq8e1nQE+CH4UIokwnWjhIQd/BccqD"
    "m8F9JqYJ8LRkOq2AMqA+fuJ3uQw/ReDa88Sda800QLxu2POT3eeNGciVI0ZTbZTFKKxuCKE88m9HGlV6gvDYZK8S50f3YTjzqit6"
    "7r2iPhiqYysFpRA9GzxZpEdhTS2KLbmaBeS011ZB2e2iavmezwXB+Due4zbwoRjK9jGn1PJ6+ozzhmQfP6DjZXUgza7OQjbpt0P5"
    "Jei4OJoGQc96ay4O9lsSPcB6/2SziSlxJWrjy/fzxDLNfaBd4IS7tnOlPv9p27qnl+wdEMLIv5uTPhMQkh1cIAYMjYbs9gu3Jzgq"
    "xjGwIEwSEG/ktPlAQ0YCktbt1TiPqYGiowDuA9zJKCG0GKBGvsHe8kN+u+m0npgC+Tr4V8QOErC6NWQn8hoqe/ffjcyzeHuDLSBV"
    "8U/WAsBwDJjJrELitikqDD++68DFxVkCNg7BqctLAqF04NoltBzJmxH8tmcqFoGWS6hP3kER3AET2YCGDFzEdSTVswhGKGJTSQ4m"
    "ceXoH0oKmkUjcZbc+8mab3D0KkcJJHEfuYGC7nP+IjEWILnIsNkRsOgsNQ4W0x9QpNcVO7hXSpaend11gIbbyBIw0AMYdr0GI4ak"
    "7y09T4Nt1UzDlz6dlrijHjamAobzDt7IdVJw5YdU9fpgXnS+8vMFeVaEdifUZAKibxVsZTTmqVh2E6QK5RTisWnupkonUFwR+lVe"
    "Nk7dEyzLWp/WzlA5Yv3S1DBPTpsY8q5nQXO2E6Rgx7Kpge4+UPL5KVYJ07kJAGUF++XX5mzh0yb1CiBkv4QUMJshLjy0EUO5UxIL"
    "XJGikCppxy0gx9MnWbRsrs43ruoPvb3JVk+1/FJM2TgJKkM+55COx751AC/1IxylygIaWoqXRjC63AQNfBFsaurGfzvVG+M+XYwk"
    "xHE9mniFD918m7HdLwVVNCxFFoode6QSl2AWtGtyv6tPPqeuhWRrsuUn96QvPbjnCpEp1EV45EKCoOimBEvGmVBobDRAagDVM/gc"
    "kb/LH5E7Lc+Rk3CLJOBE9fT7/OD4Bha2+ObNccuVLxZBRQdAUHiEzJ6A6LYVS+SQJTilhAs9lZkkODnY28zX0yoglHlrHjmvdAx7"
    "C7caFtgUMShzWp7FAPauem+np5jiUkooY+DmuATtuBygLR4MdGresduku8i7L6KCl2OZTz/H0k2wRk9ZRR12nYfUT4bhdKIdPsit"
    "NxAj69xo1ZwV0hJ5MILf11wzRtdZEgqM1Bi9ra00HFxf0CflWDU/RoqHBXGZJeKnORzrGKCbj05J/GncFLTTz3105LwLmfeo/+yr"
    "u/UgEGdj+H6WiUaY1PC+5GnLbJN/xdZnsd6cLdJ2lGl8+HknaY9joBEgjmOQfIyM5YFufwQgqND0gECzaw3uGN7eHO8UYMyVYuCO"
    "8lsiV8J9DDl8jrqUgKIhceOLYDqqO2/QZAFtsq5rQhqslWdXyQgaR61AH/B1Ici1BegFIFiDdF2/gkJSCLVrqUoVJq2vXUdx+Lms"
    "ZEETfeqIfVrJ2Np/kfx+0UKBZjKY7fYYBBI3LD7EF56wHpN6LkH9XoMgqSlKyh3dOGFjsZlAY5uT+OZmhdNisR3fJZQPi7Ub/kXn"
    "XU7DvqIeg2sDQecWojHKXRI4dgfQkY1dCX5RpeCyKzGwXvdac2XYSp4ZDvVxdSAuIiKVL9oJesI5RZ6wV10MeVXFCwloOV02RLoU"
    "nknXgFFmAUpLzdTKLTe5bBpAUuiiSMG5fn/niB18EAfFP23K/FwmgYDd4SW2ey8qUEH7GKe2WLU2IGsfRZlkSs4jKcVGA93uB89/"
    "wVY9KOYZCfYvmy/5jNDFmfxugqMcLOglO9C0fHXNWX4WUgvNiemhadRuNuDSpE0crQ3ZTCDf7c/sPOf5Kdu8698/fQZMx44IDFtD"
    "IHM41jvUaLMw3qJGi9qr8wRZOqzuGS68GzuScuwCUecbdrfI2ckEY0MLeSk7NDoLubqsFADQ1rXa+Gjl63GXbnxzGosEpWQxYbhn"
    "njCWgCE3ZZ+Mt2sUBMTI/YWd84uun90BkfQZ69rAYMweakOl3Mw7HBfpe1sKUG0QywUfbL63z8tBaVwMGT3um2Z8GYN5ZtxGzaSi"
    "BXHeOImXq6Di0tr76TNvAdczxRDvv2qj6CRNjON3KzQkM1pOer/DzfpIiAtu+6v0HrewAxYatWJ8F+zPz8zt6S9KAhDbfnjxOGDz"
    "48REsYnf5RwzAstpqJuREL79F+SgNU6RyjiI6qFWghej/ZSwZ/gHdFzynQPiOI0ksQEIo3/l0FWemgepmAzICTMqRpO51ik2boV6"
    "alJwaMBTKC+L3KOdwghTs6hLmOIug50L1V+rAWweLLrneRinezBMApyE+bMCB3aHu7YxcYEkinJho6BRLIC53GFXCOhudR5Ta4z4"
    "iRN30LCsxf6p1ffUHKpXz94umNMFBMF73apACKiSxmmI1hGjVNNWwDATPFiEFZ2mfp8/+98JQ6P/y36xXKIClnQaHX0iQ0Dagey7"
    "zPOZQ2xA6s7PZmfkfYbhy6ABJ1FmGc0OFQacwUpCplRDJ1A4af/NQUTYEm7iOM4haXb57+cN6Y/oYt86fLGM6l3IqHQ15yn0QpmX"
    "eclPA9imo8kyA56Pgk47ru8GPYfZOmYsgDIcuPzEV7/GgiNrwmeHclkulfyi1Yrfyl+zDXdH5yNlCMeM85ET+8X/sPJCRHEdJk/J"
    "KjWwzA1U8x4cPUB0N/B0xULG7Bal63SnshM5TIjNrOaQ4WzoUvUIFeJOpjpRJAhtvqz8D+YWm7FSzBcO9F6sgPv9wWQXgS2g8vhc"
    "vUUkd/oYhWrArLrfVx9vXuFA4Xwo8gzd5wuZxDCrGD5mUVdwYDy6nYvQ/ksyuywr1M6En2BtlSE7PJ4AjC59HpWF0xt9fgFb9SeO"
    "dkU/n2kGY3DzHFT8ignhFTlULNfMj0A9UjazholsQH5Qhfqg2w39XAlcWsYut2tIqLTDBJZsR5N+eNpJDoeBwqPBoYI7asCyQLT4"
    "lTzDC/7kH3A7nOkREcejxjbxOHf9h7b2ABNHFX9SBVuiu45OWxGx2X490mL+9WplFdU2lCcXzwe7slcK/Oh9ZaZ7BpXsEyyJhvBB"
    "aYxNfVnzlwzU2Y9Qe1JK/LJ60wQiJBritxfPSAijjhpCIn0zQTB1iLLJf0IGrJEE7wYOU8CKdp7kmMYJaqvBGvA88bIBt3aylUZH"
    "eaQTeUF//YbKY0FE32wgyz9KzTM6ITBo3GukHXNxQGau1Mf3BtOdggPPvIL4u4KGi3ckqDRy4O/cIuvXK2+hT/RXvhdNw/LzFq46"
    "AW7e0ym48OvwoeWHXfc2hgYVlLlbzNQmM86/qoKQPbDwgiw5CEJOlYF6E2rmdZ2sZZxsQaoIYsnjmk4tnlY7RTcRHDHad6aIjo5Y"
    "pc16k75EHR5mgyEQFt9gYHch0RrNkibWAiuw9tgnKXheDTwNCx6kdcNWwq4B0BJi6fPW/fCzpwRNuMZcxIMe4zmfTrB0fiwkiwpQ"
    "x77gRDIZc/30ElJGEIRttzgeE2r7rSsuk+l9tAtlE/EwwbvCH2iPKfwPwbjWmDz0AlGUQLK0hBLQB/RXtIR+b0V7aN7VVOEIBDry"
    "LwjiOvx0nxu4V4y+xV/RAez5lPD3n1hS+KK7gQqSXS83J3R8a2Nc7HbmGTrrftqXxiGcl+inoXYGYS8J1SZi1QG/HqKw5f5Y5Rlx"
    "U5Ow+5F2/UL4rjBjK20iJkBCJ06LfuOYbMgbbgUt7JMdQSL5Nc1G85zLnTkJjSTHyYZo7fRBKixEA5NE4M+EkPQfmH4pTDV4i2Lm"
    "uhnYlfHKTF7NOHyyjdxfLFNKGDTEp50o6RofTXSWOiJW1ZNgBkDqu/x6Zt+/QVE3EMMoQAsxjVPoioopY0jbZrMIB1754lh8wNrk"
    "6CpxjdJi4NjgCKRtg6U757O8QN8rIFkb+uL7b+D4lO71AAYyjIHK0uxclSDSXvREFYxzvYCgvntOt0bPfmds/majI8OL5jA0ymoJ"
    "4aWoVmoVlRgKum+siJOuqiDO2zPiIFpmqOsSoanom+RCHyo6vQ8RxROwGRooqFIZ+H1rMIGsFLd4CXOhGfaLBQXBLX8awGCOazwf"
    "CPfPJvdtmp8aJ3KFNL5xqoDZjPqQXHoDyJvjixNrCiMQ4MdJ5b7k9HDIhkSiLUg8vkNPkCPkQDS2IMkjnNNJoqE8ZJjqB4r5Y7FC"
    "h88Th37MQJe/25chItaAv6v/ZWmncTJD+c8w0CjLgZS1CKoJ5j6/VkEPFXBB4E86eopJ8AD3CrHr3UMDKsbUXS51ajNcWtyGstHd"
    "pa2tVhQ0MorQaRogOY6rqr35ZJ7TxKlTrL5qmvYCng2/ZSODa6CqjRKIQ0Scs21Xuy/zNLAiDPGZfr/5UxWA9x+wcE0osCb/e3WL"
    "LCPrAGxb1bqvNeezCYeyqxY19AonwOnDl7IKJbnAS2aTNruC7tWxFodqnWRGwVx06oHnNNq8wenUbrigQLlK4Uqi9AyQzlyBU6D6"
    "GYCfFOAErXmiAr2Sx9AMMR4pvUF4kfTXLhoIpxU2EIjYr+GByQKrLMtAxGQhNTS07I5+pXeqkCBLoQRENG5GMyWpVCMWlfqZpLbR"
    "SUBgKhMoahGmB4BHCv5ILOu1h50y562+CeUExKa5hVeyVQQ5ENMlNSwAF8qFhsRUIcjKeTLsx+V3AABt7nkVNCLiwFSg+HZt847W"
    "t/36EXK/whrzdPxuswCmaAE58xD9l65BQZofFH8CP1MoJa54rPD49+VcrZGp5IsTOsm7AYp/+KaqzVHRZFQnGLp/vjYga7EPms2p"
    "jxMWczZi6siiAtU9egNbikLhPNIwbk+a8n+m53jv3S3LmZvzb3Qu6OIOJ5a7LETIfTaONqds/qS5pZiuIlOtZAr9hnK7MD9doxrg"
    "RWG/3eJKt5guAAwFRaxqqKZ2DP39zW/6FAoJsUkCm9HVk6QPY4zCpACm1Mh34OmhB4FpTV2HveqaiqeZk6bsgW+WQupiza0pTce/"
    "3TujtC0MrwBSSZQfCddmgCNm/HcDaVjBCU1qfJ+OZBX4RHS+VMBlAnRd+YJxf2D1v5AuB6wy2eItMTF7yvSQs/qC3aj2r0PlOVNW"
    "P/rp8Zh2JRFkYu1uQGbvhKDcjNcowhvR1yYILaRtICskinmFYMPoVFDOWdo1RXfeZZws6RHEsDJU4mxFNtUt60aPheUs99i9FrnW"
    "HY9BRKaRwYcCUhehMPd8fopMIOvnK6rc3sz123l2z+wSjsIOJXLEj0jFSxMecwMuig3B5lloqMkrfaSJ/zBjfXXtKjVzwMIEXU4Y"
    "6AUGj9pLabXjKNXmc2kQ7oeN0dqZA8QGsNFFtvqaqtYpE7Eo5DYpbTOoZEMHrq/vhNuA9yWRiLzLAeHaqZzntJkw+l5I8B6ugy6t"
    "QUvA86lhUfl5Npo79JuzyiA7Kt3lake4e08nDT5ImP7bwOoLOE48NTUgolu3185VFGJXHiBvBbQYmrEEWV8YExNta4tGY/ckBSYO"
    "t9cjmLIqsY7VsrhGv176U/2m72GLlRBMZSr1KkbxbtUHmPN1/KjsD3vQhpy4+aDrcWI/dzEK410ApmtAp8ZmVyMvsUAyslIAV0Ze"
    "5vOw3dWfXPFx4jTll0+omDeCQaKBc4b2QFlK4IntOMZJfhseUbOKeL7dLNjlYUWYYJ6QgqWPNZ5jmjJwJMg3F02/jA35CLV3mI5j"
    "0FMq5yIdLWCFw0/i2XLSSv3ybU2R7Fls8QyQtgNcbMgl1NmpyvVSWiEo00ydxdA5TrZ5QOLCowjztcKnaFLUay7eGJvO67I7h4Yn"
    "V953eIPjGrQkd4YyQIOmUiz0yqkZOkVsQpYqRWTTCojDzkAenGkAjVYp8rYjUC0B58iklBGTfunL3YCfxYFL+UqseIbQOPxD6l0H"
    "CJwR88t2V82/ulwZljCLLw7lWMKWxFWWgJLA+QbIaMpoyqH6QeW9P/kgAGiBDsbUCzydzvjXSUOXR9AOMKAra/CqQjaaqh/lDueg"
    "6b0VvBuu3LpP1BheNSuGSrURKP430yAoNEK/YhrX2qcKxfJy+HEZCvtE9hwlyFr4S8g9JTqPpK+wKGUl0xKiu+OofxrSMFJFi0dO"
    "kaLO7EkojKTAwhY2w5IV4EBgUG3QveueHPhhXclrEjGeuy+FnW3a8TYeIazNBqd3fNYdTmUfc+sBag6DRebFdLBtT3EsMC70S++m"
    "GdBF+QIwO650tweuqj7SNtDTkSa1gZ7qgvhwI1qyxnPXt5mCNlqaAm7tx0GBrd32gbpf8sEdOFmeAd0KxjMuDaZKyG1kRakhBPmj"
    "gQTB2bSFJ5bDwe+Fsd9Rs8EplcgZ90gm3+ZFDjUzxmD/OX9DP5Weq4HyBcIAZb1El6dxL/IWuvfX2g3VkihK/WXavXZssPbnS54R"
    "HiSVlrrKiP2vSyBC0TXwiNdqlT81OJYJbBQNUF56TWm8mNTkeR2zi/w7d+8vb3vOmSXMmrztKc3nL9IbOC19QK9Ulj58c0M504ba"
    "mSANj3X8Wl0ozALD6pBQCrtcA4r7/cOnQGDWjYnPeYDPj0uhsfT6oLNATbuPkr6Cn+YV74A4iP4hhkxjQxd/3o19NfkNLE2DGLZc"
    "QkzMrQCQ26Dpp8uERTA/eaEC3WSw7v39KlADLtOj5Zcdsa/Y9sg74lIQp1/hVMzrC/noD2hmZ/S7iwMg0wqqTl2fVkEr0rZYR9vT"
    "AVsUFNLvD1fIogzcS6FpouzHoTKMNNJCNCy8tEo1pDDcRrcBCGNliQlixQIEUxsZjWtX/EXZ8TZ10PwOcP1lL0FfOWRdUa86cW/8"
    "Mv3zKauh0Byz+fFTrSGfngRjMsowXLPrZ4CF9tkENXOkjJXq/MearFyXxcgf4ox7OKvzaBXsYWMXArNLO1ArOvBgk4rR2aRkKgqD"
    "6/mUq8zBAVTaNItt7u5zYFD1qdJ/xP8+t4w6PhS6ewNY6eBjQ/tJG3pZO6yjOqIfABqSgq7f9ve0knV/QdnHILAzgrIt/amc+9/k"
    "7BtiIkCTPVybgTquDlQzms+zRFw2WY3XjwMy+DI7odwxI7kEUMuYTYfZx3gKQ8JLdhrssj0I1JkrHHRw91LTIbu+PvT1Jt/mIfq3"
    "RF8MjvC137lHp0EJtqC0IUkSaEVYJ5EXB40dPNCb7Nv+ZZY5VmTm2tafm2o0hnV/J2orCcVjCq9KR+89Uuvfq7/EYYUDoLGdSYx8"
    "XGfmEjU7xaubAhAyeTpVhhimDc9D10CNUIIPA79osmA4FrfWYimURORblAsCvzJ4NExxpvDjNxDNssdj07U1n3Uuha5xPy6G3dXQ"
    "k9v98PdKjyhlob2KSFphem1wwJoxD53ejkdbOUpEnZSslEIN0RRFoUdvMJ1hPuxS1Ti5FOAw5ln4pkT0mRmUj7m7eKF/aSo15Yn/"
    "awAAZefQLVFfFgKw+BUOZCWzuypJiDhBz7K7IFiJDHKVwaDjgzMBtlMxugkL3omCHwAMWNpOEHsBAUKWrBlyWlj1i3uCkSx7BN1F"
    "LeRL0flfqgb41eqBdANx6IOCaAVlAz9EcIX89SK1CcPKcgNWamBz2aAdnRcbWDsyG7lOMrY+ntt08gYJJN4NnKWTcPIk5LUbktMa"
    "F0kygDpYzNpiqxiczAm2YZJZgA6pe5JipBZFiLrCZwaSta7A1fm8pjVoCuAYuWen5TiJXX4TcyLRuPRkVzQW09vMeobqxlYoULKl"
    "OTPXYnxPbNRNZKFeInubNVuamZ7T4zBAURLkp14rBC9wTP0QHUWHgkDhGUBNVXyPQo0/rB3l8alcfBYkS1nh/hHfclXEXQXLRrGv"
    "Av9fh3hpAS9wuh7lvoIDYPJvaRjyew9sOCD2oGjcoRaD4a0L4b62qlcl/EUVr7OXpALl4ejY4ploHUOsJhwhiUtoMIrtKbkvED63"
    "bQYAeVAmpjCnZ6zwDRbg9Mna8aoen/MNsmlghwk4uyLH+F+yOIdtgTOiQdVI2BFhFMTJZ4DmZ4uQFuYy3AT1c4PHXpIGuDXtoIKP"
    "/gLG3EMIkOTONpbIH/k3ge9wkegpWYur/7T4FKVqIcPSExoLh9WZV8xAS+0xiuh9Jf8vbguj0AcHmaFWwbUxT2qgYX89o3k8hrNo"
    "cTxpDRrN919qJejya87Viwe2dqO2Tfvp5im3U2y0MwgpH8P8Tl0bI382MGFOgFgaRuIGjbUOIvK5FLNiee0OA2dAvWMowtuaD1BU"
    "/H/v3x6AsOFRSAex4X/FYEd9FTj382f7xn5f+bPCiyDWJxtBebJBLw58I77/qmCtWTAXnvvSJrNCE+Seum4AEDoFmJUNkKsDUkz2"
    "KT+OQ0f9O6mjdYAngw0jXRgP7Hjl7vpvGRW39kg3cGHOIBDB1Pz4FMdrpF0K3ohLVy91kjpUxujThlrNUiaDdSFP9AF036H/hprx"
    "aBPI9Jt+B6gErkORHkFZf3tvcHOvjQrA8o6vFAMExL1WWlAgdRMWgXR7qYjO1texwUz8tpgOCqROFU29R4zcP5CKHLefj9sfrAAs"
    "y2pafBP8uy0DdEEbZ5ysrOiPReaEFVZ0DZCfYoHCN7zYTJy+2Cf+xs6ZzwEU6PjPXlSLA7PM4RZoIZVG9j/sIP6Y+jZ6TEkEELoy"
    "49K/hEMe17BSsA8XEXRoGFYIW5mQsMYFfgAi5Gn0FCD67P/sdyZSzb+L0SwVbbC0/7AUwSmetHkL6cc80EP+3rB4nIBLR+Bg3uDG"
    "+pSBltc4fU8S19teqiUdm/ZrIt8p+wh+ia2ZtRqbtQEPx+9UjA0F7PeJGZpZrJCnqjtA2TxFAzK6BkRcxKkp0mJMHRpejCZ+hMl3"
    "1lQVk1ugDk2l6QCNYyp+SzBfsBMiQgiFLONeBT+Q0k/xa2ksYgDzvhueKJqfDkjVi6AIt+2kQWTbe124yH+AEVL9VNchyyV5MA2F"
    "z/BuOHSif3uU4TiW9GEeu39pkb+ozsSv2xIDgJVzRE7rB3zzi+mB7AzqeAWt+qJtb0PYIDL3yRUMiif/gaTnTSrKOvcEDg2Y99xD"
    "ni6izSLCHCvT9srTMYuNpkVZNkPURgI2VkMIfcxZMsLhgJA/Agu7zVVpodLIHkLkaCIZREizoZo7giz5Tdn8tRiW2l/AJqP9AJ3c"
    "hWjoaUhgGJXA88aV2q5UVxjDr46wLY892x1xVVxQpBoQYSoYv/mNl0U8fTkxV9chLZcOKl2lvoQCmqg15ZsUqX+cTKeFB2grI9UX"
    "qLiglQP1Yy1taP8GmAHFHTtAntSJSYmqzculUqiesmhGOcBQ0E+DmC4tobc1DhBqUqpaNmGmZeBYrgkiFC/n7R+zmEI0pIk+MIuv"
    "cX9/7yBFIM7+So5WcWR3mvonI+Q02bbZoFgIV/LN9mfnSOrcnsCfBJrcmuW19PpdCM+qXRsnU7dfIrSblOIRfVoITTAoNn2tdjJF"
    "OJPDRpcz4LcQNtXMI6h8DSxPnTtOMC8w7+oqPP0GysrKfza4ffeelgHyKnRGH1bqBtiBNHuIBO/2KKjhXGAo07jU1uuAb8mN/YnD"
    "9IClAMJjOCDWfc2FgULqhUFpaydHufGmikkIkuM7WsA7ZPYs25S55QIYIxs985fhnd7ibl1IpX5X1GgjA/bYvxfVDO92htfy6zp2"
    "yXiLMIQH6KXw3FqoZuFY0hGzdQ4FA+oKJFXqHLOO7wAtzXBTAu3nFm4+6w0830YBLLT/m1dIO9LCCS6hlm+VkGvGaQbx0tipaUGM"
    "pVyaSbhwsUV1J42A5W0sSh8BsM/xDr9oIOcHOVCIhpZ0TB1/nMt939JlaQAXKyim0jg8e4I5uSwZKDAIG6o1DfDrvSAoY6bRa0zb"
    "sr20BoQrS4JcyAxbvx7NvXlyZKWYkgxp1mDDuyh4+YuorpvQVKAbrGPSDnjUXkqOwiu0ylET18abGjAbthSlb4OwWcXMmMSNfm0W"
    "Iew2xnY/AAwzlk16bnFBIZ9mmERdA4feml6BH6FNOL6TaYAP3dLZWrtj1PtEBK91NCAfGxrPZGAZshQ+5X2rwqpS4oTjT+SmSxQ5"
    "QIwOgh+8VmGGL78VPooMmkv8fl4OUNYNyixvwa8fNhsxLed5s1OoOYlffI8iNtZOo0ZQ+rARVvz0Glm+m7gQvp+rfoUiiko9PH3L"
    "fpkid9akg+ujCTYwSTmmLyUGSo4mATlygJkua4HHG3DcoWQI44M2pUkmxQB6/9e0T2CgdGVzSRZ7ZSjreP3dPMV/DTBrGcpc7VSY"
    "8fFlFPGvFSE5UVRjSmNS2wIlly6glczKDRyC+EJ47j7W/l3HFxibRAQhhC4ka1Lie4m4BoByj4Q5G7SZ+WJh2zligxBqAYJarPNi"
    "qZjsh1UamjRoLW/gbLnnz5FhMR/R/0N7UMn8eOM2jNqCypx+8sLBPEPjAyUBW8jU4D/bOXi03kmvA8ZhuVLcfRymAMKhOVSOymE0"
    "vWrdnwFmnLrmYhqNvPAfYJZFihSk1uZ/Nv1hHB92EBdW3gn+cYStOaXbBxuW9hnerhpLYvdvoQMZ0qOdHKYKujiT2P+99U71Rtge"
    "m1DvabdxUMeCKjhofLVqYAWjJ7bwGNOQTtwh+LD/SSPcKaYJq1hsTU7ENP1pBqpHYd1gD1NaaKR4qrYCxZsFp4OHS1MvZwL/1kPS"
    "DfZoLQJ2bjNdoDSm7OcbASWR6ZeXDIpvdt93M/qXL6eBo/RoDJ8BYIgzpYfzzAZdV383AIjFS0ltxA8TpTu8asukZ5COTwVwelXA"
    "y3to8EVJb7UAh6GIDQnz5LZFMTHkPv8xfZg1ws/iGKFrizv/3SxHszNtqHf8VSxCGdBUMXDQN4AMjWG6hPyMDKNm2HWAEcx7avr5"
    "AD0A/JAlapsq7957ugHGfrTy5h9g43P3skwYqU0WYNk/xYD/Vfqg9l/SUqxQXbjwT4ENRZuIbbeewA1gp95HzGvOyvzNVwAWwZOy"
    "bs9yeMR2HSpSjEqXU9R+VBxks9n9foNJuDdej67IofZMSeIYrThMwPqODG+nyJ+LhCIOdPymOooboQaq4Re4LERpY/rFRnYLmCJL"
    "mV+878s1Aio8aZk4HMwBNoFhknl2OCO1jtQBiJnv/aZKOU6bAdYm5LK8WKYDgJpvxaf0gLbK37PBIrKB65gNH67HOPQ9xAHUfNlY"
    "1KIQFDOa6S/SyGQL/g8SQUllHNC2216dKXCJT/GfY1dGSBtU/cW3bT6JP8ehCcZ3W/j8nxU1A6V/i1UAblFvZapacSrVs4FTV4WY"
    "qhBRi0J4uQIhM4KFP23WAidvklTxdUimot2MmTvwBQWwoMuqmslxP0q1zA+K0azMgGTMyMTGpZUwmL1lW8jANcmC4wcgi26yoLNR"
    "G/D1hoUr37+kY9jt/qTxsUWzEhIC0PAGOdWhP5JRKTC2b/StCzXT1cdMDvA+fkCH++icSf8IUxEbNS5S+I9mt7cjNNwQMx3ivLfr"
    "yvkhTCVlF6mZmEnGDl9MQLUy9tqyaRwQQuIkAAIaaKbEn3GOlWTIAnJM3/FtkuSxq3KpR6NREjhqMs7YH5nhCCuaSvQS2mEYsW9x"
    "HEActodCDLCx4jA7riI34IiCKkTWrT8Q5BNFm+cw9AvHBtN+v21gh6DYKM5JtW/jaKASg7RYFAPVMAugJcJDGIsc4J51wBBv0XMf"
    "lfOHDdwQzQtO0I8jzURRAxJmERrRiCbT1T0795Pc678YDpdARzwIqweGkq0LHIQwo2arENUljTccstraN46LOPUS7I4VKYpMNAp6"
    "rKc/bxdR4qulpqE4DDy5F44ftKrVhWe5jRfQQhH/RGi5YhQSOjziyHSj4z7+eKYyYGmgoXlh/telC09nm/9lmbfuMx1wSOESfoAp"
    "kZzcbDKuPJEYPkkGF0+ZxqXZ4zuWFMGRYE5cElLULpxWhtBNz5ONrYewSVGl+xfk8mW1vAQUXAWEc1kCuR4DQaW5Ir97xnZ5ikgb"
    "Wv0MPeCBYJnsC7BBZQ9UCC1mLjDCkMHBhVyswQ/eI1KCNfYAIQJRUHuK7nJCc57y/wB7DYaOGZ3+m9w3dGTFhVq4oICrhB9YLIQS"
    "gDcbWo53QjhR8xwq/nSlWtTlW+z3j/stMZGLgCdRtESDvlLx30KZ/l4iQifgrx6cgdQQ49m0A6/1pvDZz7Pa+DNtIM+d7HW7qthw"
    "POTvXVKjmPm40+1rIUDdaPWpfww1UrHVHLq7iJUYmtqzacjh04THpgAG90xTWNaeCT5Un9gHdgtAypkaMxVZz4gTkDKbcXKKvBRA"
    "PM2qbswbfUPxbO4yjM94SJamNon7z0IAsOsfLadXdAsTY6Og/3wWIAAVkA5C8rkf0EuuhnGDa4Igfx5dpGjX0cHjBxrNMLXSBakt"
    "DwrGKZjdhy5IenXckKo3sqcCDKX/FPBbzfCzjOfYbElhC3LDTS8HxY0ADE4ejbJkCizRJ/A4BDtwjthuqZMJVNTKjmn0bX05I05n"
    "1CsGKaLUHPm6XwVfSQT6/Ctwr9Sj1swZqfH6OasSmUwXBKDEo6oAmWEM+ZsdtT3ewxIxvcmNhtEALkqBx6/YsgI2rtk81d9SfAoq"
    "grduCzUspREQ+/p9+uSkO7/UzAtIx8+zGXJ9aunMV8KQ0iLFnKAfadNsn51Hi7/WgSLn2BhZGyaZJqQTx+JjqY5ZQkA0C2woDdul"
    "xRsm6+FwtuyXToK1TK/KHwOnAV37+LwOIL7r6mHR1jbQpp6XEeDS1KL3v7n55TEbyD1GMCoW16haCxJ/K+sQ9F4ZvIXcSXg/6P8B"
    "yCu2hw=="
)


# version 09
# speedup vs baseline: 2.2030x; 2.2030x over previous
import base64
import contextlib
import sys
import zlib

import numpy as np

try:
    import concourse.bass as bass
except ImportError:
    sys.path.insert(0, "/opt/trn_rl_repo")
    import concourse.bass as bass
import concourse.mybir as mybir
from concourse.bass_isa import ReduceOp
from concourse.bass_utils import run_bass_kernel_spmd

F32 = mybir.dt.float32
BF16 = mybir.dt.bfloat16
F16 = mybir.dt.float16
AFT = mybir.ActivationFunctionType
ALU = mybir.AluOpType
AX = mybir.AxisListType

B, S, H = 4, 2048, 2048
NH, NKV, HD = 16, 4, 128
DH = 5632
TQ = 1024
NTC = 16
NQT = 8
NHC = 16
NHB = 22
NDB = 8
EPS_LN = 1.0e-5
EPS_Q = 1.0e-5
SCAL = float(HD) ** -0.5
EXP_BIAS = -4.0


def _split3(x):
    c1 = np.float32(np.ldexp(np.floor(np.ldexp(x, 14)), -14))
    r = x - float(c1)
    c2 = np.float32(np.ldexp(np.floor(np.ldexp(r, 28)), -28))
    c3 = np.float32(r - float(c2))
    return float(c1), float(c2), float(c3)


SC_HI = float(np.float32(SCAL))
SC_LO = float(np.float32(SCAL - float(np.float32(SCAL))))
NELI = 1.0 / float(H * DH)
NELI_HI = float(np.float32(NELI))
NELI_LO = float(np.float32(NELI - NELI_HI))
I127 = 1.0 / 127.0
I127_HI = float(np.float32(I127))
I127_LO = float(np.float32(I127 - I127_HI))

ARENA = 212288
O_TB, O_FR, O_KK, O_U0, O_B0 = 0, 8192, 16384, 24576, 32768
O_XTH = 0
O_WQT0, O_WQT1 = 32768, 49152
O_WKT, O_WVT = 65536, 81920
O_KT, O_VV = 98304, 114688
O_QT = 131072
O_AOUT = 0
O_WOT = 65536
O_XRES0 = 32768
O_Y1T0 = 49152
O_XINTT = 163840
O_XQT = 131072
O_HH = 0
O_WUP = 98304
O_T1 = 98304
O_T2 = 0
O_W2P0, O_W2P1 = 45056, 67584
O_W2S0, O_W2S1 = 143360, 154624
O_TBL = 163840
O_Y1N = 167936
O_MST0, O_MST1 = 186368, 190464
O_USCR, O_BSCR, O_TMP1 = 200704, 203520, 206336
O_KRAW, O_KROT = 200704, 201728
O_TMPK, O_TMPK2 = 202752, 204800
O_RHI = 206848
O_ST4 = 203520
O_WST = 90112
O_TSQ = 90112
O_OUTT0, O_OUTT1 = 0, 8192
O_COLS = 209152
O_ONESB = 209536
O_ONESR = 209600
O_P64 = 209888
O_PT0, O_PT1 = 210176, 211200

C_M1, C_M2, C_M3 = 0, 1, 2
C_TH1P, C_TH1N, C_TH2P, C_TH2N, C_TH3P, C_TH3N = 3, 4, 5, 6, 7, 8
C_IVH, C_IVL = 9, 10
C_SMAX, C_C1, C_C3, C_CH, C_C2 = 16, 24, 32, 40, 48
C_LS, C_LQ = 56, 64
C_T0, C_T1c, C_T2c, C_T3c, C_T4c = 72, 73, 74, 75, 76
C_LM, C_LV, C_LR, C_LR2 = 86, 87, 88, 89
C_MACC = 80
C_EB = 84


class Sched:
    ENGS = ("sync", "gpsimd", "vector", "scalar", "tensor")

    def __init__(self):
        self.ops = {e: [] for e in self.ENGS}
        self.val = {}
        self.cuts = []

    def sem(self, name):
        self.val.setdefault(name, 0)

    def wait(self, eng, name, val=None):
        v = self.val[name] if val is None else val
        if v > 0:
            self.ops[eng].append(("w", name, v))

    def do(self, eng, fn, inc=None, amt=1):
        if inc is not None:
            self.val[inc] += amt
            self.ops[eng].append(("i", fn, inc, amt))
        else:
            self.ops[eng].append(("i", fn, None, 0))

    def cutpoint(self):
        self.cuts.append({e: len(self.ops[e]) for e in self.ENGS})

    def barrier(self):
        self.cutpoint()
        marks = {}
        for e in self.ENGS:
            s = "bar_" + e
            self.val[s] += 1
            self.ops[e].append(("i", (lambda ee: ee.nop()), s, 1))
            marks[e] = self.val[s]
        for e in self.ENGS:
            for o in self.ENGS:
                if o != e:
                    self.ops[e].append(("w", "bar_" + o, marks[o]))


def build_program(cut=None):
    nc = bass.Bass("TRN2", target_bir_lowering=False, debug=False,
                   detect_race_conditions=False)

    def din(name, shape, dt=F32):
        return nc.dram_tensor(name, shape, dt, kind="ExternalInput").ap()

    xT = din("xT", [H, S])
    xres = din("xres", [TQ, H])
    rope = din("rope", [128, 4096])
    wqT = din("wqT", [H, H])
    wkT = din("wkT", [H, 512])
    wvT = din("wvT", [H, 512])
    woT = din("woT", [H, H])
    w1T = din("w1T", [H, DH])
    w3T = din("w3T", [H, DH])
    w2T = din("w2T", [DH, H])
    p64 = din("p64", [128, 128])
    ones32 = din("ones32", [128, 32])
    onesr = din("onesr", [1, 128])
    out = nc.dram_tensor("out", [TQ, H], F32, kind="ExternalOutput").ap()

    es = contextlib.ExitStack()
    arena = es.enter_context(nc.sbuf_tensor("arena", [128, ARENA], mybir.dt.uint8))
    base = nc.lookup_mloc(arena).addr

    def vt(name, shape, dtype, off):
        h = nc.virtual_sbuf_tensor(name, shape, dtype)
        nc._tensor(h.name, h.shape, h.dtype, type="SB")
        mloc = nc.lookup_mloc(h)
        mloc.allocated = True
        mloc.addr = base + off
        return h

    TB = vt("TB", [128, 2048], F32, O_TB)
    FR = vt("FR", [128, 2048], F32, O_FR)
    KKt = vt("KKt", [128, 2048], F32, O_KK)
    U0 = vt("U0", [128, 2048], F32, O_U0)
    B0 = vt("B0", [128, 2048], F32, O_B0)
    XTH = vt("XTH", [128, 16384], BF16, O_XTH)
    WQT = [vt("WQT0", [128, 8192], BF16, O_WQT0), vt("WQT1", [128, 8192], BF16, O_WQT1)]
    WKT = vt("WKT", [128, 8192], BF16, O_WKT)
    WVT = vt("WVT", [128, 8192], BF16, O_WVT)
    KT = vt("KT", [128, 8192], BF16, O_KT)
    VV = vt("VV", [128, 8192], BF16, O_VV)
    QT = vt("QT", [128, 16384], BF16, O_QT)
    AOUT = vt("AOUT", [128, 16384], BF16, O_AOUT)
    WOT = vt("WOT", [128, 32768], BF16, O_WOT)
    XRS = [vt("XRS0", [128, 2048], F32, O_XRES0), vt("XRS1", [128, 2048], F32, O_XRES0 + 8192)]
    Y1T = [vt("Y1Ta", [128, 2048], F32, O_Y1T0), vt("Y1Tb", [128, 2048], F32, O_Y1T0 + 8192)]
    XINTT = vt("XINTT", [128, 2048], BF16, O_XINTT)
    XQT = vt("XQT", [128, 16384], BF16, O_XQT)
    HHt = vt("HHt", [128, 45056], F16, O_HH)
    WUP = vt("WUP", [128, 16384], BF16, O_WUP)
    T1 = vt("T1", [128, 22528], F16, O_T1)
    T2 = vt("T2", [128, 22528], F16, O_T2)
    W2P = [vt("W2P0", [128, 11264], F16, O_W2P0), vt("W2P1", [128, 11264], F16, O_W2P1)]
    W2S = [vt("W2S0", [128, 2816], F32, O_W2S0), vt("W2S1", [128, 2816], F32, O_W2S1)]
    TBL = vt("TBLS", [128, 8192], F16, O_TBL)
    Y1N = [vt(f"Y1N{t}", [128, 2048], BF16, O_Y1N + 4096 * t) for t in range(8)]
    MST = [vt("MST0", [128, 1024], F32, O_MST0), vt("MST1", [128, 1024], F32, O_MST1)]
    USCR = vt("USCR", [128, 704], F32, O_USCR)
    BSCR = vt("BSCR", [128, 704], F32, O_BSCR)
    TMP1 = vt("TMP1", [128, 704], F32, O_TMP1)
    PD4 = [vt("PD4a", [128, 512], F32, O_XINTT), vt("PD4b", [128, 512], F32, O_XINTT + 2048)]
    PD6 = [vt("PD6a", [128, 704], F32, O_TSQ + 1024), vt("PD6b", [128, 704], F32, O_TSQ + 3840)]
    KRAW = vt("KRAW", [128, 512], BF16, O_KRAW)
    KROT = vt("KROT", [128, 512], BF16, O_KROT)
    TMPK = vt("TMPK", [128, 512], F32, O_TMPK)
    TMPK2 = vt("TMPK2", [128, 512], F32, O_TMPK2)
    RF = vt("RF", [1, 512], F32, O_TMPK)
    RHI = vt("RHI", [1, 512], BF16, O_RHI)
    RLO = vt("RLO", [1, 512], BF16, O_RHI + 1024)
    ST4 = [vt(f"ST4_{i}", [128, 256], F32, O_ST4 + 1024 * i) for i in range(4)]
    WST = [vt(f"WST{i}", [128, 512], F32, O_WST + 2048 * i) for i in range(4)]
    TSQ = vt("TSQ", [128, 2048], F32, O_TSQ)
    OUTT = [vt("OUTT0", [128, 2048], F32, O_OUTT0), vt("OUTT1", [128, 2048], F32, O_OUTT1)]
    COLS = vt("COLS", [128, 96], F32, O_COLS)
    ONESB = vt("ONESB", [128, 32], BF16, O_ONESB)
    ONESR = vt("ONESR", [1, 128], BF16, O_ONESR)
    P64S = vt("P64S", [128, 128], BF16, O_P64)
    PT = [vt("PTa", [128, 512], BF16, O_PT0), vt("PTb", [128, 512], BF16, O_PT1)]

    pb = [es.enter_context(nc.psum_tensor(f"pb{i}", [128, 512], F32)) for i in range(8)]

    sems = {}
    sem_names = (["c0", "m0", "m1", "x0", "x1", "tX", "tH", "u0", "u1", "u2", "u3",
                  "s0", "s1", "g0", "g1", "gq0", "gq1", "gw1", "gw2", "gc",
                  "a0", "a1", "eT", "eV", "eC", "eG"] + ["bar_" + e for e in Sched.ENGS])
    for s in sem_names:
        sems[s] = es.enter_context(nc.semaphore(s))

    sch = Sched()
    for s in sem_names:
        sch.sem(s)

    def C(i):
        return COLS[:, i:i + 1]

    def vsync():
        sch.wait("vector", "eV", sch.val["eV"])

    def vsmall(fn):
        sch.do("vector", fn, "eV")
        vsync()

    sch.do("sync", lambda e: e.dma_start(TB[:, :], rope[:, 0:2048]), "c0", 16)
    sch.do("sync", lambda e: e.dma_start(FR[:, :], rope[:, 2048:4096]), "c0", 16)
    sch.do("gpsimd", lambda e: e.dma_start(ONESB[:, :], ones32), "gc", 16)
    sch.do("gpsimd", lambda e: e.dma_start(ONESR[:, :], onesr), "gc", 16)
    sch.do("gpsimd", lambda e: e.dma_start(P64S[:, :], p64), "gc", 16)
    mk_cl = sch.val["c0"]

    sch.cutpoint()
    sch.wait("vector", "c0", mk_cl)
    sch.do("vector", lambda e: e.tensor_copy(TBL[:, 0:2048], TB[:, :]))
    sch.do("vector", lambda e: e.tensor_copy(TBL[:, 2048:4096], FR[:, :]))
    sch.do("vector", lambda e: e.tensor_scalar(U0[:, :], TB[:, :], SC_HI, None, ALU.mult))
    sch.do("vector", lambda e: e.scalar_tensor_tensor(U0[:, :], TB[:, :], SC_LO, U0[:, :], ALU.mult, ALU.add))
    sch.do("vector", lambda e: e.tensor_copy(TBL[:, 4096:6144], U0[:, :]))
    sch.do("vector", lambda e: e.tensor_scalar(U0[:, :], FR[:, :], SC_HI, None, ALU.mult))
    sch.do("vector", lambda e: e.scalar_tensor_tensor(U0[:, :], FR[:, :], SC_LO, U0[:, :], ALU.mult, ALU.add))
    sch.do("vector", lambda e: e.tensor_copy(TBL[:, 6144:8192], U0[:, :]), "eV")
    mk_tbl = sch.val["eV"]
    sch.cutpoint()

    sch.do("vector", lambda e: e.memset(COLS[:, C_MACC:C_MACC + 3], 0.0))
    sch.do("vector", lambda e: e.memset(COLS[:, C_EB:C_EB + 1], EXP_BIAS), "eV")
    mean_marks = {}
    mean_red = {}
    mean_jobs = []
    for mat_ap, rows, idx, w in [(w1T, H, 0, 704), (w3T, H, 1, 704), (w2T, DH, 2, 1024)]:
        cols = mat_ap.shape[1]
        assert cols % w == 0
        for rc in range(rows // 128):
            for cc in range(cols // w):
                mean_jobs.append((mat_ap[rc * 128:(rc + 1) * 128, cc * w:(cc + 1) * w], idx, w))
    mean_state = {"next": 0, "glob": 0, "idx": {}}

    def pump_mean(k):
        for _ in range(k):
            if mean_state["next"] >= len(mean_jobs):
                return
            src, idx, w = mean_jobs[mean_state["next"]]
            mean_state["next"] += 1
            g = mean_state["glob"]
            par = g % 2
            if g >= 2:
                sch.wait("sync", "eV", mean_marks[g - 2])
            ms = "m%d" % par
            sch.do("sync", (lambda d, s_: lambda e: e.dma_start(d, s_))(MST[par][:, 0:w], src), ms, 16)
            mk = sch.val[ms]
            sch.wait("vector", ms, mk)
            sch.do("vector", (lambda p, w_: lambda e: e.reduce_sum(C(C_T0 + p), MST[p][:, 0:w_], AX.X, apply_absolute_value=True))(par, w), "eV")
            mean_red[g] = sch.val["eV"]
            if g >= 1:
                pidx = mean_state["idx"][g - 1]
                sch.wait("vector", "eV", mean_red[g - 1])
                sch.do("vector", (lambda p, a: lambda e: e.tensor_tensor(C(a), C(a), C(C_T0 + p), ALU.add))((g - 1) % 2, C_MACC + pidx), "eV")
            else:
                sch.do("vector", lambda e: e.engine_nop(), "eV")
            mean_marks[g] = sch.val["eV"]
            mean_state["idx"][g] = idx
            mean_state["glob"] += 1

    sch.do("gpsimd", lambda e: e.dma_start(WKT[:, :].rearrange("p (hc o) -> p hc o", hc=16), wkT.rearrange("(hc p) o -> p hc o", p=128)), "g0", 16)
    sch.do("gpsimd", lambda e: e.dma_start(WVT[:, :].rearrange("p (hc o) -> p hc o", hc=16), wvT.rearrange("(hc p) o -> p hc o", p=128)), "g0", 16)
    mk_wkv = sch.val["g0"]
    sch.wait("gpsimd", "eV", mk_tbl)
    sch.do("gpsimd", lambda e: e.dma_start(XTH[:, :].rearrange("p (hc t) -> p hc t", hc=16), xT[:, 0:1024].rearrange("(hc p) t -> p hc t", p=128)), "g1", 16)
    mk_xth0 = sch.val["g1"]
    sch.cutpoint()

    ev = {}

    def rope_evac(psum_bank, dst_ap, tok0, scaled):
        sch.do("vector", (lambda b: lambda e: e.tensor_copy(KRAW[:, :], pb[b][:, :]))(psum_bank), "eV")
        ev[("pb", psum_bank)] = sch.val["eV"]
        mk_raw = sch.val["eV"]
        sch.wait("tensor", "eV", mk_raw)
        sch.wait("tensor", "eV", ev.get("rotfree", 0))
        sch.do("tensor", lambda e: e.matmul(pb[2][:, :], P64S[:, :], KRAW[:, :], start=True, stop=True), "eT")
        mk_rot = sch.val["eT"]
        co = (6144 if scaled else 2048) + tok0
        so = (4096 if scaled else 0) + tok0
        sch.wait("vector", "eT", mk_rot)
        sch.do("vector", (lambda o: lambda e: e.tensor_tensor(TMPK[:, :], KRAW[:, :], TBL[:, o:o + 512], ALU.mult))(co))
        sch.do("vector", (lambda o: lambda e: e.tensor_tensor(TMPK2[:, :], pb[2][:, :], TBL[:, o:o + 512], ALU.mult))(so))
        sch.do("vector", (lambda d: lambda e: e.tensor_tensor(d, TMPK[:, :], TMPK2[:, :], ALU.add))(dst_ap), "eV")
        ev["rotfree"] = sch.val["eV"]
        return sch.val["eV"]

    def kv_proj_half(half, xmark):
        for g in range(NKV):
            for ts in range(2):
                pump_mean(2)
                bank = (g * 2 + ts) % 2
                tok0 = half * 1024 + ts * 512
                sch.wait("tensor", "eV", ev.get(("pb", bank), 0))
                sch.wait("tensor", "g0", mk_wkv)
                sch.wait("tensor", "g1", xmark)
                for hc in range(NHC):
                    sch.do("tensor", (lambda g_, hc_, ts_, b_: lambda e: e.matmul(
                        pb[b_][:, :], WKT[:, hc_ * 512 + g_ * 128:hc_ * 512 + g_ * 128 + 128],
                        XTH[:, hc_ * 1024 + ts_ * 512:hc_ * 1024 + ts_ * 512 + 512],
                        start=(hc_ == 0), stop=(hc_ == NHC - 1)))(g, hc, ts, bank),
                        "eT" if hc == NHC - 1 else None)
                sch.wait("vector", "eT", sch.val["eT"])
                rope_evac(bank, KT[:, g * 2048 + tok0:g * 2048 + tok0 + 512], tok0, True)
        for tcr in range(8):
            pump_mean(2)
            tc = half * 8 + tcr
            bank = tcr % 2
            sch.wait("tensor", "eV", ev.get(("pb", bank), 0))
            for hc in range(NHC):
                sch.do("tensor", (lambda tcr_, hc_, b_: lambda e: e.matmul(
                    pb[b_][:, :], XTH[:, hc_ * 1024 + tcr_ * 128:hc_ * 1024 + tcr_ * 128 + 128],
                    WVT[:, hc_ * 512:hc_ * 512 + 512],
                    start=(hc_ == 0), stop=(hc_ == NHC - 1)))(tcr, hc, bank),
                    "eT" if hc == NHC - 1 else None)
            sch.wait("vector", "eT", sch.val["eT"])
            sch.do("vector", (lambda tc_, b_: lambda e: e.tensor_copy(VV[:, tc_ * 512:tc_ * 512 + 512], pb[b_][:, :]))(tc, bank), "eV")
            ev[("pb", bank)] = sch.val["eV"]

    kv_proj_half(0, mk_xth0)
    sch.cutpoint()

    wq_done = {}
    for ob in range(4):
        sch.cutpoint()
        par = ob % 2
        if ob == 0:
            sch.wait("gpsimd", "eV", mk_tbl)
        if ob >= 2:
            sch.wait("gpsimd", "eT", wq_done.get(ob - 2, 0))
        sch.do("gpsimd", (lambda o: lambda e: e.dma_start(
            WQT[o % 2][:, :].rearrange("p (hc o) -> p hc o", hc=16), wqT[:, o * 512:(o + 1) * 512].rearrange("(hc p) o -> p hc o", p=128)))(ob), "gq%d" % (ob % 2), 16)
        mk_wq = sch.val["gq%d" % (ob % 2)]
        for oc2 in range(4):
            n = ob * 4 + oc2
            for ts in range(2):
                pump_mean(2)
                bank = (oc2 * 2 + ts) % 2
                sch.wait("tensor", "eV", ev.get(("pb", bank), 0))
                sch.wait("tensor", "gq%d" % (ob % 2), mk_wq)
                for hc in range(NHC):
                    sch.do("tensor", (lambda p_, oc_, hc_, ts_, b_: lambda e: e.matmul(
                        pb[b_][:, :], WQT[p_][:, hc_ * 512 + oc_ * 128:hc_ * 512 + oc_ * 128 + 128],
                        XTH[:, hc_ * 1024 + ts_ * 512:hc_ * 1024 + ts_ * 512 + 512],
                        start=(hc_ == 0), stop=(hc_ == NHC - 1)))(par, oc2, hc, ts, bank),
                        "eT" if hc == NHC - 1 else None)
                if oc2 == 3 and ts == 1:
                    wq_done[ob] = sch.val["eT"]
                sch.wait("vector", "eT", sch.val["eT"])
                rope_evac(bank, QT[:, n * 1024 + ts * 512:n * 1024 + ts * 512 + 512], ts * 512, False)

    sch.cutpoint()
    sch.wait("gpsimd", "eT", sch.val["eT"])
    sch.do("gpsimd", lambda e: e.dma_start(XTH[:, :].rearrange("p (hc t) -> p hc t", hc=16), xT[:, 1024:2048].rearrange("(hc p) t -> p hc t", p=128)), "g1", 16)
    kv_proj_half(1, sch.val["g1"])

    sch.barrier()

    sch.do("gpsimd", lambda e: e.dma_start(
        WOT[:, 0:16384].rearrange("p (hc o) -> p hc o", hc=16), woT[:, 0:1024].rearrange("(hc p) o -> p hc o", p=128)), "gw1", 16)

    pt_exp = {}
    pt_pv = {}
    pv_free = 0
    rb_free = 0
    for n in range(NH):
        g = n // 4
        for qh in range(2):
            pump_mean(6)
            sch.wait("tensor", "eV", pv_free)
            for kc in range(NTC):
                sbank = kc % 2
                ptb = kc % 2
                sch.wait("tensor", "eC", pt_exp.get(sbank, 0))
                sch.do("tensor", (lambda g_, kc_, n_, qh_, b_: lambda e: e.matmul(
                    pb[b_][:, :], KT[:, g_ * 2048 + kc_ * 128:g_ * 2048 + kc_ * 128 + 128],
                    QT[:, n_ * 1024 + qh_ * 512:n_ * 1024 + qh_ * 512 + 512],
                    start=True, stop=True))(g, kc, n, qh, sbank), "eT")
                mk_s = sch.val["eT"]
                sch.wait("scalar", "eT", mk_s)
                sch.wait("scalar", "eT", pt_pv.get(ptb, 0))
                sch.do("scalar", (lambda b_, p_: lambda e: e.activation(
                    PT[p_][:, :], pb[b_][:, :], AFT.Exp, bias=C(C_EB), scale=1.0))(sbank, ptb), "eC")
                pt_exp[sbank] = sch.val["eC"]
                mk_exp = sch.val["eC"]
                sch.wait("tensor", "eC", mk_exp)
                sch.do("tensor", (lambda kc_, g_, p_: lambda e: e.matmul(
                    pb[2][:, :], VV[:, kc_ * 512 + g_ * 128:kc_ * 512 + g_ * 128 + 128],
                    PT[p_][:, :], start=(kc_ == 0), stop=(kc_ == NTC - 1)))(kc, g, ptb),
                    None)
                sch.do("tensor", (lambda kc_, p_: lambda e: e.matmul(
                    pb[3][0:1, :], ONESB[:, 0:1], PT[p_][:, :],
                    start=(kc_ == 0), stop=(kc_ == NTC - 1)))(kc, ptb),
                    "eT")
                pt_pv[ptb] = sch.val["eT"]
            mk_pv = sch.val["eT"]
            sch.wait("vector", "eT", mk_pv)
            sch.do("vector", lambda e: e.reciprocal(RF[:, :], pb[3][0:1, :]))
            sch.do("vector", lambda e: e.tensor_copy(RHI[:, :], RF[:, :]))
            sch.do("vector", lambda e: e.tensor_tensor(RLO[:, :], RF[:, :], RHI[:, :], ALU.subtract))
            sch.do("vector", lambda e: e.tensor_copy(TMPK2[:, :], pb[2][:, :]), "eV")
            mk_rl = sch.val["eV"]
            pv_free = mk_rl
            sch.wait("tensor", "eV", mk_rl)
            sch.wait("tensor", "eV", rb_free)
            sch.do("tensor", lambda e: e.matmul(pb[4][:, :], ONESR[:, :], RHI[:, :], start=True, stop=False))
            sch.do("tensor", lambda e: e.matmul(pb[4][:, :], ONESR[:, :], RLO[:, :], start=False, stop=True), "eT")
            mk_rb = sch.val["eT"]
            sch.wait("vector", "eT", mk_rb)
            sch.do("vector", (lambda n_, qh_: lambda e: e.tensor_tensor(
                AOUT[:, n_ * 1024 + qh_ * 512:n_ * 1024 + qh_ * 512 + 512],
                TMPK2[:, :], pb[4][:, :], ALU.mult))(n, qh), "eV")
            rb_free = sch.val["eV"]
    mk_attn = sch.val["eV"]

    sch.wait("gpsimd", "eV", mk_attn)
    sch.do("gpsimd", lambda e: e.dma_start(
        WOT[:, 16384:32768].rearrange("p (hc o) -> p hc o", hc=16), woT[:, 1024:2048].rearrange("(hc p) o -> p hc o", p=128)), "gw2", 16)
    mk_wot = sch.val["gw2"]

    pump_mean(len(mean_jobs))
    g_last = mean_state["glob"] - 1
    sch.wait("vector", "eV", mean_red[g_last])
    vsmall((lambda p, a: lambda e: e.tensor_tensor(C(a), C(a), C(C_T0 + p), ALU.add))(g_last % 2, C_MACC + mean_state["idx"][g_last]))
    NEL = float(H * DH)
    vsmall(lambda e: e.memset(KROT[:, 2:3], 0.0))
    vsmall(lambda e: e.memset(KROT[0:1, 2:3], 1.0))
    for i in range(3):
        vsmall((lambda s: lambda e: e.tensor_copy(KRAW[:, 0:1], C(C_MACC + s)))(i))
        sch.wait("tensor", "eV", sch.val["eV"])
        sch.do("tensor", lambda e: e.matmul(pb[5][0:1, 1:2], KROT[:, 2:3], KRAW[:, 0:1], start=True, stop=True), "eT")
        sch.wait("vector", "eT", sch.val["eT"])
        vsmall(lambda e: e.tensor_copy(KROT[0:1, 3:4], pb[5][0:1, 1:2]))
        sch.wait("tensor", "eV", sch.val["eV"])
        sch.do("tensor", lambda e: e.matmul(pb[6][:, 1:2], ONESR[:, :], KROT[0:1, 3:4], start=True, stop=True), "eT")
        sch.wait("vector", "eT", sch.val["eT"])
        vsmall(lambda e: e.tensor_copy(KRAW[:, 2:3], pb[6][:, 1:2]))
        vsmall((lambda s: lambda e: e.tensor_tensor(TMPK2[:, 0:1], C(C_MACC + s), KRAW[:, 2:3], ALU.subtract))(i))
        vsmall(lambda e: e.tensor_copy(KROT[:, 0:1], TMPK2[:, 0:1]))
        vsmall(lambda e: e.tensor_tensor(TMPK[:, 0:1], TMPK2[:, 0:1], KROT[:, 0:1], ALU.subtract))
        vsmall(lambda e: e.tensor_copy(KRAW[:, 1:2], TMPK[:, 0:1]))
        sch.wait("tensor", "eV", sch.val["eV"])
        sch.do("tensor", lambda e: e.matmul(pb[5][0:1, 0:1], ONESB[:, 0:1], KRAW[:, 2:3], start=True, stop=False))
        sch.do("tensor", lambda e: e.matmul(pb[5][0:1, 0:1], ONESB[:, 0:1], KROT[:, 0:1], start=False, stop=False))
        sch.do("tensor", lambda e: e.matmul(pb[5][0:1, 0:1], ONESB[:, 0:1], KRAW[:, 1:2], start=False, stop=True), "eT")
        sch.wait("vector", "eT", sch.val["eT"])
        vsmall(lambda e: e.tensor_scalar(TMPK2[0:1, 3:4], pb[5][0:1, 0:1], NELI_HI, None, ALU.mult))
        vsmall(lambda e: e.scalar_tensor_tensor(RF[0:1, 0:1], pb[5][0:1, 0:1], NELI_LO, TMPK2[0:1, 3:4], ALU.mult, ALU.add))
        vsmall(lambda e: e.tensor_scalar(RF[0:1, 0:1], RF[0:1, 0:1], 1.0, EPS_Q, ALU.mult, ALU.max))
        vsmall(lambda e: e.tensor_copy(RHI[0:1, 0:1], RF[0:1, 0:1]))
        vsmall(lambda e: e.tensor_tensor(TMPK2[0:1, 1:2], RF[0:1, 0:1], RHI[0:1, 0:1], ALU.subtract))
        vsmall(lambda e: e.tensor_copy(RLO[0:1, 0:1], TMPK2[0:1, 1:2]))
        vsmall(lambda e: e.tensor_tensor(TMPK[0:1, 1:2], TMPK2[0:1, 1:2], RLO[0:1, 0:1], ALU.subtract))
        vsmall(lambda e: e.tensor_copy(RLO[0:1, 1:2], TMPK[0:1, 1:2]))
        sch.wait("tensor", "eV", sch.val["eV"])
        sch.do("tensor", lambda e: e.matmul(pb[6][:, 0:1], ONESR[:, :], RHI[0:1, 0:1], start=True, stop=False))
        sch.do("tensor", lambda e: e.matmul(pb[6][:, 0:1], ONESR[:, :], RLO[0:1, 0:1], start=False, stop=False))
        sch.do("tensor", lambda e: e.matmul(pb[6][:, 0:1], ONESR[:, :], RLO[0:1, 1:2], start=False, stop=True), "eT")
        sch.wait("vector", "eT", sch.val["eT"])
        vsmall((lambda s: lambda e: e.tensor_copy(C(C_M1 + s), pb[6][:, 0:1]))(i))
    for i in range(3):
        vsmall((lambda s: lambda e: e.tensor_scalar(C(C_TH1P + 2 * s), C(C_M1 + s), 0.5, None, ALU.mult))(i))
        vsmall((lambda s: lambda e: e.tensor_scalar(C(C_TH1N + 2 * s), C(C_M1 + s), -0.5, None, ALU.mult))(i))
    mk_th = sch.val["eV"]

    sch.barrier()

    xqt_mk = {}
    xint_mk = {}
    sq_mk = {}
    for t in range(NQT):
        sch.cutpoint()
        par = t % 2
        if t >= 2:
            sch.wait("sync", "eV", xint_mk.get(t - 2, 0))
        xs_ = "x%d" % (t % 2)
        sch.do("sync", (lambda t_: lambda e: e.dma_start(XRS[t_ % 2][:, :], xres[t_ * 128:(t_ + 1) * 128, :]))(t), xs_, 16)
        mk_xr = sch.val[xs_]
        for ob in range(4):
            bank = ob % 2
            sch.wait("tensor", "eV", ev.get(("yb", bank), 0))
            sch.wait("tensor", "gw1", 16)
            if ob >= 2:
                sch.wait("tensor", "gw2", mk_wot)
            for hc in range(NHC):
                wo_off = (ob // 2) * 16384 + (ob % 2) * 512
                sch.do("tensor", (lambda t_, wo_, hc_, b_: lambda e: e.matmul(
                    pb[b_][:, :], AOUT[:, hc_ * 1024 + t_ * 128:hc_ * 1024 + t_ * 128 + 128],
                    WOT[:, wo_ + hc_ * 1024:wo_ + hc_ * 1024 + 512],
                    start=(hc_ == 0), stop=(hc_ == NHC - 1)))(t, wo_off, hc, bank),
                    "eT" if hc == NHC - 1 else None)
            mk_y = sch.val["eT"]
            sch.wait("vector", "eT", mk_y)
            sch.wait("vector", xs_, mk_xr)
            sch.do("vector", (lambda t_, ob_, b_: lambda e: e.tensor_tensor(
                Y1T[t_ % 2][:, ob_ * 512:ob_ * 512 + 512], pb[b_][:, :],
                XRS[t_ % 2][:, ob_ * 512:ob_ * 512 + 512], ALU.add))(t, ob, bank), "eV")
            ev[("yb", bank)] = sch.val["eV"]
        mk_y1 = sch.val["eV"]
        sch.cutpoint()
        yt = Y1T[par]
        xrt = XRS[par]
        sch.do("vector", (lambda y: lambda e: e.reduce_sum(C(C_LM), y[:, :], AX.X))(yt), "eV")
        vsync()
        sch.do("vector", (lambda y, d: lambda e: e.scalar_tensor_tensor(
            d[:, :], y[:, :], 1.0, y[:, :], ALU.mult, ALU.mult))(yt, xrt))
        sch.do("vector", (lambda d: lambda e: e.reduce_sum(C(C_LV), d[:, :], AX.X))(xrt), "eV")
        vsync()
        vsmall(lambda e: e.tensor_scalar(C(C_LM), C(C_LM), 1.0 / 2048.0, None, ALU.mult))
        sq_mk[t] = sch.val["eV"]
        vsmall(lambda e: e.tensor_scalar(C(C_LV), C(C_LV), 1.0 / 2048.0, EPS_LN, ALU.mult, ALU.add))
        vsmall(lambda e: e.tensor_tensor(C(C_LR), C(C_LM), C(C_LM), ALU.mult))
        vsmall(lambda e: e.tensor_tensor(C(C_LV), C(C_LV), C(C_LR), ALU.subtract))
        mk_var = sch.val["eV"]
        sch.cutpoint()
        sch.wait("scalar", "eV", mk_var)
        sch.do("scalar", lambda e: e.activation(C(C_LR), C(C_LV), AFT.Sqrt), "eC")
        sch.wait("vector", "eC", sch.val["eC"])
        vsmall(lambda e: e.reciprocal(C(C_LR2), C(C_LR)))
        for _ in range(2):
            vsmall(lambda e: e.tensor_tensor(C(C_LR), C(C_LR2), C(C_LR2), ALU.mult))
            vsmall(lambda e: e.tensor_tensor(C(C_LR), C(C_LR), C(C_LV), ALU.mult))
            vsmall(lambda e: e.tensor_scalar(C(C_LR), C(C_LR), -0.5, 1.5, ALU.mult, ALU.add))
            vsmall(lambda e: e.tensor_tensor(C(C_LR2), C(C_LR2), C(C_LR), ALU.mult))
        sch.do("vector", (lambda y, d: lambda e: e.tensor_scalar(
            d[:, :], y[:, :], C(C_LM), C(C_LR2), ALU.subtract, ALU.mult))(yt, xrt))
        sch.do("vector", (lambda t_, d: lambda e: e.tensor_copy(Y1N[t_][:, :], d[:, :]))(t, xrt), "eV")
        sch.cutpoint()
        sch.do("vector", (lambda d: lambda e: e.reduce_max(C(C_LV), d[:, :], AX.X, apply_absolute_value=True))(xrt), "eV")
        vsync()
        vsmall(lambda e: e.tensor_scalar(C(C_LV), C(C_LV), 1.0, EPS_Q, ALU.mult, ALU.max))
        vsmall((lambda t_: lambda e: e.tensor_copy(C(C_SMAX + t_), C(C_LV)))(t))
        vsmall(lambda e: e.reciprocal(C(C_LR), C(C_LV)))
        vsmall(lambda e: e.tensor_scalar(C(C_LR), C(C_LR), 127.0, None, ALU.mult))
        for sl in range(4):
            sch.do("vector", (lambda d, s_: lambda e: e.tensor_scalar(
                TMP1[:, 0:512], d[:, s_ * 512:s_ * 512 + 512], C(C_LR), None, ALU.mult))(xrt, sl))
            first = True
            for j in range(7, -1, -1):
                k = float(2 ** j)
                thr = k - 128.5
                src = TMP1 if first else USCR
                sch.do("vector", (lambda s, kk, th: lambda e: e.tensor_scalar(
                    BSCR[:, 0:512], s[:, 0:512], th, -kk, ALU.is_ge, ALU.mult))(src, k, thr))
                sch.do("vector", (lambda s: lambda e: e.tensor_tensor(
                    USCR[:, 0:512], s[:, 0:512], BSCR[:, 0:512], ALU.add))(src))
                first = False
            sch.do("vector", lambda e: e.tensor_tensor(BSCR[:, 0:512], TMP1[:, 0:512], USCR[:, 0:512], ALU.subtract))
            sch.do("vector", (lambda s_: lambda e: e.tensor_scalar(
                XINTT[:, s_ * 512:s_ * 512 + 512], BSCR[:, 0:512], -128.0, None, ALU.add))(sl), "eV")
        mk_xint = sch.val["eV"]
        xint_mk[t] = mk_xint
        sch.wait("vector", "eV", mk_th)
        vsmall((lambda t_: lambda e: e.tensor_scalar(C(C_LV), C(C_SMAX + t_), I127_HI, None, ALU.mult))(t))
        vsmall((lambda t_: lambda e: e.scalar_tensor_tensor(C(C_LV), C(C_SMAX + t_), I127_LO, C(C_LV), ALU.mult, ALU.add))(t))
        vsmall((lambda t_: lambda e: e.tensor_tensor(C(C_C1 + t_), C(C_LV), C(C_M1), ALU.mult))(t))
        vsmall((lambda t_: lambda e: e.tensor_tensor(C(C_C3 + t_), C(C_LV), C(C_M2), ALU.mult))(t))
        sch.wait("sync", "eV", mk_xint)
        sch.do("sync", (lambda t_: lambda e: e.dma_start_transpose(
            XQT[:, :].rearrange("p (dc tt) -> p dc tt", dc=16)[:, :, t_ * 128:(t_ + 1) * 128],
            XINTT[:, :]))(t), "tX", 16)
        xqt_mk[t] = sch.val["tX"]

    sch.barrier()

    sch.do("vector", lambda e: e.memset(COLS[:, C_CH:C_CH + 8], 0.0), "eV")
    up_mm = {}
    tern_mk = {}
    up_tern = {}

    sign_mk = {}

    def up_tern_stage(hb):
        par = hb % 2
        if hb == 0:
            sch.wait("vector", "tX", 128)
        if hb >= 2:
            sch.wait("scalar", "eT", up_mm.get(hb - 2, 0))
        for mi, (mat, thp, thn, wofs) in enumerate(
                [(w1T, C_TH1P, C_TH1N, 0), (w3T, C_TH2P, C_TH2N, 8192)]):
            for pc in range(8):
                slot = (mi * 8 + pc) % 4
                st = WST[slot]
                key = ("st", slot)
                sch.wait("sync", "eC", tern_mk.get(key, 0))
                us_ = "u%d" % slot
                src = mat[:, hb * 256:(hb + 1) * 256].rearrange("(dc p) c -> p dc c", p=128)[:, 2 * pc:2 * pc + 2, :]
                sch.do("sync", (lambda d, s_: lambda e: e.dma_start(d[:, :].rearrange("p (dc c) -> p dc c", dc=2), s_))(st, src), us_, 16)
                mk_pc = sch.val[us_]
                sch.wait("scalar", us_, mk_pc)
                dpar = (mi * 8 + pc) % 2
                sch.do("scalar", (lambda st_, tn: lambda e: e.activation(
                    TMP1[:, 0:512], st_[:, :], AFT.Relu, bias=C(tn), scale=1.0))(st, thn))
                sch.do("scalar", (lambda st_, tn: lambda e: e.activation(
                    USCR[:, 0:512], st_[:, :], AFT.Relu, bias=C(tn), scale=-1.0))(st, thn), "eC")
                tern_mk[key] = sch.val["eC"]
                sch.wait("vector", "eC", sch.val["eC"])
                sch.wait("vector", "eC", sign_mk.get(("u", dpar), 0))
                sch.do("vector", (lambda dp: lambda e: e.tensor_tensor(
                    PD4[dp][:, :], TMP1[:, 0:512], USCR[:, 0:512], ALU.subtract))(dpar), "eV")
                sch.wait("scalar", "eV", sch.val["eV"])
                dst0 = wofs + par * 4096 + pc * 512
                sch.do("scalar", (lambda d0, dp: lambda e: e.activation(
                    WUP[:, d0:d0 + 512], PD4[dp][:, :], AFT.Sign))(dst0, dpar), "eC")
                sign_mk[("u", dpar)] = sch.val["eC"]
        up_tern[hb] = sch.val["eC"]

    up_tern_stage(0)
    for hb in range(NHB):
        par = hb % 2
        if hb + 1 < NHB:
            up_tern_stage(hb + 1)
        mk_tern = up_tern[hb]
        for t in range(NQT):
            b1 = t % 2
            b3 = 2 + t % 2
            sch.wait("tensor", "eV", ev.get(("h1", b1), 0))
            sch.wait("tensor", "eV", ev.get(("h3", b3), 0))
            sch.wait("tensor", "eC", mk_tern)
            sch.wait("tensor", "tX", 128)
            for dc in range(NHC):
                sch.do("tensor", (lambda t_, dc_, p_, b_: lambda e: e.matmul(
                    pb[b_][:, 0:256], XQT[:, dc_ * 1024 + t_ * 128:dc_ * 1024 + t_ * 128 + 128],
                    WUP[:, p_ * 4096 + dc_ * 256:p_ * 4096 + dc_ * 256 + 256],
                    start=(dc_ == 0), stop=(dc_ == NHC - 1)))(t, dc, par, b1), None)
            for dc in range(NHC):
                sch.do("tensor", (lambda t_, dc_, p_, b_: lambda e: e.matmul(
                    pb[b_][:, 0:256], XQT[:, dc_ * 1024 + t_ * 128:dc_ * 1024 + t_ * 128 + 128],
                    WUP[:, 8192 + p_ * 4096 + dc_ * 256:8192 + p_ * 4096 + dc_ * 256 + 256],
                    start=(dc_ == 0), stop=(dc_ == NHC - 1)))(t, dc, par, b3),
                    "eT" if dc == NHC - 1 else None)
            mk_mm = sch.val["eT"]
            up_mm[hb] = mk_mm
            sch.wait("vector", "eT", mk_mm)
            sch.do("vector", (lambda b_, t_: lambda e: e.tensor_scalar(
                ST4[0][:, :], pb[b_][:, 0:256], C(C_C1 + t_), None, ALU.mult))(b1, t), "eV")
            mk_t1 = sch.val["eV"]
            ev[("h1", b1)] = mk_t1
            sch.wait("scalar", "eV", mk_t1)
            sch.do("scalar", lambda e: e.activation(ST4[1][:, :], ST4[0][:, :], AFT.Silu), "eC")
            mk_silu = sch.val["eC"]
            sch.wait("vector", "eC", mk_silu)
            ho = t * 5632 + hb * 256
            sch.do("vector", (lambda b_, t_, o: lambda e: e.scalar_tensor_tensor(
                HHt[:, o:o + 256], pb[b_][:, 0:256], C(C_C3 + t_), ST4[1][:, :], ALU.mult, ALU.mult))(b3, t, ho), "eV")
            ev[("h3", b3)] = sch.val["eV"]
            sch.do("vector", (lambda o: lambda e: e.reduce_max(
                C(C_T0), HHt[:, o:o + 256], AX.X, apply_absolute_value=True))(ho), "eV")
            vsync()
            sch.do("vector", (lambda t_: lambda e: e.tensor_tensor(C(C_CH + t_), C(C_CH + t_), C(C_T0), ALU.max))(t), "eV")

    sch.barrier()

    tr_mk = {}
    for t in range(NQT):
        vsmall((lambda t_: lambda e: e.tensor_scalar(C(C_T0), C(C_CH + t_), 1.0, EPS_Q, ALU.mult, ALU.max))(t))
        vsmall(lambda e: e.reciprocal(C(C_T1c), C(C_T0)))
        vsmall(lambda e: e.tensor_scalar(C(C_T1c), C(C_T1c), 127.0, None, ALU.mult))
        vsmall(lambda e: e.tensor_scalar(C(C_T2c), C(C_T0), I127_HI, None, ALU.mult))
        vsmall(lambda e: e.scalar_tensor_tensor(C(C_T2c), C(C_T0), I127_LO, C(C_T2c), ALU.mult, ALU.add))
        vsmall((lambda t_: lambda e: e.tensor_tensor(C(C_C2 + t_), C(C_T2c), C(C_M3), ALU.mult))(t))
        for sl in range(8):
            o0 = t * 5632 + sl * 704
            sch.do("vector", (lambda o: lambda e: e.tensor_scalar(
                TMP1[:, :], HHt[:, o:o + 704], C(C_T1c), None, ALU.mult))(o0))
            first = True
            for j in range(7, -1, -1):
                k = float(2 ** j)
                thr = k - 128.5
                src = TMP1 if first else USCR
                sch.do("vector", (lambda s, kk, th: lambda e: e.tensor_scalar(
                    BSCR[:, :], s[:, :], th, -kk, ALU.is_ge, ALU.mult))(src, k, thr))
                sch.do("vector", (lambda s: lambda e: e.tensor_tensor(USCR[:, :], s[:, :], BSCR[:, :], ALU.add))(src))
                first = False
            sch.do("vector", lambda e: e.tensor_tensor(BSCR[:, :], TMP1[:, :], USCR[:, :], ALU.subtract))
            sch.do("vector", (lambda o: lambda e: e.tensor_scalar(
                HHt[:, o:o + 704], BSCR[:, :], -128.0, None, ALU.add))(o0), "eV")
        mk_lad = sch.val["eV"]
        sch.wait("sync", "eV", mk_lad)
        if t < 4:
            dst = T1[:, :].rearrange("p (hc tt) -> p hc tt", hc=44)[:, :, t * 128:(t + 1) * 128]
        else:
            dst = T2[:, :].rearrange("p (hc tt) -> p hc tt", hc=44)[:, :, (t - 4) * 128:(t - 3) * 128]
            sch.wait("sync", "tH", tr_mk[3])
        sch.do("sync", (lambda t_, d: lambda e: e.dma_start_transpose(d, HHt[:, t_ * 5632:(t_ + 1) * 5632]))(t, dst), "tH", 16)
        tr_mk[t] = sch.val["tH"]
    mk_trdone = sch.val["tH"]

    sch.barrier()

    sch.do("vector", lambda e: e.memset(COLS[:, C_LS:C_LS + 16], 0.0), "eV")
    sch.wait("scalar", "tH", mk_trdone)
    w2t_mk = {}
    zev_mk = {}
    zmm_mk = {}
    w2t_done = {}

    def w2_tern_stage(db):
        par = db % 2
        if db >= 2:
            sch.wait("scalar", "eT", zmm_mk.get(db - 2, 0))
        for sla in range(4):
            stg = W2S[sla % 2]
            sch.wait("sync", "eC", w2t_mk.get((db - 1, sla + 2), 0) if db >= 1 else 0)
            if sla >= 2:
                sch.wait("sync", "eC", w2t_mk.get((db, sla - 2), 0))
            ss_ = "s%d" % (sla % 2)
            src = w2T[:, db * 256:(db + 1) * 256].rearrange("(hc p) c -> p hc c", p=128)[:, sla * 11:(sla + 1) * 11, :]
            sch.do("sync", (lambda d, s_: lambda e: e.dma_start(d[:, :].rearrange("p (hc c) -> p hc c", hc=11), s_))(stg, src), ss_, 16)
            mk_sl = sch.val[ss_]
            sch.wait("scalar", ss_, mk_sl)
            for sl in range(4):
                o0 = sl * 704
                dpar = sl % 2
                sch.do("scalar", (lambda st_, o: lambda e: e.activation(
                    TMP1[:, :], st_[:, o:o + 704], AFT.Relu, bias=C(C_TH3N), scale=1.0))(stg, o0))
                sch.do("scalar", (lambda st_, o: lambda e: e.activation(
                    USCR[:, :], st_[:, o:o + 704], AFT.Relu, bias=C(C_TH3N), scale=-1.0))(stg, o0), "eC")
                mk_ab = sch.val["eC"]
                sch.wait("vector", "eC", mk_ab)
                sch.wait("vector", "eC", sign_mk.get(("w2", dpar), 0))
                sch.do("vector", (lambda dp: lambda e: e.tensor_tensor(
                    PD6[dp][:, :], TMP1[:, :], USCR[:, :], ALU.subtract))(dpar), "eV")
                sch.wait("scalar", "eV", sch.val["eV"])
                sch.do("scalar", (lambda p_, sa, o, dp: lambda e: e.activation(
                    W2P[p_][:, sa * 2816 + o:sa * 2816 + o + 704], PD6[dp][:, :], AFT.Sign))(par, sla, o0, dpar), "eC")
                sign_mk[("w2", dpar)] = sch.val["eC"]
            w2t_mk[(db, sla)] = sch.val["eC"]
        w2t_done[db] = sch.val["eC"]

    w2_tern_stage(0)
    for db in range(NDB):
        par = db % 2
        if db + 1 < NDB:
            w2_tern_stage(db + 1)
        mk_w2t = w2t_done[db]
        for t in range(NQT):
            bank = par * 4 + (t % 4)
            half = (t // 4) * 256
            sch.wait("tensor", "eV", zev_mk.get((db - 2, t), 0))
            sch.wait("tensor", "eC", mk_w2t)
            if db == 0:
                sch.wait("tensor", "tH", mk_trdone)
            src_t = T1 if t < 4 else T2
            for hc in range(44):
                sch.do("tensor", (lambda s_, hc_, t_, b_, h_, p_: lambda e: e.matmul(
                    pb[b_][:, h_:h_ + 256],
                    s_[:, hc_ * 512 + (t_ % 4) * 128:hc_ * 512 + (t_ % 4) * 128 + 128],
                    W2P[p_][:, hc_ * 256:hc_ * 256 + 256],
                    start=(hc_ == 0), stop=(hc_ == 43)))(src_t, hc, t, bank, half, par),
                    "eT" if hc == 43 else None)
        mk_zmm = sch.val["eT"]
        zmm_mk[db] = mk_zmm
        for t in range(NQT):
            bank = par * 4 + (t % 4)
            half = (t // 4) * 256
            sch.wait("vector", "eT", mk_zmm)
            zsl = (lambda t_, d_: Y1N[t_][:, d_ * 256:d_ * 256 + 256])(t, db)
            sch.do("vector", (lambda b_, h_, t_, z: lambda e: e.scalar_tensor_tensor(
                z, pb[b_][:, h_:h_ + 256], C(C_C2 + t_), z, ALU.mult, ALU.add))(bank, half, t, zsl), "eV")
            zev_mk[(db, t)] = sch.val["eV"]
            mk_z = sch.val["eV"]
            sch.do("vector", (lambda z: lambda e: e.reduce_sum(C(C_T1c), z, AX.X))(zsl), "eV")
            vsync()
            sch.do("vector", (lambda t_: lambda e: e.tensor_tensor(C(C_LS + t_), C(C_LS + t_), C(C_T1c), ALU.add))(t))
            sch.do("vector", (lambda z: lambda e: e.scalar_tensor_tensor(
                TSQ[:, 0:256], z, 1.0, z, ALU.mult, ALU.mult))(zsl))
            sch.do("vector", lambda e: e.reduce_sum(C(C_T2c), TSQ[:, 0:256], AX.X), "eV")
            vsync()
            sch.do("vector", (lambda t_: lambda e: e.tensor_tensor(C(C_LQ + t_), C(C_LQ + t_), C(C_T2c), ALU.add))(t), "eV")

    sch.barrier()

    st_mk = {}
    for t in range(NQT):
        par = t % 2
        if t >= 2:
            sch.wait("vector", "a%d" % (t % 2), st_mk[t - 2])
        vsmall((lambda t_: lambda e: e.tensor_scalar(C(C_T0), C(C_LS + t_), 1.0 / 2048.0, None, ALU.mult))(t))
        vsmall((lambda t_: lambda e: e.tensor_scalar(C(C_T1c), C(C_LQ + t_), 1.0 / 2048.0, EPS_LN, ALU.mult, ALU.add))(t))
        vsmall(lambda e: e.tensor_tensor(C(C_T2c), C(C_T0), C(C_T0), ALU.mult))
        vsmall(lambda e: e.tensor_tensor(C(C_T1c), C(C_T1c), C(C_T2c), ALU.subtract))
        sch.wait("scalar", "eV", sch.val["eV"])
        sch.do("scalar", lambda e: e.activation(C(C_T2c), C(C_T1c), AFT.Sqrt), "eC")
        sch.wait("vector", "eC", sch.val["eC"])
        vsmall(lambda e: e.reciprocal(C(C_T3c), C(C_T2c)))
        for _ in range(2):
            vsmall(lambda e: e.tensor_tensor(C(C_T2c), C(C_T3c), C(C_T3c), ALU.mult))
            vsmall(lambda e: e.tensor_tensor(C(C_T2c), C(C_T2c), C(C_T1c), ALU.mult))
            vsmall(lambda e: e.tensor_scalar(C(C_T2c), C(C_T2c), -0.5, 1.5, ALU.mult, ALU.add))
            vsmall(lambda e: e.tensor_tensor(C(C_T3c), C(C_T3c), C(C_T2c), ALU.mult))
        sch.do("vector", (lambda t_, p_: lambda e: e.tensor_scalar(
            OUTT[p_][:, :], Y1N[t_][:, :], C(C_T0), C(C_T3c), ALU.subtract, ALU.mult))(t, par), "eV")
        mk_o = sch.val["eV"]
        sch.wait("scalar", "eV", mk_o)
        sch.do("scalar", (lambda t_, p_: lambda e: e.dma_start(out[t_ * 128:(t_ + 1) * 128, :], OUTT[p_][:, :]))(t, par), "a%d" % par, 16)
        st_mk[t] = sch.val["a%d" % par]
    sch.wait("scalar", "a0", sch.val["a0"])
    sch.wait("scalar", "a1", sch.val["a1"])
    sch.wait("sync", "a0", sch.val["a0"])
    sch.wait("sync", "a1", sch.val["a1"])

    if cut is not None:
        sch.ops = {e: sch.ops[e][:sch.cuts[cut][e]] for e in sch.ops}
        nv = {s: 0 for s in sch.val}
        for e in sch.ops:
            for op in sch.ops[e]:
                if op[0] == "i" and op[2] is not None:
                    nv[op[2]] += op[3]
        sch.val = nv
    import os
    dumps = os.environ.get("KDUMP", "")
    if dumps:
        sch.barrier()
        loc = locals()
        for i, nm in enumerate(dumps.split(",")):
            if ":" in nm:
                b, ix = nm.split(":")
                th = loc[b][int(ix)]
                nm = b + ix
            else:
                th = loc[nm]
            dsh = list(th.shape)
            dout = nc.dram_tensor(f"dump_{nm}", dsh, th.dtype, kind="ExternalOutput").ap()
            sch.do("sync", (lambda d, t: lambda e: e.dma_start(d, t[:, :]))(dout, th), "c0", 16)
        sch.wait("sync", "c0", sch.val["c0"])
    _QUEUE = {"c0": "sync", "m0": "sync", "m1": "sync", "x0": "sync", "x1": "sync",
              "tX": "sync", "tH": "sync", "u0": "sync", "u1": "sync", "u2": "sync",
              "u3": "sync", "s0": "sync", "s1": "sync",
              "g0": "gpsimd", "g1": "gpsimd", "gq0": "gpsimd", "gq1": "gpsimd",
              "gw1": "gpsimd", "gw2": "gpsimd", "gc": "gpsimd",
              "a0": "scalar", "a1": "scalar"}
    totals = {s: 0 for s in _QUEUE}
    for e in sch.ops:
        for op in sch.ops[e]:
            if op[0] == "i" and op[2] in totals:
                totals[op[2]] += op[3]
    for s, eng in _QUEUE.items():
        if totals[s]:
            sch.ops[eng].append(("w", s, totals[s]))
    with es:
        with nc.Block() as block:
            def make(ename):
                def body(eng):
                    for op in sch.ops[ename]:
                        if op[0] == "w":
                            eng.wait_ge(sems[op[1]], op[2])
                        else:
                            ins = op[1](eng)
                            if op[2] is not None:
                                ins.then_inc(sems[op[2]], op[3])
                return body

            block.sync(make("sync"))
            block.gpsimd(make("gpsimd"))
            block.vector(make("vector"))
            block.scalar(make("scalar"))
            block.tensor(make("tensor"))
    global _LAST_SCHED, _LAST_TENSORS
    _LAST_SCHED = sch
    _LAST_TENSORS = {k: v for k, v in locals().items()
                     if k in ("QT", "KT", "VV", "AOUT", "XQT", "XINTT", "HHt",
                              "T1", "T2", "Y1N", "TBL", "COLS", "WUP", "W2P", "XTH", "WOT")}
    return nc


_LAST_SCHED = None


_NC_CACHE = None


def _host_inputs(L0_attn_x, L0_attn_Wq, L0_attn_Wk, L0_attn_Wv, L0_attn_Wo,
                 L0_ffn_w1, L0_ffn_w2, L0_ffn_w3):
    x = np.ascontiguousarray(np.asarray(L0_attn_x, dtype=np.float32))
    wq = np.asarray(L0_attn_Wq, dtype=np.float32)
    wk = np.asarray(L0_attn_Wk, dtype=np.float32)
    wv = np.asarray(L0_attn_Wv, dtype=np.float32)
    wo = np.asarray(L0_attn_Wo, dtype=np.float32)
    w1 = np.asarray(L0_ffn_w1, dtype=np.float32)
    w2 = np.asarray(L0_ffn_w2, dtype=np.float32)
    w3 = np.asarray(L0_ffn_w3, dtype=np.float32)

    shared = {
        "wqT": np.ascontiguousarray(wq.T),
        "wkT": np.ascontiguousarray(wk.T),
        "wvT": np.ascontiguousarray(wv.T),
        "woT": np.ascontiguousarray(wo.T),
        "w1T": np.ascontiguousarray(w1.T),
        "w3T": np.ascontiguousarray(w3.T),
        "w2T": np.ascontiguousarray(w2.T),
    }
    inv32 = np.frombuffer(base64.b64decode(_INV_B64), dtype=np.float32)
    t32 = np.arange(S, dtype=np.float32)
    freqs = np.einsum("i,j->ij", t32, inv32)
    dsin = np.frombuffer(zlib.decompress(base64.b64decode(_DSIN_B64)), dtype=np.int8)
    dcos = np.frombuffer(zlib.decompress(base64.b64decode(_DCOS_B64)), dtype=np.int8)
    sin64 = (np.sin(freqs).view(np.int32) + dsin.reshape(S, 64).astype(np.int32)).view(np.float32)
    cos64 = (np.cos(freqs).view(np.int32) + dcos.reshape(S, 64).astype(np.int32)).view(np.float32)
    sin_t = np.concatenate([sin64, sin64], axis=1).T
    cos_t = np.concatenate([cos64, cos64], axis=1).T
    p64 = np.zeros((128, 128), dtype=np.float32)
    for d in range(64):
        p64[d + 64, d] = -1.0
        p64[d, d + 64] = 1.0
    shared["p64"] = p64
    shared["ones32"] = np.ones((128, 32), dtype=np.float32)
    shared["onesr"] = np.ones((1, 128), dtype=np.float32)

    in_maps = []
    for c in range(8):
        b, half = c // 2, c % 2
        perm = np.concatenate([np.arange(half * TQ, (half + 1) * TQ),
                               np.arange((1 - half) * TQ, (2 - half) * TQ)])
        m = dict(shared)
        m["xT"] = np.ascontiguousarray(x[b].T[:, perm])
        m["xres"] = np.ascontiguousarray(x[b, half * TQ:(half + 1) * TQ, :])
        m["rope"] = np.ascontiguousarray(
            np.concatenate([sin_t[:, perm], cos_t[:, perm]], axis=1))
        in_maps.append(m)
    return in_maps


def kernel(L0_attn_x, L0_attn_Wq, L0_attn_Wk, L0_attn_Wv, L0_attn_Wo,
           L0_ffn_w1, L0_ffn_w2, L0_ffn_w3):
    global _NC_CACHE
    in_maps = _host_inputs(L0_attn_x, L0_attn_Wq, L0_attn_Wk, L0_attn_Wv,
                           L0_attn_Wo, L0_ffn_w1, L0_ffn_w2, L0_ffn_w3)
    if _NC_CACHE is None:
        _NC_CACHE = build_program()
    res = run_bass_kernel_spmd(_NC_CACHE, in_maps, core_ids=list(range(8)))
    y = np.empty((B, S, H), dtype=np.float32)
    for c in range(8):
        b, half = c // 2, c % 2
        y[b, half * TQ:(half + 1) * TQ, :] = np.asarray(res.results[c]["out"])
    return y


_INV_B64 = (
    "AACAP9evXT8R+T8/4D0mP5r1Dz/PU/k+m+jXPhv4uj6b6KE+BDWMPiPUcj7tR1I+hxg2PkCwHT53jQg+1n/sPc3MzD14WbE9DZSZ"
    "PU3+hD3CVWY9P3ZHPRW6LD1IkxU944YBPdJU4DxPQ8I8izmoPDmtkTwzTXw88ntaPBEzPTwK1yM8LeENPLC59TsVytQ7nES4O8yR"
    "nzt3Loo7DlJvOzg+TzsPdzM7DWkbO2+UBjsoFek6XNfJOo7JrjoOXJc6bxKDOuIBYzqMlEQ6RDsqOhZqEzqsT/85JRfdOdd0vzlg"
    "y6U5cpKPORWoeDnkU1c5U3c6ORZ5ITly1As54izyOA=="
)
_DSIN_B64 = (
    "eNrFfYuyJEmKq+v/P1p3qs7JCHeQAM9qu9tmO9uPyswIf4AQINb6t7+4/o//4n/0DXi/CP+fn4BY/+d/EftrzB8In//l+483C4Lw"
    "g7xdTfDzEbxfRPkjzWPw52PPZ4GrzWf6m8tVBC8f+r84h/va/UeXGVevgT+/ivdvv1vAY/34xYvwP3xx8JtP0Rynm/uDdjsmW/Yv"
    "1ouXXwf5wuC/HrurM0S74MPFw68Bwa8lvHh+/PsO/Pzkz3Os0x5PPvoYH/y1ebxedvzDgWL6o/zu7f8DFIDj1/EcZnQf5GFJebkY"
    "1H+A0y/g/qPgZgYv146H+eX4ZFJe2iGq+UfD+zzk573F0ceN6VMGqVt/4mcXeXyQ9yfws4X3X4Dv19dhNn7z8N+YkM/n+a33pH5k"
    "Xjw6vvA/WK/lL4Bcf4b4YwP4ven6vbzfYfj/IHjY7R/WpQE6/yBOTD76BuL8BMc2lKcd5u3SIX7HG0uNMTWWvzLsdhNPyGRQDK8u"
    "71cHiJ9T/1k+xMeG+07upwV0z48OukF9Kaevj+cuH1fxG2CBdXz+zgF+hWR+bz7EMl15wp+jhGP/vvLfWCUmRHmKcLkEmAHSxgDx"
    "94MvhsDwAhEf4kHcop7NQLU3N5E0pKW4uPw4kCgDqO0A8BvGI5jwOQGjNo/940un/40vZfYdvDl+X1JgF6FCE9/z/ocRPvYFBgPS"
    "hb8F0a+rVHzibEmg3oDjC0Bvs7lTi8M4GLziIvAbRayBES9jyB844GNWyAvEAFfwBiGr2QrsyJXpT7fvzu3CkNtKzo0o8G/xJT7W"
    "+9lDvFTeIH7i58/thx2XHDRf5HJQORx98h/BPl8zfs3k8XQi4QLPlp8rf/7eYKLn80oHwiKWbG9v+vPzUAIqnMB1PuM8qviaWDmO"
    "Mm6O0PvDuPNJE9r7YiVR7mO2PxQkIKJnamikTwwELi7Ha7uvoYJUFzFE/j1sSAgz9vFIoyEjK6CkEMPVvUjC/BwRAOiA/WQR8E/H"
    "n/lCTrkghmB0lMvZnWe4QpwuAxyiuYl+dlO2ryAm7/49iRvNAzOOyMSS9H4MvpBfOsUYQHyRzONV6JOdEdcw+/yAl0whIUG6QQBh"
    "X5jd+cXXvCPCxePd12AzQjzh5sSDs7xGX9AgWMVxNf8K+AWQ2FmIYTKYJ9i+izvCUn1H3OME/OANBGf63zONjulZ2B0278JoxVi/"
    "fAo53Pp4msdLiPp7+Q1lgPoB3Sf+wW2yv2LdCu47Drcz3benGKJ0ZHR2bBy8nDCQk6DUcCiXnFn/QJjv3W8Ok7fMQ33wOYkm2f0i"
    "BwaIYtE5M3775/A9nuHJhWy/Tx8FIP03JMPA0c27sPg9fPvNY2J0xpidLafcBKQVZrLozZGGDMcmeaBP0uPAohznwfj4ebxfguiY"
    "W1uNkUcYmlHLuWL2+ZuHIC7u8eQF8HwpRvkTLkO9YFiIgHhdiVCPOFk/bbVvysCQnN6EAqCCP89K8i4UhnGA9f35jbTo3n9YTIF9"
    "wSOFMjHiln+f7SDvvXbmTZQvREvcaUuMSRzCfB22Pz64wiijjODV4SLWM1b/NvaNLOKEQzz4K8wwcH30sZ3c1gEjAa7HD+CGA2YI"
    "x36/BGx//zm2WJG0w4Aswj1ct1vgqiZqM44iSBGoclDny/qtMLFFY7dBG7FhQsUheSAaz1TtJKH+/JxC2teNAdE0C3dy7rKE9Gsk"
    "05+f12Hibx7u2wp4A9jae/GHukFxkye3al9BftP2kTwMquh4DAHG6/ZPZYfQSOgwX6xaYxgvN+JdmgAoisO8rgnANxM45i8hVvKK"
    "QTu2PTlQXl01Up3hMgKjMGF3JaDRv6PZOYj7T8ov5NQa6NwxiNuTzNAAQmUToOEvv+vnCcn33wiUMiIcM43DPPwv747kurDvA2/Y"
    "z6t6hoib6gDYGxD+YwRLlXHA2CzmuIc39rK1FpMaGij4dFE/tea8OyL7hHqrqgDgg7c3BhdPMDcvg9pNMVIv6OTmf6hDWFIAGn8j"
    "dQ1ctAHCcT/3xxgZkK5URlfXnuAwYhM38NR/AcrkzincVK7wE4GBtwHYN0wcRSzDtnVs8AjzZq49BcPOxRoAyfvQPdaM0eIO9vEP"
    "Ko9UPkTwuTgYsFkEiu3MBu4ME+zC88Rq7qsNt5/zyuv7G4gT7pymPd97A/bpeRFgbRsHbo34hOjK78/P+pL3hiuL6VvpuD3EQZ5+"
    "38R8U0Nt3BttVIn+UimTV6Q/cuuYSYdqww/JfEKggzXP5GPMwzDfYWos2wVgVOEcekyTwIxCfsMy5N8qEG6ncN5/K03qXfT+ccGY"
    "98MD0Uq4HDKbV/hkQPBl+Z6B290GQpfuX2TCC3bipowJSzbA1IE8fRTCm+ospDN8d37+GAJ64PtHpUX6ED637LcO4dVTeK8zZmXB"
    "b951XMuCFZYswXQMTx0119DRMBkhHIGMwBkmANyt+NF/yg4AQccTuGpFqWsQYC0yzYLdwgE+5BthAHwJgFHFtTcuaIkTVXEPz1Pu"
    "KHR9rst+D1hiSM04IdbjZ6DN3L9zHCVWp0azrKIXfVCG/y0ARPqRbLubCHYqHgPre4kT6YLT9uM9AqOOetExD7QB1Ix7zZm7K/Eq"
    "Jg/Ku707MfDmARIGhEFvmzn9e5nKrAwcgYNAvWB492GB87AEjyaMwOwMcCuaqeoF+vCe674WjjCsMeY10D8JvLS7rZQGKqpAVYFj"
    "VR7nF4wgneTxgU5EKtr7+wpP7Cd/FIbK4r0vSvChvw5varmSA+IR6IP6EzMOhhe0eYm2uS6FsPatRukXShSOcGnh2aYDGm2uRO0/"
    "tzNSOxIGP1Tfv9Bny5HoVbMAQoiMk/07DjUnn4f+O14zuIE6HIePz+5Jzh6ZQawr5Lkz7j9tYF37Hqt0eL/gtPEyBt8XziRv7eVV"
    "s1F3MCn15HB5ga9rQBKG+F22bwXADvIPo9CLAT1y3Lz0nE9+R58croM7eXKtpRTZsiMuY49/GSNN9mgfIkBweGj0+NDxI0b8TWCQ"
    "NYEI7bUUeznZAbPAHJcGYvZPrKAjEl0Q1TA4XD9oEDSWiMBYd9G8Cm18U+OncdihAqKT+PujifyUUA0gFMJJL/XAcuwEVeaBddG7"
    "l1bycWu40dMsgy76kJ7vizz5snEJxHSvpgAI/5wQ20vpwfr+cJ2eG/kCc7EFkG/I8uljpS5f75tsuVdRlyWn+xufHpPYKcEV6dGa"
    "B+aY8t09yCOCBHa5AxSOmMJTYGb/ULZwtrQqDkMCV1bhv1rYj+0Mznspgburg6PpP0KZq/jxLvR030WU96ejGhkIvWEOW7OvIYBi"
    "i29+wadohpXZHWYuIqjhoDfa3G1HupS83ENq3ocTyiu9KFbWU56VE0FBM2cLQEis54NMGzvcIcVzBSj0oHMJjGQQJOs5LYFAuv+G"
    "pWD14q9+2AqiPrPQ6w25j3idmHxwjwC3GzzpNeEqoXdTiPHpvv1lfgFBgdS2n4DIgdHjvcho7C7wDh+tINmyI4qoaGnc5lmrEKcp"
    "/K2oqIuXeLi+ZHSac6QzL/PYH745fBK4vf5lRFuw4z6X4J7vNC0QUcEIPPgwH30T4pGqPGkYcKCf8S9qt9rEHyUs7GuJIQ/GsBL+"
    "vSplgzIv3y3SzTZ97tr+rnqh6XF3W9XPgtiatQ4/cezrzKC/zhshPPD/JNbaxPZ+gbOCHa+Kj8Qy4FYNB6g7KFD64L9fwFzCN6hg"
    "PvNG9IYCjf2Cik4ny3BWbg9MGDVRMhcegD27GOl/KKTH0zfPMriG8IC9f3BXlKGHlIZJKm3VV5XAR9b/Xb8BRUPJZ/T7jyBT8PJu"
    "87wnA3Mh86Yw9T+79h22Npj1DwU174tN9wEvag9zNDoxeexFtjzv4HCeCnK5zJh5pGAa4E8grfUWRMyZCcWVEZ4OZDlxE9cSuu2c"
    "/m7XflGJZMFGRWNUhRre2eDRKm+kDKaMydgUKaMHwNhS1tR6KbX9YzyPUDp432GAkZAqdJDX4w/GsH8TLmav/wKV/IY5hN2NZEqk"
    "bPbfAGAMWkQTBYTpTeUa1rKfMqZrtQNhInvDN208ipvlhWHBGX+t8loiSKxvJs6J28eURfvET60Gwst+YRxdZFtHRBA0+yLUHBeu"
    "KpB5hFPAqH4xTCtyktczo36IwUynikGtxbSQBEkD9bAFbT5XWhDc1o6taG3ytaJhT2lhzUUAEt7wUOSS7g3KgkNimJnpYvpps5PI"
    "5+9PtcxJtO6tqG3+BOkSMAnT9vcuFgzyCAd0oohVb/cE+OUFz9GoxrensTMT0CqUF91N7ruaU35UgeTsa+huqSdAZsl5TtAOW9L1"
    "W7eHaeL3oItwnBmsZooZBPPxr/NQGa7hrAjeEDeoG/CpuRvcax4jMZLp+cyz4yfJc6Q9L3UXjl8AmQryRuKRf68PN6OSxe1L80KI"
    "qHA4Nkx64bQPnMHJmDNDTdmAiezY9wFl7RSkG0MeRjaFlkw011wNhOfEgrkFebCnEB1FGq8ExN1C8JaF8B8MygjeVhVzuRc+TRHv"
    "+vZU31aspbXgRRaXXM2DguEEGpoi/jfmpyz9DxTUeReSA6JSopgByq4YpGBA3ZdC2qAAKc0kmNykoeftBGeAhguAn5fEUcAiF3mi"
    "/5H/kadBpueMjdTaKeQyGAi9TYC/FwGHv9Y8YmxRls5lmusw66FmsrUZQvC2iQeCoMUtKzmtXuE5903MYuI0Xv7qQVDBIa5cQm1E"
    "HLbxRcGnfn2QRiJAbC4Gu/65bWQOv9nUD1UZ6e2bJgb4zKQgJZHPUAKdXBe9SzAEEtgSDdCR6t0wUxqI0JOrcJhawJhBBQNUbL4w"
    "ZtLE5WJTAaOqgSt04W8reTiCkfZXNSsKMVaG+yO82zTDAAmwhnH/ySVsMeH7ZxkFAWXPMkT0EfXAsVQFrAzB4zcBd69kH7RgMj8l"
    "4hCb0o5HxYOiUUd6RQltihk5CB4O+8GAPndFwYkUR4iD5UgC+G0SC1i1nkFfMVThli+/Jc5cRUMY+4vAPYTceoBRwRDrNyMBR0+h"
    "iTHMU/44cQfrqAHGNx2ZYfJdkd8dSLXP1NT58nixda0k4Bnv/TeDKCCBVNAyrDsI8Hv6iStjlVnbbzo+EebHE9PtxlIaIdejbMhg"
    "9M8ylobOQsWx1FMYoCuyEyiPJykVCh5uDroFSRNpL/j4Qv0lmlIoXuh5eqyS2KDyqGcLTEWC8rl9wBIE4LiT8ACSo2TDZ3qaAPJZ"
    "AwACOpSaQxMZfxbQr0aSu34MVWSKSQsYikWeVbAm5wnbghRWkbryGNG8zlqouEwCc/DpqJeKSfcS5U0dF38q5sk4OQzwW3mkHICA"
    "0mni6IsOfT/wsMb7v+zWo/BbEz0k5LA9N9EVGV7MeY/pWDaKU0H/cjesp10rqgR+8bVS6wmrp5xEfEBncllObI3tOq5022dANj9L"
    "JeJQ2Kr8B/grZYVyDIUOQTfdm3T+G/YJhavqS9+ZJykII0ZrwKAsNznCH9sVU9PE9lkmtqKjKMw9QrorI3HRs7w96XGczjcYcbwc"
    "2Lb6P9zEnpaf4twFCeE25d8U+CMcD8iRZvQHUX7SgFAztBr4X7XwQXvXqJYrRDb52jMavPqeUzpDju5o7HhbPCyKC+eCF/N5lKiD"
    "F2yPxY6ZUMm1/AMdGKaoIs/mKB+iPfCKGnvtF3ebfBc/jW49mtAFlEtp3H3RpT6jIiqhX1lcGLeT0hJzV2EvV0Ym9kAMlnGPddAS"
    "pgrHCjARKiJ6AnuL4DE4jWjO0AaiDWWXBTuVZC5yBiE/AJnxMWM836QwWBqipNJdkS9wyzFaz237yKIFd7Q7Iga047/e0hmGsiOu"
    "kpchZPz9Q4J8WbqYRci6SJCyEtFNTwEKu/vpIaFYuCWqQXMAouSoa9zHw1pPpi/BeswI54O57rsJGKdAPi28XANZE6R1bibeJfVo"
    "aqtekH7rNsUlEMREIZOjZ3g1kEewOY79Zcog9l/wu2dHIdM2C2WooLBlIaKGEGq2nQEYMIwT6zqJcR6HfQL6WZspintC9X3YskYE"
    "mCF4x2+UtneSKQ2kjkLBnBxrz3xff4S6CCff3VPrPzCcSG4hr14mBxD4od15z1M4eOhezM9tNrSuoiAvc6lyXVWFZfxwFDpBIwoP"
    "6fBOwhorbS2HTwN6mcrIa5A/FsEumLpiJymiK9y8v6L+pzUkP1VOhYDZ+R9kbTPjxKsmk1XkFsP03un0hgmp0G6erXU6KsD1/IPu"
    "gNPTLZph7QzmPzWYGBSlqD8BZzqQEDBoyxIWZ2FIqaOT8qgTEKLhEzrIZ/aq0KBNq47XQpxQUwDXrBcTPhBvE2hGJQHzWZR0oGBU"
    "OvSMQGEYIHUbitOiCtrvgTWh5psHzulQ5D65EHV4i9mAjB0Y7EAOxt36R/C1ipQFWgmo7bpIObWmEt2a3jR6zBUoApNODKjYFc4V"
    "r2EbCmPk3DyrJdKXSH1RwgrL+KX+0VPWi2Yw7W/JhoAxtKXFin492AEOBoFvGCtlcTlqvFheo3U2QfNwBJDet40bi3YdDH/7cl4x"
    "MnulVcO2eiovK0ZqDUJW1ZtqiPjlsJ7k73jE0Mk21NaZbDhq0J2KJPcyxlp4V9edPMx6T9UbTtriClw3TVvG8suLyOnwEAzWOw5h"
    "Y8FfNuCRwukgGxFNIJQVWKQ9k+iXAcFYU8S9aP3XMLppNJQhXiz0J4yxXOQ2JiJELoP/FDiTlps5/WZma+oSIp7AMfiiP4KIqOir"
    "s/9jFWAH8rmYQav4oJOlza2rScOyHqPJ+9OE/hwdM5yGpRjBJ8i6YGoAmez9Xe84YtsYrgIockOwriXbmj4mDa47gw0F6ljV/chZ"
    "p3udIc4hokUhQtBNgt6dG+qWj+zkmMBAVD3CjGGQyByhdx0tBRCV0+4CaMHDgRj+fi90igEA71U7yriV3Df+curfW6bAJKI70//N"
    "eTqaYQ4ll3nAjTmUNqOKhhqOOhkrTbL+qq3mgfH9Lyrgz5hJwZCbU8zvdUtnYV803AF+7hbxr0JJp0alii0weFhbnMI7QSIIDz+s"
    "/5fZB6hmElbOQLShrLXWaDdfpSuwBAms41AniIFuD+Rszes6NsurvCV+6AidWPuG/r2X7C2xAzH9gzPiadPVpfkfMbC4jVvSdCge"
    "j18Q2Ia8ex8Xy2MUuANoXQzMH2DBEXMY7hCK2Rfn05wDVpSYD2C9K2ECVnsFtqM5uAxtUr4+zYOzaKvrWxTpY+ubAJj5FWneAmVQ"
    "/+HhKdu7FC/Kbp2OPwov1dEOIUpVZsOkRLmAZw4ePaPNzLWxG5oHJSAnX4zoA7eVvRxmh2dTEHl+kE3m0bKZA9iKkoZQ7MZMRRW/"
    "wxuYxya0eXQuI2SS+ByZ5cAuYr9vzaEGwDq5mJQH+GwLavqGlLxJzmu4PHnsGw50ku3AxkmBu/q+6v5DTQAK/USNdAls+NEgyF2r"
    "hEW71H0lqlTSEriZO136KwSNVJE6qOY7dWzJNY8kKFymIKTLATenhvd2E9hjTBmBrCz6gCormx03Tc+ORgpK7P3WcyDClEFHOsTk"
    "SSRyxRKjKgS/Ii8ys4tqxSzFs/IovG0eKbT9w9NyxtMMWxdGG9FzMEpM+cxcr7jFc0Sn4MwdCEP7LNYRILqlLezuWzUAQa23AWHj"
    "rYcZrOGsp7vDqewxJfWbKh5g7h+Hz3CvQfjl9ZtFA7agL97dbC3R8ubWRW6YHNX5oaD6uGbTn/Dgi4No2OqpmFiGC7XUn299KHrY"
    "PCk6nD5nALNpwZeff6agFwGcLqQO24Gx0HiY42oTseLqRx/OdDNXmXwLJB2VXADr23Umqia96we6zcYGsbOr/TadAEDniuQARTag"
    "VlgrWsIE65voegJAy2nFf3ngG/h8euPB7weqiZK3mNy/qMDO4QepebcPfIXQk5CrzRW4NE3gVAFkPPVgV8j9SY5RDx+iCHJZUxXm"
    "/MAaAY4sbBEA8+Gv9isD3qKYCe/be3cBxlHodkOCunOqwuPIpkvjXSQNyOKLpX6beIDUATxpqzgyiEJOq3p2oLzcc4+bN1JOkd8y"
    "5XC/xTVPAB5+ArmmATmYlqMub0o3XJv4DqRwCDvR8FrrkP/SLXss3lmcK2wEJgfC9hBU4Ko62J4BjfsMCgqPI5tirZ5sTsFz1lKE"
    "c3zoKIdPNa3pMgm/aEMis3G5wmO3lRhOhIXL1ccZGCP84qiUSQEJo/YBoVjV7A6Tyo38k6YDYuvd7aGuCJB5xkDhKojhTHQmHbU5"
    "sCQ7fULlmEeSzHtx90UJRBe0MZ1J7k1Eze3bnSn9C5/3bXt72uBar0Ag59CoHKApIYC9Qds4Xj1nipZUhooc+pTuyY6+pwpD/Ft6"
    "h/n9B7OVkjHyHmdkk4UjAV0KowQBCqybJsjVqHOCM3IJ0v6g6eESOjcuGUq5zPmkRMkSM4QUDvJjnR0Q7A+HdMSxOqcYjQUXtqxB"
    "DxyGoMvuWfD55MjsrbUJeOy2mNR1JfTAmJa8OuwPZ0eXVFDIBxs7kAX0DJz7tXXEgrO1MdLmoLib60vxPPULTJ7GX/o3LLBhDuRQ"
    "YS0DqOTaIGa4UTAARWFgHgXnDZaNfPtCKvBgKxHucDWDSLm56P1ng5Cq9Dy7iPQYLwSMxl8ch0+VmvCcI2Njt5S430S0ivc/eNKj"
    "8hXMY1HHkQgSkKAw2IlD9jcJDOkAvfnvyC6cbiP7X3Ssktx9nlNO1kY8OZk4HP5TIZhY94nK8knfH4OA4/cHBTTIF76ulatJYiXH"
    "26UY3XTd/qnPYaOoJsoMfgPCrg5LcX/2GbElnGVGIMit4uTcBvLb1wMi/b/nO4warUfTZ8DEvtQd74f2BHnOMF38/o2mq+eGHQ7b"
    "jj+Fc1yE5bZHh+e9JWhJKveV+NSSfLA6FUQnPYSBsASuVOINDVl2BqKbR3vMbzBXk40aqDaWHHdsuBpKqrHSCjVAU0j88gQPZihZ"
    "s4kelo0EdSuPLWZmQhDZPZ7HylwRkudnGQ4c7p3XhpDrO6na9SYONso5DuRJyJnWRMGdmkJ5YOURasbflPD+3II9i2ZeA4PDuvqs"
    "zRPP4PS2m+VqSuEhf57KyThSVg1MyyBb4vlgHAhtW6egBiermLdzKO0GTWkbAAm9h6CJ+mkWh4f867R7IRDnp5pg24LHHBRrkR8K"
    "o4z0FlyWH2vc19o7pMBDuHvZIq84Ld0FvUABso+6q99sQnKALOw9jvDvY8uIPm5WE9u5NgVS3thlJOB7PUYMNtuN4WE8ajjRSDCf"
    "0vf41LukLNhMie/lccGntK5rfGIIVomsqVQO0YzXgSz1SUVwefQXfNYNpwxtqf2v97tq0rfRFI+Ral0BD33MApV3nIvcm2KZxoAj"
    "KOHKg0cpOA+xZBicn6VHfkR8Mr6/HOiVCursFRA/dGNcVz1KhBrzX352aq6SY2fg6GfTKZbeNEL3SmlDGf/l6vqwp9/vtA2esz9S"
    "cOXLfEgmszTFtBNzis47NWzhpEoPz7odkkJY4JwjFGg5sNK8OoZ0RI8B/Zy7OJJU1MfqRzCcwmXYPbfDsdn4gpHtQ+wJooK/atis"
    "ZVSKKn8E+aWq2rU5+zgRKYYdzGr996ltbQee96azEQx0pOYKsig2TrhtuCxbCsTfTOz/OYOTNbsRjwc+Du+j3cClUrsqYkgalGq8"
    "cTODkwhG4tVvZUvBUVwEiBgRJQEc8DYOk1SNlMU5JygPckB5/+TryMpJT74HdEnpjnrK3ShY+PsDOZxqm16+5uqBeyWoKuuzFxfS"
    "xgKT9fe8CCwQZQXhSyFXjrnT20AYihZ6Y4G1ahonQ0SceWCmdZzOWEZjhSAYzT35iRjh5BNPTwAFXp2Nht6ZRaaHlt1pxmhX961C"
    "hqusr3u2oOYpm48VDp+9BHq6G6rW2xLeWLrmlcGRzYgAFVXBmFGsahwqR8IHp8FH32stCinYgK6BpvOnUIWebr7SXl17AVTfOaAo"
    "se3z5Tg5m3o4aZCaAQPWd3x4bpy9YBiaq09dLVzdZKaXwOplvblNOM0S/pw8c6+024mpAKfmRzO4RzLuUxtT0XcRRdoCKCiAz80a"
    "6zJXc3rBHSiSyTARFsujPrd0xA10suAYDoRaCPBDl+9nTM0htc+8647mWErP4HYveKJzUBGUI+4lF7P78xsSX3YYBCsbfkZqeVx6"
    "P+BJ5I95DEFmi8XDBpCVj45WApaVDI6XLTJFadqmQ+FXEL7rtFOxK5Vya39mr0HNiJqx5Gclv1Dwq4corM43x/lVDNjk3o0D+okc"
    "jRBatdPukZcNnEtkFkosEaS+yvwRVzJoZ9MaaEY4tG4jxwvk0bs3REGn7cQ5Q4YVD6dixJEijhN/8tLcksMXZ/tD8LDprEerTWBD"
    "9trdQ+ThKxUDxMiN6Bpwg2wkVHDG5Qwlg8ddMv6twuZN/AqGzyOHBaRtqRFd/7fIHU1+863CIxMbGpuX0XMGaGoPaVKCxj+NJh/B"
    "+3kO5JjU4qJjKkX/OS7axAowwrDSRn0XTexxYezxFGL2i50F7lAyU3Mtj1Jdg8XDP6Zaricc1SwQesYM7RAyRmbt7CTGh1sGTRxA"
    "jye3xvqmDsU2YRZU+4lUuIcUrIfZROPDluEc1ZNb+Rdtd0zYxdN0sRJAvLAx7nzzsB3Mdsg2PZc3BBmsUuBXOcZgT/7iHIECf+T4"
    "wQlgMJSUXLhQpGNJE7R2hdYfepX08txMTCK5Fxmm8Wfm/qNEIxXfzOoAReYo82usA2h+ikGDQUBM4tmBzKpUbqUxHdHnycKVsyuj"
    "kOAVA2hj9yQaYT+8CsLb/OqDWSstEF/n+2QEMaCyxOzPZnAQ07GMwyZBP0yMmiYVj0SpS1vZMDwsPj7y62g1oMUYrCf3/z9EiakQ"
    "R+Acf0H3q+AGmfDZe0ckyQqWBA4BOAyK/WTzYio0pNUiHJwQctuX9f/KvcCVmCvVwg47nCXOPkK3I7i4Wny/Tl2JfENlhBN0drHn"
    "jcbDyzSlcR6NV4GLlu5FrJyPpMxQgSqrk7AFFAKqYHaActEicU4xuo4sYCFNspBHnoPJj5oRFI48NfVufWNa1Hk8EcF0tiRcxMCg"
    "pyNuJn47/+1escRvETZtpcv0I0AYtOd05ArkhKr39YHKOfePYtCYhjL1JJhMk+I1ehCo3lnOo18SAoGtZe/7yexsQ06RpHWyCpE+"
    "KJBRAlafB4QiH2ZTQM9gjp94lbskEOsIDJHz+v0WDOXYYoICUf+IVeJA7SvUABRfu3YeFpjQQwMtfxmReQmqyCsIRTJUNyCSy8Vs"
    "J7S5aRNanSN/wlTZpxPDBjDRPWOgYJxPFmgkxqoCHJmshlBAkOEXIC0tRtOfP8efqYQXA/qIFNNuTd8Y7FdwTzduIxmQG46pk58I"
    "g6Nwr+WD5JcupAhWzjNFQdWWpaFMMTQNDMclZlmmBQ4oJBiw920jtQ4R/M2OeijwzwHDwZk9JtaMA9k9crd2UtaWOxG5Wvx1JOj2"
    "4g3EIaTF9kEOV1gzBVWeZRas+HlaSuENzzJTXEt9ATCHZGfqyhG+1JkvKhdhzgdPnSwIGiPZxwL/OM/L4ZVEjNUMR1iAaUua1uP0"
    "3pzrNyqkzBeSPj8Iob2XCASWRpM19FyDmllFsVdS15kvoWnMX+abLd9mo1r6Z3oKxxgjIxzB1LJGjC9FelXxgZKYRkUJlA5ZKo8K"
    "StOKizwFoOzcla+dp3pPmtidycmQ7bnJZ2G/NQz7OGBOoCYmY7DqB88KiHhlRkQnl3I5y5uGCViDBqy32Zo51WBkDHXwpXZOx49x"
    "9HIqPEA3iAI1Tz7JPCJ/i1V1sO/QdVnTTZ85q6xSEE+W73+UjJuxHcYNnVVucCka7tyYrnhENRqYmSUY1TQP4x9LEAXcz9p97TP6"
    "uHZzzsn5OxUTWeGa4XH6JQHo7IdQn2b03kcvivMgsdaZby0FYPtRB56dtrdCY7CIlziJoCh5jAwkUNapINvJSpI6j3tj6FbZ5B1q"
    "w1/MfnX+T+uHI904uvUo4uDdHHKowFXnFmEL2BHijJVP9PTH35Ipypxlx2cwbQcSQP52lhnJZhOIEdCoAvSggLjWXgzScFAJd1Qd"
    "Zef3yfz1irlM2h1wAxnnRp+G0URaIGgshRBcCr1eyuhVrRFloourk62ioWSp5PKFw8CAP7qaorbbwNnBR8vsjggv8Ob3BsV9IRNu"
    "okdYdri8u00+jrmmYJkwmeGYUpxfHyi4ynwy2wF1V0qeugBtxBqBU7fErJlSZl25WpKKdvvqErI9axSrHVyF5oMnAEGmDP8qlMuX"
    "1PIa1ZFrjcQmg8aq9ryRP9BnTP0+jf2HaxGOFVbCJQFZvR1zC7cKflNyKjXHQ2PcJ1HL7i/Pc4C7FxmNKaj+vZnAhFAjz6O9HyXf"
    "6DKQSsc/ijhbA7UVSuPI4nCtchi9gw/Ruh3bAcz2cOVo2VtKyCQIG/LmgeiUnO0x8BXb6vSToJaYlzoVAUFERhiphZ+CY4JtTUKq"
    "eRNZuSTA948yYhwKHhoT+oE8DRHb5xfRUuR9j0qIVnB/hXlrmF3C7nK4/BjPHnzmASO/q1c4ly3RTlZTMs8rn5AjYmHeXpvj1x8M"
    "p90VK5Ytcm5G1Vh9AI3DQO6gLFsHTvuxZs8W+KRJyZCRhFAacmNIJIt9L+Tsdz3EZ5NrIaa0DKckHnRIQKs8OKvghaaYDkh5nO16"
    "Ek5NkNR7uBc87s1AHICHc7rIM+vvLOwnSjSENdNdJlcNM1P1T5PHYnHZ0vkDNUEOZrFaDHmMmJnj+k/+evjsm4HvLwhHDExNMhd4"
    "iwaDTT0H4bGkfNY58JTDlTC3931+XBGY6OxQttSmX+bxbL0QUdZoQorlmjSW6HlOGWPCBtsw77/bQ1RjOc3y5OjND7h+6yV/tI+x"
    "d3O7mQggrVXEcLaLHB90U7kRYEfCnzf3+RjGA5b+D2qIpScAy/5pNpREDR+4QzDduFaoCdBdCAw+X0Rvhg5SPSufboezSAv9+uFA"
    "x0PG/bT9av7ECnOfAF0AmZLmdmqm2fa/6RJATd2KnDSHZyCzx6gjD1TsnVQw5Wn6KE0f9xGmbWPq8YuDElhY0i54gIGCpRrOBzeC"
    "m3oGOiNxct2qw1V3zNYjSJOEa2rtoNZJF4yftkNlASdHFkvMCS8nMqJ+Lt/N+0uKTSU21QbvS1shCKgldZUAenWeruHHE1v6QOcS"
    "wqSZv58/TRqGdfS0BwztOcTBQrypjboCD1pSUh7tZSEIlFHmnlgBB0uKHMl9ijcuFUXCEHT0uXx20Eia+TSkAsdCbpElLXatuKHq"
    "uCBHWYhevfRfkHwtHncOVU5x/l1iP6NoQl0GhBPKEYwdeS3TYEfyhAec3D/Ka95Pom+VH7Bk+Y0kqEgHzdVr8D0GOIburEcBufMf"
    "qJ88VmKtpYdS6RnTSQFUTCfDcjaXGHgNyJAZgsrO0AEIGpFQaustmMRjfE8TppVZOYlPxjMsM5lySTdKCs74VMUeMxLWDETojKmp"
    "FgeHRKL1njyiii8koTCs+x/8Ccw4yEFiEy3/i08SB0YiGpGkPHCDH9h6oEHq7V+RKRstn+LuUQn+SJLpc8TIYtoS5meAvhaOXVIA"
    "u0GHkOX0+7tnzA6vyYUus70TlXpuY3V+mPwMlyZOWcPPVOsQOouaYJzoduovNnD8WexCP6EZV1ED1ONrvOcfNnjMHQc8ZgaVnY1g"
    "3rVXAaDYv9Arx80QVgrAAUBAmYo4ivSCwQ1zTHTmIJO81KIVV0VFhb6dq0BA1kJgKGhfe4cLSsZDBHIsJ3Bo8X4eNb3zCSCyKtNR"
    "xGJOSCi5uEpfIh21vgAFn+jOkSBoqsp+g9292x7nsayIU9TNIYD0dBMYs48eYcn/KSCYa95AO0wVGjfDHS4095grCcJh2MyY+38Z"
    "M2hCQfPR3Djfia3j2V4IitLA4X9oocEBH+qIkWKALxd2mWGUsSXlKUNbnq7i766nbacdCbt29gKASVi+2P9nMWXuhy2HTDy6VVEI"
    "CQ6s4/CCvAueAioRH+Ka4di9dxBbRZXoSygiP+drp8SVKRXNjT0pONjf9pSDpshqzGPKmBjvtVle/cYTSTcM4Fa4TGVDDw2sICIK"
    "S0mRwwun7H73SQSOZNmbzmbuyhkonnaN0H5fKDc+HNykVVjZESf9OKKe3FKFQcAOZ2rlCZZhUw6D3UvDuc1UHEsxSrcb/mD/Dbnq"
    "GrLcW7Uwv6fVqWamVljyA4HPZLj+3NMFktIcsSRoaQQk0HsxUEmQD33q4Wlkzv2ia5XdZEIBQO1fP4zN0NCkD9EaZt7x2Szuv8ma"
    "NhNGtyHgYIErivjpdZwsJn3Ks8oucqpSMHShEpf2BzYE5WPqWFhGJzuIUoWonNBWjyflNAQEAvnMA9RNbZ+U/ecIiFmM21790GTF"
    "OdWrx+wGX47G7vIgM5/vaHvKYA/dXsUNFBSn/QUmoDAgXmAPhiUd6RlGk5c2zmPJ7MCQ8l2qGIcmSpn+wnhS1NZRGyv4MTlxxS+k"
    "Bp41nhXPcxYybEiE6sIwngLQZbqgyUNHehh0xUMS6qBvagJGe66HFxio3wl+BixRIIpiDgSyAeYTUXfxVI1ilr1mjWJWGL6CrZbA"
    "bjKSHsc7gOoqYKRlZ3EZR+b7XUs/b8gZXEbu+gJYW1NkDTje7sck3TSSkD0c7tbt9rfKEZbfOPafiaYiVmWJdojMeO/2oCxlOhBv"
    "DwvieTKfnjFOYJugZYpbjnIdcJDPRBW7BlF/+xWMijWqiqKyXtyefau5qBUk44QEriDixpq+cP7x08q2l7IYoMncmojQg2NASFFW"
    "+ztvCPnAQcYtWueO+1jimsNAJKDjDIU6dkm2fAuWaeJfnP/HJFYacvIYmyo3BAy9/3iPKrhWpz8K2+8qVDub4JnLDvY57A4bGsjB"
    "c1bsj6htBBTNrhdASahlGR+l6pzXKEWeNHZ35ldyTk/2YdAsZi+sQR/hj0qwqcnbs33+7AnWO5y8D8zI4ZXFmfMDYPnJ7cJ1hOOJ"
    "WDY4ubzUV9fSavCI/PMSqAFMlUvfB1fCBcjIkAj628pFSHPrPiYZZrB5eodkw8vYK9wxyCDYpF8LkDoZ8j1kTzar8+KuXGLKRH/j"
    "uAFHL8UgC4XjBrJuIEJz4KIBMZIeoC2NLWWfTJb2gKOt9MyvWKDZnaNnEPMWXOaQThdwR2QE7WB6AQWWfIV+fy8g8juxoUA66zxy"
    "+3SUXVOQGzaGMgmWhELgPX0DgCKufrloUFoGF8RDvVl9x3/jM9fB1knWUHFFPCnEFHcREpcIJTIV11CNweGAIXsqdGi5Rugz6e5P"
    "Wkt6XswG+xlmMhafltl4m9TJDWAmD8XoeFK3Lw4BnUiMYp1RW5hpPVbgyMvUMuGEgn4MsuzHuWtLG7AtFljZj+UPzsBWGCNmEKIX"
    "PoJRWkekGXIg+emtgEYX54RqVUCpZtJtVr8YxxWijN/RKSVFSD+GivHciQZiW1hPoxgGoNd9h0FJMDzFpJzUTJmYgC19VlFbCkkM"
    "0kVgtbJKbLXBcxeLdIFDNgEaQZeumYeZ5j9y0lBVOroJlmD6U4SY+oXEkaiTjaDbUku2JOowUkiQIbAUO3Uq2+KElJp9xTQR06FD"
    "5R7BozGA25SurItPQf0tpSIwvGsM9OBavXovV34TmBlUJVLxcLpiKqjKO2GZH4qLhvMs0Ez1qkyJTAWhaSCiO10IYZ35UcbOD5xd"
    "b4CP316lAYSm2ZU9Gx1n9+BKsoyfazEbfIqECUNOVBTpiOt5Q8bAXCD+mxKBK/Cy6fg5MXFaBM0YJVRWkoWhoQ6MWhZvFw+ZjaM3"
    "VGFZ6tF1NHRFrl5jmD6kJqvzl5Q2hPBLwT5liAMf1rLh75gfH0KH1/D9fqddAkPppXJd9rzSKZjjHWlx4gdqb0ld2ZgK9MpqLlhq"
    "AZsgmY5AmLuqWQ3vgAJsx8QLhspLTzbtSQdrKdmSsNSq59ja9M244hryFDrO3PIcDDQzzwkOMaQ8vO6GH7rKRzNHkWfsI/yX1zKm"
    "NKEsZWL00SM98++Hiy+oqSNpmKLHqFQ+m5GCtJPkMA3N1gKnnqEhpGBAHU79ihv+hEquAlURoM7JD98qFlAUNK3xf8gtUp9L4hvo"
    "4eLEfesZ64DcweVOu8c6oj8UANwEKK1J+pQ9vTk8+rAyGCM9cZzVeCescvBrfX83zd23vgtnQSpyNJPAIvVP85RFaFGUqYFmvjbL"
    "FozDxvcUM6hphn4ZD71Kqk0EGkJbnGn6zNk8fjYf1X2IrtpJHQ9UE8TBxrBgIjaBEpAwKGrOcquf36eO//ryASaWhVuAgLlCiIkm"
    "yKHlp9uAv+GXmP8GHfQzLsApNgKpG45ODk0Z2FPLy+eAUbqkmVdMhaT684jjEul5+NFf6MBf2XxHO6E5NeSN5j8hLmdqwMI0KIU7"
    "uVCUbaQKwRFNDIpfHeCWqigEEemyPn0SWTE3gNtrGJrrsKvVs4kWcFwUE412YuTw55CT+Htjydg8qTCOm7gp1bSmkvJjagFsGm4w"
    "mUxiwnhO7DSNdaKE/2/8/v4/WCzijEBvgkD6sfZdoUK5SNiSEJhJPWsHGoo5NuQMG5YeaWqU/BCrK7tDScyLr4WmORCNNyoBvfRU"
    "rCRpawcApSVebsVZ1ojCKMbKPB9RJnhfDbl+i66ecauRpuKLQuW4sI9UjJQAJ3VMolKurigPZrpVqgDfBIM4XJ+ChWSyIEa4xh+I"
    "oYSb7lKhDPVkpl1mbh0l7YpUJO3yP3TLJ48PtkY3zYofkUZ79/AUTUvbggEA1f7bdOF1+A+RFWyVn7WoPCQxYUFg/d3wgg7WcMC3"
    "1aNFXyN/pTQpdTxHcdvKGZKiX5UKCFMsLBSTTxlxTOKHWloTmhFdKF5XCCGnVD64V4gwkFAFfY8oLphqIdhBbqQOcJawjAN4TB9A"
    "Q5gXlAX2KMhOl8yVPkSGccBJWQOK0IImMNKXkk6psK6hdeVNsGxlH1MwdqbTGfFq8K1oLTGOR+3TW3kGeXLAerbZcYup2YXUzcui"
    "W6HAteEQwd8fchLOnxWzqHT0TlOHJugLhh1HnJviB9B41lX1Gy13FcFStH8DAYwinibztRsqurwwKly+i/0dd3gdwJuNmztCiFZY"
    "QqJ7S/RA6+GfZcLP0TuMLWv2joGYZiYPbeAevVoqyhMFyJ5fRWp3wKp/Hwb/jLUPsPVLQQQC7v0pGKCP/ZZa6pCQGAhFYjnyTLwu"
    "s7ZLdzDnNMs5PAgdp1pKUp2N1eOp4me5ALeGaM7S9QjHmXWqIE0oXHvASxYT2NpRbkrKjB1o5pnrhzLH59mzUPswLaIjsrtREP6z"
    "/MSnWpIA0gTdToZNlkmxc4ynN8vjGmdV/1EFESylc0SrV0rrWdiJlsHeWcPkJZkDCpF55XZgTulwRn3EVOLJ1xpzsGp05GtFQKzY"
    "r8QwjhPJHXZp+Ar7z/7acRn0O4FJ7BwVAV2vH9ZRtRR68YEZgbVsrY0ihhIHTObE81/aDcZesIp8f6o4wDiJjC7+iz9Nn05uAUA0"
    "WeuxwbCR0fl+kIH6djsoJxiL6nqdgCq0hXhB2KmNfYoZuPfV9IGGTHricwJSgpMjludaAZJHMep8AGRho6nCPFVj88Q4VEL6zQQl"
    "/TxcY/NDbRVhkJEtdmXEa4Sc4JVB5nlYYbazQpRCo+otjknrZxIg2aM3ST/FlAlh1U+CDM7fgqly/e+ncLAXFIxoUPCFNkXLtMpF"
    "RYAyVXV9CUzGT9iW1AIiVZeyAMiLF7DUpEXMK+hzkRu18pVFJGSw/9TUdRKFScWA25UQSXVWaYDP6o263s1/5QaJno3ReP1B2Ucj"
    "SVJ1AdUggmaQhB6Igx6upnJWLjeBxbFGfwsv4wRSW4lASIqRff+wKnsvF2XFkBjejR6VfV0Bd2ZzwYnk6Fp5RvjDKfCNwQqZJ292"
    "eJ4/JWrGyZMV6QCzkMg5TRQ7y1becK7fB7f55QQPTY/S14vkakRLXPuHplB3fgnoEAa3XRyMMImygbMv8T+fwh8VnSvhM5PPAkrZ"
    "YwaK9g6Ff9RpZxJhgJEvE/32/zPDFEAEY64oUQYz+IhToQlBhy+sKj1kP1tDsI0FQ25gT4uQ0qU8lStZ7N9qZ0MnkbGor0C48QcK"
    "97JE7CY3MEViOLEhrXqLFfE4xjfzmE/KksLOMSnD/V1+nvLbuqwmOVOfSeR9UvuKzoURTrbKCTBkcfuuxKE6ANxFbA70NJhAr5Qu"
    "3xloJv86USRnrpgXL5CLGSifTrx4a13ISa7A/pvNA2Ki7BJb7ZBtI1dZPiKm8R379+IXevo5cSsYGWJBubypaXSQaGBMug6NbGjR"
    "CVCvU+A3KzBBrA4MeYGGjWyiUVdlNJY7prb4DLmBKAzO/Gg4u3lZk6p4tpqULULNDINcGLP1p2DmhGhm2MqZqa7VOPG8GGwEK+Lv"
    "Kec6O1/9Se5GgiSlJcSOfa66HT4fQMbGG3L/8hvf78Q6Y5gqh36xjZI4zmaEQ13ZN1tszQ1MIuG7z/6GqbTX86IjxIOOTOk4KTcC"
    "w1sR0INDpEiYliDP0IGGfh54W+5jdJlcKgJ/EjoO2LxyOH9U1yJKCa/HnsVoptZxoefu6Qj4tdX2HVJwhDr/REZUKwUxyTjlDGKt"
    "LO/z/foIwvxHjupv0yXH21vDPdTTBByUg927oOvcIZ8HZVpxjqaM2Hkvbf7Oxi+uJ2M5sSMPXt0/IPq8MAR38iJbVh2rrXKK8B0j"
    "mcD8+16qNv8LguX9FS6J31jw3ZrkpgYUnF/lP266ocqBW6Z8kAfBOFqlRDGgDpDCBN06Uo3d6KmtaLBXyFjJz0BvgrEpTxN+hoPr"
    "F7RHA1Z+61M5wz9shIxbq/OJVbFHdTInZpvxY5KxDWkQxj+eJ2fL+yDn32Ei/QCKSG9/kmdzaphq/beAWyk/nGUXKJp+fUoAj3va"
    "k/QpQlqlgbf3D3Fg4wnepAAOmx+zDcAzGYyU6GMRN1WyVu6WKv3HU+0jCVjs+QNoK8cdGFuJK1VH/QQZFOcnydf6dUMzG6SMPyn1"
    "x84Fk5/3EyY20IkGtfw0rrI+SKDlPWRtMif4W9MTYYjpoA+/cLXkKj+PldLGMaj3LeiprMKZRHMsAvOOo2ClE4AAqCcvjzCwaEd3"
    "/gsNDmIIzrmpTmB1FdCRp3Tnr/L8PINWNu7vJNp4/DShdDPhq79BUSnEdBIHYtxHJzHWTQTv/uSoAVYY8y0MwVxu7Cr4EnIjlyhc"
    "fGKS50y8xFPFwiRk27lOxKaF1Y6Iyophx6SMQm2nF7ZI9Clna2Bv2zAQ2447QkHfOHry46vIr2LUuNhqON3GiRLu87zQ5RQFGTyr"
    "qQTw5SpMJJwzQSF8kXWBaITZuXe53Cztsgk/OLGy0ZE6UKDQceXpAVHFxqGBAbLp/UQHXdrBtkErAZAMs6oMqXwqPh8byLFHZobc"
    "sEQ5MRfycfZq7oZjj7xB6CHS1AibUJl92H12+HoBCJpyoTC6g51NzMweOzyPia0oSnn7Ddhlg2oZPF06Ewf1Mo+PK2vrY0uPx9ww"
    "WQHJHAxqFclqJE0+P2otEIvmMf59gz629SNq+mUrCeCOCgrd+IH35m3118Dsl+jubHREMS3LFaDQTqHrRq/BmubO47BhQbGa/Eog"
    "nynsMBDiByl14/McCCrfZ1c5nEab0TNCCZZFjTauDw/J4dmDoKuZ2R06dt0epQ3SXPThYRB0yv73Yqr0gyp1+ZMei+Ldc3GrXIWl"
    "mpz189f/AyyVpFI="
)
_DCOS_B64 = (
    "eNqtXYuWJMlqC/3/R8v2dFdlABKQvd5z7Ds729WVDwKEAHHO//c/fPXTKH94/Q/+28d/P8r/8uHv7+DffhMO+J/u4fyXe/i9Bv7p"
    "8/iv75Cf68bfrh+fJ/hfHgHPf7iE6xG8/gW/D/3f5/BjPfgPdvj+CfDnEvD7DoCXr5Hl1fOPj/97FXj9Wcb7wd8vBe8/heu9o5yJ"
    "F8+Q+HUEf7tgPgcZfz+FvO7pP7rUly4LuG7oxQUguPB/D5Hv7vz7Zc9B+NtNfMyXeP/0PrfBP37yf+/6c3yw9yJQf8AfLuJ6Gnjj"
    "xaAfNv8cUH686H+JA698AO7j/u9rfy0Jb685/iv+du38WNBfHNAvlOL7aI5T0MPaAHH+E/bhKZ/nn1xRfAovjP0kH/YfPl/gxN4K"
    "Gc/w9pn++gyoiIk/WCBfwyD/lZyDONOfcb8LLG6G+dofCILN/X+OPP6ck7D9Ky7cZ8XPeBFIxI/hjQvk924RffD2LOA/IAgWIEre"
    "14UXX8/vI3txfi43Fo/1CyDHmjbicyPYvkL8PYaF08LXh57hATAcqfHXsYZK3nHsle1c3/i32wnYgXiVU/JGkc/zwBsT4mtOAsZN"
    "vfEA2o7xRcPcvYTwP/gXlxeHgBAY+nMAXqLI9H0vEyF80ARfkCqIP4MrD1k5M4ualiYgnOzXnXBrtwgp4NcOVj4Q36+qPucNij4p"
    "bcHq9CA9M+COvQTwxu55/nBu+b18/CEXRgJ7vHMfvrTeV9BBxXEWq5su4+uz+DwKnhOYDL5LAN/xiRTQRWZRbyFpiWcNhsbzLKjv"
    "ZXEPjCa1jfyoycvr7JcavO7T6CdWIByKvxFH1Ii+565YzwDX3ygPIAdsrlxV4ONmPpPKz8fsoX0JAGEvjOePPMB7EM7gDH+h67qi"
    "gMygfQ7AEsHgghKBACRenSHkJGo04y7MYmOA+LlZFNICu1d42y8L8cF3EQSZRXjNw1Rjwnj/8wH7A4353vhPDX38c0nt45DGJLgU"
    "TnDelSR1kvvNybCgIqATqVUZwLzz1b0rFowaE0wPAAqULGJoyl5xGc7q+P7vCcPX9P8DiQmFGvDqANwI4A+FNGbaDvskvnyCfzy+"
    "JZ+bflOmD1AcOCYMF6kG3OiBowP9vxwVIvt7rKq/Bcc5vwFe7HJd7lAQowMKb5EL0ytmwveWwAIIMCaB0G7+4wD4JXGWX8yb//2t"
    "ZOGF9SJbVUYQaNEjw//jn3zw3xAATi5hy5zSvgUW773yHdfDLVf6Y1jvM4l47aMRoieT3iCw50F8STDypetrbEp94Mk1vm/wdmbg"
    "u6P34X2+5POrSjKPog/aI2xyNX5/FfGCQeVDPZ6XZYxv0GWCj1hlfwk2vSs7IpHmgVeYfhV5auZ5H6/FY8C5W6h4IRC8uYnCus4p"
    "IOEZtY8D3aWQrOGO/wJAfw8M70txMMCL8wuVwns8x8+vZ4EQL6EAwxdfhSwOkeKn5kELjfo6Gp7yX+kgWpG4TLGXyew4O2CeI/Ei"
    "Dl4BEMQGiCUPQo/juHdASLUDLCkEPtxtjb2L68f3IaNyYBsAp2MEl9iPwgGEiE+svPbTBnpjFsyNFJfFACHr53ndjCez2TckJJVR"
    "v0v/HuQ3Ik/FN2ZrxmSE6ID5mAW42ISbzMJYQONVu+eJHOSqhtWknHMzHyGI+MsURgTDDIXvgDQBINwQ9fH8SA5wBvEQXXcY2kkh"
    "4F+IfQsKrSu3zgCExXHklI1j9M8MHLbe5zDdLvL52+TfzQOYAyhS+C8AovVjGJ4Klt6Dpu60qgkz9S6m6DFHoPYpcczf8AliJRX/"
    "yz/IuVt3gEFcwVaZLScKoMY4usyKHj88OBGvuaj7k/LLeg7sn6tnk/piEQEE3bXv5KacX+ANrPpfQ1N92wYff1uYs0lW/vUJAj9F"
    "NP4xA8O7AlykgPjKiL5nH5+MjImPWz5C4UDxrpMgzGNsatkxhYsRmUv0RNl6MmbAaBziHQnmA0SNSs7AASF2HYORAsGrImTJIJ+G"
    "PPzlDKHiu876dcnyxfXjD6UbmoCp5nksV1mv/M59sEBwzL5QAOIND12w18yGgZW0uilIril8vrANiQJkWs7T9FhAwhzGTnjMNmrp"
    "0n8ZAF844fti8Umx2Vls7dmh4BJf8lB8kwFH4vX6BdfnwenmK474DYCT98MYUf4wSvhhstoMivYyfgMgZsfHGnvPVJReAHsuKahf"
    "8+fFAJc+zpXXoOp8RAGW7gKg8XUsiJw1uMWGO6AMhSxOGbYf5js8AdyHDsEt+SaOmB5UZ0SE52rov0op4Gt/HweITfi6fhBn1QaI"
    "7/+R+W8WRni1LN0UavgdgwHf+IIyj+vmUSFslrqdZuE8KX1510pV5k4qA9XVE7rcgLvLz807uaGiMmo1cWJOs/JQ/YyhxRn5VGGJ"
    "5QPEzfdh4fsFNEpM4O/5e4Mhb/f9faNYPfybfPo2oXAaxKDmKbYMDA1Nx7MbZRSnG2GWkDv/a108d4dOXtCuCzAXjnBTb6vsKRKX"
    "QAOqN4zkt5ONsxOGJB3PuosbdepAkcm9BbtpTToiUZsAE680zORy4GAn7ocXSK+p86aPn4VGRhygw46Bu7rAvk1xXBxjINs6gpLN"
    "3INwPS0yxeTnMLF3X9fkBKuewoJ7DMeOeRh1dYAjgsRdUm1jEJsKP749EJh80HdYCTmA+gDS9v4tyPFT0ENmjx6CumU+48PCBcAw"
    "XiENJD4nFDTQM4f1FUEmSR4/QALZXRu/BQTjG1Kk8xU/Kaj1Fgf8woX7rqbE2dG8r3oHXaR/y4BfjWwLBu6DPR6Ip9mqRQcJFKhf"
    "vX/odHyhQwRLwH6H8d4J2jAcmdubvDDiUFREH34DA+CiI5eP7nOtiIOQUw3685+BgDrJbQigJ0VuUR7X/QfxBl9WP0LXfiimYwzB"
    "NL7zvvddD9szTJZSGA9AdQMv9xomaFmEZfiOff7w5HJvwZFExKb4RBGe9OfRp07mnOFs+89haOpPTEaHHOVxWEySEyVf6oJDz/1e"
    "oCJ0JDfnn0c0b8b50WUTYGqURLnLLvNrual1+Mi1rqWChmdVFxME7JIort7go3wBM8bNsQNDVeARewiw5A6+OTtubQN48z2ZYomc"
    "C44v5rOlgy4qfof/BBSd/efU2o3XxlPAAFOd1yQBPIX7muTEmKMXJTe8OQPomJWxgSaAPab6Azkm35seDfhXRFeCwpRAYPFWOYlo"
    "sakubMSsavPxcpYc0RIe6oopuV+W0Ow3zncBZDZ5Pia8Zp+YD3ssILC5LsqU9cJALgfHnRlCv2cc34XCXQLQ2c9vm0z8cmTzXgCw"
    "KrwTB9rp3MdviRChh49lKBEevrFUG/hi8FU+9+rBewbs/j6ihNbefaIa+Rf3vpiCkgUAPP8L4/tmZoI7RTyeOx67XjLx61kKP9ip"
    "sMDyyF+zv3pYVgo/jxt7oSGEBo+vmqAhhilwOU/02A1RdoilE+tPjYxjIoWW3fu9MPxTOaGDf/ykKbxpk1QC3wT/Atwvz7Oawiv9"
    "D0xadlxBF4WsdzE0BWCs6Id7ZAbuJbRxn78ZWiU/+cyHYwAOiDiYCOnQvosrcAizEExkPIDgVrEZY+eGmd2Jiep/Wd4/7DAiX6ig"
    "lXEGweDDHaE8iff1/SS6Jjbc+WqGDGxfIDINCFWz+ifTsRBiYFfJsfgh5csI5FUbv4JYJ0KxV8hK/C296CtA2GFmTsFHGvL1HKYg"
    "QjBiVdxxo31+UbTxZo7YZCSSQnLIjHyRQde5XyvmSpfRZ0FZdhE2lo815zlPMXMGq7tD8RQg7haYMYHDaFbtNaDtBeGrzmwUFn75"
    "FFhIpf5mts+cCx2c8Aessm83aFsCSSvDsegMjJ+3mRCjLz5cYfgf88Y1Oh49Am6XrCOP+hvuBVSvHjP7DFp9UBSfyTBW5fqAM0sS"
    "MyHeYsa7G6EJi6dRg5Flr1/8xkizTg4QhsX+SpmhjW3MnEfw7aZDWOG4oJsxwDumpFsc2yWCY7b6pMv0QozjhM/QJeBs/h0hgFlH"
    "ieKHqIKKsd+A+6CQ7zOeOb0DEVJ4NpKoEI6IHTtwPE/yLT2T5TlOHJwRDmDs0erPDGtk4VCB2XV2L3MXaFCxq8Ax9VBu9Mf55W29"
    "ljY7AITuth8dsSV6FEqu3CkowDgn7NyH6D+MOdviASATtwjp+MIyoFDcRKA8WZAGrav2Q7t55JJR2iiZ2Ydq0uhPyxyZktGTmiOw"
    "a2PWYuJ8QUCq1IqbMjJb/hoTcQn+ZYFejhZGy7nLtcr6NdeI1Sd555HeQQmD3PLIJrHaDBHcMCTIWL1yyVQ96+RCAIbpnUYCdSwB"
    "0sHBrkficTI4TjaqVLDSzpZLagPmQHErgJPpe/QJGBTDp2CDLpPQsdr5gmlmkGKdBCl/gMQMtk/vbliP9rdZKsgAYNPcwgsRFKMi"
    "gjd5cVBP55KC/Rlz49XQnGTU0EEeeioWG/4u3QqQtnHYJiPApwesjtoFPSLU+hYs38IF3qk8bIJx1fmMDnUDXUKTKe3sfh98fwuG"
    "YE7AGVAIegPETV1fLTG5hbWnS1FQTXP9yK3hPKWXaqFgC/VpnquC0z/EfmQIAzutl5asZsDoQv0/ZR18zq0dIy0bJ6hw8Gj1n6WP"
    "LCXrS83knQoTERUYYNUNzMFFM3WsQ0iMsEgvtZ0fRlISOHfn1KIFV0i44PSjj+X6ERrR4jauroMgKhWwhvZuvzcrVcgnebmGMMfs"
    "D6LeMqrwjIuXn3e3yCLRUn8cEjCq3JnHUXcihahdHBvw4IhJbAIwh9ujSgtlhFWd4Pf7g3/I55nSDioAgZ83E5LIAJmCVuoniIWz"
    "A1NmMM4Q5cNLsdOte4Rx3FrtlHvBQsn3J+IgKHzZjQanJBLH7a7iucnrzsoYgESY3NwReJILP9f3z4f3y3XDEJMcZ9BwDZMypOQ7"
    "8vFuQ6rt916IH7d/Tb7yIlM4nG4oe5xnwJJ8bFiGhJEARUdk81khxX7BwwtkQZe3pcCtdnGNXWSsQSCSA05AyXqq5S41dP+6oR6v"
    "+jCCbDDbBIyyr1eMtHfcK2uHOeVPjQ+AsVfn2ik/UbD37PGhCg1clwo0TuXIvHz1g6hULdjLmCJT1L+ZEW9eD/PFXz2o6ifeaFld"
    "GgzT85cPFzerTHZNSVQRnRqjmLwNxRpuVgI/+Qg9xLxBNOouIcPUkB7DMR7QdfnzkxMxivjA69+r1rns/+kKeRSXzIpfNjOgvGTM"
    "9vu82/7WENTRwe/HT0PsjukLqOAQWcb8pwTsPFDAJgO9lDKoIwA8jgsYXC/gRZcd8GTNDkPyoFUwoIRK4zLY0mGCOEl6jQAMNQrY"
    "jBJz/kmkGBL2qSMzmvKXPDTrvUP6uRkojmtQhbiykbRk8eaIxa4znLNR/r1uryQ7uK6VrchzBsApWgHHVaBhk+Za0YNn3LGwUf+Q"
    "a9qZWgC9BjmEw4fCHg0JYpZ+c9e9QUz8ycR9siDdIMZz2prm0BmA+JJbKCuXyG72dzwH79byuTfBofU/H8PNaj7XMugGvLUzRPdo"
    "ABrzQTwLLNzascMN+tdnFb03W+BYifXJV7mfwK2vYyqr2yg9ww8zLNQTqOjYSN3NgP6gQaU9o/qrZB9x+V809mvzMw72f7UbiCh1"
    "pj0yQrw9jJOiCjNL+EWmd4TfsMbIyfXEEUz8t4u8RKHkkm2/Oth2C0lRrBEC/nMIAOEPFNe/F6PPXQELCMujyRSOHfg0JPiHAByg"
    "juTa7kVq8A7jWZx3B3TKHG9HwPLsBnAvaoUKHYWlADMK8714iyICK0iCYaqu2E9HZaX3N9s9wt6jkIBSJmv/AhPulcUhqFFA4TN7"
    "YqfnYvlzPaGFXQFG8tZQBLCMtAjD8YwpIG4mY4p9FUUlAbktNP53UrFM0afhD7QcmpYVyNrf+DI5HC0w5X3oF/nxdLWlxQjb7We/"
    "WqV3FehXn24hoZhzmrTGknMCfSNpntwYqy7h4hWZadhIIB8uQHu9oplAut4rLd8/bUg6rsnmETHlGyEiVHptWAkYig65cY0/RLvh"
    "P43wxGssfOv/fJpPgnwaYJ8uKay9TOeIOYWfLeN8lpdVZjb9xolGV+u8hwjO4HbJVyHfhl+5D0UOGDETWQU/dh4ymUakY57cewhA"
    "6HqsW/sXuIoC0nPOXazYP6wGUcjYXm+b/jc2JzUn6kKNnnqgYRU55Fj8NrsC3g2dvoAqwwPqCOLxTZK4ExXzX0/nXZH3P2ThA+5O"
    "0F165Zm1bwtUeQyOgUja2UbqQYRZidCG+ig+PvljPAaEy5KK4/KHCJKRqhnOUsb6AT/p/U8JXHJEjFkNx2p8jxlpjofRPUvCQnT4"
    "LZ8d5Pl1trGgXmVUbCeP2s7j6dWkAXTUNeRQJxUU0WYYTRIYx0ieRbbY+FTIEbg++UMh7lM5Bqvie9bSJcahJ+ZeS57j5jj0fQCJ"
    "aY8joBw3YBFN5SmMnvcDbHcFjgVQoClAoBovtQhmUyOArU9yNXyWFCP5HSEly17rXRbaEHUlgdEc1g3O4BNgFufLMs6BnvW9BYOi"
    "53piK0cNUvrec73ZXd7vnfFj/f60Q1uMvaARbHgDwZOBoIQWroVAKdN7vYbumVUiRWojL8ZcQI6SqbefbQTKIeGJBEXiTx2LFH4C"
    "bBsap8eh+rsOMduOXkRuPg/zTadpSxuhjxpUQcgJXSoTqvclk8ebfPLaTBw3SvzJ/X2bO+KCC2FbqFJtH9+PFr99vE9uJnj+8k9b"
    "aJObrI2CefsCUl6FZjeDFFi5RPtYCMQz1CBWycIKSOJs11ExKh066+XMvvaypc0OTTlDcztjlNZm3Xke28WRNLjq+yOSFz73Qjys"
    "fH8NSdpexAFAr06J9r0lzHX9hd5jy1fGJDgW9Kvo8l/tdp9CWmcE9V33htyyB7nXHQNx9yCl6+Fh3KPN5XPU3ZPZ5j5XKuYZ2viF"
    "hMRCk1fkZTZ6i1J7FIPah4rFTwFXfEB3FZJNJH0bvyjOpXyGkAY5biHbSJz1DTzy82ShlYhhjFVxqgpT2O+9JYvi1mBz/bMu6FWA"
    "fvXK/nFfN+QB9w9P/Th+czXjReFmHng//yaFvmfI/rwOCiiJ27gFqLT7RbVEtWC1pB+S5IF6Ev1B+AiW3YedyiOIIIYJWfkOZjsA"
    "H+8TTZOZoDnLSMSkAcoBnWGkwtFszmbHeKJPp/i7xlQEvofskeI9pEwS63ewDsBFB8m+g3ocL5ESgTQ4QjSy8zT5633C2NMzLfCD"
    "yXlVTqeYvCRYHwoYlM1iSHdD1PUpTzf1WpFfGGDrg1GdB9LHJb3E8kdEFZpxmSjKgKKJGRwlwPP+gFiVMfZH855Rwcw+g0XcipbN"
    "yZpoKqCkCjrWEl6Qj1zUP5t9hR8a3YxVR6bYt22R7RacT+lLMLzX/XPxzCeHxBHmyNBKimCbjxo0oswd8DYBpAjQYFTe4mqKVlYi"
    "feyXc3XITtMRwGI8CLwKOJgZ4X635Uk5qx3iUH/P8v2Ygj9ceu4JOCB2efO37Mfw/Ri3qNE86EbX4Uu2gYNkzh3anYbsXWN4Jhgw"
    "YSRWuXM1E6g1iIn4CNLYBrMq/B/YD36nZ4CGeYdL9+/Jx3kFd3l3nw56XJE9/gDCg4Qy7Jk+5XApP99f9Wo1LFhvr7yp3ai1hLOJ"
    "3YjVcpQ5qmulMvfvH8LDYRyWk3Y9StBMHQn3EpCO+M3BLq5Bvy5HphBg2Z38s/VhKfhO3XrHXfsONZV5x8/x6EAhoHMvkWn1P8XT"
    "oelNt2fEnikX/9Q5yZXH3+unq46jLxkgqXq29yCaTJ+gNONnEaKJIfq5w/fZIHMzFJhqeDDn71A/IGLxGju7ZdHgYwK2D33GhuFn"
    "hU5CVbKNI+zA7C/H5S3gOyiJ5FLPRID6lIADdoxbqo+TrgxEhTVg5vxWbgFF533887Of37r1F2N4Um+K5+6RbtYOQk5gjxIK0Lw7"
    "BoY7Ko7cXRdIS1VQyYr+KVZvY0dLsHArmsAQwoDUP9+1wLHNDr4SMhaEUkE6bu1/2EEwTCDJ04vSDMcx/4kXjPXm2aISmiwVyO+f"
    "OvCwWOyugMQNg0uDH92YONVlUg3yFslWSLr9I2Gldc1LHM8lxaGB5ruF/MsDQzgZnpkSQeZqn2oE9XkMrkeUYRS55Wy6rI5CL2EG"
    "XT4WQNScv69w6TKnfBE/jn9slre6vD8VAV4pfoTkBiZifXbjlESXbmWS4xHWDwBFvGXgX5AEw5jxB2kDghy8L7SYnGCNGDJXlSCX"
    "aA1JA+6bvwoB9BMgSTdQ6hE7AAod8ssuZXa+c7meZzoiZEtaDRN4tR8WSct8c0nMmn48YqZq8e1nQE+CH4UIokwnWjhIQd/BccqD"
    "m8F9JqYJ8LRkOq2AMqA+fuJ3uQw/ReDa88Sda800QLxu2POT3eeNGciVI0ZTbZTFKKxuCKE88m9HGlV6gvDYZK8S50f3YTjzqit6"
    "7r2iPhiqYysFpRA9GzxZpEdhTS2KLbmaBeS011ZB2e2iavmezwXB+Due4zbwoRjK9jGn1PJ6+ozzhmQfP6DjZXUgza7OQjbpt0P5"
    "Jei4OJoGQc96ay4O9lsSPcB6/2SziSlxJWrjy/fzxDLNfaBd4IS7tnOlPv9p27qnl+wdEMLIv5uTPhMQkh1cIAYMjYbs9gu3Jzgq"
    "xjGwIEwSEG/ktPlAQ0YCktbt1TiPqYGiowDuA9zJKCG0GKBGvsHe8kN+u+m0npgC+Tr4V8QOErC6NWQn8hoqe/ffjcyzeHuDLSBV"
    "8U/WAsBwDJjJrELitikqDD++68DFxVkCNg7BqctLAqF04NoltBzJmxH8tmcqFoGWS6hP3kER3AET2YCGDFzEdSTVswhGKGJTSQ4m"
    "ceXoH0oKmkUjcZbc+8mab3D0KkcJJHEfuYGC7nP+IjEWILnIsNkRsOgsNQ4W0x9QpNcVO7hXSpaend11gIbbyBIw0AMYdr0GI4ak"
    "7y09T4Nt1UzDlz6dlrijHjamAobzDt7IdVJw5YdU9fpgXnS+8vMFeVaEdifUZAKibxVsZTTmqVh2E6QK5RTisWnupkonUFwR+lVe"
    "Nk7dEyzLWp/WzlA5Yv3S1DBPTpsY8q5nQXO2E6Rgx7Kpge4+UPL5KVYJ07kJAGUF++XX5mzh0yb1CiBkv4QUMJshLjy0EUO5UxIL"
    "XJGikCppxy0gx9MnWbRsrs43ruoPvb3JVk+1/FJM2TgJKkM+55COx751AC/1IxylygIaWoqXRjC63AQNfBFsaurGfzvVG+M+XYwk"
    "xHE9mniFD918m7HdLwVVNCxFFoode6QSl2AWtGtyv6tPPqeuhWRrsuUn96QvPbjnCpEp1EV45EKCoOimBEvGmVBobDRAagDVM/gc"
    "kb/LH5E7Lc+Rk3CLJOBE9fT7/OD4Bha2+ObNccuVLxZBRQdAUHiEzJ6A6LYVS+SQJTilhAs9lZkkODnY28zX0yoglHlrHjmvdAx7"
    "C7caFtgUMShzWp7FAPauem+np5jiUkooY+DmuATtuBygLR4MdGresduku8i7L6KCl2OZTz/H0k2wRk9ZRR12nYfUT4bhdKIdPsit"
    "NxAj69xo1ZwV0hJ5MILf11wzRtdZEgqM1Bi9ra00HFxf0CflWDU/RoqHBXGZJeKnORzrGKCbj05J/GncFLTTz3105LwLmfeo/+yr"
    "u/UgEGdj+H6WiUaY1PC+5GnLbJN/xdZnsd6cLdJ2lGl8+HknaY9joBEgjmOQfIyM5YFufwQgqND0gECzaw3uGN7eHO8UYMyVYuCO"
    "8lsiV8J9DDl8jrqUgKIhceOLYDqqO2/QZAFtsq5rQhqslWdXyQgaR61AH/B1Ici1BegFIFiDdF2/gkJSCLVrqUoVJq2vXUdx+Lms"
    "ZEETfeqIfVrJ2Np/kfx+0UKBZjKY7fYYBBI3LD7EF56wHpN6LkH9XoMgqSlKyh3dOGFjsZlAY5uT+OZmhdNisR3fJZQPi7Ub/kXn"
    "XU7DvqIeg2sDQecWojHKXRI4dgfQkY1dCX5RpeCyKzGwXvdac2XYSp4ZDvVxdSAuIiKVL9oJesI5RZ6wV10MeVXFCwloOV02RLoU"
    "nknXgFFmAUpLzdTKLTe5bBpAUuiiSMG5fn/niB18EAfFP23K/FwmgYDd4SW2ey8qUEH7GKe2WLU2IGsfRZlkSs4jKcVGA93uB89/"
    "wVY9KOYZCfYvmy/5jNDFmfxugqMcLOglO9C0fHXNWX4WUgvNiemhadRuNuDSpE0crQ3ZTCDf7c/sPOf5Kdu8698/fQZMx44IDFtD"
    "IHM41jvUaLMw3qJGi9qr8wRZOqzuGS68GzuScuwCUecbdrfI2ckEY0MLeSk7NDoLubqsFADQ1rXa+Gjl63GXbnxzGosEpWQxYbhn"
    "njCWgCE3ZZ+Mt2sUBMTI/YWd84uun90BkfQZ69rAYMweakOl3Mw7HBfpe1sKUG0QywUfbL63z8tBaVwMGT3um2Z8GYN5ZtxGzaSi"
    "BXHeOImXq6Di0tr76TNvAdczxRDvv2qj6CRNjON3KzQkM1pOer/DzfpIiAtu+6v0HrewAxYatWJ8F+zPz8zt6S9KAhDbfnjxOGDz"
    "48REsYnf5RwzAstpqJuREL79F+SgNU6RyjiI6qFWghej/ZSwZ/gHdFzynQPiOI0ksQEIo3/l0FWemgepmAzICTMqRpO51ik2boV6"
    "alJwaMBTKC+L3KOdwghTs6hLmOIug50L1V+rAWweLLrneRinezBMApyE+bMCB3aHu7YxcYEkinJho6BRLIC53GFXCOhudR5Ta4z4"
    "iRN30LCsxf6p1ffUHKpXz94umNMFBMF73apACKiSxmmI1hGjVNNWwDATPFiEFZ2mfp8/+98JQ6P/y36xXKIClnQaHX0iQ0Dagey7"
    "zPOZQ2xA6s7PZmfkfYbhy6ABJ1FmGc0OFQacwUpCplRDJ1A4af/NQUTYEm7iOM4haXb57+cN6Y/oYt86fLGM6l3IqHQ15yn0QpmX"
    "eclPA9imo8kyA56Pgk47ru8GPYfZOmYsgDIcuPzEV7/GgiNrwmeHclkulfyi1Yrfyl+zDXdH5yNlCMeM85ET+8X/sPJCRHEdJk/J"
    "KjWwzA1U8x4cPUB0N/B0xULG7Bal63SnshM5TIjNrOaQ4WzoUvUIFeJOpjpRJAhtvqz8D+YWm7FSzBcO9F6sgPv9wWQXgS2g8vhc"
    "vUUkd/oYhWrArLrfVx9vXuFA4Xwo8gzd5wuZxDCrGD5mUVdwYDy6nYvQ/ksyuywr1M6En2BtlSE7PJ4AjC59HpWF0xt9fgFb9SeO"
    "dkU/n2kGY3DzHFT8ignhFTlULNfMj0A9UjazholsQH5Qhfqg2w39XAlcWsYut2tIqLTDBJZsR5N+eNpJDoeBwqPBoYI7asCyQLT4"
    "lTzDC/7kH3A7nOkREcejxjbxOHf9h7b2ABNHFX9SBVuiu45OWxGx2X490mL+9WplFdU2lCcXzwe7slcK/Oh9ZaZ7BpXsEyyJhvBB"
    "aYxNfVnzlwzU2Y9Qe1JK/LJ60wQiJBritxfPSAijjhpCIn0zQTB1iLLJf0IGrJEE7wYOU8CKdp7kmMYJaqvBGvA88bIBt3aylUZH"
    "eaQTeUF//YbKY0FE32wgyz9KzTM6ITBo3GukHXNxQGau1Mf3BtOdggPPvIL4u4KGi3ckqDRy4O/cIuvXK2+hT/RXvhdNw/LzFq46"
    "AW7e0ym48OvwoeWHXfc2hgYVlLlbzNQmM86/qoKQPbDwgiw5CEJOlYF6E2rmdZ2sZZxsQaoIYsnjmk4tnlY7RTcRHDHad6aIjo5Y"
    "pc16k75EHR5mgyEQFt9gYHch0RrNkibWAiuw9tgnKXheDTwNCx6kdcNWwq4B0BJi6fPW/fCzpwRNuMZcxIMe4zmfTrB0fiwkiwpQ"
    "x77gRDIZc/30ElJGEIRttzgeE2r7rSsuk+l9tAtlE/EwwbvCH2iPKfwPwbjWmDz0AlGUQLK0hBLQB/RXtIR+b0V7aN7VVOEIBDry"
    "LwjiOvx0nxu4V4y+xV/RAez5lPD3n1hS+KK7gQqSXS83J3R8a2Nc7HbmGTrrftqXxiGcl+inoXYGYS8J1SZi1QG/HqKw5f5Y5Rlx"
    "U5Ow+5F2/UL4rjBjK20iJkBCJ06LfuOYbMgbbgUt7JMdQSL5Nc1G85zLnTkJjSTHyYZo7fRBKixEA5NE4M+EkPQfmH4pTDV4i2Lm"
    "uhnYlfHKTF7NOHyyjdxfLFNKGDTEp50o6RofTXSWOiJW1ZNgBkDqu/x6Zt+/QVE3EMMoQAsxjVPoioopY0jbZrMIB1754lh8wNrk"
    "6CpxjdJi4NjgCKRtg6U757O8QN8rIFkb+uL7b+D4lO71AAYyjIHK0uxclSDSXvREFYxzvYCgvntOt0bPfmds/majI8OL5jA0ymoJ"
    "4aWoVmoVlRgKum+siJOuqiDO2zPiIFpmqOsSoanom+RCHyo6vQ8RxROwGRooqFIZ+H1rMIGsFLd4CXOhGfaLBQXBLX8awGCOazwf"
    "CPfPJvdtmp8aJ3KFNL5xqoDZjPqQXHoDyJvjixNrCiMQ4MdJ5b7k9HDIhkSiLUg8vkNPkCPkQDS2IMkjnNNJoqE8ZJjqB4r5Y7FC"
    "h88Th37MQJe/25chItaAv6v/ZWmncTJD+c8w0CjLgZS1CKoJ5j6/VkEPFXBB4E86eopJ8AD3CrHr3UMDKsbUXS51ajNcWtyGstHd"
    "pa2tVhQ0MorQaRogOY6rqr35ZJ7TxKlTrL5qmvYCng2/ZSODa6CqjRKIQ0Scs21Xuy/zNLAiDPGZfr/5UxWA9x+wcE0osCb/e3WL"
    "LCPrAGxb1bqvNeezCYeyqxY19AonwOnDl7IKJbnAS2aTNruC7tWxFodqnWRGwVx06oHnNNq8wenUbrigQLlK4Uqi9AyQzlyBU6D6"
    "GYCfFOAErXmiAr2Sx9AMMR4pvUF4kfTXLhoIpxU2EIjYr+GByQKrLMtAxGQhNTS07I5+pXeqkCBLoQRENG5GMyWpVCMWlfqZpLbR"
    "SUBgKhMoahGmB4BHCv5ILOu1h50y562+CeUExKa5hVeyVQQ5ENMlNSwAF8qFhsRUIcjKeTLsx+V3AABt7nkVNCLiwFSg+HZt847W"
    "t/36EXK/whrzdPxuswCmaAE58xD9l65BQZofFH8CP1MoJa54rPD49+VcrZGp5IsTOsm7AYp/+KaqzVHRZFQnGLp/vjYga7EPms2p"
    "jxMWczZi6siiAtU9egNbikLhPNIwbk+a8n+m53jv3S3LmZvzb3Qu6OIOJ5a7LETIfTaONqds/qS5pZiuIlOtZAr9hnK7MD9doxrg"
    "RWG/3eJKt5guAAwFRaxqqKZ2DP39zW/6FAoJsUkCm9HVk6QPY4zCpACm1Mh34OmhB4FpTV2HveqaiqeZk6bsgW+WQupiza0pTce/"
    "3TujtC0MrwBSSZQfCddmgCNm/HcDaVjBCU1qfJ+OZBX4RHS+VMBlAnRd+YJxf2D1v5AuB6wy2eItMTF7yvSQs/qC3aj2r0PlOVNW"
    "P/rp8Zh2JRFkYu1uQGbvhKDcjNcowhvR1yYILaRtICskinmFYMPoVFDOWdo1RXfeZZws6RHEsDJU4mxFNtUt60aPheUs99i9FrnW"
    "HY9BRKaRwYcCUhehMPd8fopMIOvnK6rc3sz123l2z+wSjsIOJXLEj0jFSxMecwMuig3B5lloqMkrfaSJ/zBjfXXtKjVzwMIEXU4Y"
    "6AUGj9pLabXjKNXmc2kQ7oeN0dqZA8QGsNFFtvqaqtYpE7Eo5DYpbTOoZEMHrq/vhNuA9yWRiLzLAeHaqZzntJkw+l5I8B6ugy6t"
    "QUvA86lhUfl5Npo79JuzyiA7Kt3lake4e08nDT5ImP7bwOoLOE48NTUgolu3185VFGJXHiBvBbQYmrEEWV8YExNta4tGY/ckBSYO"
    "t9cjmLIqsY7VsrhGv176U/2m72GLlRBMZSr1KkbxbtUHmPN1/KjsD3vQhpy4+aDrcWI/dzEK410ApmtAp8ZmVyMvsUAyslIAV0Ze"
    "5vOw3dWfXPFx4jTll0+omDeCQaKBc4b2QFlK4IntOMZJfhseUbOKeL7dLNjlYUWYYJ6QgqWPNZ5jmjJwJMg3F02/jA35CLV3mI5j"
    "0FMq5yIdLWCFw0/i2XLSSv3ybU2R7Fls8QyQtgNcbMgl1NmpyvVSWiEo00ydxdA5TrZ5QOLCowjztcKnaFLUay7eGJvO67I7h4Yn"
    "V953eIPjGrQkd4YyQIOmUiz0yqkZOkVsQpYqRWTTCojDzkAenGkAjVYp8rYjUC0B58iklBGTfunL3YCfxYFL+UqseIbQOPxD6l0H"
    "CJwR88t2V82/ulwZljCLLw7lWMKWxFWWgJLA+QbIaMpoyqH6QeW9P/kgAGiBDsbUCzydzvjXSUOXR9AOMKAra/CqQjaaqh/lDueg"
    "6b0VvBuu3LpP1BheNSuGSrURKP430yAoNEK/YhrX2qcKxfJy+HEZCvtE9hwlyFr4S8g9JTqPpK+wKGUl0xKiu+OofxrSMFJFi0dO"
    "kaLO7EkojKTAwhY2w5IV4EBgUG3QveueHPhhXclrEjGeuy+FnW3a8TYeIazNBqd3fNYdTmUfc+sBag6DRebFdLBtT3EsMC70S++m"
    "GdBF+QIwO650tweuqj7SNtDTkSa1gZ7qgvhwI1qyxnPXt5mCNlqaAm7tx0GBrd32gbpf8sEdOFmeAd0KxjMuDaZKyG1kRakhBPmj"
    "gQTB2bSFJ5bDwe+Fsd9Rs8EplcgZ90gm3+ZFDjUzxmD/OX9DP5Weq4HyBcIAZb1El6dxL/IWuvfX2g3VkihK/WXavXZssPbnS54R"
    "HiSVlrrKiP2vSyBC0TXwiNdqlT81OJYJbBQNUF56TWm8mNTkeR2zi/w7d+8vb3vOmSXMmrztKc3nL9IbOC19QK9Ulj58c0M504ba"
    "mSANj3X8Wl0ozALD6pBQCrtcA4r7/cOnQGDWjYnPeYDPj0uhsfT6oLNATbuPkr6Cn+YV74A4iP4hhkxjQxd/3o19NfkNLE2DGLZc"
    "QkzMrQCQ26Dpp8uERTA/eaEC3WSw7v39KlADLtOj5Zcdsa/Y9sg74lIQp1/hVMzrC/noD2hmZ/S7iwMg0wqqTl2fVkEr0rZYR9vT"
    "AVsUFNLvD1fIogzcS6FpouzHoTKMNNJCNCy8tEo1pDDcRrcBCGNliQlixQIEUxsZjWtX/EXZ8TZ10PwOcP1lL0FfOWRdUa86cW/8"
    "Mv3zKauh0Byz+fFTrSGfngRjMsowXLPrZ4CF9tkENXOkjJXq/MearFyXxcgf4ox7OKvzaBXsYWMXArNLO1ArOvBgk4rR2aRkKgqD"
    "6/mUq8zBAVTaNItt7u5zYFD1qdJ/xP8+t4w6PhS6ewNY6eBjQ/tJG3pZO6yjOqIfABqSgq7f9ve0knV/QdnHILAzgrIt/amc+9/k"
    "7BtiIkCTPVybgTquDlQzms+zRFw2WY3XjwMy+DI7odwxI7kEUMuYTYfZx3gKQ8JLdhrssj0I1JkrHHRw91LTIbu+PvT1Jt/mIfq3"
    "RF8MjvC137lHp0EJtqC0IUkSaEVYJ5EXB40dPNCb7Nv+ZZY5VmTm2tafm2o0hnV/J2orCcVjCq9KR+89Uuvfq7/EYYUDoLGdSYx8"
    "XGfmEjU7xaubAhAyeTpVhhimDc9D10CNUIIPA79osmA4FrfWYimURORblAsCvzJ4NExxpvDjNxDNssdj07U1n3Uuha5xPy6G3dXQ"
    "k9v98PdKjyhlob2KSFphem1wwJoxD53ejkdbOUpEnZSslEIN0RRFoUdvMJ1hPuxS1Ti5FOAw5ln4pkT0mRmUj7m7eKF/aSo15Yn/"
    "awAAZefQLVFfFgKw+BUOZCWzuypJiDhBz7K7IFiJDHKVwaDjgzMBtlMxugkL3omCHwAMWNpOEHsBAUKWrBlyWlj1i3uCkSx7BN1F"
    "LeRL0flfqgb41eqBdANx6IOCaAVlAz9EcIX89SK1CcPKcgNWamBz2aAdnRcbWDsyG7lOMrY+ntt08gYJJN4NnKWTcPIk5LUbktMa"
    "F0kygDpYzNpiqxiczAm2YZJZgA6pe5JipBZFiLrCZwaSta7A1fm8pjVoCuAYuWen5TiJXX4TcyLRuPRkVzQW09vMeobqxlYoULKl"
    "OTPXYnxPbNRNZKFeInubNVuamZ7T4zBAURLkp14rBC9wTP0QHUWHgkDhGUBNVXyPQo0/rB3l8alcfBYkS1nh/hHfclXEXQXLRrGv"
    "Av9fh3hpAS9wuh7lvoIDYPJvaRjyew9sOCD2oGjcoRaD4a0L4b62qlcl/EUVr7OXpALl4ejY4ploHUOsJhwhiUtoMIrtKbkvED63"
    "bQYAeVAmpjCnZ6zwDRbg9Mna8aoen/MNsmlghwk4uyLH+F+yOIdtgTOiQdVI2BFhFMTJZ4DmZ4uQFuYy3AT1c4PHXpIGuDXtoIKP"
    "/gLG3EMIkOTONpbIH/k3ge9wkegpWYur/7T4FKVqIcPSExoLh9WZV8xAS+0xiuh9Jf8vbguj0AcHmaFWwbUxT2qgYX89o3k8hrNo"
    "cTxpDRrN919qJejya87Viwe2dqO2Tfvp5im3U2y0MwgpH8P8Tl0bI382MGFOgFgaRuIGjbUOIvK5FLNiee0OA2dAvWMowtuaD1BU"
    "/H/v3x6AsOFRSAex4X/FYEd9FTj382f7xn5f+bPCiyDWJxtBebJBLw58I77/qmCtWTAXnvvSJrNCE+Seum4AEDoFmJUNkKsDUkz2"
    "KT+OQ0f9O6mjdYAngw0jXRgP7Hjl7vpvGRW39kg3cGHOIBDB1Pz4FMdrpF0K3ohLVy91kjpUxujThlrNUiaDdSFP9AF036H/hprx"
    "aBPI9Jt+B6gErkORHkFZf3tvcHOvjQrA8o6vFAMExL1WWlAgdRMWgXR7qYjO1texwUz8tpgOCqROFU29R4zcP5CKHLefj9sfrAAs"
    "y2pafBP8uy0DdEEbZ5ysrOiPReaEFVZ0DZCfYoHCN7zYTJy+2Cf+xs6ZzwEU6PjPXlSLA7PM4RZoIZVG9j/sIP6Y+jZ6TEkEELoy"
    "49K/hEMe17BSsA8XEXRoGFYIW5mQsMYFfgAi5Gn0FCD67P/sdyZSzb+L0SwVbbC0/7AUwSmetHkL6cc80EP+3rB4nIBLR+Bg3uDG"
    "+pSBltc4fU8S19teqiUdm/ZrIt8p+wh+ia2ZtRqbtQEPx+9UjA0F7PeJGZpZrJCnqjtA2TxFAzK6BkRcxKkp0mJMHRpejCZ+hMl3"
    "1lQVk1ugDk2l6QCNYyp+SzBfsBMiQgiFLONeBT+Q0k/xa2ksYgDzvhueKJqfDkjVi6AIt+2kQWTbe124yH+AEVL9VNchyyV5MA2F"
    "z/BuOHSif3uU4TiW9GEeu39pkb+ozsSv2xIDgJVzRE7rB3zzi+mB7AzqeAWt+qJtb0PYIDL3yRUMiif/gaTnTSrKOvcEDg2Y99xD"
    "ni6izSLCHCvT9srTMYuNpkVZNkPURgI2VkMIfcxZMsLhgJA/Agu7zVVpodLIHkLkaCIZREizoZo7giz5Tdn8tRiW2l/AJqP9AJ3c"
    "hWjoaUhgGJXA88aV2q5UVxjDr46wLY892x1xVVxQpBoQYSoYv/mNl0U8fTkxV9chLZcOKl2lvoQCmqg15ZsUqX+cTKeFB2grI9UX"
    "qLiglQP1Yy1taP8GmAHFHTtAntSJSYmqzculUqiesmhGOcBQ0E+DmC4tobc1DhBqUqpaNmGmZeBYrgkiFC/n7R+zmEI0pIk+MIuv"
    "cX9/7yBFIM7+So5WcWR3mvonI+Q02bbZoFgIV/LN9mfnSOrcnsCfBJrcmuW19PpdCM+qXRsnU7dfIrSblOIRfVoITTAoNn2tdjJF"
    "OJPDRpcz4LcQNtXMI6h8DSxPnTtOMC8w7+oqPP0GysrKfza4ffeelgHyKnRGH1bqBtiBNHuIBO/2KKjhXGAo07jU1uuAb8mN/YnD"
    "9IClAMJjOCDWfc2FgULqhUFpaydHufGmikkIkuM7WsA7ZPYs25S55QIYIxs985fhnd7ibl1IpX5X1GgjA/bYvxfVDO92htfy6zp2"
    "yXiLMIQH6KXw3FqoZuFY0hGzdQ4FA+oKJFXqHLOO7wAtzXBTAu3nFm4+6w0830YBLLT/m1dIO9LCCS6hlm+VkGvGaQbx0tipaUGM"
    "pVyaSbhwsUV1J42A5W0sSh8BsM/xDr9oIOcHOVCIhpZ0TB1/nMt939JlaQAXKyim0jg8e4I5uSwZKDAIG6o1DfDrvSAoY6bRa0zb"
    "sr20BoQrS4JcyAxbvx7NvXlyZKWYkgxp1mDDuyh4+YuorpvQVKAbrGPSDnjUXkqOwiu0ylET18abGjAbthSlb4OwWcXMmMSNfm0W"
    "Iew2xnY/AAwzlk16bnFBIZ9mmERdA4feml6BH6FNOL6TaYAP3dLZWrtj1PtEBK91NCAfGxrPZGAZshQ+5X2rwqpS4oTjT+SmSxQ5"
    "QIwOgh+8VmGGL78VPooMmkv8fl4OUNYNyixvwa8fNhsxLed5s1OoOYlffI8iNtZOo0ZQ+rARVvz0Glm+m7gQvp+rfoUiiko9PH3L"
    "fpkid9akg+ujCTYwSTmmLyUGSo4mATlygJkua4HHG3DcoWQI44M2pUkmxQB6/9e0T2CgdGVzSRZ7ZSjreP3dPMV/DTBrGcpc7VSY"
    "8fFlFPGvFSE5UVRjSmNS2wIlly6glczKDRyC+EJ47j7W/l3HFxibRAQhhC4ka1Lie4m4BoByj4Q5G7SZ+WJh2zligxBqAYJarPNi"
    "qZjsh1UamjRoLW/gbLnnz5FhMR/R/0N7UMn8eOM2jNqCypx+8sLBPEPjAyUBW8jU4D/bOXi03kmvA8ZhuVLcfRymAMKhOVSOymE0"
    "vWrdnwFmnLrmYhqNvPAfYJZFihSk1uZ/Nv1hHB92EBdW3gn+cYStOaXbBxuW9hnerhpLYvdvoQMZ0qOdHKYKujiT2P+99U71Rtge"
    "m1DvabdxUMeCKjhofLVqYAWjJ7bwGNOQTtwh+LD/SSPcKaYJq1hsTU7ENP1pBqpHYd1gD1NaaKR4qrYCxZsFp4OHS1MvZwL/1kPS"
    "DfZoLQJ2bjNdoDSm7OcbASWR6ZeXDIpvdt93M/qXL6eBo/RoDJ8BYIgzpYfzzAZdV383AIjFS0ltxA8TpTu8asukZ5COTwVwelXA"
    "y3to8EVJb7UAh6GIDQnz5LZFMTHkPv8xfZg1ws/iGKFrizv/3SxHszNtqHf8VSxCGdBUMXDQN4AMjWG6hPyMDKNm2HWAEcx7avr5"
    "AD0A/JAlapsq7957ugHGfrTy5h9g43P3skwYqU0WYNk/xYD/Vfqg9l/SUqxQXbjwT4ENRZuIbbeewA1gp95HzGvOyvzNVwAWwZOy"
    "bs9yeMR2HSpSjEqXU9R+VBxks9n9foNJuDdej67IofZMSeIYrThMwPqODG+nyJ+LhCIOdPymOooboQaq4Re4LERpY/rFRnYLmCJL"
    "mV+878s1Aio8aZk4HMwBNoFhknl2OCO1jtQBiJnv/aZKOU6bAdYm5LK8WKYDgJpvxaf0gLbK37PBIrKB65gNH67HOPQ9xAHUfNlY"
    "1KIQFDOa6S/SyGQL/g8SQUllHNC2216dKXCJT/GfY1dGSBtU/cW3bT6JP8ehCcZ3W/j8nxU1A6V/i1UAblFvZapacSrVs4FTV4WY"
    "qhBRi0J4uQIhM4KFP23WAidvklTxdUimot2MmTvwBQWwoMuqmslxP0q1zA+K0azMgGTMyMTGpZUwmL1lW8jANcmC4wcgi26yoLNR"
    "G/D1hoUr37+kY9jt/qTxsUWzEhIC0PAGOdWhP5JRKTC2b/StCzXT1cdMDvA+fkCH++icSf8IUxEbNS5S+I9mt7cjNNwQMx3ivLfr"
    "yvkhTCVlF6mZmEnGDl9MQLUy9tqyaRwQQuIkAAIaaKbEn3GOlWTIAnJM3/FtkuSxq3KpR6NREjhqMs7YH5nhCCuaSvQS2mEYsW9x"
    "HEActodCDLCx4jA7riI34IiCKkTWrT8Q5BNFm+cw9AvHBtN+v21gh6DYKM5JtW/jaKASg7RYFAPVMAugJcJDGIsc4J51wBBv0XMf"
    "lfOHDdwQzQtO0I8jzURRAxJmERrRiCbT1T0795Pc678YDpdARzwIqweGkq0LHIQwo2arENUljTccstraN46LOPUS7I4VKYpMNAp6"
    "rKc/bxdR4qulpqE4DDy5F44ftKrVhWe5jRfQQhH/RGi5YhQSOjziyHSj4z7+eKYyYGmgoXlh/telC09nm/9lmbfuMx1wSOESfoAp"
    "kZzcbDKuPJEYPkkGF0+ZxqXZ4zuWFMGRYE5cElLULpxWhtBNz5ONrYewSVGl+xfk8mW1vAQUXAWEc1kCuR4DQaW5Ir97xnZ5ikgb"
    "Wv0MPeCBYJnsC7BBZQ9UCC1mLjDCkMHBhVyswQ/eI1KCNfYAIQJRUHuK7nJCc57y/wB7DYaOGZ3+m9w3dGTFhVq4oICrhB9YLIQS"
    "gDcbWo53QjhR8xwq/nSlWtTlW+z3j/stMZGLgCdRtESDvlLx30KZ/l4iQifgrx6cgdQQ49m0A6/1pvDZz7Pa+DNtIM+d7HW7qthw"
    "POTvXVKjmPm40+1rIUDdaPWpfww1UrHVHLq7iJUYmtqzacjh04THpgAG90xTWNaeCT5Un9gHdgtAypkaMxVZz4gTkDKbcXKKvBRA"
    "PM2qbswbfUPxbO4yjM94SJamNon7z0IAsOsfLadXdAsTY6Og/3wWIAAVkA5C8rkf0EuuhnGDa4Igfx5dpGjX0cHjBxrNMLXSBakt"
    "DwrGKZjdhy5IenXckKo3sqcCDKX/FPBbzfCzjOfYbElhC3LDTS8HxY0ADE4ejbJkCizRJ/A4BDtwjthuqZMJVNTKjmn0bX05I05n"
    "1CsGKaLUHPm6XwVfSQT6/Ctwr9Sj1swZqfH6OasSmUwXBKDEo6oAmWEM+ZsdtT3ewxIxvcmNhtEALkqBx6/YsgI2rtk81d9SfAoq"
    "grduCzUspREQ+/p9+uSkO7/UzAtIx8+zGXJ9aunMV8KQ0iLFnKAfadNsn51Hi7/WgSLn2BhZGyaZJqQTx+JjqY5ZQkA0C2woDdul"
    "xRsm6+FwtuyXToK1TK/KHwOnAV37+LwOIL7r6mHR1jbQpp6XEeDS1KL3v7n55TEbyD1GMCoW16haCxJ/K+sQ9F4ZvIXcSXg/6P8B"
    "yCu2hw=="
)


# version 13 (final)
# speedup vs baseline: 2.5868x; 1.1742x over previous
import base64
import contextlib
import sys
import zlib

import numpy as np

try:
    import concourse.bass as bass
except ImportError:
    sys.path.insert(0, "/opt/trn_rl_repo")
    import concourse.bass as bass
import concourse.mybir as mybir
from concourse.bass_isa import ReduceOp
from concourse.bass_utils import run_bass_kernel_spmd

F32 = mybir.dt.float32
BF16 = mybir.dt.bfloat16
F16 = mybir.dt.float16
AFT = mybir.ActivationFunctionType
ALU = mybir.AluOpType
AX = mybir.AxisListType

B, S, H = 4, 2048, 2048
NH, NKV, HD = 16, 4, 128
DH = 5632
TQ = 1024
NTC = 16
NQT = 8
NHC = 16
NHB = 22
NDB = 8
EPS_LN = 1.0e-5
EPS_Q = 1.0e-5
SCAL = float(HD) ** -0.5
EXP_BIAS = -4.0


def _split3(x):
    c1 = np.float32(np.ldexp(np.floor(np.ldexp(x, 14)), -14))
    r = x - float(c1)
    c2 = np.float32(np.ldexp(np.floor(np.ldexp(r, 28)), -28))
    c3 = np.float32(r - float(c2))
    return float(c1), float(c2), float(c3)


SC_HI = float(np.float32(SCAL))
SC_LO = float(np.float32(SCAL - float(np.float32(SCAL))))
NELI = 1.0 / float(H * DH)
NELI_HI = float(np.float32(NELI))
NELI_LO = float(np.float32(NELI - NELI_HI))
I127 = 1.0 / 127.0
I127_HI = float(np.float32(I127))
I127_LO = float(np.float32(I127 - I127_HI))

ARENA = 212288
O_TB, O_FR, O_KK, O_U0, O_B0 = 0, 8192, 16384, 24576, 32768
O_XTH = 0
O_WQT0, O_WQT1 = 32768, 49152
O_WKT, O_WVT = 65536, 81920
O_KT, O_VV = 98304, 114688
O_QT = 131072
O_AOUT = 0
O_WOT = 65536
O_XRES0 = 32768
O_Y1T0 = 49152
O_XINTT = 163840
O_XQT = 131072
O_HH = 0
O_WUP = 98304
O_T1 = 98304
O_T2 = 0
O_W2P0, O_W2P1 = 45056, 67584
O_W2S0, O_W2S1 = 143360, 154624
O_TBL = 163840
O_Y1N = 167936
O_MST0, O_MST1 = 186368, 190464
O_USCR, O_BSCR, O_TMP1 = 200704, 203520, 206336
O_KRAW, O_KROT = 200704, 201728
O_TMPK, O_TMPK2 = 202752, 204800
O_RHI = 206848
O_ST4 = 203520
O_WST = 90112
O_TSQ = 90112
O_OUTT0, O_OUTT1 = 0, 8192
O_COLS = 209152
O_ONESB = 209536
O_ONESR = 209600
O_P64 = 209888
O_PT0, O_PT1 = 210176, 211200

C_M1, C_M2, C_M3 = 0, 1, 2
C_TH1P, C_TH1N, C_TH2P, C_TH2N, C_TH3P, C_TH3N = 3, 4, 5, 6, 7, 8
C_IVH, C_IVL = 9, 10
C_SMAX, C_C1, C_C3, C_CH, C_C2 = 16, 24, 32, 40, 48
C_LS, C_LQ = 56, 64
C_T0, C_T1c, C_T2c, C_T3c, C_T4c = 72, 73, 74, 75, 76
C_LM, C_LV, C_LR, C_LR2 = 86, 87, 88, 89
C_MACC = 80
C_EB = 84


class Sched:
    ENGS = ("sync", "gpsimd", "vector", "scalar", "tensor")

    def __init__(self):
        self.ops = {e: [] for e in self.ENGS}
        self.val = {}
        self.cuts = []

    def sem(self, name):
        self.val.setdefault(name, 0)

    def wait(self, eng, name, val=None):
        v = self.val[name] if val is None else val
        if v > 0:
            self.ops[eng].append(("w", name, v))

    def do(self, eng, fn, inc=None, amt=1):
        if inc is not None:
            self.val[inc] += amt
            self.ops[eng].append(("i", fn, inc, amt))
        else:
            self.ops[eng].append(("i", fn, None, 0))

    def cutpoint(self):
        self.cuts.append({e: len(self.ops[e]) for e in self.ENGS})

    def barrier(self):
        self.cutpoint()
        marks = {}
        for e in self.ENGS:
            s = "bar_" + e
            self.val[s] += 1
            self.ops[e].append(("i", (lambda ee: ee.nop()), s, 1))
            marks[e] = self.val[s]
        for e in self.ENGS:
            for o in self.ENGS:
                if o != e:
                    self.ops[e].append(("w", "bar_" + o, marks[o]))


def build_program(cut=None):
    nc = bass.Bass("TRN2", target_bir_lowering=False, debug=False,
                   detect_race_conditions=False)

    def din(name, shape, dt=F32):
        return nc.dram_tensor(name, shape, dt, kind="ExternalInput").ap()

    xT = din("xT", [H, S])
    xres = din("xres", [TQ, H])
    rope = din("rope", [128, 4096])
    wqT = din("wqT", [H, H])
    wkT = din("wkT", [H, 512])
    wvT = din("wvT", [H, 512])
    woT = din("woT", [H, H])
    w1T = din("w1T", [H, DH])
    w3T = din("w3T", [H, DH])
    w2T = din("w2T", [DH, H])
    p64 = din("p64", [128, 128])
    ones32 = din("ones32", [128, 32])
    onesr = din("onesr", [1, 128])
    out = nc.dram_tensor("out", [TQ, H], F32, kind="ExternalOutput").ap()

    es = contextlib.ExitStack()
    arena = es.enter_context(nc.sbuf_tensor("arena", [128, ARENA], mybir.dt.uint8))
    base = nc.lookup_mloc(arena).addr

    def vt(name, shape, dtype, off):
        h = nc.virtual_sbuf_tensor(name, shape, dtype)
        nc._tensor(h.name, h.shape, h.dtype, type="SB")
        mloc = nc.lookup_mloc(h)
        mloc.allocated = True
        mloc.addr = base + off
        return h

    TB = vt("TB", [128, 2048], F32, O_TB)
    FR = vt("FR", [128, 2048], F32, O_FR)
    KKt = vt("KKt", [128, 2048], F32, O_KK)
    U0 = vt("U0", [128, 2048], F32, O_U0)
    B0 = vt("B0", [128, 2048], F32, O_B0)
    XTH = vt("XTH", [128, 16384], BF16, O_XTH)
    WQT = [vt("WQT0", [128, 8192], BF16, O_WQT0), vt("WQT1", [128, 8192], BF16, O_WQT1)]
    WKT = vt("WKT", [128, 8192], BF16, O_WKT)
    WVT = vt("WVT", [128, 8192], BF16, O_WVT)
    KT = vt("KT", [128, 8192], BF16, O_KT)
    VV = vt("VV", [128, 8192], BF16, O_VV)
    QT = vt("QT", [128, 16384], BF16, O_QT)
    AOUT = vt("AOUT", [128, 16384], BF16, O_AOUT)
    WOT = vt("WOT", [128, 32768], BF16, O_WOT)
    XRS = [vt("XRS0", [128, 2048], F32, O_XRES0), vt("XRS1", [128, 2048], F32, O_XRES0 + 8192)]
    Y1T = [vt("Y1Ta", [128, 2048], F32, O_Y1T0), vt("Y1Tb", [128, 2048], F32, O_Y1T0 + 8192)]
    XINTT = vt("XINTT", [128, 2048], BF16, O_XINTT)
    XQT = vt("XQT", [128, 16384], BF16, O_XQT)
    HHt = vt("HHt", [128, 45056], F16, O_HH)
    WUP = vt("WUP", [128, 16384], BF16, O_WUP)
    T1 = vt("T1", [128, 22528], F16, O_T1)
    T2 = vt("T2", [128, 22528], F16, O_T2)
    W2P = [vt("W2P0", [128, 11264], F16, O_W2P0), vt("W2P1", [128, 11264], F16, O_W2P1)]
    W2S = [vt("W2S0", [128, 2816], F32, O_W2S0), vt("W2S1", [128, 2816], F32, O_W2S1)]
    TBL = vt("TBLS", [128, 8192], F16, O_TBL)
    Y1N = [vt(f"Y1N{t}", [128, 2048], BF16, O_Y1N + 4096 * t) for t in range(8)]
    MST = [vt("MST0", [128, 1024], F32, O_MST0), vt("MST1", [128, 1024], F32, O_MST1)]
    USCR = vt("USCR", [128, 704], F32, O_USCR)
    BSCR = vt("BSCR", [128, 704], F32, O_BSCR)
    TMP1 = vt("TMP1", [128, 704], F32, O_TMP1)
    KRAW = vt("KRAW", [128, 512], BF16, O_KRAW)
    KROT = vt("KROT", [128, 512], BF16, O_KROT)
    TMPK = vt("TMPK", [128, 512], F32, O_TMPK)
    TMPK2 = vt("TMPK2", [128, 512], F32, O_TMPK2)
    RF = vt("RF", [1, 512], F32, O_TMPK)
    RHI = vt("RHI", [1, 512], BF16, O_RHI)
    RLO = vt("RLO", [1, 512], BF16, O_RHI + 1024)
    ST4 = [vt(f"ST4_{i}", [128, 256], F32, O_ST4 + 1024 * i) for i in range(4)]
    WST = [vt(f"WST{i}", [128, 512], F32, O_WST + 2048 * i) for i in range(4)]
    TSQ = vt("TSQ", [128, 2048], F32, O_TSQ)
    OUTT = [vt("OUTT0", [128, 2048], F32, O_OUTT0), vt("OUTT1", [128, 2048], F32, O_OUTT1)]
    COLS = vt("COLS", [128, 96], F32, O_COLS)
    ONESB = vt("ONESB", [128, 32], BF16, O_ONESB)
    ONESR = vt("ONESR", [1, 128], BF16, O_ONESR)
    P64S = vt("P64S", [128, 128], BF16, O_P64)
    PT = [vt("PTa", [128, 512], BF16, O_PT0), vt("PTb", [128, 512], BF16, O_PT1)]

    pb = [es.enter_context(nc.psum_tensor(f"pb{i}", [128, 512], F32)) for i in range(8)]

    sems = {}
    sem_names = (["c0", "m0", "m1", "x0", "x1", "tX", "tH", "u0", "u1", "u2", "u3",
                  "s0", "s1", "g0", "g1", "gq0", "gq1", "gw1", "gw2", "gc",
                  "a0", "a1", "eT", "eV", "eC", "eG"] + ["bar_" + e for e in Sched.ENGS])
    for s in sem_names:
        sems[s] = es.enter_context(nc.semaphore(s))

    sch = Sched()
    for s in sem_names:
        sch.sem(s)

    def C(i):
        return COLS[:, i:i + 1]

    def vsync():
        sch.wait("vector", "eV", sch.val["eV"])

    def vsmall(fn):
        sch.do("vector", fn, "eV")
        vsync()

    sch.do("sync", lambda e: e.dma_start(TB[:, :], rope[:, 0:2048]), "c0", 16)
    sch.do("sync", lambda e: e.dma_start(FR[:, :], rope[:, 2048:4096]), "c0", 16)
    sch.do("gpsimd", lambda e: e.dma_start(ONESB[:, :], ones32), "gc", 16)
    sch.do("gpsimd", lambda e: e.dma_start(ONESR[:, :], onesr), "gc", 16)
    sch.do("gpsimd", lambda e: e.dma_start(P64S[:, :], p64), "gc", 16)
    mk_cl = sch.val["c0"]

    sch.cutpoint()
    sch.wait("vector", "c0", mk_cl)
    sch.do("vector", lambda e: e.tensor_copy(TBL[:, 0:2048], TB[:, :]))
    sch.do("vector", lambda e: e.tensor_copy(TBL[:, 2048:4096], FR[:, :]))
    sch.do("vector", lambda e: e.tensor_scalar(U0[:, :], TB[:, :], SC_HI, None, ALU.mult))
    sch.do("vector", lambda e: e.scalar_tensor_tensor(U0[:, :], TB[:, :], SC_LO, U0[:, :], ALU.mult, ALU.add))
    sch.do("vector", lambda e: e.tensor_copy(TBL[:, 4096:6144], U0[:, :]))
    sch.do("vector", lambda e: e.tensor_scalar(U0[:, :], FR[:, :], SC_HI, None, ALU.mult))
    sch.do("vector", lambda e: e.scalar_tensor_tensor(U0[:, :], FR[:, :], SC_LO, U0[:, :], ALU.mult, ALU.add))
    sch.do("vector", lambda e: e.tensor_copy(TBL[:, 6144:8192], U0[:, :]), "eV")
    mk_tbl = sch.val["eV"]
    sch.cutpoint()

    sch.do("vector", lambda e: e.memset(COLS[:, C_MACC:C_MACC + 3], 0.0))
    sch.do("vector", lambda e: e.memset(COLS[:, C_EB:C_EB + 1], EXP_BIAS), "eV")
    mean_marks = {}
    mean_red = {}
    mean_jobs = []
    for mat_ap, rows, idx, w in [(w1T, H, 0, 704), (w3T, H, 1, 704), (w2T, DH, 2, 1024)]:
        cols = mat_ap.shape[1]
        assert cols % w == 0
        for rc in range(rows // 128):
            for cc in range(cols // w):
                mean_jobs.append((mat_ap[rc * 128:(rc + 1) * 128, cc * w:(cc + 1) * w], idx, w))
    mean_state = {"next": 0, "glob": 0, "idx": {}}

    def pump_mean(k):
        for _ in range(k):
            if mean_state["next"] >= len(mean_jobs):
                return
            src, idx, w = mean_jobs[mean_state["next"]]
            mean_state["next"] += 1
            g = mean_state["glob"]
            par = g % 2
            if g >= 2:
                sch.wait("sync", "eV", mean_marks[g - 2])
            ms = "m%d" % par
            sch.do("sync", (lambda d, s_: lambda e: e.dma_start(d, s_))(MST[par][:, 0:w], src), ms, 16)
            mk = sch.val[ms]
            sch.wait("vector", ms, mk)
            sch.do("vector", (lambda p, w_: lambda e: e.reduce_sum(C(C_T0 + p), MST[p][:, 0:w_], AX.X, apply_absolute_value=True))(par, w), "eV")
            mean_red[g] = sch.val["eV"]
            if g >= 1:
                pidx = mean_state["idx"][g - 1]
                sch.wait("vector", "eV", mean_red[g - 1])
                sch.do("vector", (lambda p, a: lambda e: e.tensor_tensor(C(a), C(a), C(C_T0 + p), ALU.add))((g - 1) % 2, C_MACC + pidx), "eV")
            else:
                sch.do("vector", lambda e: e.engine_nop(), "eV")
            mean_marks[g] = sch.val["eV"]
            mean_state["idx"][g] = idx
            mean_state["glob"] += 1

    sch.do("gpsimd", lambda e: e.dma_start(WKT[:, :].rearrange("p (hc o) -> p hc o", hc=16), wkT.rearrange("(hc p) o -> p hc o", p=128)), "g0", 16)
    sch.do("gpsimd", lambda e: e.dma_start(WVT[:, :].rearrange("p (hc o) -> p hc o", hc=16), wvT.rearrange("(hc p) o -> p hc o", p=128)), "g0", 16)
    mk_wkv = sch.val["g0"]
    sch.wait("gpsimd", "eV", mk_tbl)
    sch.do("gpsimd", lambda e: e.dma_start(XTH[:, :].rearrange("p (hc t) -> p hc t", hc=16), xT[:, 0:1024].rearrange("(hc p) t -> p hc t", p=128)), "g1", 16)
    mk_xth0 = sch.val["g1"]
    sch.cutpoint()

    ev = {}

    def rope_evac(psum_bank, dst_ap, tok0, scaled):
        sch.do("vector", (lambda b: lambda e: e.tensor_copy(KRAW[:, :], pb[b][:, :]))(psum_bank), "eV")
        ev[("pb", psum_bank)] = sch.val["eV"]
        mk_raw = sch.val["eV"]
        sch.wait("tensor", "eV", mk_raw)
        sch.wait("tensor", "eV", ev.get("rotfree", 0))
        sch.do("tensor", lambda e: e.matmul(pb[2][:, :], P64S[:, :], KRAW[:, :], start=True, stop=True), "eT")
        mk_rot = sch.val["eT"]
        co = (6144 if scaled else 2048) + tok0
        so = (4096 if scaled else 0) + tok0
        sch.wait("vector", "eT", mk_rot)
        sch.do("vector", (lambda o: lambda e: e.tensor_tensor(TMPK[:, :], KRAW[:, :], TBL[:, o:o + 512], ALU.mult))(co))
        sch.do("vector", (lambda o: lambda e: e.tensor_tensor(TMPK2[:, :], pb[2][:, :], TBL[:, o:o + 512], ALU.mult))(so))
        sch.do("vector", (lambda d: lambda e: e.tensor_tensor(d, TMPK[:, :], TMPK2[:, :], ALU.add))(dst_ap), "eV")
        ev["rotfree"] = sch.val["eV"]
        return sch.val["eV"]

    def kv_proj_half(half, xmark):
        for g in range(NKV):
            for ts in range(2):
                pump_mean(2)
                bank = (g * 2 + ts) % 2
                tok0 = half * 1024 + ts * 512
                sch.wait("tensor", "eV", ev.get(("pb", bank), 0))
                sch.wait("tensor", "g0", mk_wkv)
                sch.wait("tensor", "g1", xmark)
                for hc in range(NHC):
                    sch.do("tensor", (lambda g_, hc_, ts_, b_: lambda e: e.matmul(
                        pb[b_][:, :], WKT[:, hc_ * 512 + g_ * 128:hc_ * 512 + g_ * 128 + 128],
                        XTH[:, hc_ * 1024 + ts_ * 512:hc_ * 1024 + ts_ * 512 + 512],
                        start=(hc_ == 0), stop=(hc_ == NHC - 1)))(g, hc, ts, bank),
                        "eT" if hc == NHC - 1 else None)
                sch.wait("vector", "eT", sch.val["eT"])
                rope_evac(bank, KT[:, g * 2048 + tok0:g * 2048 + tok0 + 512], tok0, True)
        for tcr in range(8):
            pump_mean(2)
            tc = half * 8 + tcr
            bank = tcr % 2
            sch.wait("tensor", "eV", ev.get(("pb", bank), 0))
            for hc in range(NHC):
                sch.do("tensor", (lambda tcr_, hc_, b_: lambda e: e.matmul(
                    pb[b_][:, :], XTH[:, hc_ * 1024 + tcr_ * 128:hc_ * 1024 + tcr_ * 128 + 128],
                    WVT[:, hc_ * 512:hc_ * 512 + 512],
                    start=(hc_ == 0), stop=(hc_ == NHC - 1)))(tcr, hc, bank),
                    "eT" if hc == NHC - 1 else None)
            sch.wait("vector", "eT", sch.val["eT"])
            sch.do("vector", (lambda tc_, b_: lambda e: e.tensor_copy(VV[:, tc_ * 512:tc_ * 512 + 512], pb[b_][:, :]))(tc, bank), "eV")
            ev[("pb", bank)] = sch.val["eV"]

    kv_proj_half(0, mk_xth0)
    sch.cutpoint()

    wq_done = {}
    for ob in range(4):
        sch.cutpoint()
        par = ob % 2
        if ob == 0:
            sch.wait("gpsimd", "eV", mk_tbl)
        if ob >= 2:
            sch.wait("gpsimd", "eT", wq_done.get(ob - 2, 0))
        sch.do("gpsimd", (lambda o: lambda e: e.dma_start(
            WQT[o % 2][:, :].rearrange("p (hc o) -> p hc o", hc=16), wqT[:, o * 512:(o + 1) * 512].rearrange("(hc p) o -> p hc o", p=128)))(ob), "gq%d" % (ob % 2), 16)
        mk_wq = sch.val["gq%d" % (ob % 2)]
        for oc2 in range(4):
            n = ob * 4 + oc2
            for ts in range(2):
                pump_mean(2)
                bank = (oc2 * 2 + ts) % 2
                sch.wait("tensor", "eV", ev.get(("pb", bank), 0))
                sch.wait("tensor", "gq%d" % (ob % 2), mk_wq)
                for hc in range(NHC):
                    sch.do("tensor", (lambda p_, oc_, hc_, ts_, b_: lambda e: e.matmul(
                        pb[b_][:, :], WQT[p_][:, hc_ * 512 + oc_ * 128:hc_ * 512 + oc_ * 128 + 128],
                        XTH[:, hc_ * 1024 + ts_ * 512:hc_ * 1024 + ts_ * 512 + 512],
                        start=(hc_ == 0), stop=(hc_ == NHC - 1)))(par, oc2, hc, ts, bank),
                        "eT" if hc == NHC - 1 else None)
                if oc2 == 3 and ts == 1:
                    wq_done[ob] = sch.val["eT"]
                sch.wait("vector", "eT", sch.val["eT"])
                rope_evac(bank, QT[:, n * 1024 + ts * 512:n * 1024 + ts * 512 + 512], ts * 512, False)

    sch.cutpoint()
    sch.wait("gpsimd", "eT", sch.val["eT"])
    sch.do("gpsimd", lambda e: e.dma_start(XTH[:, :].rearrange("p (hc t) -> p hc t", hc=16), xT[:, 1024:2048].rearrange("(hc p) t -> p hc t", p=128)), "g1", 16)
    kv_proj_half(1, sch.val["g1"])

    sch.barrier()

    sch.do("gpsimd", lambda e: e.dma_start(
        WOT[:, 0:16384].rearrange("p (hc o) -> p hc o", hc=16), woT[:, 0:1024].rearrange("(hc p) o -> p hc o", p=128)), "gw1", 16)

    pt_exp = {}
    pt_pv = {}
    pv_free = 0
    rb_free = 0
    for n in range(NH):
        g = n // 4
        for qh in range(2):
            pump_mean(6)
            sch.wait("tensor", "eV", pv_free)
            for kc in range(NTC):
                sbank = kc % 2
                ptb = kc % 2
                sch.wait("tensor", "eC", pt_exp.get(sbank, 0))
                sch.do("tensor", (lambda g_, kc_, n_, qh_, b_: lambda e: e.matmul(
                    pb[b_][:, :], KT[:, g_ * 2048 + kc_ * 128:g_ * 2048 + kc_ * 128 + 128],
                    QT[:, n_ * 1024 + qh_ * 512:n_ * 1024 + qh_ * 512 + 512],
                    start=True, stop=True))(g, kc, n, qh, sbank), "eT")
                mk_s = sch.val["eT"]
                sch.wait("scalar", "eT", mk_s)
                sch.wait("scalar", "eT", pt_pv.get(ptb, 0))
                sch.do("scalar", (lambda b_, p_: lambda e: e.activation(
                    PT[p_][:, :], pb[b_][:, :], AFT.Exp, bias=C(C_EB), scale=1.0))(sbank, ptb), "eC")
                pt_exp[sbank] = sch.val["eC"]
                mk_exp = sch.val["eC"]
                sch.wait("tensor", "eC", mk_exp)
                sch.do("tensor", (lambda kc_, g_, p_: lambda e: e.matmul(
                    pb[2][:, :], VV[:, kc_ * 512 + g_ * 128:kc_ * 512 + g_ * 128 + 128],
                    PT[p_][:, :], start=(kc_ == 0), stop=(kc_ == NTC - 1)))(kc, g, ptb),
                    None)
                sch.do("tensor", (lambda kc_, p_: lambda e: e.matmul(
                    pb[3][0:1, :], ONESB[:, 0:1], PT[p_][:, :],
                    start=(kc_ == 0), stop=(kc_ == NTC - 1)))(kc, ptb),
                    "eT")
                pt_pv[ptb] = sch.val["eT"]
            mk_pv = sch.val["eT"]
            sch.wait("vector", "eT", mk_pv)
            sch.do("vector", lambda e: e.reciprocal(RF[:, :], pb[3][0:1, :]))
            sch.do("vector", lambda e: e.tensor_copy(RHI[:, :], RF[:, :]))
            sch.do("vector", lambda e: e.tensor_tensor(RLO[:, :], RF[:, :], RHI[:, :], ALU.subtract))
            sch.do("vector", lambda e: e.tensor_copy(TMPK2[:, :], pb[2][:, :]), "eV")
            mk_rl = sch.val["eV"]
            pv_free = mk_rl
            sch.wait("tensor", "eV", mk_rl)
            sch.wait("tensor", "eV", rb_free)
            sch.do("tensor", lambda e: e.matmul(pb[4][:, :], ONESR[:, :], RHI[:, :], start=True, stop=False))
            sch.do("tensor", lambda e: e.matmul(pb[4][:, :], ONESR[:, :], RLO[:, :], start=False, stop=True), "eT")
            mk_rb = sch.val["eT"]
            sch.wait("vector", "eT", mk_rb)
            sch.do("vector", (lambda n_, qh_: lambda e: e.tensor_tensor(
                AOUT[:, n_ * 1024 + qh_ * 512:n_ * 1024 + qh_ * 512 + 512],
                TMPK2[:, :], pb[4][:, :], ALU.mult))(n, qh), "eV")
            rb_free = sch.val["eV"]
    mk_attn = sch.val["eV"]

    sch.wait("gpsimd", "eV", mk_attn)
    sch.do("gpsimd", lambda e: e.dma_start(
        WOT[:, 16384:32768].rearrange("p (hc o) -> p hc o", hc=16), woT[:, 1024:2048].rearrange("(hc p) o -> p hc o", p=128)), "gw2", 16)
    mk_wot = sch.val["gw2"]

    pump_mean(len(mean_jobs))
    g_last = mean_state["glob"] - 1
    sch.wait("vector", "eV", mean_red[g_last])
    vsmall((lambda p, a: lambda e: e.tensor_tensor(C(a), C(a), C(C_T0 + p), ALU.add))(g_last % 2, C_MACC + mean_state["idx"][g_last]))
    NEL = float(H * DH)
    vsmall(lambda e: e.memset(KROT[:, 2:3], 0.0))
    vsmall(lambda e: e.memset(KROT[0:1, 2:3], 1.0))
    for i in range(3):
        vsmall((lambda s: lambda e: e.tensor_copy(KRAW[:, 0:1], C(C_MACC + s)))(i))
        sch.wait("tensor", "eV", sch.val["eV"])
        sch.do("tensor", lambda e: e.matmul(pb[5][0:1, 1:2], KROT[:, 2:3], KRAW[:, 0:1], start=True, stop=True), "eT")
        sch.wait("vector", "eT", sch.val["eT"])
        vsmall(lambda e: e.tensor_copy(KROT[0:1, 3:4], pb[5][0:1, 1:2]))
        sch.wait("tensor", "eV", sch.val["eV"])
        sch.do("tensor", lambda e: e.matmul(pb[6][:, 1:2], ONESR[:, :], KROT[0:1, 3:4], start=True, stop=True), "eT")
        sch.wait("vector", "eT", sch.val["eT"])
        vsmall(lambda e: e.tensor_copy(KRAW[:, 2:3], pb[6][:, 1:2]))
        vsmall((lambda s: lambda e: e.tensor_tensor(TMPK2[:, 0:1], C(C_MACC + s), KRAW[:, 2:3], ALU.subtract))(i))
        vsmall(lambda e: e.tensor_copy(KROT[:, 0:1], TMPK2[:, 0:1]))
        vsmall(lambda e: e.tensor_tensor(TMPK[:, 0:1], TMPK2[:, 0:1], KROT[:, 0:1], ALU.subtract))
        vsmall(lambda e: e.tensor_copy(KRAW[:, 1:2], TMPK[:, 0:1]))
        sch.wait("tensor", "eV", sch.val["eV"])
        sch.do("tensor", lambda e: e.matmul(pb[5][0:1, 0:1], ONESB[:, 0:1], KRAW[:, 2:3], start=True, stop=False))
        sch.do("tensor", lambda e: e.matmul(pb[5][0:1, 0:1], ONESB[:, 0:1], KROT[:, 0:1], start=False, stop=False))
        sch.do("tensor", lambda e: e.matmul(pb[5][0:1, 0:1], ONESB[:, 0:1], KRAW[:, 1:2], start=False, stop=True), "eT")
        sch.wait("vector", "eT", sch.val["eT"])
        vsmall(lambda e: e.tensor_scalar(TMPK2[0:1, 3:4], pb[5][0:1, 0:1], NELI_HI, None, ALU.mult))
        vsmall(lambda e: e.scalar_tensor_tensor(RF[0:1, 0:1], pb[5][0:1, 0:1], NELI_LO, TMPK2[0:1, 3:4], ALU.mult, ALU.add))
        vsmall(lambda e: e.tensor_scalar(RF[0:1, 0:1], RF[0:1, 0:1], 1.0, EPS_Q, ALU.mult, ALU.max))
        vsmall(lambda e: e.tensor_copy(RHI[0:1, 0:1], RF[0:1, 0:1]))
        vsmall(lambda e: e.tensor_tensor(TMPK2[0:1, 1:2], RF[0:1, 0:1], RHI[0:1, 0:1], ALU.subtract))
        vsmall(lambda e: e.tensor_copy(RLO[0:1, 0:1], TMPK2[0:1, 1:2]))
        vsmall(lambda e: e.tensor_tensor(TMPK[0:1, 1:2], TMPK2[0:1, 1:2], RLO[0:1, 0:1], ALU.subtract))
        vsmall(lambda e: e.tensor_copy(RLO[0:1, 1:2], TMPK[0:1, 1:2]))
        sch.wait("tensor", "eV", sch.val["eV"])
        sch.do("tensor", lambda e: e.matmul(pb[6][:, 0:1], ONESR[:, :], RHI[0:1, 0:1], start=True, stop=False))
        sch.do("tensor", lambda e: e.matmul(pb[6][:, 0:1], ONESR[:, :], RLO[0:1, 0:1], start=False, stop=False))
        sch.do("tensor", lambda e: e.matmul(pb[6][:, 0:1], ONESR[:, :], RLO[0:1, 1:2], start=False, stop=True), "eT")
        sch.wait("vector", "eT", sch.val["eT"])
        vsmall((lambda s: lambda e: e.tensor_copy(C(C_M1 + s), pb[6][:, 0:1]))(i))
    for i in range(3):
        vsmall((lambda s: lambda e: e.tensor_scalar(C(C_TH1P + 2 * s), C(C_M1 + s), 0.5, None, ALU.mult))(i))
        vsmall((lambda s: lambda e: e.tensor_scalar(C(C_TH1N + 2 * s), C(C_M1 + s), -0.5, None, ALU.mult))(i))
    mk_th = sch.val["eV"]

    sch.barrier()

    xqt_mk = {}
    xint_mk = {}
    sq_mk = {}
    for t in range(NQT):
        sch.cutpoint()
        par = t % 2
        if t >= 2:
            sch.wait("sync", "eV", xint_mk.get(t - 2, 0))
        xs_ = "x%d" % (t % 2)
        sch.do("sync", (lambda t_: lambda e: e.dma_start(XRS[t_ % 2][:, :], xres[t_ * 128:(t_ + 1) * 128, :]))(t), xs_, 16)
        mk_xr = sch.val[xs_]
        for ob in range(4):
            bank = ob % 2
            sch.wait("tensor", "eV", ev.get(("yb", bank), 0))
            sch.wait("tensor", "gw1", 16)
            if ob >= 2:
                sch.wait("tensor", "gw2", mk_wot)
            for hc in range(NHC):
                wo_off = (ob // 2) * 16384 + (ob % 2) * 512
                sch.do("tensor", (lambda t_, wo_, hc_, b_: lambda e: e.matmul(
                    pb[b_][:, :], AOUT[:, hc_ * 1024 + t_ * 128:hc_ * 1024 + t_ * 128 + 128],
                    WOT[:, wo_ + hc_ * 1024:wo_ + hc_ * 1024 + 512],
                    start=(hc_ == 0), stop=(hc_ == NHC - 1)))(t, wo_off, hc, bank),
                    "eT" if hc == NHC - 1 else None)
            mk_y = sch.val["eT"]
            sch.wait("vector", "eT", mk_y)
            sch.wait("vector", xs_, mk_xr)
            sch.do("vector", (lambda t_, ob_, b_: lambda e: e.tensor_tensor(
                Y1T[t_ % 2][:, ob_ * 512:ob_ * 512 + 512], pb[b_][:, :],
                XRS[t_ % 2][:, ob_ * 512:ob_ * 512 + 512], ALU.add))(t, ob, bank), "eV")
            ev[("yb", bank)] = sch.val["eV"]
        mk_y1 = sch.val["eV"]
        sch.cutpoint()
        yt = Y1T[par]
        xrt = XRS[par]
        sch.do("vector", (lambda y: lambda e: e.reduce_sum(C(C_LM), y[:, :], AX.X))(yt), "eV")
        vsync()
        sch.do("vector", (lambda y, d: lambda e: e.scalar_tensor_tensor(
            d[:, :], y[:, :], 1.0, y[:, :], ALU.mult, ALU.mult))(yt, xrt))
        sch.do("vector", (lambda d: lambda e: e.reduce_sum(C(C_LV), d[:, :], AX.X))(xrt), "eV")
        vsync()
        vsmall(lambda e: e.tensor_scalar(C(C_LM), C(C_LM), 1.0 / 2048.0, None, ALU.mult))
        sq_mk[t] = sch.val["eV"]
        vsmall(lambda e: e.tensor_scalar(C(C_LV), C(C_LV), 1.0 / 2048.0, EPS_LN, ALU.mult, ALU.add))
        vsmall(lambda e: e.tensor_tensor(C(C_LR), C(C_LM), C(C_LM), ALU.mult))
        vsmall(lambda e: e.tensor_tensor(C(C_LV), C(C_LV), C(C_LR), ALU.subtract))
        mk_var = sch.val["eV"]
        sch.cutpoint()
        sch.wait("scalar", "eV", mk_var)
        sch.do("scalar", lambda e: e.activation(C(C_LR), C(C_LV), AFT.Sqrt), "eC")
        sch.wait("vector", "eC", sch.val["eC"])
        vsmall(lambda e: e.reciprocal(C(C_LR2), C(C_LR)))
        for _ in range(2):
            vsmall(lambda e: e.tensor_tensor(C(C_LR), C(C_LR2), C(C_LR2), ALU.mult))
            vsmall(lambda e: e.tensor_tensor(C(C_LR), C(C_LR), C(C_LV), ALU.mult))
            vsmall(lambda e: e.tensor_scalar(C(C_LR), C(C_LR), -0.5, 1.5, ALU.mult, ALU.add))
            vsmall(lambda e: e.tensor_tensor(C(C_LR2), C(C_LR2), C(C_LR), ALU.mult))
        sch.do("vector", (lambda y, d: lambda e: e.tensor_scalar(
            d[:, :], y[:, :], C(C_LM), C(C_LR2), ALU.subtract, ALU.mult))(yt, xrt))
        sch.do("vector", (lambda t_, d: lambda e: e.tensor_copy(Y1N[t_][:, :], d[:, :]))(t, xrt), "eV")
        sch.cutpoint()
        sch.do("vector", (lambda d: lambda e: e.reduce_max(C(C_LV), d[:, :], AX.X, apply_absolute_value=True))(xrt), "eV")
        vsync()
        vsmall(lambda e: e.tensor_scalar(C(C_LV), C(C_LV), 1.0, EPS_Q, ALU.mult, ALU.max))
        vsmall((lambda t_: lambda e: e.tensor_copy(C(C_SMAX + t_), C(C_LV)))(t))
        vsmall(lambda e: e.reciprocal(C(C_LR), C(C_LV)))
        vsmall(lambda e: e.tensor_scalar(C(C_LR), C(C_LR), 127.0, None, ALU.mult))
        for sl in range(4):
            sch.do("vector", (lambda d, s_: lambda e: e.tensor_scalar(
                TMP1[:, 0:512], d[:, s_ * 512:s_ * 512 + 512], C(C_LR), None, ALU.mult))(xrt, sl))
            first = True
            for j in range(7, -1, -1):
                k = float(2 ** j)
                thr = k - 128.5
                src = TMP1 if first else USCR
                sch.do("vector", (lambda s, kk, th: lambda e: e.tensor_scalar(
                    BSCR[:, 0:512], s[:, 0:512], th, -kk, ALU.is_ge, ALU.mult))(src, k, thr))
                sch.do("vector", (lambda s: lambda e: e.tensor_tensor(
                    USCR[:, 0:512], s[:, 0:512], BSCR[:, 0:512], ALU.add))(src))
                first = False
            sch.do("vector", lambda e: e.tensor_tensor(BSCR[:, 0:512], TMP1[:, 0:512], USCR[:, 0:512], ALU.subtract))
            sch.do("vector", (lambda s_: lambda e: e.tensor_scalar(
                XINTT[:, s_ * 512:s_ * 512 + 512], BSCR[:, 0:512], -128.0, None, ALU.add))(sl), "eV")
        mk_xint = sch.val["eV"]
        xint_mk[t] = mk_xint
        sch.wait("vector", "eV", mk_th)
        vsmall((lambda t_: lambda e: e.tensor_scalar(C(C_LV), C(C_SMAX + t_), I127_HI, None, ALU.mult))(t))
        vsmall((lambda t_: lambda e: e.scalar_tensor_tensor(C(C_LV), C(C_SMAX + t_), I127_LO, C(C_LV), ALU.mult, ALU.add))(t))
        vsmall((lambda t_: lambda e: e.tensor_tensor(C(C_C1 + t_), C(C_LV), C(C_M1), ALU.mult))(t))
        vsmall((lambda t_: lambda e: e.tensor_tensor(C(C_C3 + t_), C(C_LV), C(C_M2), ALU.mult))(t))
        sch.wait("sync", "eV", mk_xint)
        sch.do("sync", (lambda t_: lambda e: e.dma_start_transpose(
            XQT[:, :].rearrange("p (dc tt) -> p dc tt", dc=16)[:, :, t_ * 128:(t_ + 1) * 128],
            XINTT[:, :]))(t), "tX", 16)
        xqt_mk[t] = sch.val["tX"]

    sch.barrier()

    sch.do("vector", lambda e: e.memset(COLS[:, C_CH:C_CH + 8], 0.0), "eV")
    up_mm = {}
    tern_mk = {}
    up_tern = {}

    def up_tern_stage(hb):
        par = hb % 2
        if hb >= 2:
            sch.wait("vector", "eT", up_mm.get(hb - 2, 0))
        for mi, (mat, thp, thn, wofs) in enumerate(
                [(w1T, C_TH1P, C_TH1N, 0), (w3T, C_TH2P, C_TH2N, 8192)]):
            for pc in range(8):
                slot = (mi * 8 + pc) % 4
                st = WST[slot]
                key = ("st", slot)
                sch.wait("sync", "eV", tern_mk.get(key, 0))
                us_ = "u%d" % slot
                src = mat[:, hb * 256:(hb + 1) * 256].rearrange("(dc p) c -> p dc c", p=128)[:, 2 * pc:2 * pc + 2, :]
                sch.do("sync", (lambda d, s_: lambda e: e.dma_start(d[:, :].rearrange("p (dc c) -> p dc c", dc=2), s_))(st, src), us_, 16)
                mk_pc = sch.val[us_]
                sch.wait("vector", us_, mk_pc)
                dst0 = wofs + par * 4096 + pc * 512
                sch.do("vector", (lambda st_, tp: lambda e: e.tensor_scalar(
                    TMP1[:, 0:512], st_[:, :], C(tp), None, ALU.is_gt))(st, thp))
                sch.do("vector", (lambda st_, tn: lambda e: e.tensor_scalar(
                    USCR[:, 0:512], st_[:, :], C(tn), None, ALU.is_lt))(st, thn))
                sch.do("vector", (lambda d0: lambda e: e.tensor_tensor(
                    WUP[:, d0:d0 + 512], TMP1[:, 0:512], USCR[:, 0:512], ALU.subtract))(dst0), "eV")
                tern_mk[key] = sch.val["eV"]
        up_tern[hb] = sch.val["eV"]

    up_tern_stage(0)
    for hb in range(NHB):
        par = hb % 2
        if hb + 1 < NHB:
            up_tern_stage(hb + 1)
        mk_tern = up_tern[hb]
        for t in range(NQT):
            b1 = t % 4
            b3 = 4 + t % 4
            sch.wait("tensor", "eV", ev.get(("h1", b1), 0))
            sch.wait("tensor", "eV", ev.get(("h3", b3), 0))
            sch.wait("tensor", "eV", mk_tern)
            sch.wait("tensor", "tX", 128)
            for dc in range(NHC):
                sch.do("tensor", (lambda t_, dc_, p_, b_: lambda e: e.matmul(
                    pb[b_][:, 0:256], XQT[:, dc_ * 1024 + t_ * 128:dc_ * 1024 + t_ * 128 + 128],
                    WUP[:, p_ * 4096 + dc_ * 256:p_ * 4096 + dc_ * 256 + 256],
                    start=(dc_ == 0), stop=(dc_ == NHC - 1)))(t, dc, par, b1), None)
            for dc in range(NHC):
                sch.do("tensor", (lambda t_, dc_, p_, b_: lambda e: e.matmul(
                    pb[b_][:, 0:256], XQT[:, dc_ * 1024 + t_ * 128:dc_ * 1024 + t_ * 128 + 128],
                    WUP[:, 8192 + p_ * 4096 + dc_ * 256:8192 + p_ * 4096 + dc_ * 256 + 256],
                    start=(dc_ == 0), stop=(dc_ == NHC - 1)))(t, dc, par, b3),
                    "eT" if dc == NHC - 1 else None)
            mk_mm = sch.val["eT"]
            up_mm[hb] = mk_mm
            sch.wait("vector", "eT", mk_mm)
            sch.do("vector", (lambda b_, t_: lambda e: e.tensor_scalar(
                ST4[0][:, :], pb[b_][:, 0:256], C(C_C1 + t_), None, ALU.mult))(b1, t), "eV")
            mk_t1 = sch.val["eV"]
            ev[("h1", b1)] = mk_t1
            sch.wait("scalar", "eV", mk_t1)
            sch.do("scalar", lambda e: e.activation(ST4[1][:, :], ST4[0][:, :], AFT.Silu), "eC")
            mk_silu = sch.val["eC"]
            sch.wait("vector", "eC", mk_silu)
            ho = t * 5632 + hb * 256
            sch.do("vector", (lambda b_, t_, o: lambda e: e.scalar_tensor_tensor(
                HHt[:, o:o + 256], pb[b_][:, 0:256], C(C_C3 + t_), ST4[1][:, :], ALU.mult, ALU.mult))(b3, t, ho), "eV")
            ev[("h3", b3)] = sch.val["eV"]
            sch.do("vector", (lambda o: lambda e: e.reduce_max(
                C(C_T0), HHt[:, o:o + 256], AX.X, apply_absolute_value=True))(ho), "eV")
            vsync()
            sch.do("vector", (lambda t_: lambda e: e.tensor_tensor(C(C_CH + t_), C(C_CH + t_), C(C_T0), ALU.max))(t), "eV")

    sch.barrier()

    tr_mk = {}
    for t in range(NQT):
        vsmall((lambda t_: lambda e: e.tensor_scalar(C(C_T0), C(C_CH + t_), 1.0, EPS_Q, ALU.mult, ALU.max))(t))
        vsmall(lambda e: e.reciprocal(C(C_T1c), C(C_T0)))
        vsmall(lambda e: e.tensor_scalar(C(C_T1c), C(C_T1c), 127.0, None, ALU.mult))
        vsmall(lambda e: e.tensor_scalar(C(C_T2c), C(C_T0), I127_HI, None, ALU.mult))
        vsmall(lambda e: e.scalar_tensor_tensor(C(C_T2c), C(C_T0), I127_LO, C(C_T2c), ALU.mult, ALU.add))
        vsmall((lambda t_: lambda e: e.tensor_tensor(C(C_C2 + t_), C(C_T2c), C(C_M3), ALU.mult))(t))
        for sl in range(8):
            o0 = t * 5632 + sl * 704
            sch.do("vector", (lambda o: lambda e: e.tensor_scalar(
                TMP1[:, :], HHt[:, o:o + 704], C(C_T1c), None, ALU.mult))(o0))
            first = True
            for j in range(7, -1, -1):
                k = float(2 ** j)
                thr = k - 128.5
                src = TMP1 if first else USCR
                sch.do("vector", (lambda s, kk, th: lambda e: e.tensor_scalar(
                    BSCR[:, :], s[:, :], th, -kk, ALU.is_ge, ALU.mult))(src, k, thr))
                sch.do("vector", (lambda s: lambda e: e.tensor_tensor(USCR[:, :], s[:, :], BSCR[:, :], ALU.add))(src))
                first = False
            sch.do("vector", lambda e: e.tensor_tensor(BSCR[:, :], TMP1[:, :], USCR[:, :], ALU.subtract))
            sch.do("vector", (lambda o: lambda e: e.tensor_scalar(
                HHt[:, o:o + 704], BSCR[:, :], -128.0, None, ALU.add))(o0), "eV")
        mk_lad = sch.val["eV"]
        sch.wait("sync", "eV", mk_lad)
        if t < 4:
            dst = T1[:, :].rearrange("p (hc tt) -> p hc tt", hc=44)[:, :, t * 128:(t + 1) * 128]
        else:
            dst = T2[:, :].rearrange("p (hc tt) -> p hc tt", hc=44)[:, :, (t - 4) * 128:(t - 3) * 128]
            sch.wait("sync", "tH", tr_mk[3])
        sch.do("sync", (lambda t_, d: lambda e: e.dma_start_transpose(d, HHt[:, t_ * 5632:(t_ + 1) * 5632]))(t, dst), "tH", 16)
        tr_mk[t] = sch.val["tH"]
    mk_trdone = sch.val["tH"]

    sch.barrier()

    sch.do("vector", lambda e: e.memset(COLS[:, C_LS:C_LS + 16], 0.0), "eV")
    sch.wait("vector", "tH", mk_trdone)
    w2t_mk = {}
    zev_mk = {}
    zmm_mk = {}
    w2t_done = {}

    def w2_tern_stage(db):
        par = db % 2
        if db >= 2:
            sch.wait("vector", "eT", zmm_mk.get(db - 2, 0))
        for sla in range(4):
            stg = W2S[sla % 2]
            sch.wait("sync", "eV", w2t_mk.get((db - 1, sla + 2), 0) if db >= 1 else 0)
            if sla >= 2:
                sch.wait("sync", "eV", w2t_mk.get((db, sla - 2), 0))
            ss_ = "s%d" % (sla % 2)
            src = w2T[:, db * 256:(db + 1) * 256].rearrange("(hc p) c -> p hc c", p=128)[:, sla * 11:(sla + 1) * 11, :]
            sch.do("sync", (lambda d, s_: lambda e: e.dma_start(d[:, :].rearrange("p (hc c) -> p hc c", hc=11), s_))(stg, src), ss_, 16)
            mk_sl = sch.val[ss_]
            sch.wait("vector", ss_, mk_sl)
            for sl in range(4):
                o0 = sl * 704
                sch.do("vector", (lambda st_, o: lambda e: e.tensor_scalar(
                    TMP1[:, :], st_[:, o:o + 704], C(C_TH3P), None, ALU.is_gt))(stg, o0))
                sch.do("vector", (lambda st_, o: lambda e: e.tensor_scalar(
                    USCR[:, :], st_[:, o:o + 704], C(C_TH3N), None, ALU.is_lt))(stg, o0))
                sch.do("vector", (lambda p_, sa, o: lambda e: e.tensor_tensor(
                    W2P[p_][:, sa * 2816 + o:sa * 2816 + o + 704], TMP1[:, :], USCR[:, :], ALU.subtract))(par, sla, o0), "eV")
            w2t_mk[(db, sla)] = sch.val["eV"]
        w2t_done[db] = sch.val["eV"]

    w2_tern_stage(0)
    for db in range(NDB):
        par = db % 2
        if db + 1 < NDB:
            w2_tern_stage(db + 1)
        mk_w2t = w2t_done[db]
        for t in range(NQT):
            bank = par * 4 + (t % 4)
            half = (t // 4) * 256
            sch.wait("tensor", "eV", zev_mk.get((db - 2, t), 0))
            sch.wait("tensor", "eV", mk_w2t)
            if db == 0:
                sch.wait("tensor", "tH", mk_trdone)
            src_t = T1 if t < 4 else T2
            for hc in range(44):
                sch.do("tensor", (lambda s_, hc_, t_, b_, h_, p_: lambda e: e.matmul(
                    pb[b_][:, h_:h_ + 256],
                    s_[:, hc_ * 512 + (t_ % 4) * 128:hc_ * 512 + (t_ % 4) * 128 + 128],
                    W2P[p_][:, hc_ * 256:hc_ * 256 + 256],
                    start=(hc_ == 0), stop=(hc_ == 43)))(src_t, hc, t, bank, half, par),
                    "eT" if hc == 43 else None)
        mk_zmm = sch.val["eT"]
        zmm_mk[db] = mk_zmm
        for t in range(NQT):
            bank = par * 4 + (t % 4)
            half = (t // 4) * 256
            sch.wait("vector", "eT", mk_zmm)
            zsl = (lambda t_, d_: Y1N[t_][:, d_ * 256:d_ * 256 + 256])(t, db)
            sch.do("vector", (lambda b_, h_, t_, z: lambda e: e.scalar_tensor_tensor(
                z, pb[b_][:, h_:h_ + 256], C(C_C2 + t_), z, ALU.mult, ALU.add))(bank, half, t, zsl), "eV")
            zev_mk[(db, t)] = sch.val["eV"]
            mk_z = sch.val["eV"]
            sch.do("vector", (lambda z: lambda e: e.reduce_sum(C(C_T1c), z, AX.X))(zsl), "eV")
            vsync()
            sch.do("vector", (lambda t_: lambda e: e.tensor_tensor(C(C_LS + t_), C(C_LS + t_), C(C_T1c), ALU.add))(t))
            sch.do("vector", (lambda z: lambda e: e.scalar_tensor_tensor(
                TSQ[:, 0:256], z, 1.0, z, ALU.mult, ALU.mult))(zsl))
            sch.do("vector", lambda e: e.reduce_sum(C(C_T2c), TSQ[:, 0:256], AX.X), "eV")
            vsync()
            sch.do("vector", (lambda t_: lambda e: e.tensor_tensor(C(C_LQ + t_), C(C_LQ + t_), C(C_T2c), ALU.add))(t), "eV")

    sch.barrier()

    st_mk = {}
    for t in range(NQT):
        par = t % 2
        if t >= 2:
            sch.wait("vector", "a%d" % (t % 2), st_mk[t - 2])
        vsmall((lambda t_: lambda e: e.tensor_scalar(C(C_T0), C(C_LS + t_), 1.0 / 2048.0, None, ALU.mult))(t))
        vsmall((lambda t_: lambda e: e.tensor_scalar(C(C_T1c), C(C_LQ + t_), 1.0 / 2048.0, EPS_LN, ALU.mult, ALU.add))(t))
        vsmall(lambda e: e.tensor_tensor(C(C_T2c), C(C_T0), C(C_T0), ALU.mult))
        vsmall(lambda e: e.tensor_tensor(C(C_T1c), C(C_T1c), C(C_T2c), ALU.subtract))
        sch.wait("scalar", "eV", sch.val["eV"])
        sch.do("scalar", lambda e: e.activation(C(C_T2c), C(C_T1c), AFT.Sqrt), "eC")
        sch.wait("vector", "eC", sch.val["eC"])
        vsmall(lambda e: e.reciprocal(C(C_T3c), C(C_T2c)))
        for _ in range(2):
            vsmall(lambda e: e.tensor_tensor(C(C_T2c), C(C_T3c), C(C_T3c), ALU.mult))
            vsmall(lambda e: e.tensor_tensor(C(C_T2c), C(C_T2c), C(C_T1c), ALU.mult))
            vsmall(lambda e: e.tensor_scalar(C(C_T2c), C(C_T2c), -0.5, 1.5, ALU.mult, ALU.add))
            vsmall(lambda e: e.tensor_tensor(C(C_T3c), C(C_T3c), C(C_T2c), ALU.mult))
        sch.do("vector", (lambda t_, p_: lambda e: e.tensor_scalar(
            OUTT[p_][:, :], Y1N[t_][:, :], C(C_T0), C(C_T3c), ALU.subtract, ALU.mult))(t, par), "eV")
        mk_o = sch.val["eV"]
        sch.wait("scalar", "eV", mk_o)
        sch.do("scalar", (lambda t_, p_: lambda e: e.dma_start(out[t_ * 128:(t_ + 1) * 128, :], OUTT[p_][:, :]))(t, par), "a%d" % par, 16)
        st_mk[t] = sch.val["a%d" % par]
    sch.wait("scalar", "a0", sch.val["a0"])
    sch.wait("scalar", "a1", sch.val["a1"])
    sch.wait("sync", "a0", sch.val["a0"])
    sch.wait("sync", "a1", sch.val["a1"])

    if cut is not None:
        sch.ops = {e: sch.ops[e][:sch.cuts[cut][e]] for e in sch.ops}
        nv = {s: 0 for s in sch.val}
        for e in sch.ops:
            for op in sch.ops[e]:
                if op[0] == "i" and op[2] is not None:
                    nv[op[2]] += op[3]
        sch.val = nv
    import os
    dumps = os.environ.get("KDUMP", "")
    if dumps:
        sch.barrier()
        loc = locals()
        for i, nm in enumerate(dumps.split(",")):
            if ":" in nm:
                b, ix = nm.split(":")
                th = loc[b][int(ix)]
                nm = b + ix
            else:
                th = loc[nm]
            dsh = list(th.shape)
            dout = nc.dram_tensor(f"dump_{nm}", dsh, th.dtype, kind="ExternalOutput").ap()
            sch.do("sync", (lambda d, t: lambda e: e.dma_start(d, t[:, :]))(dout, th), "c0", 16)
        sch.wait("sync", "c0", sch.val["c0"])
    _QUEUE = {"c0": "sync", "m0": "sync", "m1": "sync", "x0": "sync", "x1": "sync",
              "tX": "sync", "tH": "sync", "u0": "sync", "u1": "sync", "u2": "sync",
              "u3": "sync", "s0": "sync", "s1": "sync",
              "g0": "gpsimd", "g1": "gpsimd", "gq0": "gpsimd", "gq1": "gpsimd",
              "gw1": "gpsimd", "gw2": "gpsimd", "gc": "gpsimd",
              "a0": "scalar", "a1": "scalar"}
    totals = {s: 0 for s in _QUEUE}
    for e in sch.ops:
        for op in sch.ops[e]:
            if op[0] == "i" and op[2] in totals:
                totals[op[2]] += op[3]
    for s, eng in _QUEUE.items():
        if totals[s]:
            sch.ops[eng].append(("w", s, totals[s]))
    with es:
        with nc.Block() as block:
            def make(ename):
                def body(eng):
                    for op in sch.ops[ename]:
                        if op[0] == "w":
                            eng.wait_ge(sems[op[1]], op[2])
                        else:
                            ins = op[1](eng)
                            if op[2] is not None:
                                ins.then_inc(sems[op[2]], op[3])
                return body

            block.sync(make("sync"))
            block.gpsimd(make("gpsimd"))
            block.vector(make("vector"))
            block.scalar(make("scalar"))
            block.tensor(make("tensor"))
    global _LAST_SCHED, _LAST_TENSORS
    _LAST_SCHED = sch
    _LAST_TENSORS = {k: v for k, v in locals().items()
                     if k in ("QT", "KT", "VV", "AOUT", "XQT", "XINTT", "HHt",
                              "T1", "T2", "Y1N", "TBL", "COLS", "WUP", "W2P", "XTH", "WOT")}
    return nc


_LAST_SCHED = None


_NC_CACHE = None


def _host_inputs(L0_attn_x, L0_attn_Wq, L0_attn_Wk, L0_attn_Wv, L0_attn_Wo,
                 L0_ffn_w1, L0_ffn_w2, L0_ffn_w3):
    x = np.ascontiguousarray(np.asarray(L0_attn_x, dtype=np.float32))
    wq = np.asarray(L0_attn_Wq, dtype=np.float32)
    wk = np.asarray(L0_attn_Wk, dtype=np.float32)
    wv = np.asarray(L0_attn_Wv, dtype=np.float32)
    wo = np.asarray(L0_attn_Wo, dtype=np.float32)
    w1 = np.asarray(L0_ffn_w1, dtype=np.float32)
    w2 = np.asarray(L0_ffn_w2, dtype=np.float32)
    w3 = np.asarray(L0_ffn_w3, dtype=np.float32)

    shared = {
        "wqT": np.ascontiguousarray(wq.T),
        "wkT": np.ascontiguousarray(wk.T),
        "wvT": np.ascontiguousarray(wv.T),
        "woT": np.ascontiguousarray(wo.T),
        "w1T": np.ascontiguousarray(w1.T),
        "w3T": np.ascontiguousarray(w3.T),
        "w2T": np.ascontiguousarray(w2.T),
    }
    inv32 = np.frombuffer(base64.b64decode(_INV_B64), dtype=np.float32)
    t32 = np.arange(S, dtype=np.float32)
    freqs = np.einsum("i,j->ij", t32, inv32)
    dsin = np.frombuffer(zlib.decompress(base64.b64decode(_DSIN_B64)), dtype=np.int8)
    dcos = np.frombuffer(zlib.decompress(base64.b64decode(_DCOS_B64)), dtype=np.int8)
    sin64 = (np.sin(freqs).view(np.int32) + dsin.reshape(S, 64).astype(np.int32)).view(np.float32)
    cos64 = (np.cos(freqs).view(np.int32) + dcos.reshape(S, 64).astype(np.int32)).view(np.float32)
    sin_t = np.concatenate([sin64, sin64], axis=1).T
    cos_t = np.concatenate([cos64, cos64], axis=1).T
    p64 = np.zeros((128, 128), dtype=np.float32)
    for d in range(64):
        p64[d + 64, d] = -1.0
        p64[d, d + 64] = 1.0
    shared["p64"] = p64
    shared["ones32"] = np.ones((128, 32), dtype=np.float32)
    shared["onesr"] = np.ones((1, 128), dtype=np.float32)

    in_maps = []
    for c in range(8):
        b, half = c // 2, c % 2
        perm = np.concatenate([np.arange(half * TQ, (half + 1) * TQ),
                               np.arange((1 - half) * TQ, (2 - half) * TQ)])
        m = dict(shared)
        m["xT"] = np.ascontiguousarray(x[b].T[:, perm])
        m["xres"] = np.ascontiguousarray(x[b, half * TQ:(half + 1) * TQ, :])
        m["rope"] = np.ascontiguousarray(
            np.concatenate([sin_t[:, perm], cos_t[:, perm]], axis=1))
        in_maps.append(m)
    return in_maps


def kernel(L0_attn_x, L0_attn_Wq, L0_attn_Wk, L0_attn_Wv, L0_attn_Wo,
           L0_ffn_w1, L0_ffn_w2, L0_ffn_w3):
    global _NC_CACHE
    in_maps = _host_inputs(L0_attn_x, L0_attn_Wq, L0_attn_Wk, L0_attn_Wv,
                           L0_attn_Wo, L0_ffn_w1, L0_ffn_w2, L0_ffn_w3)
    if _NC_CACHE is None:
        _NC_CACHE = build_program()
    res = run_bass_kernel_spmd(_NC_CACHE, in_maps, core_ids=list(range(8)))
    y = np.empty((B, S, H), dtype=np.float32)
    for c in range(8):
        b, half = c // 2, c % 2
        y[b, half * TQ:(half + 1) * TQ, :] = np.asarray(res.results[c]["out"])
    return y


_INV_B64 = (
    "AACAP9evXT8R+T8/4D0mP5r1Dz/PU/k+m+jXPhv4uj6b6KE+BDWMPiPUcj7tR1I+hxg2PkCwHT53jQg+1n/sPc3MzD14WbE9DZSZ"
    "PU3+hD3CVWY9P3ZHPRW6LD1IkxU944YBPdJU4DxPQ8I8izmoPDmtkTwzTXw88ntaPBEzPTwK1yM8LeENPLC59TsVytQ7nES4O8yR"
    "nzt3Loo7DlJvOzg+TzsPdzM7DWkbO2+UBjsoFek6XNfJOo7JrjoOXJc6bxKDOuIBYzqMlEQ6RDsqOhZqEzqsT/85JRfdOdd0vzlg"
    "y6U5cpKPORWoeDnkU1c5U3c6ORZ5ITly1As54izyOA=="
)
_DSIN_B64 = (
    "eNrFfYuyJEmKq+v/P1p3qs7JCHeQAM9qu9tmO9uPyswIf4AQINb6t7+4/o//4n/0DXi/CP+fn4BY/+d/EftrzB8In//l+483C4Lw"
    "g7xdTfDzEbxfRPkjzWPw52PPZ4GrzWf6m8tVBC8f+r84h/va/UeXGVevgT+/ivdvv1vAY/34xYvwP3xx8JtP0Rynm/uDdjsmW/Yv"
    "1ouXXwf5wuC/HrurM0S74MPFw68Bwa8lvHh+/PsO/Pzkz3Os0x5PPvoYH/y1ebxedvzDgWL6o/zu7f8DFIDj1/EcZnQf5GFJebkY"
    "1H+A0y/g/qPgZgYv146H+eX4ZFJe2iGq+UfD+zzk573F0ceN6VMGqVt/4mcXeXyQ9yfws4X3X4Dv19dhNn7z8N+YkM/n+a33pH5k"
    "Xjw6vvA/WK/lL4Bcf4b4YwP4ven6vbzfYfj/IHjY7R/WpQE6/yBOTD76BuL8BMc2lKcd5u3SIX7HG0uNMTWWvzLsdhNPyGRQDK8u"
    "71cHiJ9T/1k+xMeG+07upwV0z48OukF9Kaevj+cuH1fxG2CBdXz+zgF+hWR+bz7EMl15wp+jhGP/vvLfWCUmRHmKcLkEmAHSxgDx"
    "94MvhsDwAhEf4kHcop7NQLU3N5E0pKW4uPw4kCgDqO0A8BvGI5jwOQGjNo/940un/40vZfYdvDl+X1JgF6FCE9/z/ocRPvYFBgPS"
    "hb8F0a+rVHzibEmg3oDjC0Bvs7lTi8M4GLziIvAbRayBES9jyB844GNWyAvEAFfwBiGr2QrsyJXpT7fvzu3CkNtKzo0o8G/xJT7W"
    "+9lDvFTeIH7i58/thx2XHDRf5HJQORx98h/BPl8zfs3k8XQi4QLPlp8rf/7eYKLn80oHwiKWbG9v+vPzUAIqnMB1PuM8qviaWDmO"
    "Mm6O0PvDuPNJE9r7YiVR7mO2PxQkIKJnamikTwwELi7Ha7uvoYJUFzFE/j1sSAgz9vFIoyEjK6CkEMPVvUjC/BwRAOiA/WQR8E/H"
    "n/lCTrkghmB0lMvZnWe4QpwuAxyiuYl+dlO2ryAm7/49iRvNAzOOyMSS9H4MvpBfOsUYQHyRzONV6JOdEdcw+/yAl0whIUG6QQBh"
    "X5jd+cXXvCPCxePd12AzQjzh5sSDs7xGX9AgWMVxNf8K+AWQ2FmIYTKYJ9i+izvCUn1H3OME/OANBGf63zONjulZ2B0278JoxVi/"
    "fAo53Pp4msdLiPp7+Q1lgPoB3Sf+wW2yv2LdCu47Drcz3benGKJ0ZHR2bBy8nDCQk6DUcCiXnFn/QJjv3W8Ok7fMQ33wOYkm2f0i"
    "BwaIYtE5M3775/A9nuHJhWy/Tx8FIP03JMPA0c27sPg9fPvNY2J0xpidLafcBKQVZrLozZGGDMcmeaBP0uPAohznwfj4ebxfguiY"
    "W1uNkUcYmlHLuWL2+ZuHIC7u8eQF8HwpRvkTLkO9YFiIgHhdiVCPOFk/bbVvysCQnN6EAqCCP89K8i4UhnGA9f35jbTo3n9YTIF9"
    "wSOFMjHiln+f7SDvvXbmTZQvREvcaUuMSRzCfB22Pz64wiijjODV4SLWM1b/NvaNLOKEQzz4K8wwcH30sZ3c1gEjAa7HD+CGA2YI"
    "x36/BGx//zm2WJG0w4Aswj1ct1vgqiZqM44iSBGoclDny/qtMLFFY7dBG7FhQsUheSAaz1TtJKH+/JxC2teNAdE0C3dy7rKE9Gsk"
    "05+f12Hibx7u2wp4A9jae/GHukFxkye3al9BftP2kTwMquh4DAHG6/ZPZYfQSOgwX6xaYxgvN+JdmgAoisO8rgnANxM45i8hVvKK"
    "QTu2PTlQXl01Up3hMgKjMGF3JaDRv6PZOYj7T8ov5NQa6NwxiNuTzNAAQmUToOEvv+vnCcn33wiUMiIcM43DPPwv747kurDvA2/Y"
    "z6t6hoib6gDYGxD+YwRLlXHA2CzmuIc39rK1FpMaGij4dFE/tea8OyL7hHqrqgDgg7c3BhdPMDcvg9pNMVIv6OTmf6hDWFIAGn8j"
    "dQ1ctAHCcT/3xxgZkK5URlfXnuAwYhM38NR/AcrkzincVK7wE4GBtwHYN0wcRSzDtnVs8AjzZq49BcPOxRoAyfvQPdaM0eIO9vEP"
    "Ko9UPkTwuTgYsFkEiu3MBu4ME+zC88Rq7qsNt5/zyuv7G4gT7pymPd97A/bpeRFgbRsHbo34hOjK78/P+pL3hiuL6VvpuD3EQZ5+"
    "38R8U0Nt3BttVIn+UimTV6Q/cuuYSYdqww/JfEKggzXP5GPMwzDfYWos2wVgVOEcekyTwIxCfsMy5N8qEG6ncN5/K03qXfT+ccGY"
    "98MD0Uq4HDKbV/hkQPBl+Z6B290GQpfuX2TCC3bipowJSzbA1IE8fRTCm+ospDN8d37+GAJ64PtHpUX6ED637LcO4dVTeK8zZmXB"
    "b951XMuCFZYswXQMTx0119DRMBkhHIGMwBkmANyt+NF/yg4AQccTuGpFqWsQYC0yzYLdwgE+5BthAHwJgFHFtTcuaIkTVXEPz1Pu"
    "KHR9rst+D1hiSM04IdbjZ6DN3L9zHCVWp0azrKIXfVCG/y0ARPqRbLubCHYqHgPre4kT6YLT9uM9AqOOetExD7QB1Ix7zZm7K/Eq"
    "Jg/Ku707MfDmARIGhEFvmzn9e5nKrAwcgYNAvWB492GB87AEjyaMwOwMcCuaqeoF+vCe674WjjCsMeY10D8JvLS7rZQGKqpAVYFj"
    "VR7nF4wgneTxgU5EKtr7+wpP7Cd/FIbK4r0vSvChvw5varmSA+IR6IP6EzMOhhe0eYm2uS6FsPatRukXShSOcGnh2aYDGm2uRO0/"
    "tzNSOxIGP1Tfv9Bny5HoVbMAQoiMk/07DjUnn4f+O14zuIE6HIePz+5Jzh6ZQawr5Lkz7j9tYF37Hqt0eL/gtPEyBt8XziRv7eVV"
    "s1F3MCn15HB5ga9rQBKG+F22bwXADvIPo9CLAT1y3Lz0nE9+R58croM7eXKtpRTZsiMuY49/GSNN9mgfIkBweGj0+NDxI0b8TWCQ"
    "NYEI7bUUeznZAbPAHJcGYvZPrKAjEl0Q1TA4XD9oEDSWiMBYd9G8Cm18U+OncdihAqKT+PujifyUUA0gFMJJL/XAcuwEVeaBddG7"
    "l1bycWu40dMsgy76kJ7vizz5snEJxHSvpgAI/5wQ20vpwfr+cJ2eG/kCc7EFkG/I8uljpS5f75tsuVdRlyWn+xufHpPYKcEV6dGa"
    "B+aY8t09yCOCBHa5AxSOmMJTYGb/ULZwtrQqDkMCV1bhv1rYj+0Mznspgburg6PpP0KZq/jxLvR030WU96ejGhkIvWEOW7OvIYBi"
    "i29+wadohpXZHWYuIqjhoDfa3G1HupS83ENq3ocTyiu9KFbWU56VE0FBM2cLQEis54NMGzvcIcVzBSj0oHMJjGQQJOs5LYFAuv+G"
    "pWD14q9+2AqiPrPQ6w25j3idmHxwjwC3GzzpNeEqoXdTiPHpvv1lfgFBgdS2n4DIgdHjvcho7C7wDh+tINmyI4qoaGnc5lmrEKcp"
    "/K2oqIuXeLi+ZHSac6QzL/PYH745fBK4vf5lRFuw4z6X4J7vNC0QUcEIPPgwH30T4pGqPGkYcKCf8S9qt9rEHyUs7GuJIQ/GsBL+"
    "vSplgzIv3y3SzTZ97tr+rnqh6XF3W9XPgtiatQ4/cezrzKC/zhshPPD/JNbaxPZ+gbOCHa+Kj8Qy4FYNB6g7KFD64L9fwFzCN6hg"
    "PvNG9IYCjf2Cik4ny3BWbg9MGDVRMhcegD27GOl/KKTH0zfPMriG8IC9f3BXlKGHlIZJKm3VV5XAR9b/Xb8BRUPJZ/T7jyBT8PJu"
    "87wnA3Mh86Yw9T+79h22Npj1DwU174tN9wEvag9zNDoxeexFtjzv4HCeCnK5zJh5pGAa4E8grfUWRMyZCcWVEZ4OZDlxE9cSuu2c"
    "/m7XflGJZMFGRWNUhRre2eDRKm+kDKaMydgUKaMHwNhS1tR6KbX9YzyPUDp432GAkZAqdJDX4w/GsH8TLmav/wKV/IY5hN2NZEqk"
    "bPbfAGAMWkQTBYTpTeUa1rKfMqZrtQNhInvDN208ipvlhWHBGX+t8loiSKxvJs6J28eURfvET60Gwst+YRxdZFtHRBA0+yLUHBeu"
    "KpB5hFPAqH4xTCtyktczo36IwUynikGtxbSQBEkD9bAFbT5XWhDc1o6taG3ytaJhT2lhzUUAEt7wUOSS7g3KgkNimJnpYvpps5PI"
    "5+9PtcxJtO6tqG3+BOkSMAnT9vcuFgzyCAd0oohVb/cE+OUFz9GoxrensTMT0CqUF91N7ruaU35UgeTsa+huqSdAZsl5TtAOW9L1"
    "W7eHaeL3oItwnBmsZooZBPPxr/NQGa7hrAjeEDeoG/CpuRvcax4jMZLp+cyz4yfJc6Q9L3UXjl8AmQryRuKRf68PN6OSxe1L80KI"
    "qHA4Nkx64bQPnMHJmDNDTdmAiezY9wFl7RSkG0MeRjaFlkw011wNhOfEgrkFebCnEB1FGq8ExN1C8JaF8B8MygjeVhVzuRc+TRHv"
    "+vZU31aspbXgRRaXXM2DguEEGpoi/jfmpyz9DxTUeReSA6JSopgByq4YpGBA3ZdC2qAAKc0kmNykoeftBGeAhguAn5fEUcAiF3mi"
    "/5H/kadBpueMjdTaKeQyGAi9TYC/FwGHv9Y8YmxRls5lmusw66FmsrUZQvC2iQeCoMUtKzmtXuE5903MYuI0Xv7qQVDBIa5cQm1E"
    "HLbxRcGnfn2QRiJAbC4Gu/65bWQOv9nUD1UZ6e2bJgb4zKQgJZHPUAKdXBe9SzAEEtgSDdCR6t0wUxqI0JOrcJhawJhBBQNUbL4w"
    "ZtLE5WJTAaOqgSt04W8reTiCkfZXNSsKMVaG+yO82zTDAAmwhnH/ySVsMeH7ZxkFAWXPMkT0EfXAsVQFrAzB4zcBd69kH7RgMj8l"
    "4hCb0o5HxYOiUUd6RQltihk5CB4O+8GAPndFwYkUR4iD5UgC+G0SC1i1nkFfMVThli+/Jc5cRUMY+4vAPYTceoBRwRDrNyMBR0+h"
    "iTHMU/44cQfrqAHGNx2ZYfJdkd8dSLXP1NT58nixda0k4Bnv/TeDKCCBVNAyrDsI8Hv6iStjlVnbbzo+EebHE9PtxlIaIdejbMhg"
    "9M8ylobOQsWx1FMYoCuyEyiPJykVCh5uDroFSRNpL/j4Qv0lmlIoXuh5eqyS2KDyqGcLTEWC8rl9wBIE4LiT8ACSo2TDZ3qaAPJZ"
    "AwACOpSaQxMZfxbQr0aSu34MVWSKSQsYikWeVbAm5wnbghRWkbryGNG8zlqouEwCc/DpqJeKSfcS5U0dF38q5sk4OQzwW3mkHICA"
    "0mni6IsOfT/wsMb7v+zWo/BbEz0k5LA9N9EVGV7MeY/pWDaKU0H/cjesp10rqgR+8bVS6wmrp5xEfEBncllObI3tOq5022dANj9L"
    "JeJQ2Kr8B/grZYVyDIUOQTfdm3T+G/YJhavqS9+ZJykII0ZrwKAsNznCH9sVU9PE9lkmtqKjKMw9QrorI3HRs7w96XGczjcYcbwc"
    "2Lb6P9zEnpaf4twFCeE25d8U+CMcD8iRZvQHUX7SgFAztBr4X7XwQXvXqJYrRDb52jMavPqeUzpDju5o7HhbPCyKC+eCF/N5lKiD"
    "F2yPxY6ZUMm1/AMdGKaoIs/mKB+iPfCKGnvtF3ebfBc/jW49mtAFlEtp3H3RpT6jIiqhX1lcGLeT0hJzV2EvV0Ym9kAMlnGPddAS"
    "pgrHCjARKiJ6AnuL4DE4jWjO0AaiDWWXBTuVZC5yBiE/AJnxMWM836QwWBqipNJdkS9wyzFaz237yKIFd7Q7Iga047/e0hmGsiOu"
    "kpchZPz9Q4J8WbqYRci6SJCyEtFNTwEKu/vpIaFYuCWqQXMAouSoa9zHw1pPpi/BeswI54O57rsJGKdAPi28XANZE6R1bibeJfVo"
    "aqtekH7rNsUlEMREIZOjZ3g1kEewOY79Zcog9l/wu2dHIdM2C2WooLBlIaKGEGq2nQEYMIwT6zqJcR6HfQL6WZspintC9X3YskYE"
    "mCF4x2+UtneSKQ2kjkLBnBxrz3xff4S6CCff3VPrPzCcSG4hr14mBxD4od15z1M4eOhezM9tNrSuoiAvc6lyXVWFZfxwFDpBIwoP"
    "6fBOwhorbS2HTwN6mcrIa5A/FsEumLpiJymiK9y8v6L+pzUkP1VOhYDZ+R9kbTPjxKsmk1XkFsP03un0hgmp0G6erXU6KsD1/IPu"
    "gNPTLZph7QzmPzWYGBSlqD8BZzqQEDBoyxIWZ2FIqaOT8qgTEKLhEzrIZ/aq0KBNq47XQpxQUwDXrBcTPhBvE2hGJQHzWZR0oGBU"
    "OvSMQGEYIHUbitOiCtrvgTWh5psHzulQ5D65EHV4i9mAjB0Y7EAOxt36R/C1ipQFWgmo7bpIObWmEt2a3jR6zBUoApNODKjYFc4V"
    "r2EbCmPk3DyrJdKXSH1RwgrL+KX+0VPWi2Yw7W/JhoAxtKXFin492AEOBoFvGCtlcTlqvFheo3U2QfNwBJDet40bi3YdDH/7cl4x"
    "MnulVcO2eiovK0ZqDUJW1ZtqiPjlsJ7k73jE0Mk21NaZbDhq0J2KJPcyxlp4V9edPMx6T9UbTtriClw3TVvG8suLyOnwEAzWOw5h"
    "Y8FfNuCRwukgGxFNIJQVWKQ9k+iXAcFYU8S9aP3XMLppNJQhXiz0J4yxXOQ2JiJELoP/FDiTlps5/WZma+oSIp7AMfiiP4KIqOir"
    "s/9jFWAH8rmYQav4oJOlza2rScOyHqPJ+9OE/hwdM5yGpRjBJ8i6YGoAmez9Xe84YtsYrgIockOwriXbmj4mDa47gw0F6ljV/chZ"
    "p3udIc4hokUhQtBNgt6dG+qWj+zkmMBAVD3CjGGQyByhdx0tBRCV0+4CaMHDgRj+fi90igEA71U7yriV3Df+curfW6bAJKI70//N"
    "eTqaYQ4ll3nAjTmUNqOKhhqOOhkrTbL+qq3mgfH9Lyrgz5hJwZCbU8zvdUtnYV803AF+7hbxr0JJp0alii0weFhbnMI7QSIIDz+s"
    "/5fZB6hmElbOQLShrLXWaDdfpSuwBAms41AniIFuD+Rszes6NsurvCV+6AidWPuG/r2X7C2xAzH9gzPiadPVpfkfMbC4jVvSdCge"
    "j18Q2Ia8ex8Xy2MUuANoXQzMH2DBEXMY7hCK2Rfn05wDVpSYD2C9K2ECVnsFtqM5uAxtUr4+zYOzaKvrWxTpY+ubAJj5FWneAmVQ"
    "/+HhKdu7FC/Kbp2OPwov1dEOIUpVZsOkRLmAZw4ePaPNzLWxG5oHJSAnX4zoA7eVvRxmh2dTEHl+kE3m0bKZA9iKkoZQ7MZMRRW/"
    "wxuYxya0eXQuI2SS+ByZ5cAuYr9vzaEGwDq5mJQH+GwLavqGlLxJzmu4PHnsGw50ku3AxkmBu/q+6v5DTQAK/USNdAls+NEgyF2r"
    "hEW71H0lqlTSEriZO136KwSNVJE6qOY7dWzJNY8kKFymIKTLATenhvd2E9hjTBmBrCz6gCormx03Tc+ORgpK7P3WcyDClEFHOsTk"
    "SSRyxRKjKgS/Ii8ys4tqxSzFs/IovG0eKbT9w9NyxtMMWxdGG9FzMEpM+cxcr7jFc0Sn4MwdCEP7LNYRILqlLezuWzUAQa23AWHj"
    "rYcZrOGsp7vDqewxJfWbKh5g7h+Hz3CvQfjl9ZtFA7agL97dbC3R8ubWRW6YHNX5oaD6uGbTn/Dgi4No2OqpmFiGC7XUn299KHrY"
    "PCk6nD5nALNpwZeff6agFwGcLqQO24Gx0HiY42oTseLqRx/OdDNXmXwLJB2VXADr23Umqia96we6zcYGsbOr/TadAEDniuQARTag"
    "VlgrWsIE65voegJAy2nFf3ngG/h8euPB7weqiZK3mNy/qMDO4QepebcPfIXQk5CrzRW4NE3gVAFkPPVgV8j9SY5RDx+iCHJZUxXm"
    "/MAaAY4sbBEA8+Gv9isD3qKYCe/be3cBxlHodkOCunOqwuPIpkvjXSQNyOKLpX6beIDUATxpqzgyiEJOq3p2oLzcc4+bN1JOkd8y"
    "5XC/xTVPAB5+ArmmATmYlqMub0o3XJv4DqRwCDvR8FrrkP/SLXss3lmcK2wEJgfC9hBU4Ko62J4BjfsMCgqPI5tirZ5sTsFz1lKE"
    "c3zoKIdPNa3pMgm/aEMis3G5wmO3lRhOhIXL1ccZGCP84qiUSQEJo/YBoVjV7A6Tyo38k6YDYuvd7aGuCJB5xkDhKojhTHQmHbU5"
    "sCQ7fULlmEeSzHtx90UJRBe0MZ1J7k1Eze3bnSn9C5/3bXt72uBar0Ag59CoHKApIYC9Qds4Xj1nipZUhooc+pTuyY6+pwpD/Ft6"
    "h/n9B7OVkjHyHmdkk4UjAV0KowQBCqybJsjVqHOCM3IJ0v6g6eESOjcuGUq5zPmkRMkSM4QUDvJjnR0Q7A+HdMSxOqcYjQUXtqxB"
    "DxyGoMvuWfD55MjsrbUJeOy2mNR1JfTAmJa8OuwPZ0eXVFDIBxs7kAX0DJz7tXXEgrO1MdLmoLib60vxPPULTJ7GX/o3LLBhDuRQ"
    "YS0DqOTaIGa4UTAARWFgHgXnDZaNfPtCKvBgKxHucDWDSLm56P1ng5Cq9Dy7iPQYLwSMxl8ch0+VmvCcI2Njt5S430S0ivc/eNKj"
    "8hXMY1HHkQgSkKAw2IlD9jcJDOkAvfnvyC6cbiP7X3Ssktx9nlNO1kY8OZk4HP5TIZhY94nK8knfH4OA4/cHBTTIF76ulatJYiXH"
    "26UY3XTd/qnPYaOoJsoMfgPCrg5LcX/2GbElnGVGIMit4uTcBvLb1wMi/b/nO4warUfTZ8DEvtQd74f2BHnOMF38/o2mq+eGHQ7b"
    "jj+Fc1yE5bZHh+e9JWhJKveV+NSSfLA6FUQnPYSBsASuVOINDVl2BqKbR3vMbzBXk40aqDaWHHdsuBpKqrHSCjVAU0j88gQPZihZ"
    "s4kelo0EdSuPLWZmQhDZPZ7HylwRkudnGQ4c7p3XhpDrO6na9SYONso5DuRJyJnWRMGdmkJ5YOURasbflPD+3II9i2ZeA4PDuvqs"
    "zRPP4PS2m+VqSuEhf57KyThSVg1MyyBb4vlgHAhtW6egBiermLdzKO0GTWkbAAm9h6CJ+mkWh4f867R7IRDnp5pg24LHHBRrkR8K"
    "o4z0FlyWH2vc19o7pMBDuHvZIq84Ld0FvUABso+6q99sQnKALOw9jvDvY8uIPm5WE9u5NgVS3thlJOB7PUYMNtuN4WE8ajjRSDCf"
    "0vf41LukLNhMie/lccGntK5rfGIIVomsqVQO0YzXgSz1SUVwefQXfNYNpwxtqf2v97tq0rfRFI+Ral0BD33MApV3nIvcm2KZxoAj"
    "KOHKg0cpOA+xZBicn6VHfkR8Mr6/HOiVCursFRA/dGNcVz1KhBrzX352aq6SY2fg6GfTKZbeNEL3SmlDGf/l6vqwp9/vtA2esz9S"
    "cOXLfEgmszTFtBNzis47NWzhpEoPz7odkkJY4JwjFGg5sNK8OoZ0RI8B/Zy7OJJU1MfqRzCcwmXYPbfDsdn4gpHtQ+wJooK/atis"
    "ZVSKKn8E+aWq2rU5+zgRKYYdzGr996ltbQee96azEQx0pOYKsig2TrhtuCxbCsTfTOz/OYOTNbsRjwc+Du+j3cClUrsqYkgalGq8"
    "cTODkwhG4tVvZUvBUVwEiBgRJQEc8DYOk1SNlMU5JygPckB5/+TryMpJT74HdEnpjnrK3ShY+PsDOZxqm16+5uqBeyWoKuuzFxfS"
    "xgKT9fe8CCwQZQXhSyFXjrnT20AYihZ6Y4G1ahonQ0SceWCmdZzOWEZjhSAYzT35iRjh5BNPTwAFXp2Nht6ZRaaHlt1pxmhX961C"
    "hqusr3u2oOYpm48VDp+9BHq6G6rW2xLeWLrmlcGRzYgAFVXBmFGsahwqR8IHp8FH32stCinYgK6BpvOnUIWebr7SXl17AVTfOaAo"
    "se3z5Tg5m3o4aZCaAQPWd3x4bpy9YBiaq09dLVzdZKaXwOplvblNOM0S/pw8c6+024mpAKfmRzO4RzLuUxtT0XcRRdoCKCiAz80a"
    "6zJXc3rBHSiSyTARFsujPrd0xA10suAYDoRaCPBDl+9nTM0htc+8647mWErP4HYveKJzUBGUI+4lF7P78xsSX3YYBCsbfkZqeVx6"
    "P+BJ5I95DEFmi8XDBpCVj45WApaVDI6XLTJFadqmQ+FXEL7rtFOxK5Vya39mr0HNiJqx5Gclv1Dwq4corM43x/lVDNjk3o0D+okc"
    "jRBatdPukZcNnEtkFkosEaS+yvwRVzJoZ9MaaEY4tG4jxwvk0bs3REGn7cQ5Q4YVD6dixJEijhN/8tLcksMXZ/tD8LDprEerTWBD"
    "9trdQ+ThKxUDxMiN6Bpwg2wkVHDG5Qwlg8ddMv6twuZN/AqGzyOHBaRtqRFd/7fIHU1+863CIxMbGpuX0XMGaGoPaVKCxj+NJh/B"
    "+3kO5JjU4qJjKkX/OS7axAowwrDSRn0XTexxYezxFGL2i50F7lAyU3Mtj1Jdg8XDP6Zaricc1SwQesYM7RAyRmbt7CTGh1sGTRxA"
    "jye3xvqmDsU2YRZU+4lUuIcUrIfZROPDluEc1ZNb+Rdtd0zYxdN0sRJAvLAx7nzzsB3Mdsg2PZc3BBmsUuBXOcZgT/7iHIECf+T4"
    "wQlgMJSUXLhQpGNJE7R2hdYfepX08txMTCK5Fxmm8Wfm/qNEIxXfzOoAReYo82usA2h+ikGDQUBM4tmBzKpUbqUxHdHnycKVsyuj"
    "kOAVA2hj9yQaYT+8CsLb/OqDWSstEF/n+2QEMaCyxOzPZnAQ07GMwyZBP0yMmiYVj0SpS1vZMDwsPj7y62g1oMUYrCf3/z9EiakQ"
    "R+Acf0H3q+AGmfDZe0ckyQqWBA4BOAyK/WTzYio0pNUiHJwQctuX9f/KvcCVmCvVwg47nCXOPkK3I7i4Wny/Tl2JfENlhBN0drHn"
    "jcbDyzSlcR6NV4GLlu5FrJyPpMxQgSqrk7AFFAKqYHaActEicU4xuo4sYCFNspBHnoPJj5oRFI48NfVufWNa1Hk8EcF0tiRcxMCg"
    "pyNuJn47/+1escRvETZtpcv0I0AYtOd05ArkhKr39YHKOfePYtCYhjL1JJhMk+I1ehCo3lnOo18SAoGtZe/7yexsQ06RpHWyCpE+"
    "KJBRAlafB4QiH2ZTQM9gjp94lbskEOsIDJHz+v0WDOXYYoICUf+IVeJA7SvUABRfu3YeFpjQQwMtfxmReQmqyCsIRTJUNyCSy8Vs"
    "J7S5aRNanSN/wlTZpxPDBjDRPWOgYJxPFmgkxqoCHJmshlBAkOEXIC0tRtOfP8efqYQXA/qIFNNuTd8Y7FdwTzduIxmQG46pk58I"
    "g6Nwr+WD5JcupAhWzjNFQdWWpaFMMTQNDMclZlmmBQ4oJBiw920jtQ4R/M2OeijwzwHDwZk9JtaMA9k9crd2UtaWOxG5Wvx1JOj2"
    "4g3EIaTF9kEOV1gzBVWeZRas+HlaSuENzzJTXEt9ATCHZGfqyhG+1JkvKhdhzgdPnSwIGiPZxwL/OM/L4ZVEjNUMR1iAaUua1uP0"
    "3pzrNyqkzBeSPj8Iob2XCASWRpM19FyDmllFsVdS15kvoWnMX+abLd9mo1r6Z3oKxxgjIxzB1LJGjC9FelXxgZKYRkUJlA5ZKo8K"
    "StOKizwFoOzcla+dp3pPmtidycmQ7bnJZ2G/NQz7OGBOoCYmY7DqB88KiHhlRkQnl3I5y5uGCViDBqy32Zo51WBkDHXwpXZOx49x"
    "9HIqPEA3iAI1Tz7JPCJ/i1V1sO/QdVnTTZ85q6xSEE+W73+UjJuxHcYNnVVucCka7tyYrnhENRqYmSUY1TQP4x9LEAXcz9p97TP6"
    "uHZzzsn5OxUTWeGa4XH6JQHo7IdQn2b03kcvivMgsdaZby0FYPtRB56dtrdCY7CIlziJoCh5jAwkUNapINvJSpI6j3tj6FbZ5B1q"
    "w1/MfnX+T+uHI904uvUo4uDdHHKowFXnFmEL2BHijJVP9PTH35Ipypxlx2cwbQcSQP52lhnJZhOIEdCoAvSggLjWXgzScFAJd1Qd"
    "Zef3yfz1irlM2h1wAxnnRp+G0URaIGgshRBcCr1eyuhVrRFloourk62ioWSp5PKFw8CAP7qaorbbwNnBR8vsjggv8Ob3BsV9IRNu"
    "okdYdri8u00+jrmmYJkwmeGYUpxfHyi4ynwy2wF1V0qeugBtxBqBU7fErJlSZl25WpKKdvvqErI9axSrHVyF5oMnAEGmDP8qlMuX"
    "1PIa1ZFrjcQmg8aq9ryRP9BnTP0+jf2HaxGOFVbCJQFZvR1zC7cKflNyKjXHQ2PcJ1HL7i/Pc4C7FxmNKaj+vZnAhFAjz6O9HyXf"
    "6DKQSsc/ijhbA7UVSuPI4nCtchi9gw/Ruh3bAcz2cOVo2VtKyCQIG/LmgeiUnO0x8BXb6vSToJaYlzoVAUFERhiphZ+CY4JtTUKq"
    "eRNZuSTA948yYhwKHhoT+oE8DRHb5xfRUuR9j0qIVnB/hXlrmF3C7nK4/BjPHnzmASO/q1c4ly3RTlZTMs8rn5AjYmHeXpvj1x8M"
    "p90VK5Ytcm5G1Vh9AI3DQO6gLFsHTvuxZs8W+KRJyZCRhFAacmNIJIt9L+Tsdz3EZ5NrIaa0DKckHnRIQKs8OKvghaaYDkh5nO16"
    "Ek5NkNR7uBc87s1AHICHc7rIM+vvLOwnSjSENdNdJlcNM1P1T5PHYnHZ0vkDNUEOZrFaDHmMmJnj+k/+evjsm4HvLwhHDExNMhd4"
    "iwaDTT0H4bGkfNY58JTDlTC3931+XBGY6OxQttSmX+bxbL0QUdZoQorlmjSW6HlOGWPCBtsw77/bQ1RjOc3y5OjND7h+6yV/tI+x"
    "d3O7mQggrVXEcLaLHB90U7kRYEfCnzf3+RjGA5b+D2qIpScAy/5pNpREDR+4QzDduFaoCdBdCAw+X0Rvhg5SPSufboezSAv9+uFA"
    "x0PG/bT9av7ECnOfAF0AmZLmdmqm2fa/6RJATd2KnDSHZyCzx6gjD1TsnVQw5Wn6KE0f9xGmbWPq8YuDElhY0i54gIGCpRrOBzeC"
    "m3oGOiNxct2qw1V3zNYjSJOEa2rtoNZJF4yftkNlASdHFkvMCS8nMqJ+Lt/N+0uKTSU21QbvS1shCKgldZUAenWeruHHE1v6QOcS"
    "wqSZv58/TRqGdfS0BwztOcTBQrypjboCD1pSUh7tZSEIlFHmnlgBB0uKHMl9ijcuFUXCEHT0uXx20Eia+TSkAsdCbpElLXatuKHq"
    "uCBHWYhevfRfkHwtHncOVU5x/l1iP6NoQl0GhBPKEYwdeS3TYEfyhAec3D/Ka95Pom+VH7Bk+Y0kqEgHzdVr8D0GOIburEcBufMf"
    "qJ88VmKtpYdS6RnTSQFUTCfDcjaXGHgNyJAZgsrO0AEIGpFQaustmMRjfE8TppVZOYlPxjMsM5lySTdKCs74VMUeMxLWDETojKmp"
    "FgeHRKL1njyiii8koTCs+x/8Ccw4yEFiEy3/i08SB0YiGpGkPHCDH9h6oEHq7V+RKRstn+LuUQn+SJLpc8TIYtoS5meAvhaOXVIA"
    "u0GHkOX0+7tnzA6vyYUus70TlXpuY3V+mPwMlyZOWcPPVOsQOouaYJzoduovNnD8WexCP6EZV1ED1ONrvOcfNnjMHQc8ZgaVnY1g"
    "3rVXAaDYv9Arx80QVgrAAUBAmYo4ivSCwQ1zTHTmIJO81KIVV0VFhb6dq0BA1kJgKGhfe4cLSsZDBHIsJ3Bo8X4eNb3zCSCyKtNR"
    "xGJOSCi5uEpfIh21vgAFn+jOkSBoqsp+g9292x7nsayIU9TNIYD0dBMYs48eYcn/KSCYa95AO0wVGjfDHS4095grCcJh2MyY+38Z"
    "M2hCQfPR3Djfia3j2V4IitLA4X9oocEBH+qIkWKALxd2mWGUsSXlKUNbnq7i766nbacdCbt29gKASVi+2P9nMWXuhy2HTDy6VVEI"
    "CQ6s4/CCvAueAioRH+Ka4di9dxBbRZXoSygiP+drp8SVKRXNjT0pONjf9pSDpshqzGPKmBjvtVle/cYTSTcM4Fa4TGVDDw2sICIK"
    "S0mRwwun7H73SQSOZNmbzmbuyhkonnaN0H5fKDc+HNykVVjZESf9OKKe3FKFQcAOZ2rlCZZhUw6D3UvDuc1UHEsxSrcb/mD/Dbnq"
    "GrLcW7Uwv6fVqWamVljyA4HPZLj+3NMFktIcsSRoaQQk0HsxUEmQD33q4Wlkzv2ia5XdZEIBQO1fP4zN0NCkD9EaZt7x2Szuv8ma"
    "NhNGtyHgYIErivjpdZwsJn3Ks8oucqpSMHShEpf2BzYE5WPqWFhGJzuIUoWonNBWjyflNAQEAvnMA9RNbZ+U/ecIiFmM21790GTF"
    "OdWrx+wGX47G7vIgM5/vaHvKYA/dXsUNFBSn/QUmoDAgXmAPhiUd6RlGk5c2zmPJ7MCQ8l2qGIcmSpn+wnhS1NZRGyv4MTlxxS+k"
    "Bp41nhXPcxYybEiE6sIwngLQZbqgyUNHehh0xUMS6qBvagJGe66HFxio3wl+BixRIIpiDgSyAeYTUXfxVI1ilr1mjWJWGL6CrZbA"
    "bjKSHsc7gOoqYKRlZ3EZR+b7XUs/b8gZXEbu+gJYW1NkDTje7sck3TSSkD0c7tbt9rfKEZbfOPafiaYiVmWJdojMeO/2oCxlOhBv"
    "DwvieTKfnjFOYJugZYpbjnIdcJDPRBW7BlF/+xWMijWqiqKyXtyefau5qBUk44QEriDixpq+cP7x08q2l7IYoMncmojQg2NASFFW"
    "+ztvCPnAQcYtWueO+1jimsNAJKDjDIU6dkm2fAuWaeJfnP/HJFYacvIYmyo3BAy9/3iPKrhWpz8K2+8qVDub4JnLDvY57A4bGsjB"
    "c1bsj6htBBTNrhdASahlGR+l6pzXKEWeNHZ35ldyTk/2YdAsZi+sQR/hj0qwqcnbs33+7AnWO5y8D8zI4ZXFmfMDYPnJ7cJ1hOOJ"
    "WDY4ubzUV9fSavCI/PMSqAFMlUvfB1fCBcjIkAj628pFSHPrPiYZZrB5eodkw8vYK9wxyCDYpF8LkDoZ8j1kTzar8+KuXGLKRH/j"
    "uAFHL8UgC4XjBrJuIEJz4KIBMZIeoC2NLWWfTJb2gKOt9MyvWKDZnaNnEPMWXOaQThdwR2QE7WB6AQWWfIV+fy8g8juxoUA66zxy"
    "+3SUXVOQGzaGMgmWhELgPX0DgCKufrloUFoGF8RDvVl9x3/jM9fB1knWUHFFPCnEFHcREpcIJTIV11CNweGAIXsqdGi5Rugz6e5P"
    "Wkt6XswG+xlmMhafltl4m9TJDWAmD8XoeFK3Lw4BnUiMYp1RW5hpPVbgyMvUMuGEgn4MsuzHuWtLG7AtFljZj+UPzsBWGCNmEKIX"
    "PoJRWkekGXIg+emtgEYX54RqVUCpZtJtVr8YxxWijN/RKSVFSD+GivHciQZiW1hPoxgGoNd9h0FJMDzFpJzUTJmYgC19VlFbCkkM"
    "0kVgtbJKbLXBcxeLdIFDNgEaQZeumYeZ5j9y0lBVOroJlmD6U4SY+oXEkaiTjaDbUku2JOowUkiQIbAUO3Uq2+KElJp9xTQR06FD"
    "5R7BozGA25SurItPQf0tpSIwvGsM9OBavXovV34TmBlUJVLxcLpiKqjKO2GZH4qLhvMs0Ez1qkyJTAWhaSCiO10IYZ35UcbOD5xd"
    "b4CP316lAYSm2ZU9Gx1n9+BKsoyfazEbfIqECUNOVBTpiOt5Q8bAXCD+mxKBK/Cy6fg5MXFaBM0YJVRWkoWhoQ6MWhZvFw+ZjaM3"
    "VGFZ6tF1NHRFrl5jmD6kJqvzl5Q2hPBLwT5liAMf1rLh75gfH0KH1/D9fqddAkPppXJd9rzSKZjjHWlx4gdqb0ld2ZgK9MpqLlhq"
    "AZsgmY5AmLuqWQ3vgAJsx8QLhspLTzbtSQdrKdmSsNSq59ja9M244hryFDrO3PIcDDQzzwkOMaQ8vO6GH7rKRzNHkWfsI/yX1zKm"
    "NKEsZWL00SM98++Hiy+oqSNpmKLHqFQ+m5GCtJPkMA3N1gKnnqEhpGBAHU79ihv+hEquAlURoM7JD98qFlAUNK3xf8gtUp9L4hvo"
    "4eLEfesZ64DcweVOu8c6oj8UANwEKK1J+pQ9vTk8+rAyGCM9cZzVeCescvBrfX83zd23vgtnQSpyNJPAIvVP85RFaFGUqYFmvjbL"
    "FozDxvcUM6hphn4ZD71Kqk0EGkJbnGn6zNk8fjYf1X2IrtpJHQ9UE8TBxrBgIjaBEpAwKGrOcquf36eO//ryASaWhVuAgLlCiIkm"
    "yKHlp9uAv+GXmP8GHfQzLsApNgKpG45ODk0Z2FPLy+eAUbqkmVdMhaT684jjEul5+NFf6MBf2XxHO6E5NeSN5j8hLmdqwMI0KIU7"
    "uVCUbaQKwRFNDIpfHeCWqigEEemyPn0SWTE3gNtrGJrrsKvVs4kWcFwUE412YuTw55CT+Htjydg8qTCOm7gp1bSmkvJjagFsGm4w"
    "mUxiwnhO7DSNdaKE/2/8/v4/WCzijEBvgkD6sfZdoUK5SNiSEJhJPWsHGoo5NuQMG5YeaWqU/BCrK7tDScyLr4WmORCNNyoBvfRU"
    "rCRpawcApSVebsVZ1ojCKMbKPB9RJnhfDbl+i66ecauRpuKLQuW4sI9UjJQAJ3VMolKurigPZrpVqgDfBIM4XJ+ChWSyIEa4xh+I"
    "oYSb7lKhDPVkpl1mbh0l7YpUJO3yP3TLJ48PtkY3zYofkUZ79/AUTUvbggEA1f7bdOF1+A+RFWyVn7WoPCQxYUFg/d3wgg7WcMC3"
    "1aNFXyN/pTQpdTxHcdvKGZKiX5UKCFMsLBSTTxlxTOKHWloTmhFdKF5XCCGnVD64V4gwkFAFfY8oLphqIdhBbqQOcJawjAN4TB9A"
    "Q5gXlAX2KMhOl8yVPkSGccBJWQOK0IImMNKXkk6psK6hdeVNsGxlH1MwdqbTGfFq8K1oLTGOR+3TW3kGeXLAerbZcYup2YXUzcui"
    "W6HAteEQwd8fchLOnxWzqHT0TlOHJugLhh1HnJviB9B41lX1Gy13FcFStH8DAYwinibztRsqurwwKly+i/0dd3gdwJuNmztCiFZY"
    "QqJ7S/RA6+GfZcLP0TuMLWv2joGYZiYPbeAevVoqyhMFyJ5fRWp3wKp/Hwb/jLUPsPVLQQQC7v0pGKCP/ZZa6pCQGAhFYjnyTLwu"
    "s7ZLdzDnNMs5PAgdp1pKUp2N1eOp4me5ALeGaM7S9QjHmXWqIE0oXHvASxYT2NpRbkrKjB1o5pnrhzLH59mzUPswLaIjsrtREP6z"
    "/MSnWpIA0gTdToZNlkmxc4ynN8vjGmdV/1EFESylc0SrV0rrWdiJlsHeWcPkJZkDCpF55XZgTulwRn3EVOLJ1xpzsGp05GtFQKzY"
    "r8QwjhPJHXZp+Ar7z/7acRn0O4FJ7BwVAV2vH9ZRtRR68YEZgbVsrY0ihhIHTObE81/aDcZesIp8f6o4wDiJjC7+iz9Nn05uAUA0"
    "WeuxwbCR0fl+kIH6djsoJxiL6nqdgCq0hXhB2KmNfYoZuPfV9IGGTHricwJSgpMjludaAZJHMep8AGRho6nCPFVj88Q4VEL6zQQl"
    "/TxcY/NDbRVhkJEtdmXEa4Sc4JVB5nlYYbazQpRCo+otjknrZxIg2aM3ST/FlAlh1U+CDM7fgqly/e+ncLAXFIxoUPCFNkXLtMpF"
    "RYAyVXV9CUzGT9iW1AIiVZeyAMiLF7DUpEXMK+hzkRu18pVFJGSw/9TUdRKFScWA25UQSXVWaYDP6o263s1/5QaJno3ReP1B2Ucj"
    "SVJ1AdUggmaQhB6Igx6upnJWLjeBxbFGfwsv4wRSW4lASIqRff+wKnsvF2XFkBjejR6VfV0Bd2ZzwYnk6Fp5RvjDKfCNwQqZJ292"
    "eJ4/JWrGyZMV6QCzkMg5TRQ7y1becK7fB7f55QQPTY/S14vkakRLXPuHplB3fgnoEAa3XRyMMImygbMv8T+fwh8VnSvhM5PPAkrZ"
    "YwaK9g6Ff9RpZxJhgJEvE/32/zPDFEAEY64oUQYz+IhToQlBhy+sKj1kP1tDsI0FQ25gT4uQ0qU8lStZ7N9qZ0MnkbGor0C48QcK"
    "97JE7CY3MEViOLEhrXqLFfE4xjfzmE/KksLOMSnD/V1+nvLbuqwmOVOfSeR9UvuKzoURTrbKCTBkcfuuxKE6ANxFbA70NJhAr5Qu"
    "3xloJv86USRnrpgXL5CLGSifTrx4a13ISa7A/pvNA2Ki7BJb7ZBtI1dZPiKm8R379+IXevo5cSsYGWJBubypaXSQaGBMug6NbGjR"
    "CVCvU+A3KzBBrA4MeYGGjWyiUVdlNJY7prb4DLmBKAzO/Gg4u3lZk6p4tpqULULNDINcGLP1p2DmhGhm2MqZqa7VOPG8GGwEK+Lv"
    "Kec6O1/9Se5GgiSlJcSOfa66HT4fQMbGG3L/8hvf78Q6Y5gqh36xjZI4zmaEQ13ZN1tszQ1MIuG7z/6GqbTX86IjxIOOTOk4KTcC"
    "w1sR0INDpEiYliDP0IGGfh54W+5jdJlcKgJ/EjoO2LxyOH9U1yJKCa/HnsVoptZxoefu6Qj4tdX2HVJwhDr/REZUKwUxyTjlDGKt"
    "LO/z/foIwvxHjupv0yXH21vDPdTTBByUg927oOvcIZ8HZVpxjqaM2Hkvbf7Oxi+uJ2M5sSMPXt0/IPq8MAR38iJbVh2rrXKK8B0j"
    "mcD8+16qNv8LguX9FS6J31jw3ZrkpgYUnF/lP266ocqBW6Z8kAfBOFqlRDGgDpDCBN06Uo3d6KmtaLBXyFjJz0BvgrEpTxN+hoPr"
    "F7RHA1Z+61M5wz9shIxbq/OJVbFHdTInZpvxY5KxDWkQxj+eJ2fL+yDn32Ei/QCKSG9/kmdzaphq/beAWyk/nGUXKJp+fUoAj3va"
    "k/QpQlqlgbf3D3Fg4wnepAAOmx+zDcAzGYyU6GMRN1WyVu6WKv3HU+0jCVjs+QNoK8cdGFuJK1VH/QQZFOcnydf6dUMzG6SMPyn1"
    "x84Fk5/3EyY20IkGtfw0rrI+SKDlPWRtMif4W9MTYYjpoA+/cLXkKj+PldLGMaj3LeiprMKZRHMsAvOOo2ClE4AAqCcvjzCwaEd3"
    "/gsNDmIIzrmpTmB1FdCRp3Tnr/L8PINWNu7vJNp4/DShdDPhq79BUSnEdBIHYtxHJzHWTQTv/uSoAVYY8y0MwVxu7Cr4EnIjlyhc"
    "fGKS50y8xFPFwiRk27lOxKaF1Y6Iyophx6SMQm2nF7ZI9Clna2Bv2zAQ2447QkHfOHry46vIr2LUuNhqON3GiRLu87zQ5RQFGTyr"
    "qQTw5SpMJJwzQSF8kXWBaITZuXe53Cztsgk/OLGy0ZE6UKDQceXpAVHFxqGBAbLp/UQHXdrBtkErAZAMs6oMqXwqPh8byLFHZobc"
    "sEQ5MRfycfZq7oZjj7xB6CHS1AibUJl92H12+HoBCJpyoTC6g51NzMweOzyPia0oSnn7Ddhlg2oZPF06Ewf1Mo+PK2vrY0uPx9ww"
    "WQHJHAxqFclqJE0+P2otEIvmMf59gz629SNq+mUrCeCOCgrd+IH35m3118Dsl+jubHREMS3LFaDQTqHrRq/BmubO47BhQbGa/Eog"
    "nynsMBDiByl14/McCCrfZ1c5nEab0TNCCZZFjTauDw/J4dmDoKuZ2R06dt0epQ3SXPThYRB0yv73Yqr0gyp1+ZMei+Ldc3GrXIWl"
    "mpz189f/AyyVpFI="
)
_DCOS_B64 = (
    "eNqtXYuWJMlqC/3/R8v2dFdlABKQvd5z7Ds729WVDwKEAHHO//c/fPXTKH94/Q/+28d/P8r/8uHv7+DffhMO+J/u4fyXe/i9Bv7p"
    "8/iv75Cf68bfrh+fJ/hfHgHPf7iE6xG8/gW/D/3f5/BjPfgPdvj+CfDnEvD7DoCXr5Hl1fOPj/97FXj9Wcb7wd8vBe8/heu9o5yJ"
    "F8+Q+HUEf7tgPgcZfz+FvO7pP7rUly4LuG7oxQUguPB/D5Hv7vz7Zc9B+NtNfMyXeP/0PrfBP37yf+/6c3yw9yJQf8AfLuJ6Gnjj"
    "xaAfNv8cUH686H+JA698AO7j/u9rfy0Jb685/iv+du38WNBfHNAvlOL7aI5T0MPaAHH+E/bhKZ/nn1xRfAovjP0kH/YfPl/gxN4K"
    "Gc/w9pn++gyoiIk/WCBfwyD/lZyDONOfcb8LLG6G+dofCILN/X+OPP6ck7D9Ky7cZ8XPeBFIxI/hjQvk924RffD2LOA/IAgWIEre"
    "14UXX8/vI3txfi43Fo/1CyDHmjbicyPYvkL8PYaF08LXh57hATAcqfHXsYZK3nHsle1c3/i32wnYgXiVU/JGkc/zwBsT4mtOAsZN"
    "vfEA2o7xRcPcvYTwP/gXlxeHgBAY+nMAXqLI9H0vEyF80ARfkCqIP4MrD1k5M4ualiYgnOzXnXBrtwgp4NcOVj4Q36+qPucNij4p"
    "bcHq9CA9M+COvQTwxu55/nBu+b18/CEXRgJ7vHMfvrTeV9BBxXEWq5su4+uz+DwKnhOYDL5LAN/xiRTQRWZRbyFpiWcNhsbzLKjv"
    "ZXEPjCa1jfyoycvr7JcavO7T6CdWIByKvxFH1Ii+565YzwDX3ygPIAdsrlxV4ONmPpPKz8fsoX0JAGEvjOePPMB7EM7gDH+h67qi"
    "gMygfQ7AEsHgghKBACRenSHkJGo04y7MYmOA+LlZFNICu1d42y8L8cF3EQSZRXjNw1Rjwnj/8wH7A4353vhPDX38c0nt45DGJLgU"
    "TnDelSR1kvvNybCgIqATqVUZwLzz1b0rFowaE0wPAAqULGJoyl5xGc7q+P7vCcPX9P8DiQmFGvDqANwI4A+FNGbaDvskvnyCfzy+"
    "JZ+bflOmD1AcOCYMF6kG3OiBowP9vxwVIvt7rKq/Bcc5vwFe7HJd7lAQowMKb5EL0ytmwveWwAIIMCaB0G7+4wD4JXGWX8yb//2t"
    "ZOGF9SJbVUYQaNEjw//jn3zw3xAATi5hy5zSvgUW773yHdfDLVf6Y1jvM4l47aMRoieT3iCw50F8STDypetrbEp94Mk1vm/wdmbg"
    "u6P34X2+5POrSjKPog/aI2xyNX5/FfGCQeVDPZ6XZYxv0GWCj1hlfwk2vSs7IpHmgVeYfhV5auZ5H6/FY8C5W6h4IRC8uYnCus4p"
    "IOEZtY8D3aWQrOGO/wJAfw8M70txMMCL8wuVwns8x8+vZ4EQL6EAwxdfhSwOkeKn5kELjfo6Gp7yX+kgWpG4TLGXyew4O2CeI/Ei"
    "Dl4BEMQGiCUPQo/juHdASLUDLCkEPtxtjb2L68f3IaNyYBsAp2MEl9iPwgGEiE+svPbTBnpjFsyNFJfFACHr53ndjCez2TckJJVR"
    "v0v/HuQ3Ik/FN2ZrxmSE6ID5mAW42ISbzMJYQONVu+eJHOSqhtWknHMzHyGI+MsURgTDDIXvgDQBINwQ9fH8SA5wBvEQXXcY2kkh"
    "4F+IfQsKrSu3zgCExXHklI1j9M8MHLbe5zDdLvL52+TfzQOYAyhS+C8AovVjGJ4Klt6Dpu60qgkz9S6m6DFHoPYpcczf8AliJRX/"
    "yz/IuVt3gEFcwVaZLScKoMY4usyKHj88OBGvuaj7k/LLeg7sn6tnk/piEQEE3bXv5KacX+ANrPpfQ1N92wYff1uYs0lW/vUJAj9F"
    "NP4xA8O7AlykgPjKiL5nH5+MjImPWz5C4UDxrpMgzGNsatkxhYsRmUv0RNl6MmbAaBziHQnmA0SNSs7AASF2HYORAsGrImTJIJ+G"
    "PPzlDKHiu876dcnyxfXjD6UbmoCp5nksV1mv/M59sEBwzL5QAOIND12w18yGgZW0uilIril8vrANiQJkWs7T9FhAwhzGTnjMNmrp"
    "0n8ZAF844fti8Umx2Vls7dmh4BJf8lB8kwFH4vX6BdfnwenmK474DYCT98MYUf4wSvhhstoMivYyfgMgZsfHGnvPVJReAHsuKahf"
    "8+fFAJc+zpXXoOp8RAGW7gKg8XUsiJw1uMWGO6AMhSxOGbYf5js8AdyHDsEt+SaOmB5UZ0SE52rov0op4Gt/HweITfi6fhBn1QaI"
    "7/+R+W8WRni1LN0UavgdgwHf+IIyj+vmUSFslrqdZuE8KX1510pV5k4qA9XVE7rcgLvLz807uaGiMmo1cWJOs/JQ/YyhxRn5VGGJ"
    "5QPEzfdh4fsFNEpM4O/5e4Mhb/f9faNYPfybfPo2oXAaxKDmKbYMDA1Nx7MbZRSnG2GWkDv/a108d4dOXtCuCzAXjnBTb6vsKRKX"
    "QAOqN4zkt5ONsxOGJB3PuosbdepAkcm9BbtpTToiUZsAE680zORy4GAn7ocXSK+p86aPn4VGRhygw46Bu7rAvk1xXBxjINs6gpLN"
    "3INwPS0yxeTnMLF3X9fkBKuewoJ7DMeOeRh1dYAjgsRdUm1jEJsKP749EJh80HdYCTmA+gDS9v4tyPFT0ENmjx6CumU+48PCBcAw"
    "XiENJD4nFDTQM4f1FUEmSR4/QALZXRu/BQTjG1Kk8xU/Kaj1Fgf8woX7rqbE2dG8r3oHXaR/y4BfjWwLBu6DPR6Ip9mqRQcJFKhf"
    "vX/odHyhQwRLwH6H8d4J2jAcmdubvDDiUFREH34DA+CiI5eP7nOtiIOQUw3685+BgDrJbQigJ0VuUR7X/QfxBl9WP0LXfiimYwzB"
    "NL7zvvddD9szTJZSGA9AdQMv9xomaFmEZfiOff7w5HJvwZFExKb4RBGe9OfRp07mnOFs+89haOpPTEaHHOVxWEySEyVf6oJDz/1e"
    "oCJ0JDfnn0c0b8b50WUTYGqURLnLLvNrual1+Mi1rqWChmdVFxME7JIort7go3wBM8bNsQNDVeARewiw5A6+OTtubQN48z2ZYomc"
    "C44v5rOlgy4qfof/BBSd/efU2o3XxlPAAFOd1yQBPIX7muTEmKMXJTe8OQPomJWxgSaAPab6Azkm35seDfhXRFeCwpRAYPFWOYlo"
    "sakubMSsavPxcpYc0RIe6oopuV+W0Ow3zncBZDZ5Pia8Zp+YD3ssILC5LsqU9cJALgfHnRlCv2cc34XCXQLQ2c9vm0z8cmTzXgCw"
    "KrwTB9rp3MdviRChh49lKBEevrFUG/hi8FU+9+rBewbs/j6ihNbefaIa+Rf3vpiCkgUAPP8L4/tmZoI7RTyeOx67XjLx61kKP9ip"
    "sMDyyF+zv3pYVgo/jxt7oSGEBo+vmqAhhilwOU/02A1RdoilE+tPjYxjIoWW3fu9MPxTOaGDf/ykKbxpk1QC3wT/Atwvz7Oawiv9"
    "D0xadlxBF4WsdzE0BWCs6Id7ZAbuJbRxn78ZWiU/+cyHYwAOiDiYCOnQvosrcAizEExkPIDgVrEZY+eGmd2Jiep/Wd4/7DAiX6ig"
    "lXEGweDDHaE8iff1/SS6Jjbc+WqGDGxfIDINCFWz+ifTsRBiYFfJsfgh5csI5FUbv4JYJ0KxV8hK/C296CtA2GFmTsFHGvL1HKYg"
    "QjBiVdxxo31+UbTxZo7YZCSSQnLIjHyRQde5XyvmSpfRZ0FZdhE2lo815zlPMXMGq7tD8RQg7haYMYHDaFbtNaDtBeGrzmwUFn75"
    "FFhIpf5mts+cCx2c8Aessm83aFsCSSvDsegMjJ+3mRCjLz5cYfgf88Y1Oh49Am6XrCOP+hvuBVSvHjP7DFp9UBSfyTBW5fqAM0sS"
    "MyHeYsa7G6EJi6dRg5Flr1/8xkizTg4QhsX+SpmhjW3MnEfw7aZDWOG4oJsxwDumpFsc2yWCY7b6pMv0QozjhM/QJeBs/h0hgFlH"
    "ieKHqIKKsd+A+6CQ7zOeOb0DEVJ4NpKoEI6IHTtwPE/yLT2T5TlOHJwRDmDs0erPDGtk4VCB2XV2L3MXaFCxq8Ax9VBu9Mf55W29"
    "ljY7AITuth8dsSV6FEqu3CkowDgn7NyH6D+MOdviASATtwjp+MIyoFDcRKA8WZAGrav2Q7t55JJR2iiZ2Ydq0uhPyxyZktGTmiOw"
    "a2PWYuJ8QUCq1IqbMjJb/hoTcQn+ZYFejhZGy7nLtcr6NdeI1Sd555HeQQmD3PLIJrHaDBHcMCTIWL1yyVQ96+RCAIbpnUYCdSwB"
    "0sHBrkficTI4TjaqVLDSzpZLagPmQHErgJPpe/QJGBTDp2CDLpPQsdr5gmlmkGKdBCl/gMQMtk/vbliP9rdZKsgAYNPcwgsRFKMi"
    "gjd5cVBP55KC/Rlz49XQnGTU0EEeeioWG/4u3QqQtnHYJiPApwesjtoFPSLU+hYs38IF3qk8bIJx1fmMDnUDXUKTKe3sfh98fwuG"
    "YE7AGVAIegPETV1fLTG5hbWnS1FQTXP9yK3hPKWXaqFgC/VpnquC0z/EfmQIAzutl5asZsDoQv0/ZR18zq0dIy0bJ6hw8Gj1n6WP"
    "LCXrS83knQoTERUYYNUNzMFFM3WsQ0iMsEgvtZ0fRlISOHfn1KIFV0i44PSjj+X6ERrR4jauroMgKhWwhvZuvzcrVcgnebmGMMfs"
    "D6LeMqrwjIuXn3e3yCLRUn8cEjCq3JnHUXcihahdHBvw4IhJbAIwh9ujSgtlhFWd4Pf7g3/I55nSDioAgZ83E5LIAJmCVuoniIWz"
    "A1NmMM4Q5cNLsdOte4Rx3FrtlHvBQsn3J+IgKHzZjQanJBLH7a7iucnrzsoYgESY3NwReJILP9f3z4f3y3XDEJMcZ9BwDZMypOQ7"
    "8vFuQ6rt916IH7d/Tb7yIlM4nG4oe5xnwJJ8bFiGhJEARUdk81khxX7BwwtkQZe3pcCtdnGNXWSsQSCSA05AyXqq5S41dP+6oR6v"
    "+jCCbDDbBIyyr1eMtHfcK2uHOeVPjQ+AsVfn2ik/UbD37PGhCg1clwo0TuXIvHz1g6hULdjLmCJT1L+ZEW9eD/PFXz2o6ifeaFld"
    "GgzT85cPFzerTHZNSVQRnRqjmLwNxRpuVgI/+Qg9xLxBNOouIcPUkB7DMR7QdfnzkxMxivjA69+r1rns/+kKeRSXzIpfNjOgvGTM"
    "9vu82/7WENTRwe/HT0PsjukLqOAQWcb8pwTsPFDAJgO9lDKoIwA8jgsYXC/gRZcd8GTNDkPyoFUwoIRK4zLY0mGCOEl6jQAMNQrY"
    "jBJz/kmkGBL2qSMzmvKXPDTrvUP6uRkojmtQhbiykbRk8eaIxa4znLNR/r1uryQ7uK6VrchzBsApWgHHVaBhk+Za0YNn3LGwUf+Q"
    "a9qZWgC9BjmEw4fCHg0JYpZ+c9e9QUz8ycR9siDdIMZz2prm0BmA+JJbKCuXyG72dzwH79byuTfBofU/H8PNaj7XMugGvLUzRPdo"
    "ABrzQTwLLNzascMN+tdnFb03W+BYifXJV7mfwK2vYyqr2yg9ww8zLNQTqOjYSN3NgP6gQaU9o/qrZB9x+V809mvzMw72f7UbiCh1"
    "pj0yQrw9jJOiCjNL+EWmd4TfsMbIyfXEEUz8t4u8RKHkkm2/Oth2C0lRrBEC/nMIAOEPFNe/F6PPXQELCMujyRSOHfg0JPiHAByg"
    "juTa7kVq8A7jWZx3B3TKHG9HwPLsBnAvaoUKHYWlADMK8714iyICK0iCYaqu2E9HZaX3N9s9wt6jkIBSJmv/AhPulcUhqFFA4TN7"
    "YqfnYvlzPaGFXQFG8tZQBLCMtAjD8YwpIG4mY4p9FUUlAbktNP53UrFM0afhD7QcmpYVyNrf+DI5HC0w5X3oF/nxdLWlxQjb7We/"
    "WqV3FehXn24hoZhzmrTGknMCfSNpntwYqy7h4hWZadhIIB8uQHu9oplAut4rLd8/bUg6rsnmETHlGyEiVHptWAkYig65cY0/RLvh"
    "P43wxGssfOv/fJpPgnwaYJ8uKay9TOeIOYWfLeN8lpdVZjb9xolGV+u8hwjO4HbJVyHfhl+5D0UOGDETWQU/dh4ymUakY57cewhA"
    "6HqsW/sXuIoC0nPOXazYP6wGUcjYXm+b/jc2JzUn6kKNnnqgYRU55Fj8NrsC3g2dvoAqwwPqCOLxTZK4ExXzX0/nXZH3P2ThA+5O"
    "0F165Zm1bwtUeQyOgUja2UbqQYRZidCG+ig+PvljPAaEy5KK4/KHCJKRqhnOUsb6AT/p/U8JXHJEjFkNx2p8jxlpjofRPUvCQnT4"
    "LZ8d5Pl1trGgXmVUbCeP2s7j6dWkAXTUNeRQJxUU0WYYTRIYx0ieRbbY+FTIEbg++UMh7lM5Bqvie9bSJcahJ+ZeS57j5jj0fQCJ"
    "aY8joBw3YBFN5SmMnvcDbHcFjgVQoClAoBovtQhmUyOArU9yNXyWFCP5HSEly17rXRbaEHUlgdEc1g3O4BNgFufLMs6BnvW9BYOi"
    "53piK0cNUvrec73ZXd7vnfFj/f60Q1uMvaARbHgDwZOBoIQWroVAKdN7vYbumVUiRWojL8ZcQI6SqbefbQTKIeGJBEXiTx2LFH4C"
    "bBsap8eh+rsOMduOXkRuPg/zTadpSxuhjxpUQcgJXSoTqvclk8ebfPLaTBw3SvzJ/X2bO+KCC2FbqFJtH9+PFr99vE9uJnj+8k9b"
    "aJObrI2CefsCUl6FZjeDFFi5RPtYCMQz1CBWycIKSOJs11ExKh066+XMvvaypc0OTTlDcztjlNZm3Xke28WRNLjq+yOSFz73Qjys"
    "fH8NSdpexAFAr06J9r0lzHX9hd5jy1fGJDgW9Kvo8l/tdp9CWmcE9V33htyyB7nXHQNx9yCl6+Fh3KPN5XPU3ZPZ5j5XKuYZ2viF"
    "hMRCk1fkZTZ6i1J7FIPah4rFTwFXfEB3FZJNJH0bvyjOpXyGkAY5biHbSJz1DTzy82ShlYhhjFVxqgpT2O+9JYvi1mBz/bMu6FWA"
    "fvXK/nFfN+QB9w9P/Th+czXjReFmHng//yaFvmfI/rwOCiiJ27gFqLT7RbVEtWC1pB+S5IF6Ev1B+AiW3YedyiOIIIYJWfkOZjsA"
    "H+8TTZOZoDnLSMSkAcoBnWGkwtFszmbHeKJPp/i7xlQEvofskeI9pEwS63ewDsBFB8m+g3ocL5ESgTQ4QjSy8zT5633C2NMzLfCD"
    "yXlVTqeYvCRYHwoYlM1iSHdD1PUpTzf1WpFfGGDrg1GdB9LHJb3E8kdEFZpxmSjKgKKJGRwlwPP+gFiVMfZH855Rwcw+g0XcipbN"
    "yZpoKqCkCjrWEl6Qj1zUP5t9hR8a3YxVR6bYt22R7RacT+lLMLzX/XPxzCeHxBHmyNBKimCbjxo0oswd8DYBpAjQYFTe4mqKVlYi"
    "feyXc3XITtMRwGI8CLwKOJgZ4X635Uk5qx3iUH/P8v2Ygj9ceu4JOCB2efO37Mfw/Ri3qNE86EbX4Uu2gYNkzh3anYbsXWN4Jhgw"
    "YSRWuXM1E6g1iIn4CNLYBrMq/B/YD36nZ4CGeYdL9+/Jx3kFd3l3nw56XJE9/gDCg4Qy7Jk+5XApP99f9Wo1LFhvr7yp3ai1hLOJ"
    "3YjVcpQ5qmulMvfvH8LDYRyWk3Y9StBMHQn3EpCO+M3BLq5Bvy5HphBg2Z38s/VhKfhO3XrHXfsONZV5x8/x6EAhoHMvkWn1P8XT"
    "oelNt2fEnikX/9Q5yZXH3+unq46jLxkgqXq29yCaTJ+gNONnEaKJIfq5w/fZIHMzFJhqeDDn71A/IGLxGju7ZdHgYwK2D33GhuFn"
    "hU5CVbKNI+zA7C/H5S3gOyiJ5FLPRID6lIADdoxbqo+TrgxEhTVg5vxWbgFF533887Of37r1F2N4Um+K5+6RbtYOQk5gjxIK0Lw7"
    "BoY7Ko7cXRdIS1VQyYr+KVZvY0dLsHArmsAQwoDUP9+1wLHNDr4SMhaEUkE6bu1/2EEwTCDJ04vSDMcx/4kXjPXm2aISmiwVyO+f"
    "OvCwWOyugMQNg0uDH92YONVlUg3yFslWSLr9I2Gldc1LHM8lxaGB5ruF/MsDQzgZnpkSQeZqn2oE9XkMrkeUYRS55Wy6rI5CL2EG"
    "XT4WQNScv69w6TKnfBE/jn9slre6vD8VAV4pfoTkBiZifXbjlESXbmWS4xHWDwBFvGXgX5AEw5jxB2kDghy8L7SYnGCNGDJXlSCX"
    "aA1JA+6bvwoB9BMgSTdQ6hE7AAod8ssuZXa+c7meZzoiZEtaDRN4tR8WSct8c0nMmn48YqZq8e1nQE+CH4UIokwnWjhIQd/BccqD"
    "m8F9JqYJ8LRkOq2AMqA+fuJ3uQw/ReDa88Sda800QLxu2POT3eeNGciVI0ZTbZTFKKxuCKE88m9HGlV6gvDYZK8S50f3YTjzqit6"
    "7r2iPhiqYysFpRA9GzxZpEdhTS2KLbmaBeS011ZB2e2iavmezwXB+Due4zbwoRjK9jGn1PJ6+ozzhmQfP6DjZXUgza7OQjbpt0P5"
    "Jei4OJoGQc96ay4O9lsSPcB6/2SziSlxJWrjy/fzxDLNfaBd4IS7tnOlPv9p27qnl+wdEMLIv5uTPhMQkh1cIAYMjYbs9gu3Jzgq"
    "xjGwIEwSEG/ktPlAQ0YCktbt1TiPqYGiowDuA9zJKCG0GKBGvsHe8kN+u+m0npgC+Tr4V8QOErC6NWQn8hoqe/ffjcyzeHuDLSBV"
    "8U/WAsBwDJjJrELitikqDD++68DFxVkCNg7BqctLAqF04NoltBzJmxH8tmcqFoGWS6hP3kER3AET2YCGDFzEdSTVswhGKGJTSQ4m"
    "ceXoH0oKmkUjcZbc+8mab3D0KkcJJHEfuYGC7nP+IjEWILnIsNkRsOgsNQ4W0x9QpNcVO7hXSpaend11gIbbyBIw0AMYdr0GI4ak"
    "7y09T4Nt1UzDlz6dlrijHjamAobzDt7IdVJw5YdU9fpgXnS+8vMFeVaEdifUZAKibxVsZTTmqVh2E6QK5RTisWnupkonUFwR+lVe"
    "Nk7dEyzLWp/WzlA5Yv3S1DBPTpsY8q5nQXO2E6Rgx7Kpge4+UPL5KVYJ07kJAGUF++XX5mzh0yb1CiBkv4QUMJshLjy0EUO5UxIL"
    "XJGikCppxy0gx9MnWbRsrs43ruoPvb3JVk+1/FJM2TgJKkM+55COx751AC/1IxylygIaWoqXRjC63AQNfBFsaurGfzvVG+M+XYwk"
    "xHE9mniFD918m7HdLwVVNCxFFoode6QSl2AWtGtyv6tPPqeuhWRrsuUn96QvPbjnCpEp1EV45EKCoOimBEvGmVBobDRAagDVM/gc"
    "kb/LH5E7Lc+Rk3CLJOBE9fT7/OD4Bha2+ObNccuVLxZBRQdAUHiEzJ6A6LYVS+SQJTilhAs9lZkkODnY28zX0yoglHlrHjmvdAx7"
    "C7caFtgUMShzWp7FAPauem+np5jiUkooY+DmuATtuBygLR4MdGresduku8i7L6KCl2OZTz/H0k2wRk9ZRR12nYfUT4bhdKIdPsit"
    "NxAj69xo1ZwV0hJ5MILf11wzRtdZEgqM1Bi9ra00HFxf0CflWDU/RoqHBXGZJeKnORzrGKCbj05J/GncFLTTz3105LwLmfeo/+yr"
    "u/UgEGdj+H6WiUaY1PC+5GnLbJN/xdZnsd6cLdJ2lGl8+HknaY9joBEgjmOQfIyM5YFufwQgqND0gECzaw3uGN7eHO8UYMyVYuCO"
    "8lsiV8J9DDl8jrqUgKIhceOLYDqqO2/QZAFtsq5rQhqslWdXyQgaR61AH/B1Ici1BegFIFiDdF2/gkJSCLVrqUoVJq2vXUdx+Lms"
    "ZEETfeqIfVrJ2Np/kfx+0UKBZjKY7fYYBBI3LD7EF56wHpN6LkH9XoMgqSlKyh3dOGFjsZlAY5uT+OZmhdNisR3fJZQPi7Ub/kXn"
    "XU7DvqIeg2sDQecWojHKXRI4dgfQkY1dCX5RpeCyKzGwXvdac2XYSp4ZDvVxdSAuIiKVL9oJesI5RZ6wV10MeVXFCwloOV02RLoU"
    "nknXgFFmAUpLzdTKLTe5bBpAUuiiSMG5fn/niB18EAfFP23K/FwmgYDd4SW2ey8qUEH7GKe2WLU2IGsfRZlkSs4jKcVGA93uB89/"
    "wVY9KOYZCfYvmy/5jNDFmfxugqMcLOglO9C0fHXNWX4WUgvNiemhadRuNuDSpE0crQ3ZTCDf7c/sPOf5Kdu8698/fQZMx44IDFtD"
    "IHM41jvUaLMw3qJGi9qr8wRZOqzuGS68GzuScuwCUecbdrfI2ckEY0MLeSk7NDoLubqsFADQ1rXa+Gjl63GXbnxzGosEpWQxYbhn"
    "njCWgCE3ZZ+Mt2sUBMTI/YWd84uun90BkfQZ69rAYMweakOl3Mw7HBfpe1sKUG0QywUfbL63z8tBaVwMGT3um2Z8GYN5ZtxGzaSi"
    "BXHeOImXq6Di0tr76TNvAdczxRDvv2qj6CRNjON3KzQkM1pOer/DzfpIiAtu+6v0HrewAxYatWJ8F+zPz8zt6S9KAhDbfnjxOGDz"
    "48REsYnf5RwzAstpqJuREL79F+SgNU6RyjiI6qFWghej/ZSwZ/gHdFzynQPiOI0ksQEIo3/l0FWemgepmAzICTMqRpO51ik2boV6"
    "alJwaMBTKC+L3KOdwghTs6hLmOIug50L1V+rAWweLLrneRinezBMApyE+bMCB3aHu7YxcYEkinJho6BRLIC53GFXCOhudR5Ta4z4"
    "iRN30LCsxf6p1ffUHKpXz94umNMFBMF73apACKiSxmmI1hGjVNNWwDATPFiEFZ2mfp8/+98JQ6P/y36xXKIClnQaHX0iQ0Dagey7"
    "zPOZQ2xA6s7PZmfkfYbhy6ABJ1FmGc0OFQacwUpCplRDJ1A4af/NQUTYEm7iOM4haXb57+cN6Y/oYt86fLGM6l3IqHQ15yn0QpmX"
    "eclPA9imo8kyA56Pgk47ru8GPYfZOmYsgDIcuPzEV7/GgiNrwmeHclkulfyi1Yrfyl+zDXdH5yNlCMeM85ET+8X/sPJCRHEdJk/J"
    "KjWwzA1U8x4cPUB0N/B0xULG7Bal63SnshM5TIjNrOaQ4WzoUvUIFeJOpjpRJAhtvqz8D+YWm7FSzBcO9F6sgPv9wWQXgS2g8vhc"
    "vUUkd/oYhWrArLrfVx9vXuFA4Xwo8gzd5wuZxDCrGD5mUVdwYDy6nYvQ/ksyuywr1M6En2BtlSE7PJ4AjC59HpWF0xt9fgFb9SeO"
    "dkU/n2kGY3DzHFT8ignhFTlULNfMj0A9UjazholsQH5Qhfqg2w39XAlcWsYut2tIqLTDBJZsR5N+eNpJDoeBwqPBoYI7asCyQLT4"
    "lTzDC/7kH3A7nOkREcejxjbxOHf9h7b2ABNHFX9SBVuiu45OWxGx2X490mL+9WplFdU2lCcXzwe7slcK/Oh9ZaZ7BpXsEyyJhvBB"
    "aYxNfVnzlwzU2Y9Qe1JK/LJ60wQiJBritxfPSAijjhpCIn0zQTB1iLLJf0IGrJEE7wYOU8CKdp7kmMYJaqvBGvA88bIBt3aylUZH"
    "eaQTeUF//YbKY0FE32wgyz9KzTM6ITBo3GukHXNxQGau1Mf3BtOdggPPvIL4u4KGi3ckqDRy4O/cIuvXK2+hT/RXvhdNw/LzFq46"
    "AW7e0ym48OvwoeWHXfc2hgYVlLlbzNQmM86/qoKQPbDwgiw5CEJOlYF6E2rmdZ2sZZxsQaoIYsnjmk4tnlY7RTcRHDHad6aIjo5Y"
    "pc16k75EHR5mgyEQFt9gYHch0RrNkibWAiuw9tgnKXheDTwNCx6kdcNWwq4B0BJi6fPW/fCzpwRNuMZcxIMe4zmfTrB0fiwkiwpQ"
    "x77gRDIZc/30ElJGEIRttzgeE2r7rSsuk+l9tAtlE/EwwbvCH2iPKfwPwbjWmDz0AlGUQLK0hBLQB/RXtIR+b0V7aN7VVOEIBDry"
    "LwjiOvx0nxu4V4y+xV/RAez5lPD3n1hS+KK7gQqSXS83J3R8a2Nc7HbmGTrrftqXxiGcl+inoXYGYS8J1SZi1QG/HqKw5f5Y5Rlx"
    "U5Ow+5F2/UL4rjBjK20iJkBCJ06LfuOYbMgbbgUt7JMdQSL5Nc1G85zLnTkJjSTHyYZo7fRBKixEA5NE4M+EkPQfmH4pTDV4i2Lm"
    "uhnYlfHKTF7NOHyyjdxfLFNKGDTEp50o6RofTXSWOiJW1ZNgBkDqu/x6Zt+/QVE3EMMoQAsxjVPoioopY0jbZrMIB1754lh8wNrk"
    "6CpxjdJi4NjgCKRtg6U757O8QN8rIFkb+uL7b+D4lO71AAYyjIHK0uxclSDSXvREFYxzvYCgvntOt0bPfmds/majI8OL5jA0ymoJ"
    "4aWoVmoVlRgKum+siJOuqiDO2zPiIFpmqOsSoanom+RCHyo6vQ8RxROwGRooqFIZ+H1rMIGsFLd4CXOhGfaLBQXBLX8awGCOazwf"
    "CPfPJvdtmp8aJ3KFNL5xqoDZjPqQXHoDyJvjixNrCiMQ4MdJ5b7k9HDIhkSiLUg8vkNPkCPkQDS2IMkjnNNJoqE8ZJjqB4r5Y7FC"
    "h88Th37MQJe/25chItaAv6v/ZWmncTJD+c8w0CjLgZS1CKoJ5j6/VkEPFXBB4E86eopJ8AD3CrHr3UMDKsbUXS51ajNcWtyGstHd"
    "pa2tVhQ0MorQaRogOY6rqr35ZJ7TxKlTrL5qmvYCng2/ZSODa6CqjRKIQ0Scs21Xuy/zNLAiDPGZfr/5UxWA9x+wcE0osCb/e3WL"
    "LCPrAGxb1bqvNeezCYeyqxY19AonwOnDl7IKJbnAS2aTNruC7tWxFodqnWRGwVx06oHnNNq8wenUbrigQLlK4Uqi9AyQzlyBU6D6"
    "GYCfFOAErXmiAr2Sx9AMMR4pvUF4kfTXLhoIpxU2EIjYr+GByQKrLMtAxGQhNTS07I5+pXeqkCBLoQRENG5GMyWpVCMWlfqZpLbR"
    "SUBgKhMoahGmB4BHCv5ILOu1h50y562+CeUExKa5hVeyVQQ5ENMlNSwAF8qFhsRUIcjKeTLsx+V3AABt7nkVNCLiwFSg+HZt847W"
    "t/36EXK/whrzdPxuswCmaAE58xD9l65BQZofFH8CP1MoJa54rPD49+VcrZGp5IsTOsm7AYp/+KaqzVHRZFQnGLp/vjYga7EPms2p"
    "jxMWczZi6siiAtU9egNbikLhPNIwbk+a8n+m53jv3S3LmZvzb3Qu6OIOJ5a7LETIfTaONqds/qS5pZiuIlOtZAr9hnK7MD9doxrg"
    "RWG/3eJKt5guAAwFRaxqqKZ2DP39zW/6FAoJsUkCm9HVk6QPY4zCpACm1Mh34OmhB4FpTV2HveqaiqeZk6bsgW+WQupiza0pTce/"
    "3TujtC0MrwBSSZQfCddmgCNm/HcDaVjBCU1qfJ+OZBX4RHS+VMBlAnRd+YJxf2D1v5AuB6wy2eItMTF7yvSQs/qC3aj2r0PlOVNW"
    "P/rp8Zh2JRFkYu1uQGbvhKDcjNcowhvR1yYILaRtICskinmFYMPoVFDOWdo1RXfeZZws6RHEsDJU4mxFNtUt60aPheUs99i9FrnW"
    "HY9BRKaRwYcCUhehMPd8fopMIOvnK6rc3sz123l2z+wSjsIOJXLEj0jFSxMecwMuig3B5lloqMkrfaSJ/zBjfXXtKjVzwMIEXU4Y"
    "6AUGj9pLabXjKNXmc2kQ7oeN0dqZA8QGsNFFtvqaqtYpE7Eo5DYpbTOoZEMHrq/vhNuA9yWRiLzLAeHaqZzntJkw+l5I8B6ugy6t"
    "QUvA86lhUfl5Npo79JuzyiA7Kt3lake4e08nDT5ImP7bwOoLOE48NTUgolu3185VFGJXHiBvBbQYmrEEWV8YExNta4tGY/ckBSYO"
    "t9cjmLIqsY7VsrhGv176U/2m72GLlRBMZSr1KkbxbtUHmPN1/KjsD3vQhpy4+aDrcWI/dzEK410ApmtAp8ZmVyMvsUAyslIAV0Ze"
    "5vOw3dWfXPFx4jTll0+omDeCQaKBc4b2QFlK4IntOMZJfhseUbOKeL7dLNjlYUWYYJ6QgqWPNZ5jmjJwJMg3F02/jA35CLV3mI5j"
    "0FMq5yIdLWCFw0/i2XLSSv3ybU2R7Fls8QyQtgNcbMgl1NmpyvVSWiEo00ydxdA5TrZ5QOLCowjztcKnaFLUay7eGJvO67I7h4Yn"
    "V953eIPjGrQkd4YyQIOmUiz0yqkZOkVsQpYqRWTTCojDzkAenGkAjVYp8rYjUC0B58iklBGTfunL3YCfxYFL+UqseIbQOPxD6l0H"
    "CJwR88t2V82/ulwZljCLLw7lWMKWxFWWgJLA+QbIaMpoyqH6QeW9P/kgAGiBDsbUCzydzvjXSUOXR9AOMKAra/CqQjaaqh/lDueg"
    "6b0VvBuu3LpP1BheNSuGSrURKP430yAoNEK/YhrX2qcKxfJy+HEZCvtE9hwlyFr4S8g9JTqPpK+wKGUl0xKiu+OofxrSMFJFi0dO"
    "kaLO7EkojKTAwhY2w5IV4EBgUG3QveueHPhhXclrEjGeuy+FnW3a8TYeIazNBqd3fNYdTmUfc+sBag6DRebFdLBtT3EsMC70S++m"
    "GdBF+QIwO650tweuqj7SNtDTkSa1gZ7qgvhwI1qyxnPXt5mCNlqaAm7tx0GBrd32gbpf8sEdOFmeAd0KxjMuDaZKyG1kRakhBPmj"
    "gQTB2bSFJ5bDwe+Fsd9Rs8EplcgZ90gm3+ZFDjUzxmD/OX9DP5Weq4HyBcIAZb1El6dxL/IWuvfX2g3VkihK/WXavXZssPbnS54R"
    "HiSVlrrKiP2vSyBC0TXwiNdqlT81OJYJbBQNUF56TWm8mNTkeR2zi/w7d+8vb3vOmSXMmrztKc3nL9IbOC19QK9Ulj58c0M504ba"
    "mSANj3X8Wl0ozALD6pBQCrtcA4r7/cOnQGDWjYnPeYDPj0uhsfT6oLNATbuPkr6Cn+YV74A4iP4hhkxjQxd/3o19NfkNLE2DGLZc"
    "QkzMrQCQ26Dpp8uERTA/eaEC3WSw7v39KlADLtOj5Zcdsa/Y9sg74lIQp1/hVMzrC/noD2hmZ/S7iwMg0wqqTl2fVkEr0rZYR9vT"
    "AVsUFNLvD1fIogzcS6FpouzHoTKMNNJCNCy8tEo1pDDcRrcBCGNliQlixQIEUxsZjWtX/EXZ8TZ10PwOcP1lL0FfOWRdUa86cW/8"
    "Mv3zKauh0Byz+fFTrSGfngRjMsowXLPrZ4CF9tkENXOkjJXq/MearFyXxcgf4ox7OKvzaBXsYWMXArNLO1ArOvBgk4rR2aRkKgqD"
    "6/mUq8zBAVTaNItt7u5zYFD1qdJ/xP8+t4w6PhS6ewNY6eBjQ/tJG3pZO6yjOqIfABqSgq7f9ve0knV/QdnHILAzgrIt/amc+9/k"
    "7BtiIkCTPVybgTquDlQzms+zRFw2WY3XjwMy+DI7odwxI7kEUMuYTYfZx3gKQ8JLdhrssj0I1JkrHHRw91LTIbu+PvT1Jt/mIfq3"
    "RF8MjvC137lHp0EJtqC0IUkSaEVYJ5EXB40dPNCb7Nv+ZZY5VmTm2tafm2o0hnV/J2orCcVjCq9KR+89Uuvfq7/EYYUDoLGdSYx8"
    "XGfmEjU7xaubAhAyeTpVhhimDc9D10CNUIIPA79osmA4FrfWYimURORblAsCvzJ4NExxpvDjNxDNssdj07U1n3Uuha5xPy6G3dXQ"
    "k9v98PdKjyhlob2KSFphem1wwJoxD53ejkdbOUpEnZSslEIN0RRFoUdvMJ1hPuxS1Ti5FOAw5ln4pkT0mRmUj7m7eKF/aSo15Yn/"
    "awAAZefQLVFfFgKw+BUOZCWzuypJiDhBz7K7IFiJDHKVwaDjgzMBtlMxugkL3omCHwAMWNpOEHsBAUKWrBlyWlj1i3uCkSx7BN1F"
    "LeRL0flfqgb41eqBdANx6IOCaAVlAz9EcIX89SK1CcPKcgNWamBz2aAdnRcbWDsyG7lOMrY+ntt08gYJJN4NnKWTcPIk5LUbktMa"
    "F0kygDpYzNpiqxiczAm2YZJZgA6pe5JipBZFiLrCZwaSta7A1fm8pjVoCuAYuWen5TiJXX4TcyLRuPRkVzQW09vMeobqxlYoULKl"
    "OTPXYnxPbNRNZKFeInubNVuamZ7T4zBAURLkp14rBC9wTP0QHUWHgkDhGUBNVXyPQo0/rB3l8alcfBYkS1nh/hHfclXEXQXLRrGv"
    "Av9fh3hpAS9wuh7lvoIDYPJvaRjyew9sOCD2oGjcoRaD4a0L4b62qlcl/EUVr7OXpALl4ejY4ploHUOsJhwhiUtoMIrtKbkvED63"
    "bQYAeVAmpjCnZ6zwDRbg9Mna8aoen/MNsmlghwk4uyLH+F+yOIdtgTOiQdVI2BFhFMTJZ4DmZ4uQFuYy3AT1c4PHXpIGuDXtoIKP"
    "/gLG3EMIkOTONpbIH/k3ge9wkegpWYur/7T4FKVqIcPSExoLh9WZV8xAS+0xiuh9Jf8vbguj0AcHmaFWwbUxT2qgYX89o3k8hrNo"
    "cTxpDRrN919qJejya87Viwe2dqO2Tfvp5im3U2y0MwgpH8P8Tl0bI382MGFOgFgaRuIGjbUOIvK5FLNiee0OA2dAvWMowtuaD1BU"
    "/H/v3x6AsOFRSAex4X/FYEd9FTj382f7xn5f+bPCiyDWJxtBebJBLw58I77/qmCtWTAXnvvSJrNCE+Seum4AEDoFmJUNkKsDUkz2"
    "KT+OQ0f9O6mjdYAngw0jXRgP7Hjl7vpvGRW39kg3cGHOIBDB1Pz4FMdrpF0K3ohLVy91kjpUxujThlrNUiaDdSFP9AF036H/hprx"
    "aBPI9Jt+B6gErkORHkFZf3tvcHOvjQrA8o6vFAMExL1WWlAgdRMWgXR7qYjO1texwUz8tpgOCqROFU29R4zcP5CKHLefj9sfrAAs"
    "y2pafBP8uy0DdEEbZ5ysrOiPReaEFVZ0DZCfYoHCN7zYTJy+2Cf+xs6ZzwEU6PjPXlSLA7PM4RZoIZVG9j/sIP6Y+jZ6TEkEELoy"
    "49K/hEMe17BSsA8XEXRoGFYIW5mQsMYFfgAi5Gn0FCD67P/sdyZSzb+L0SwVbbC0/7AUwSmetHkL6cc80EP+3rB4nIBLR+Bg3uDG"
    "+pSBltc4fU8S19teqiUdm/ZrIt8p+wh+ia2ZtRqbtQEPx+9UjA0F7PeJGZpZrJCnqjtA2TxFAzK6BkRcxKkp0mJMHRpejCZ+hMl3"
    "1lQVk1ugDk2l6QCNYyp+SzBfsBMiQgiFLONeBT+Q0k/xa2ksYgDzvhueKJqfDkjVi6AIt+2kQWTbe124yH+AEVL9VNchyyV5MA2F"
    "z/BuOHSif3uU4TiW9GEeu39pkb+ozsSv2xIDgJVzRE7rB3zzi+mB7AzqeAWt+qJtb0PYIDL3yRUMiif/gaTnTSrKOvcEDg2Y99xD"
    "ni6izSLCHCvT9srTMYuNpkVZNkPURgI2VkMIfcxZMsLhgJA/Agu7zVVpodLIHkLkaCIZREizoZo7giz5Tdn8tRiW2l/AJqP9AJ3c"
    "hWjoaUhgGJXA88aV2q5UVxjDr46wLY892x1xVVxQpBoQYSoYv/mNl0U8fTkxV9chLZcOKl2lvoQCmqg15ZsUqX+cTKeFB2grI9UX"
    "qLiglQP1Yy1taP8GmAHFHTtAntSJSYmqzculUqiesmhGOcBQ0E+DmC4tobc1DhBqUqpaNmGmZeBYrgkiFC/n7R+zmEI0pIk+MIuv"
    "cX9/7yBFIM7+So5WcWR3mvonI+Q02bbZoFgIV/LN9mfnSOrcnsCfBJrcmuW19PpdCM+qXRsnU7dfIrSblOIRfVoITTAoNn2tdjJF"
    "OJPDRpcz4LcQNtXMI6h8DSxPnTtOMC8w7+oqPP0GysrKfza4ffeelgHyKnRGH1bqBtiBNHuIBO/2KKjhXGAo07jU1uuAb8mN/YnD"
    "9IClAMJjOCDWfc2FgULqhUFpaydHufGmikkIkuM7WsA7ZPYs25S55QIYIxs985fhnd7ibl1IpX5X1GgjA/bYvxfVDO92htfy6zp2"
    "yXiLMIQH6KXw3FqoZuFY0hGzdQ4FA+oKJFXqHLOO7wAtzXBTAu3nFm4+6w0830YBLLT/m1dIO9LCCS6hlm+VkGvGaQbx0tipaUGM"
    "pVyaSbhwsUV1J42A5W0sSh8BsM/xDr9oIOcHOVCIhpZ0TB1/nMt939JlaQAXKyim0jg8e4I5uSwZKDAIG6o1DfDrvSAoY6bRa0zb"
    "sr20BoQrS4JcyAxbvx7NvXlyZKWYkgxp1mDDuyh4+YuorpvQVKAbrGPSDnjUXkqOwiu0ylET18abGjAbthSlb4OwWcXMmMSNfm0W"
    "Iew2xnY/AAwzlk16bnFBIZ9mmERdA4feml6BH6FNOL6TaYAP3dLZWrtj1PtEBK91NCAfGxrPZGAZshQ+5X2rwqpS4oTjT+SmSxQ5"
    "QIwOgh+8VmGGL78VPooMmkv8fl4OUNYNyixvwa8fNhsxLed5s1OoOYlffI8iNtZOo0ZQ+rARVvz0Glm+m7gQvp+rfoUiiko9PH3L"
    "fpkid9akg+ujCTYwSTmmLyUGSo4mATlygJkua4HHG3DcoWQI44M2pUkmxQB6/9e0T2CgdGVzSRZ7ZSjreP3dPMV/DTBrGcpc7VSY"
    "8fFlFPGvFSE5UVRjSmNS2wIlly6glczKDRyC+EJ47j7W/l3HFxibRAQhhC4ka1Lie4m4BoByj4Q5G7SZ+WJh2zligxBqAYJarPNi"
    "qZjsh1UamjRoLW/gbLnnz5FhMR/R/0N7UMn8eOM2jNqCypx+8sLBPEPjAyUBW8jU4D/bOXi03kmvA8ZhuVLcfRymAMKhOVSOymE0"
    "vWrdnwFmnLrmYhqNvPAfYJZFihSk1uZ/Nv1hHB92EBdW3gn+cYStOaXbBxuW9hnerhpLYvdvoQMZ0qOdHKYKujiT2P+99U71Rtge"
    "m1DvabdxUMeCKjhofLVqYAWjJ7bwGNOQTtwh+LD/SSPcKaYJq1hsTU7ENP1pBqpHYd1gD1NaaKR4qrYCxZsFp4OHS1MvZwL/1kPS"
    "DfZoLQJ2bjNdoDSm7OcbASWR6ZeXDIpvdt93M/qXL6eBo/RoDJ8BYIgzpYfzzAZdV383AIjFS0ltxA8TpTu8asukZ5COTwVwelXA"
    "y3to8EVJb7UAh6GIDQnz5LZFMTHkPv8xfZg1ws/iGKFrizv/3SxHszNtqHf8VSxCGdBUMXDQN4AMjWG6hPyMDKNm2HWAEcx7avr5"
    "AD0A/JAlapsq7957ugHGfrTy5h9g43P3skwYqU0WYNk/xYD/Vfqg9l/SUqxQXbjwT4ENRZuIbbeewA1gp95HzGvOyvzNVwAWwZOy"
    "bs9yeMR2HSpSjEqXU9R+VBxks9n9foNJuDdej67IofZMSeIYrThMwPqODG+nyJ+LhCIOdPymOooboQaq4Re4LERpY/rFRnYLmCJL"
    "mV+878s1Aio8aZk4HMwBNoFhknl2OCO1jtQBiJnv/aZKOU6bAdYm5LK8WKYDgJpvxaf0gLbK37PBIrKB65gNH67HOPQ9xAHUfNlY"
    "1KIQFDOa6S/SyGQL/g8SQUllHNC2216dKXCJT/GfY1dGSBtU/cW3bT6JP8ehCcZ3W/j8nxU1A6V/i1UAblFvZapacSrVs4FTV4WY"
    "qhBRi0J4uQIhM4KFP23WAidvklTxdUimot2MmTvwBQWwoMuqmslxP0q1zA+K0azMgGTMyMTGpZUwmL1lW8jANcmC4wcgi26yoLNR"
    "G/D1hoUr37+kY9jt/qTxsUWzEhIC0PAGOdWhP5JRKTC2b/StCzXT1cdMDvA+fkCH++icSf8IUxEbNS5S+I9mt7cjNNwQMx3ivLfr"
    "yvkhTCVlF6mZmEnGDl9MQLUy9tqyaRwQQuIkAAIaaKbEn3GOlWTIAnJM3/FtkuSxq3KpR6NREjhqMs7YH5nhCCuaSvQS2mEYsW9x"
    "HEActodCDLCx4jA7riI34IiCKkTWrT8Q5BNFm+cw9AvHBtN+v21gh6DYKM5JtW/jaKASg7RYFAPVMAugJcJDGIsc4J51wBBv0XMf"
    "lfOHDdwQzQtO0I8jzURRAxJmERrRiCbT1T0795Pc678YDpdARzwIqweGkq0LHIQwo2arENUljTccstraN46LOPUS7I4VKYpMNAp6"
    "rKc/bxdR4qulpqE4DDy5F44ftKrVhWe5jRfQQhH/RGi5YhQSOjziyHSj4z7+eKYyYGmgoXlh/telC09nm/9lmbfuMx1wSOESfoAp"
    "kZzcbDKuPJEYPkkGF0+ZxqXZ4zuWFMGRYE5cElLULpxWhtBNz5ONrYewSVGl+xfk8mW1vAQUXAWEc1kCuR4DQaW5Ir97xnZ5ikgb"
    "Wv0MPeCBYJnsC7BBZQ9UCC1mLjDCkMHBhVyswQ/eI1KCNfYAIQJRUHuK7nJCc57y/wB7DYaOGZ3+m9w3dGTFhVq4oICrhB9YLIQS"
    "gDcbWo53QjhR8xwq/nSlWtTlW+z3j/stMZGLgCdRtESDvlLx30KZ/l4iQifgrx6cgdQQ49m0A6/1pvDZz7Pa+DNtIM+d7HW7qthw"
    "POTvXVKjmPm40+1rIUDdaPWpfww1UrHVHLq7iJUYmtqzacjh04THpgAG90xTWNaeCT5Un9gHdgtAypkaMxVZz4gTkDKbcXKKvBRA"
    "PM2qbswbfUPxbO4yjM94SJamNon7z0IAsOsfLadXdAsTY6Og/3wWIAAVkA5C8rkf0EuuhnGDa4Igfx5dpGjX0cHjBxrNMLXSBakt"
    "DwrGKZjdhy5IenXckKo3sqcCDKX/FPBbzfCzjOfYbElhC3LDTS8HxY0ADE4ejbJkCizRJ/A4BDtwjthuqZMJVNTKjmn0bX05I05n"
    "1CsGKaLUHPm6XwVfSQT6/Ctwr9Sj1swZqfH6OasSmUwXBKDEo6oAmWEM+ZsdtT3ewxIxvcmNhtEALkqBx6/YsgI2rtk81d9SfAoq"
    "grduCzUspREQ+/p9+uSkO7/UzAtIx8+zGXJ9aunMV8KQ0iLFnKAfadNsn51Hi7/WgSLn2BhZGyaZJqQTx+JjqY5ZQkA0C2woDdul"
    "xRsm6+FwtuyXToK1TK/KHwOnAV37+LwOIL7r6mHR1jbQpp6XEeDS1KL3v7n55TEbyD1GMCoW16haCxJ/K+sQ9F4ZvIXcSXg/6P8B"
    "yCu2hw=="
)
